# version 03
# baseline (speedup 1.0000x reference)
import os
import dataclasses as _dc
from contextlib import ExitStack
import numpy as np
import ml_dtypes
import concourse.bass as bass
import concourse.mybir as mybir
from concourse.bass_utils import run_bass_kernel_spmd

F32 = mybir.dt.float32
BF16 = mybir.dt.bfloat16
AF = mybir.ActivationFunctionType
ALU = mybir.AluOpType
AX = mybir.AxisListType

T = 2048
NT = 16
D = 1024
MAGIC = 12582912.0
DEV_STAGE = os.environ.get("KDEV_STAGE", "full")
DEV_TAP = os.environ.get("KDEV_TAP", "")
ARENA_N = 85000


def _prod(xs):
    r = 1
    for x in xs:
        r *= x
    return r


class Sem:
    def __init__(self, nc, ctx, name):
        self.h = ctx.enter_context(nc.semaphore(name))
        self.n = 0

    def inc(self, instr, k=1):
        instr.then_inc(self.h, k)
        self.n += k
        return instr

    def wait(self, eng, val=None):
        eng.wait_ge(self.h, self.n if val is None else val)


class Arena:
    def __init__(self, k, n):
        self.t = k.nc.sbuf_tensor("ARENA", [128, n], BF16)
        self.th = k.ctx.enter_context(self.t)
        self.off = 0
        self.n = n
        self.stack = []

    def bf(self, *shape):
        m = _prod(shape)
        assert self.off + m <= self.n, f"arena overflow: {self.off}+{m} > {self.n}"
        ap = self.th[:, self.off:self.off + m]
        self.off += m
        if len(shape) > 1:
            names = " ".join(f"d{i}" for i in range(len(shape)))
            kw = {f"d{i}": s for i, s in enumerate(shape)}
            ap = ap.rearrange(f"p ({names}) -> p {names}", **kw)
        return ap

    def f32(self, *shape):
        m = _prod(shape)
        if self.off % 2:
            self.off += 1
        assert self.off + 2 * m <= self.n, f"arena overflow f32: {self.off}+{2*m} > {self.n}"
        ap = self.th[:, self.off:self.off + 2 * m].bitcast(F32)
        self.off += 2 * m
        if len(shape) > 1:
            names = " ".join(f"d{i}" for i in range(len(shape)))
            kw = {f"d{i}": s for i, s in enumerate(shape)}
            ap = ap.rearrange(f"p ({names}) -> p {names}", **kw)
        return ap

    def mark(self):
        self.stack.append(self.off)

    def release(self):
        self.off = self.stack.pop()


class KB:
    def __init__(self, nc, ctx):
        self.nc = nc
        self.ctx = ctx
        self.te, self.ve, self.se, self.ge, self.sy = \
            nc.tensor, nc.vector, nc.scalar, nc.gpsimd, nc.sync
        self.s_ld = Sem(nc, ctx, "s_ld")
        self.s_mm = Sem(nc, ctx, "s_mm")
        self.s_v = Sem(nc, ctx, "s_v")
        self.s_s = Sem(nc, ctx, "s_s")
        self.s_g = Sem(nc, ctx, "s_g")
        self.s_st = Sem(nc, ctx, "s_st")
        self.s_x = Sem(nc, ctx, "s_x")
        self.s_la = Sem(nc, ctx, "s_la")
        self.s_lb = Sem(nc, ctx, "s_lb")
        self.s_lg = Sem(nc, ctx, "s_lg")
        self.s_lh = Sem(nc, ctx, "s_lh")
        self.s_lc = Sem(nc, ctx, "s_lc")
        self.s_dmp = Sem(nc, ctx, "s_dmp")

    def sb(self, name, shape, dtype):
        return self.ctx.enter_context(self.nc.sbuf_tensor(name, shape, dtype))

    def ps(self, name, shape):
        return self.ctx.enter_context(self.nc.psum_tensor(name, shape, F32))


def dump_fm(k, buf, out_h, nt):
    full_barrier(k)
    ov = out_h.rearrange("(a p) f -> p a f", p=128)
    W = buf.shape[-1]
    for ft in range(nt):
        for hf in range(2):
            lo = 1024 * hf
            hi = min(lo + 1024, W)
            if lo >= hi:
                continue
            k.s_dmp.inc(k.ge.dma_start(out=ov[:, 2 * ft + hf, 0:hi - lo],
                                      in_=buf[:, ft, lo:hi]), 16)
    k.s_dmp.wait(k.ge)
    k.s_dmp.wait(k.sy)


def full_barrier(k, engines="tvsg", dma=True):
    s = k.s_x
    if "t" in engines:
        s.inc(k.te.drain())
    if "v" in engines:
        s.inc(k.ve.drain())
    if "s" in engines:
        s.inc(k.se.drain())
    if "g" in engines:
        s.inc(k.ge.drain())
    for e in (k.te, k.ve, k.se, k.ge, k.sy):
        e.wait_ge(s.h, s.n)
        if dma and engines == "tvsg":
            for sem in (k.s_ld, k.s_la, k.s_lb, k.s_lg, k.s_lh, k.s_lc, k.s_st, k.s_dmp):
                sem.wait(e)


def build_constants(k):
    k.mcol = k.sb("c_mcol", [128, 1], F32)
    k.rc6 = k.sb("c_rc6", [128, 1], F32)
    k.rc12 = k.sb("c_rc12", [128, 1], F32)
    k.idf = k.sb("c_idf", [128, 128], F32)
    k.idb = k.sb("c_idb", [128, 128], BF16)
    k.ones1b = k.sb("c_ones1b", [128, 128], BF16)
    k.M01 = k.sb("c_m01", [128, 128], BF16)
    k.P32 = k.sb("c_p32", [128, 128], BF16)
    k.P4 = k.sb("c_p4", [128, 128], BF16)
    k.qbm = k.sb("qbm", [128, 176], F32)
    k.qib6 = k.sb("qib6", [128, 176], F32)
    k.qbm12 = k.sb("qbm12", [128, 176], BF16)
    k.stats = k.sb("stats", [128, 48], F32)
    k.smallc = k.sb("smallc", [128, 16], F32)
    v, g = k.ve, k.ge
    v.memset(k.mcol[:], 1.0)
    v.drain()
    v.tensor_scalar(k.mcol[:], k.mcol[:], 2.0 ** -30, None, op0=ALU.add)
    v.drain()
    v.tensor_scalar(k.mcol[:], k.mcol[:], -1.0, None, op0=ALU.add)
    v.drain()
    v.tensor_scalar(k.mcol[:], k.mcol[:], 1.5 * 2.0 ** 82, MAGIC,
                    op0=ALU.mult, op1=ALU.add)
    v.memset(k.rc6[:], 6.0)
    v.memset(k.rc12[:], 12.0)
    v.memset(k.ones1b[:], 1.0)
    v.drain()
    v.reciprocal(k.rc6[:], k.rc6[:])
    v.reciprocal(k.rc12[:], k.rc12[:])
    k.s_g.inc(g.iota(k.idf[:], pattern=[[1, 128]], base=0, channel_multiplier=-1,
                     allow_small_or_imprecise_dtypes=True))
    k.s_g.wait(v)
    v.tensor_scalar(k.idb[:], k.idf[:], 0.0, None, op0=ALU.is_equal)
    v.tensor_scalar(k.M01[:], k.idf[:], 0.0, None, op0=ALU.is_ge)
    v.drain()
    k.s_v.inc(v.tensor_scalar(k.idf[:], k.idf[:], 0.0, None, op0=ALU.is_equal))
    DIFq = k.qA[:, 0:128]
    CM32 = k.qB[:, 0:128]
    SC2 = k.qC[:, 4:260]
    g.iota(DIFq, pattern=[[1, 128]], base=0, channel_multiplier=-1,
           allow_small_or_imprecise_dtypes=True)
    g.iota(CM32, pattern=[[0, 4], [1, 32]], base=0, channel_multiplier=0,
           allow_small_or_imprecise_dtypes=True)
    k.s_g.inc(g.drain())
    k.s_g.wait(v)
    k.pcol = k.sb("c_pcol", [128, 1], F32)
    k.pm16 = k.sb("c_pm16", [128, 1], F32)
    k.altc = k.sb("c_altc", [128, 1], F32)
    v.tensor_scalar(k.pcol[:], DIFq[:, 0:1], -1.0, None, op0=ALU.mult)
    v.drain()
    v.tensor_scalar(k.pm16[:], k.pcol[:], -7.5, 0.0625, op0=ALU.add, op1=ALU.mult)
    v.drain()
    v.tensor_scalar(k.pm16[:], k.pm16[:], k.mcol[:, 0:1], k.mcol[:, 0:1],
                    op0=ALU.add, op1=ALU.subtract)
    v.drain()
    v.scalar_tensor_tensor(k.pm16[:], k.pm16[:], -16.0, k.pcol[:],
                           op0=ALU.mult, op1=ALU.add)
    v.drain()
    v.tensor_scalar(k.altc[:], k.pcol[:], -15.5, 0.03125, op0=ALU.add, op1=ALU.mult)
    v.drain()
    v.tensor_scalar(k.altc[:], k.altc[:], k.mcol[:, 0:1], k.mcol[:, 0:1],
                    op0=ALU.add, op1=ALU.subtract)
    v.drain()
    v.scalar_tensor_tensor(k.altc[:], k.altc[:], -32.0, k.pcol[:],
                           op0=ALU.mult, op1=ALU.add)
    v.drain()
    v.tensor_scalar(k.altc[:], k.altc[:], 16.0, None, op0=ALU.is_ge)
    v.drain()
    v.tensor_scalar(k.altc[:], k.altc[:], -2.0, 1.0, op0=ALU.mult, op1=ALU.add)
    v.drain()
    CMOD = k.qA[:, 128:256]
    g.iota(CMOD, pattern=[[0, 4], [1, 32]], base=0, channel_multiplier=0,
           allow_small_or_imprecise_dtypes=True)
    k.s_g.inc(g.drain())
    k.s_g.wait(v)
    v.tensor_scalar(k.P32[:], DIFq, 16.0, None, op0=ALU.is_equal)
    v.tensor_scalar(SC2[:, 0:128], DIFq, -16.0, None, op0=ALU.is_equal)
    v.tensor_scalar(SC2[:, 128:256], CMOD, 16.0, None, op0=ALU.is_ge)
    v.drain()
    v.tensor_tensor(k.P32[:], k.P32[:], SC2[:, 128:256], op=ALU.mult)
    v.drain()
    v.tensor_scalar(SC2[:, 128:256], SC2[:, 128:256], -1.0, 1.0, op0=ALU.mult, op1=ALU.add)
    v.drain()
    v.tensor_tensor(SC2[:, 0:128], SC2[:, 0:128], SC2[:, 128:256], op=ALU.mult)
    v.drain()
    v.tensor_tensor(k.P32[:], k.P32[:], SC2[:, 0:128], op=ALU.add)
    v.drain()
    v.tensor_scalar(k.P32[:], k.P32[:], k.altc[:, 0:1], None, op0=ALU.mult)
    v.drain()
    v.tensor_scalar(SC2[:, 0:128], DIFq, 0.0, None, op0=ALU.is_le)
    v.tensor_scalar(SC2[:, 128:256], DIFq, -32.0, None, op0=ALU.is_gt)
    v.drain()
    v.tensor_scalar(CM32, CM32, 0.0, None, op0=ALU.is_equal)
    v.tensor_tensor(k.P4[:], SC2[:, 0:128], SC2[:, 128:256], op=ALU.mult)
    v.drain()
    k.s_v.inc(v.tensor_tensor(k.P4[:], k.P4[:], CM32, op=ALU.mult))
    ln4, pi4, twopi, i2pi, hpi, sig, pip, pin = (k.smallc[:, i:i + 1] for i in range(8))
    v.memset(ln4, 10000.0)
    v.memset(pi4, 1.0)
    v.memset(sig, 32.0)
    k.s_v.inc(v.drain())
    k.s_v.wait(k.se)
    k.se.activation(ln4, ln4, AF.Ln)
    k.se.activation(pi4, pi4, AF.Arctan)
    k.s_s.inc(k.se.drain())
    k.s_s.wait(v)
    v.tensor_scalar(twopi, pi4, 8.0, None, op0=ALU.mult)
    v.tensor_scalar(hpi, pi4, 2.0, None, op0=ALU.mult)
    v.tensor_scalar(pip, pi4, 4.0, None, op0=ALU.mult)
    v.tensor_scalar(pin, pi4, -4.0, None, op0=ALU.mult)
    v.reciprocal(sig, sig)
    v.drain()
    v.reciprocal(i2pi, twopi)
    k.s_v.inc(v.drain())
    k.s_v.wait(k.se)
    k.s_s.inc(k.se.activation(sig, sig, AF.Sqrt))
    k.ln4, k.pi4, k.twopi, k.i2pi, k.hpi, k.sig = ln4, pi4, twopi, i2pi, hpi, sig
    k.pip, k.pin = pip, pin


def emit_quant(k, S, Q, ge_q=True):
    v, g = k.ve, k.ge
    W = S.shape[-1]
    nb = W // 16
    A, B, C = k.qA, k.qB, k.qC
    Ab, Bb, Cb = k.qAb, k.qBb, k.qCb
    bm, ib6, bm12 = k.qbm, k.qib6, k.qbm12
    S3 = S.rearrange("p (a b) -> p a b", b=16)

    def vstep(fn):
        v.wait_ge(k.s_g.h, k.s_g.n)
        k.s_v.inc(fn())

    def gstep(fn):
        g.wait_ge(k.s_v.h, k.s_v.n)
        k.s_g.inc(fn())

    vstep(lambda: v.tensor_reduce(bm[:, 0:nb], S3, axis=AX.X, op=ALU.max,
                                  apply_absolute_value=True))
    gstep(lambda: g.tensor_scalar(bm[:, 0:nb], bm[:, 0:nb], 6e-12, None, op0=ALU.max))
    vstep(lambda: v.reciprocal(ib6[:, 0:nb], bm[:, 0:nb]))
    gstep(lambda: g.tensor_scalar(bm12[:, 0:nb], bm[:, 0:nb], 1.0 / 6.0, None, op0=ALU.mult))
    ib6b = ib6[:, 0:nb].unsqueeze(2).broadcast_to((128, nb, 16))
    vstep(lambda: v.scalar_tensor_tensor(A[:, 0:W].rearrange("p (a b) -> p a b", b=16),
                                         S3, 6.0, ib6b, op0=ALU.mult, op1=ALU.mult))
    gstep(lambda: g.tensor_scalar(B[:, 0:W], A[:, 0:W], -2.0, 2.0, op0=ALU.max, op1=ALU.min))
    vstep(lambda: v.tensor_scalar(C[:, 0:W], A[:, 0:W], -4.0, 4.0, op0=ALU.max, op1=ALU.min))
    gstep(lambda: g.tensor_tensor(C[:, 0:W], C[:, 0:W], A[:, 0:W], op=ALU.add))
    vstep(lambda: v.scalar_tensor_tensor(A[:, 0:W], C[:, 0:W], 0.5, B[:, 0:W],
                                         op0=ALU.mult, op1=ALU.add))
    gstep(lambda: g.memset(k.smallc[:, 14:15], 0.0))
    z = Bb[:, 0:W]
    vstep(lambda: v.tensor_scalar(z, A[:, 0:W], k.mcol[:, 0:1], k.mcol[:, 0:1],
                                  op0=ALU.add, op1=ALU.subtract))
    m1 = Cb[:, 0:W]
    c6 = Ab[:, 0:W]
    z2 = Ab[:, W:2 * W]
    s1 = Bb[:, W:2 * W]
    gstep(lambda: g.tensor_scalar(m1, z, 0.5, -2.0, op0=ALU.mult, op1=ALU.max))
    vstep(lambda: v.tensor_scalar(c6, z, -6.0, 6.0, op0=ALU.max, op1=ALU.min))
    gstep(lambda: g.tensor_scalar(z2, z, 2.0, None, op0=ALU.mult))
    vstep(lambda: v.tensor_scalar(m1, m1, 2.0, None, op0=ALU.min))
    gstep(lambda: g.tensor_tensor(s1, m1, c6, op=ALU.add))
    psi = Cb[:, 0:W]
    vstep(lambda: v.tensor_tensor(psi, z2, s1, op=ALU.subtract))
    bm12b = bm12[:, 0:nb].unsqueeze(2).broadcast_to((128, nb, 16))
    Q3 = Q.rearrange("p (a b) -> p a b", b=16)
    psi3 = psi.rearrange("p (a b) -> p a b", b=16)
    gstep(lambda: g.tensor_tensor(Q3, psi3, bm12b, op=ALU.mult))


def emit_transposes(k, pairs, identity):
    ve = k.ve
    if not hasattr(k, "_tp_i"):
        k._tp_i = 0
    for (src, dst) in pairs:
        k.te.wait_ge(k.s_v.h, max(0, k.s_v.n - 3))
        pv = k.psA[:, k._tp_i % 4, :].bitcast(BF16)[:, 0:dst.shape[-1]]
        k.s_mm.inc(k.te.transpose(pv, src, identity))
        k.s_mm.wait(ve)
        k.s_v.inc(ve.tensor_copy(dst, pv))
        k._tp_i += 1


def emit_ln_stats(k, xsrc_hbm, XTa, SCR, nt=NT):
    v, s, sy = k.ve, k.se, k.sy
    ssum = k.stats[:, 0:nt]
    ssq = k.stats[:, nt:2 * nt]
    inv = k.stats[:, 2 * nt:3 * nt]
    xr = xsrc_hbm.rearrange("(j p) f -> p j f", p=128)
    sems = (k.s_la, k.s_lb)
    for j in range(min(2, nt)):
        sems[j % 2].inc(sy.dma_start(out=XTa[:, j % 2, :], in_=xr[:, j, :]), 16)
    for j in range(nt):
        xt = XTa[:, j % 2, :]
        sems[j % 2].wait(v)
        sems[j % 2].wait(s)
        v.tensor_reduce(ssum[:, j:j + 1], xt, axis=AX.X, op=ALU.add)
        s.activation(SCR[:, (j % 2) * D:(j % 2) * D + D], xt, AF.Square,
                     accum_out=ssq[:, j:j + 1])
        if j + 2 < nt:
            k.s_s.inc(s.drain())
            k.s_v.inc(v.drain())
            k.s_s.wait(sy)
            k.s_v.wait(sy)
            sems[j % 2].inc(sy.dma_start(out=XTa[:, j % 2, :], in_=xr[:, j + 2, :]), 16)
    k.s_s.inc(s.drain())
    k.s_s.wait(v)
    v.drain()
    v.tensor_scalar(ssum[:], ssum[:], 1.0 / D, None, op0=ALU.mult)
    v.drain()
    v.tensor_tensor(inv[:], ssum[:], ssum[:], op=ALU.mult)
    v.tensor_scalar(ssq[:], ssq[:], 1.0 / D, None, op0=ALU.mult)
    v.drain()
    v.tensor_tensor(ssq[:], ssq[:], inv[:], op=ALU.subtract)
    v.drain()
    v.tensor_scalar(ssq[:], ssq[:], 1e-5, None, op0=ALU.add)
    v.drain()
    v.reciprocal(ssq[:], ssq[:])
    k.s_v.inc(v.drain())
    k.s_v.wait(s)
    s.activation(inv[:], ssq[:], AF.Sqrt)
    k.s_s.inc(s.drain())
    k.s_s.wait(v)
    k.s_s.wait(sy)
    k.s_v.wait(sy)



def emit_l0(k, h, out_h):
    te, ve, se, ge, sy = k.te, k.ve, k.se, k.ge, k.sy
    ar = k.ar

    XQF = k.arXQF
    ar.mark()
    FM0 = ar.bf(8, T)
    ar.mark()
    XTa = ar.f32(2, D)
    XNT = ar.f32(D)
    XB = ar.bf(D)
    XQT = ar.bf(D)
    WLD = ar.f32(2816)
    WQB = ar.bf(2816)

    emit_ln_stats(k, h["x_in"], XTa, k.qA[:, 0:2 * D])

    xr = h["x_in"].rearrange("(j p) f -> p j f", p=128)
    outr = out_h.rearrange("(j p) f -> p j f", p=128)
    ssum = k.stats[:, 0:NT]
    inv = k.stats[:, 2 * NT:3 * NT]
    sems = (k.s_la, k.s_lb)
    for j in range(min(2, NT)):
        sems[j % 2].inc(sy.dma_start(out=XTa[:, j % 2, :], in_=xr[:, j, :]), 16)
    for j in range(NT):
        xt = XTa[:, j % 2, :]
        sems[j % 2].wait(ve)
        k.s_st.wait(ve)
        k.s_s.wait(ve)
        ve.tensor_scalar(XNT, xt, ssum[:, j:j + 1], inv[:, j:j + 1],
                         op0=ALU.subtract, op1=ALU.mult)
        k.s_v.inc(ve.drain())
        if j + 2 < NT:
            k.s_v.wait(sy)
            sems[j % 2].inc(sy.dma_start(out=XTa[:, j % 2, :], in_=xr[:, j + 2, :]), 16)
        if DEV_STAGE == "ln":
            k.s_v.wait(sy)
            k.s_st.inc(sy.dma_start(out=outr[:, j, :], in_=XNT), 16)
            continue
        k.s_v.wait(se)
        k.s_s.inc(se.activation(XB, XNT, AF.Copy, bias=0.0))
        emit_quant(k, XNT, XQT)
        if DEV_STAGE == "xq":
            k.s_g.wait(ge)
            k.s_lg.inc(ge.dma_start(out=outr[:, j, :], in_=XQT), 16)
            k.s_lg.wait(ve)
            continue
        k.s_s.wait(te)
        k.s_v.wait(te)
        k.s_g.wait(te)
        pairs = [(XB[:, 128 * fb:128 * fb + 128], FM0[:, fb, 128 * j:128 * j + 128])
                 for fb in range(8)]
        pairs += [(XQT[:, 128 * fb:128 * fb + 128], XQF[:, fb, 128 * j:128 * j + 128])
                  for fb in range(8)]
        emit_transposes(k, pairs, k.idb[:])
        k.s_mm.wait(se)
        k.s_mm.wait(ve)
        k.s_mm.wait(ge)

    if DEV_STAGE in ("ln", "xq"):
        k.s_st.wait(sy)
        k.s_lg.wait(sy)
        return

    emit_l0_wquant(k, h, WLD, WQB, XTa.rearrange("p a b -> p (a b)").bitcast(BF16)[:, 0:2816])
    full_barrier(k)
    ar.release()
    emit_l0_attn(k, h, FM0, out_h)


def emit_l0_attn(k, h, FM0, out_h):
    te, ve, se, ge, sy = k.te, k.ve, k.se, k.ge, k.sy
    ar = k.ar
    Q = k.arQ
    VT = k.arVT
    K2 = k.arK2

    ar.mark()
    WQ = ar.bf(8, D)
    WKV = ar.bf(8, 512)
    VB = ar.bf(2, T)
    COSA = ar.bf(T)
    SINA = ar.bf(T)
    COSQ = ar.bf(T)
    SINQ = ar.bf(T)
    invc = ar.f32(1)

    k.s_lg.inc(ge.dma_start(out=WQ, in_=h["wqT"].rearrange("(a p) o -> p a o", p=128)), 16)
    k.s_lg.inc(ge.dma_start(out=WKV, in_=h["wkvT"].rearrange("(a p) o -> p a o", p=128)), 16)

    invf = invc
    ve.tensor_scalar(invf, k.pm16[:], -0.0625, None, op0=ALU.mult)
    ve.drain()
    ve.tensor_scalar(invf, invf, k.ln4, None, op0=ALU.mult)
    k.s_v.inc(ve.drain())
    k.s_v.wait(se)
    k.s_s.inc(se.activation(invf, invf, AF.Exp))

    TGRID = k.qA[:, 0:T]
    ARG = k.qB[:, 0:T]
    KQ = k.qC[:, 0:T]
    k.s_g.inc(ge.iota(TGRID, pattern=[[1, T]], base=0, channel_multiplier=0,
                      allow_small_or_imprecise_dtypes=True))
    k.s_g.wait(ve)
    k.s_s.wait(ve)

    def build_trig(dst, use_hpi):
        if use_hpi:
            ve.tensor_scalar(ARG, TGRID, invf, k.hpi, op0=ALU.mult, op1=ALU.add)
        else:
            ve.tensor_scalar(ARG, TGRID, invf, None, op0=ALU.mult)
        ve.drain()
        ve.tensor_scalar(KQ, ARG, k.i2pi, None, op0=ALU.mult)
        ve.drain()
        ve.tensor_scalar(KQ, KQ, k.mcol[:, 0:1], k.mcol[:, 0:1], op0=ALU.add, op1=ALU.subtract)
        ve.drain()
        ve.scalar_tensor_tensor(KQ, KQ, k.twopi, ARG, op0=ALU.mult, op1=ALU.subtract)
        ve.drain()
        ve.tensor_scalar(KQ, KQ, k.pin, k.pip, op0=ALU.max, op1=ALU.min)
        k.s_v.inc(ve.drain())
        k.s_v.wait(se)
        k.s_s.inc(se.activation(KQ, KQ, AF.Sin))
        k.s_s.wait(ve)
        ve.tensor_scalar(dst, KQ, -1.0, None, op0=ALU.mult)
        k.s_v.inc(ve.drain())

    build_trig(COSA, True)
    build_trig(SINA, False)
    k.s_v.wait(ve)
    ve.tensor_scalar(COSQ, COSA, k.sig, None, op0=ALU.mult)
    k.s_v.inc(ve.tensor_scalar(SINQ, SINA, k.sig, None, op0=ALU.mult))

    k.s_ld.wait(te)
    k.s_lg.wait(te)
    k.s_v.wait(te)

    def proj(WT, nout, dst, woff=0):
        for oc in range(nout):
            for tc in range(4):
                pb = k.psA[:, tc % 4, :]
                te.wait_ge(k.s_s.h, max(0, k.s_s.n - 3))
                for ic in range(8):
                    mm = te.matmul(pb, WT[:, ic, 128 * (oc + woff):128 * (oc + woff) + 128],
                                   FM0[:, ic, 512 * tc:512 * tc + 512],
                                   start=(ic == 0), stop=(ic == 7))
                k.s_mm.inc(mm)
                k.s_mm.wait(se)
                k.s_s.inc(se.activation(dst[:, oc, 512 * tc:512 * tc + 512], pb,
                                        AF.Copy, bias=0.0))

    proj(WQ, 8, Q)
    if DEV_TAP == "qpre":
        dump_fm(k, Q, k.out_h, 8)
        return
    proj(WKV, 2, K2)
    k.s_v.wait(ve)
    ve.memset(VT[:], 1.0)
    k.s_v.inc(ve.drain())
    proj(WKV, 2, VB.rearrange("p a t -> p a t"), woff=2)
    k.s_s.wait(te)
    k.s_v.wait(te)
    base_v = k.s_v.n
    i = 0
    for hk in range(4):
        fr, ft = (64 * hk) % 128, (64 * hk) // 128
        for j in range(NT):
            if i >= 4:
                te.wait_ge(k.s_v.h, base_v + i - 3)
            pv = k.psA[:, i % 4, :].bitcast(BF16)[:, 0:64]
            idsl = k.idb[fr:fr + 64, fr:fr + 64]
            k.s_mm.inc(te.transpose(pv, VB[fr:fr + 64, ft, 128 * j:128 * j + 128], idsl))
            k.s_mm.wait(ve)
            k.s_v.inc(ve.tensor_copy(VT[:, j, hk, 0:64], pv))
            i += 1

    RT1 = k.qA[:].bitcast(BF16)[:, 0:T]
    full_barrier(k, "tvs")

    def rope(SRC, n_tiles, COS, SIN):
        for ft in range(n_tiles):
            k.s_v.wait(te)
            for c in range(4):
                k.s_mm.inc(te.matmul(k.psB[:, c, :], k.P32[:], SRC[:, ft, 512 * c:512 * c + 512],
                                     start=True, stop=True))
            ve.tensor_tensor(RT1, SRC[:, ft, :], COS, op=ALU.mult)
            k.s_mm.wait(ve)
            for c in range(4):
                ve.tensor_tensor(k.psB[:, c, :], k.psB[:, c, :],
                                 SIN[:, 512 * c:512 * c + 512], op=ALU.mult)
            ve.drain()
            for c in range(4):
                ve.tensor_tensor(SRC[:, ft, 512 * c:512 * c + 512],
                                 RT1[:, 512 * c:512 * c + 512], k.psB[:, c, :], op=ALU.add)
            k.s_v.inc(ve.drain())

    rope(Q, 8, COSQ, SINQ)
    if DEV_TAP == "qrope":
        dump_fm(k, Q, k.out_h, 8)
        return
    rope(K2, 2, COSA, SINA)
    if DEV_TAP == "krope":
        dump_fm(k, K2, k.out_h, 2)
        return
    full_barrier(k, "tv")
    ar.release()
    emit_l0_scores(k, h, Q, K2, VT, out_h)


def emit_l0_scores(k, h, Q, K2, VT, out_h):
    nc = k.nc
    te, ve, se, ge, sy = k.te, k.ve, k.se, k.ge, k.sy
    ar = k.ar
    P4t = k.P4

    ar.mark()
    KARR = ar.bf(4, T)
    BT = ar.bf(T)
    UT = ar.bf(4, 512)
    OE = ar.bf(4, 512)
    OTT = ar.bf(4, 4, 72)
    YC = ar.bf(4, 2, 64)
    WROW = ar.bf(64)
    RC = ar.f32(24)
    KMG = ar.f32(2)
    KMC = ar.f32(4)
    NLAM = ar.f32(16)
    LAMC = ar.f32(1)

    k.s_v.wait(ve)
    for v_ in range(4):
        src = K2[64 * (v_ % 2):64 * (v_ % 2) + 64, v_ // 2, :]
        ve.tensor_copy(KARR[0:64, v_, :], src)
        ve.tensor_copy(KARR[64:128, v_, :], src)
    k.s_v.inc(ve.drain())

    lam_s = nc.dram_tensor("lam_scr", [16, 1], F32, kind="Internal").ap()
    km_s = nc.dram_tensor("km_scr", [128, 2], F32, kind="Internal").ap()
    k.s_ld.inc(sy.dma_start(out=LAMC[0:16, :], in_=h["lam"][:, :]), 16)
    k.s_ld.wait(se)
    se.activation(LAMC[0:16, :], LAMC[0:16, :], AF.Sigmoid)
    k.s_s.inc(se.drain())
    k.s_s.wait(ve)
    k.s_v.inc(ve.tensor_scalar(LAMC[0:16, :], LAMC[0:16, :], -1.0, None, op0=ALU.mult))
    k.s_v.wait(sy)
    k.s_st.inc(sy.dma_start(out=lam_s[:, :], in_=LAMC[0:16, :]), 16)
    k.s_st.wait(sy)
    nl_in = _dc.replace(lam_s, offset=0, ap=[[0, 128], [1, 16]])
    k.s_ld.inc(sy.dma_start(out=NLAM, in_=nl_in), 16)

    SQB = k.qA[:].bitcast(BF16)[:, 0:T]
    BTS = k.qB[:, 0:T]
    for kt2 in range(2):
        k.s_v.wait(se)
        k.s_g.wait(se)
        k.s_s.inc(se.activation(SQB, K2[:, kt2, :], AF.Square))
        k.s_s.wait(te)
        for cc in range(4):
            k.s_mm.inc(te.matmul(k.psB[:, cc, :], P4t[:], SQB[:, 512 * cc:512 * cc + 512],
                                 start=True, stop=True))
        k.s_mm.wait(ve)
        for cc in range(4):
            ve.tensor_reduce(KMG[:, kt2:kt2 + 1] if cc == 0 else BTS[:, cc:cc + 1],
                             k.psB[:, cc, :], axis=AX.X, op=ALU.max)
        ve.drain()
        ve.tensor_tensor(KMG[:, kt2:kt2 + 1], KMG[:, kt2:kt2 + 1], BTS[:, 1:2], op=ALU.max)
        ve.drain()
        ve.tensor_tensor(BTS[:, 2:3], BTS[:, 2:3], BTS[:, 3:4], op=ALU.max)
        ve.drain()
        ve.tensor_tensor(KMG[:, kt2:kt2 + 1], KMG[:, kt2:kt2 + 1], BTS[:, 2:3], op=ALU.max)
        k.s_v.inc(ve.drain())
    k.s_v.wait(se)
    k.s_s.inc(se.activation(KMG, KMG, AF.Sqrt))
    k.s_s.wait(sy)
    k.s_st.inc(sy.dma_start(out=km_s[:, :], in_=KMG), 16)
    k.s_st.wait(sy)
    for kv in range(4):
        for g4 in range(4):
            off = 128 * (kv % 2) + kv // 2 + 64 * (g4 % 2)
            src = _dc.replace(km_s, offset=off, ap=[[0, 32], [1, 1]])
            k.s_ld.inc(sy.dma_start(out=KMC[32 * g4:32 * g4 + 32, kv:kv + 1], in_=src), 16)

    k.s_lg.inc(ge.dma_start(out=WROW, in_=h["rmsw"][:, :]), 16)

    ones1b = k.ones1b
    M01 = k.M01
    k.s_ld.wait(ve)
    k.s_lg.wait(ve)
    k.s_ld.wait(te)
    for qt in range(8):
        kvv = qt // 2
        k.s_v.wait(se)
        k.s_g.wait(se)
        k.s_s.inc(se.activation(SQB, Q[:, qt, :], AF.Square))
        k.s_s.wait(te)
        k.s_v.wait(te)
        for cc in range(4):
            k.s_mm.inc(te.matmul(k.psB[:, cc, :], P4t[:], SQB[:, 512 * cc:512 * cc + 512],
                                 start=True, stop=True))
        k.s_mm.wait(se)
        for cc in range(4):
            se.activation(BTS[:, 512 * cc:512 * cc + 512], k.psB[:, cc, :], AF.Sqrt)
        k.s_s.inc(se.drain())
        k.s_s.wait(ve)
        k.s_v.inc(ve.tensor_scalar(BT, BTS, KMC[:, kvv:kvv + 1], -1.0,
                                   op0=ALU.mult, op1=ALU.mult))
        if DEV_TAP == "bt" and qt == 0:
            dump_fm(k, BT.unsqueeze(1), k.out_h, 1)
            return
        k.s_v.wait(te)

        for c in range(4):
            nkt = 4 * c + 4
            k.s_v.wait(te)
            for kt in range(nkt):
                masked = kt >= 4 * c
                off = 128 * (kt - 4 * c) if masked else 0
                w = 512 - off
                te.wait_ge(k.s_s.h, k.s_s.n)
                for jj in range(4):
                    te.matmul(k.psA[:, jj, off:512], ones1b[32 * jj:32 * jj + 1, 0:128],
                              BT[32 * jj:32 * jj + 1, 512 * c + off:512 * c + 512],
                              start=True, stop=False, tile_position=(32 * jj, 0))
                    mm = te.matmul(k.psA[:, jj, off:512],
                                   KARR[32 * jj:32 * jj + 32, kvv, 128 * kt:128 * kt + 128],
                                   Q[32 * jj:32 * jj + 32, qt, 512 * c + off:512 * c + 512],
                                   start=False, stop=True, tile_position=(32 * jj, 0))
                    k.s_mm.inc(mm)
                k.s_mm.wait(se)
                k.s_v.wait(se)
                for jj in range(4):
                    se.activation(UT[:, jj, off:512], k.psA[:, jj, off:512], AF.Exp)
                k.s_s.inc(se.drain())
                if masked:
                    k.s_s.wait(ve)
                    uview = UT[:, :, off:off + 128]
                    m01b = M01[:].unsqueeze(1).broadcast_to((128, 4, 128))
                    k.s_v.inc(ve.tensor_tensor(uview, uview, m01b, op=ALU.mult))
                if DEV_TAP == "ut" and qt == 0 and c == 3 and kt == 0:
                    full_barrier(k)
                    dump_fm(k, UT.rearrange("p a b -> p (a b)").unsqueeze(1), k.out_h, 1)
                    return
                k.s_s.wait(te)
                if masked:
                    k.s_v.wait(te)
                for jj in range(4):
                    k.s_mm.inc(te.matmul(k.psB[0:65, jj, off:512], VT[:, kt, kvv, 0:65],
                                         UT[:, jj, off:512], start=(kt == 0), stop=(kt == nkt - 1)))
            k.s_mm.wait(ve)
            for jj in range(4):
                ve.tensor_copy(OE[0:65, jj, :], k.psB[0:65, jj, :])
            k.s_v.inc(ve.drain())
            k.s_v.wait(te)
            i = 0
            for cc in range(4):
                for jj in range(4):
                    if i >= 4:
                        te.wait_ge(k.s_v.h, k.s_v.n - 3)
                    pv = k.psA[:, i % 4, :].bitcast(BF16)[:, 0:65]
                    k.s_mm.inc(te.transpose(pv, OE[0:65, jj, 128 * cc:128 * cc + 128],
                                            k.idb[0:65, 0:65]))
                    k.s_mm.wait(ve)
                    k.s_v.inc(ve.tensor_copy(OTT[:, cc, jj, 0:65], pv))
                    i += 1
            if DEV_TAP == "ott" and qt == 0 and c == 0:
                full_barrier(k)
                dump_fm(k, OTT.rearrange("p a b c -> p (a b c)").unsqueeze(1), k.out_h, 1)
                return
            ve.drain()
            ve.tensor_copy(RC[:, 0:16].rearrange("p (b a) -> p b a", b=4),
                           OTT[:, :, :, 64:65].rearrange("p a b 1 -> p a b").transpose([0, 2, 1]))
            ve.drain()
            ve.reciprocal(RC[:, 0:16], RC[:, 0:16])
            ve.drain()
            for hh2 in range(2):
                hglob = 2 * qt + hh2
                j1, j2 = 2 * hh2, 2 * hh2 + 1
                r1 = RC[:, 4 * j1:4 * j1 + 4].unsqueeze(2).broadcast_to((128, 4, 64))
                ve.tensor_tensor(YC[:, :, hh2, :], OTT[:, :, j1, 0:64], r1, op=ALU.mult)
                ve.tensor_scalar(RC[:, 16 + 4 * hh2:20 + 4 * hh2], RC[:, 4 * j2:4 * j2 + 4],
                                 NLAM[:, hglob:hglob + 1], None, op0=ALU.mult)
            ve.drain()
            for hh2 in range(2):
                j2 = 2 * hh2 + 1
                c2b = RC[:, 16 + 4 * hh2:20 + 4 * hh2].unsqueeze(2).broadcast_to((128, 4, 64))
                ve.tensor_tensor(OTT[:, :, j2, 0:64], OTT[:, :, j2, 0:64], c2b, op=ALU.mult)
            ve.drain()
            ve.tensor_tensor(YC[:], YC[:], OTT[:, :, 1::2, 0:64], op=ALU.add)
            k.s_v.inc(ve.drain())
            if DEV_TAP == "yc" and qt == 0 and c == 0:
                full_barrier(k)
                dump_fm(k, YC.rearrange("p a b c -> p (a b c)").unsqueeze(1), k.out_h, 1)
                return
            if DEV_TAP == "rc" and qt == 0 and c == 0:
                full_barrier(k)
                dump_fm(k, RC.bitcast(BF16)[:, 0:48].unsqueeze(1), k.out_h, 1)
                return
            k.s_v.wait(se)
            se.activation(BTS[:, 0:512], YC[:].rearrange("p a b d -> p (a b d)"), AF.Square)
            k.s_s.inc(se.drain())
            k.s_s.wait(ve)
            ve.tensor_reduce(RC[:, 0:8], BTS[:, 0:512].rearrange("p (a d) -> p a d", d=64),
                             axis=AX.X, op=ALU.add)
            ve.drain()
            ve.tensor_scalar(RC[:, 0:8], RC[:, 0:8], 1.0 / 64.0, 1e-6, op0=ALU.mult, op1=ALU.add)
            ve.drain()
            ve.reciprocal(RC[:, 0:8], RC[:, 0:8])
            k.s_v.inc(ve.drain())
            k.s_v.wait(se)
            se.activation(RC[:, 0:8], RC[:, 0:8], AF.Sqrt)
            k.s_s.inc(se.drain())
            k.s_s.wait(ve)
            rib = RC[:, 0:8].rearrange("p (a b) -> p a b", b=2).unsqueeze(3) \
                .broadcast_to((128, 4, 2, 64))
            ve.tensor_tensor(YC[:], YC[:], rib, op=ALU.mult)
            ve.drain()
            wrb = WROW.unsqueeze(1).unsqueeze(1).broadcast_to((128, 4, 2, 64))
            ve.tensor_tensor(YC[:], YC[:], wrb, op=ALU.mult)
            k.s_v.inc(ve.drain())
            if DEV_TAP == "yn" and qt == 0 and c == 0:
                full_barrier(k)
                dump_fm(k, YC.rearrange("p a b c -> p (a b c)").unsqueeze(1), k.out_h, 1)
                return
            k.s_v.wait(te)
            for cc in range(4):
                pv = k.psA[:, cc, :].bitcast(BF16)[:, 0:128]
                k.s_mm.inc(te.transpose(pv, YC[:, cc, :, :].rearrange("p a b -> p (a b)"),
                                        k.idb[:]))
            k.s_mm.wait(ve)
            for cc in range(4):
                ve.tensor_copy(Q[:, qt, 512 * c + 128 * cc:512 * c + 128 * cc + 128],
                               k.psA[:, cc, :].bitcast(BF16)[:, 0:128])
            k.s_v.inc(ve.drain())
            if DEV_TAP in ("y5", "y6", "y7", "y8", "y9", "yz") and qt == {"y5":2,"y6":3,"y7":4,"y8":5,"y9":6,"yz":7}[DEV_TAP] and c == 3:
                dump_fm(k, Q, k.out_h, 8)
                return
            if DEV_TAP == "y3" and qt == 0 and c == 3:
                dump_fm(k, Q, k.out_h, 8)
                return
            if DEV_TAP == "y4" and qt == 1 and c == 3:
                dump_fm(k, Q, k.out_h, 8)
                return
            if DEV_TAP == "y2" and qt == 0 and c == 0:
                dump_fm(k, Q, k.out_h, 8)
                return
            if DEV_TAP == "y1" and qt == 0 and c == 0:
                full_barrier(k)
                dump_fm(k, Q[:, 0:1, 0:512].rearrange("p a b -> p (a b)").unsqueeze(1), k.out_h, 1)
                return

    full_barrier(k)
    ar.release()
    ar.release()
    ar.mark()
    X1T = ar.f32(2, D)
    WOB = ar.bf(8, D)
    k.s_lg.inc(ge.dma_start(out=WOB, in_=h["woT"].rearrange("(a p) o -> p a o", p=128)), 16)
    k.s_ld.wait(te)
    k.s_lg.wait(te)
    x1r = h["x1_s"].rearrange("(j p) f -> p j f", p=128)
    outr2 = out_h.rearrange("(j p) f -> p j f", p=128)
    for j in range(NT):
        k.s_st.wait(ve)
        for oc in range(2):
            pb = k.psB[:, 2 * (j % 2) + oc, :]
            te.wait_ge(k.s_v.h, max(0, k.s_v.n - 3))
            for ft in range(8):
                mm = te.matmul(pb, Q[:, ft, 128 * j:128 * j + 128],
                               WOB[:, ft, 512 * oc:512 * oc + 512],
                               start=(ft == 0), stop=(ft == 7))
            k.s_mm.inc(mm)
            k.s_mm.wait(ve)
            k.s_v.inc(ve.tensor_copy(X1T[:, j % 2, 512 * oc:512 * oc + 512], pb))
        k.s_v.wait(sy)
        k.s_st.inc(sy.dma_start(out=x1r[:, j, :], in_=X1T[:, j % 2, :]), 16)
        if DEV_STAGE == "l0attn":
            k.s_st.inc(sy.dma_start(out=outr2[:, j, :], in_=X1T[:, j % 2, :]), 16)
    if DEV_STAGE == "l0attn":
        k.s_st.wait(sy)
        return
    full_barrier(k)
    ar.release()
    ar.release()
    emit_l0_ffn(k, h, out_h)


def emit_l0_wquant(k, h, WLD, WQB, STG):
    te, ve, se, ge, sy = k.te, k.ve, k.se, k.ge, k.sy
    specs = [(h["wg0"], h["wqgT_s"], 22, 1024),
             (h["wu0"], h["wquT_s"], 22, 1024),
             (h["wd0"], h["wdqT_s"], 8, 2816)]
    for src, dst, nrt, W in specs:
        srcr = src.rearrange("(r p) w -> p r w", p=128)
        dstr = dst.rearrange("(bb p) o -> p bb o", p=128)
        nb = W // 128
        for r in range(nrt):
            k.s_v.wait(sy)
            k.s_g.wait(sy)
            k.s_la.inc(sy.dma_start(out=WLD[:, 0:W], in_=srcr[:, r, :]), 16)
            k.s_la.wait(ve)
            k.s_la.wait(ge)
            emit_quant(k, WLD[:, 0:W], WQB[:, 0:W])
            k.s_g.wait(te)
            k.s_v.wait(te)
            k.s_st.wait(ve)
            pairs = [(WQB[:, 128 * b:128 * b + 128], STG[:, 128 * b:128 * b + 128])
                     for b in range(nb)]
            emit_transposes(k, pairs, k.idb[:])
            k.s_mm.wait(ve)
            k.s_mm.wait(ge)
            k.s_v.wait(sy)
            k.s_st.inc(sy.dma_start(out=dstr[:, 0:nb, 128 * r:128 * r + 128],
                                    in_=STG[:, 0:W].rearrange("p (bb c) -> p bb c", c=128)), 16)


def emit_l0_ffn(k, h, out_h):
    te, ve, se, ge, sy = k.te, k.ve, k.se, k.ge, k.sy
    ar = k.ar
    XQF = k.arXQF
    ar.mark()
    WGS = ar.bf(8, 512)
    WUS = ar.bf(8, 512)
    WDS = ar.bf(22, 1024)
    HQ = [ar.bf(22, 128) for _ in range(4)]
    X1T = ar.f32(2, D)
    wds_src = h["wdqT_s"].rearrange("(c p) o -> p c o", p=128)
    k.s_ld.inc(sy.dma_start(out=WDS, in_=wds_src), 16)
    wg_src = h["wqgT_s"].rearrange("(c p) o -> p c o", p=128)
    wu_src = h["wquT_s"].rearrange("(c p) o -> p c o", p=128)
    xr = h["x_in"].rearrange("(j p) f -> p j f", p=128)
    x1r = h["x1_s"].rearrange("(j p) f -> p j f", p=128)
    outr = out_h.rearrange("(j p) f -> p j f", p=128)
    HG = k.qA[:, 1536:2048]
    HQB = k.qCb[:, 4096:4608]

    for jg in range(4):
        k.s_mm.wait(ve)
        k.s_mm.wait(se)
        for oc in range(6):
            ocw = 512 if oc < 5 else 256
            k.s_mm.wait(sy)
            k.s_la.inc(sy.dma_start(out=WGS[:, :, 0:ocw], in_=wg_src[:, :, 512 * oc:512 * oc + ocw]), 16)
            k.s_lb.inc(sy.dma_start(out=WUS[:, :, 0:ocw], in_=wu_src[:, :, 512 * oc:512 * oc + ocw]), 16)
            k.s_la.wait(te)
            k.s_lb.wait(te)
            for j4 in range(4):
                j = 4 * jg + j4
                te.wait_ge(k.s_s.h, k.s_s.n)
                te.wait_ge(k.s_v.h, k.s_v.n)
                for ic in range(8):
                    te.matmul(k.psA[:, 0, 0:ocw], XQF[:, ic, 128 * j:128 * j + 128],
                              WGS[:, ic, 0:ocw], start=(ic == 0), stop=(ic == 7))
                for ic in range(8):
                    mm = te.matmul(k.psA[:, 1, 0:ocw], XQF[:, ic, 128 * j:128 * j + 128],
                                   WUS[:, ic, 0:ocw], start=(ic == 0), stop=(ic == 7))
                k.s_mm.inc(mm)
                k.s_mm.wait(se)
                k.s_s.inc(se.activation(HG[:, 0:ocw], k.psA[:, 0, 0:ocw], AF.Silu))
                k.s_s.wait(ve)
                ve.tensor_tensor(HG[:, 0:ocw], HG[:, 0:ocw], k.psA[:, 1, 0:ocw], op=ALU.mult)
                k.s_v.inc(ve.drain())
                emit_quant(k, HG[:, 0:ocw], HQB[:, 0:ocw])
                k.s_g.wait(te)
                k.s_v.wait(te)
                pairs = [(HQB[:, 128 * b:128 * b + 128], HQ[j4][:, 4 * oc + b, :])
                         for b in range(ocw // 128)]
                emit_transposes(k, pairs, k.idb[:])
                k.s_mm.wait(ve)
                k.s_mm.wait(ge)
                k.s_mm.wait(se)
        k.s_ld.wait(te)
        for j4 in range(4):
            j = 4 * jg + j4
            k.s_st.wait(sy)
            k.s_lc.inc(sy.dma_start(out=X1T[:, j4 % 2, :], in_=x1r[:, j, :]), 16)
            for dc in range(2):
                te.wait_ge(k.s_v.h, k.s_v.n)
                for c in range(22):
                    mm = te.matmul(k.psB[:, dc, :], HQ[j4][:, c, :],
                                   WDS[:, c, 512 * dc:512 * dc + 512],
                                   start=(c == 0), stop=(c == 21))
                k.s_mm.inc(mm)
            k.s_mm.wait(ve)
            k.s_lc.wait(ve)
            for dc in range(2):
                ve.tensor_tensor(X1T[:, j4 % 2, 512 * dc:512 * dc + 512],
                                 X1T[:, j4 % 2, 512 * dc:512 * dc + 512],
                                 k.psB[:, dc, :], op=ALU.add)
            k.s_v.inc(ve.drain())
            k.s_v.wait(ge)
            k.s_lg.inc(ge.dma_start(out=X1T[:, j4 % 2, :], in_=xr[:, j, :], accum_op=ALU.add), 16)
            k.s_lg.wait(sy)
            k.s_st.inc(sy.dma_start(out=x1r[:, j, :], in_=X1T[:, j4 % 2, :]), 16)
            if DEV_STAGE == "l0":
                k.s_st.inc(sy.dma_start(out=outr[:, j, :], in_=X1T[:, j4 % 2, :]), 16)
    full_barrier(k)
    ar.release()
    if DEV_STAGE == "l0":
        k.s_st.wait(sy)
        return
    ar.release()
    emit_l1(k, h, out_h)


def emit_l1(k, h, out_h):
    te, ve, se, ge, sy = k.te, k.ve, k.se, k.ge, k.sy
    ar = k.ar

    XINT = ar.bf(NT, D)
    ar.mark()
    XNF = ar.bf(8, T)
    XTa = ar.f32(2, D)

    emit_ln_stats(k, h["x1_s"], XTa, k.qA[:, 0:2 * D])
    xr = h["x1_s"].rearrange("(j p) f -> p j f", p=128)
    ssum = k.stats[:, 0:NT]
    inv = k.stats[:, 2 * NT:3 * NT]
    sems = (k.s_la, k.s_lb)
    for j in range(min(2, NT)):
        sems[j % 2].inc(sy.dma_start(out=XTa[:, j % 2, :], in_=xr[:, j, :]), 16)
    for j in range(NT):
        xt = XTa[:, j % 2, :]
        sems[j % 2].wait(ve)
        k.s_s.wait(ve)
        ve.tensor_scalar(XINT[:, j, :], xt, ssum[:, j:j + 1], inv[:, j:j + 1],
                         op0=ALU.subtract, op1=ALU.mult)
        k.s_v.inc(ve.drain())
        if j + 2 < NT:
            k.s_v.wait(sy)
            sems[j % 2].inc(sy.dma_start(out=XTa[:, j % 2, :], in_=xr[:, j + 2, :]), 16)
        k.s_v.wait(te)
        pairs = [(XINT[:, j, 128 * fb:128 * fb + 128], XNF[:, fb, 128 * j:128 * j + 128])
                 for fb in range(8)]
        emit_transposes(k, pairs, k.idb[:])
        k.s_mm.wait(ve)

    if DEV_STAGE == "l1ln":
        dump_xint(k, out_h, XINT)
        return

    TH = ar.bf(2, T)
    PHI = ar.bf(NT, 256)
    GT = ar.bf(NT, D)
    MB = ar.bf(2, D)
    WTH = ar.bf(8, 256)
    WPH = ar.bf(8, 256)
    WG = ar.bf(8, D)
    k.s_lg.inc(ge.dma_start(out=WTH, in_=h["wthT"].rearrange("(a p) o -> p a o", p=128)), 16)
    k.s_lg.inc(ge.dma_start(out=WPH, in_=h["wphT"].rearrange("(a p) o -> p a o", p=128)), 16)
    k.s_lg.inc(ge.dma_start(out=WG, in_=h["wgT"].rearrange("(a p) o -> p a o", p=128)), 16)
    full_barrier(k)

    for oc in range(2):
        for tc in range(4):
            pb = k.psA[:, tc % 4, :]
            te.wait_ge(k.s_s.h, max(0, k.s_s.n - 3))
            for ic in range(8):
                mm = te.matmul(pb, WTH[:, ic, 128 * oc:128 * oc + 128],
                               XNF[:, ic, 512 * tc:512 * tc + 512],
                               start=(ic == 0), stop=(ic == 7))
            k.s_mm.inc(mm)
            k.s_mm.wait(se)
            k.s_s.inc(se.activation(TH[:, oc, 512 * tc:512 * tc + 512], pb, AF.Copy, bias=0.0))
    def tproj(W, dst, ncol):
        for j in range(NT):
            noc = (ncol + 511) // 512
            for oc in range(noc):
                ow = min(512, ncol - 512 * oc)
                pb = k.psB[:, (j * noc + oc) % 4, :]
                te.wait_ge(k.s_s.h, max(0, k.s_s.n - 3))
                for ic in range(8):
                    mm = te.matmul(pb[:, 0:ow], XNF[:, ic, 128 * j:128 * j + 128],
                                   W[:, ic, 512 * oc:512 * oc + ow],
                                   start=(ic == 0), stop=(ic == 7))
                k.s_mm.inc(mm)
                k.s_mm.wait(se)
                k.s_s.inc(se.activation(dst[:, j, 512 * oc:512 * oc + ow], pb[:, 0:ow],
                                        AF.Copy, bias=0.0))
    tproj(WPH, PHI, 256)
    tproj(WG, GT, 1024)
    full_barrier(k)

    for lc in range(2):
        for ocg in range(2):
            pb = k.psA[:, 2 * lc + ocg, :]
            for j in range(NT):
                mm = te.matmul(pb, PHI[:, j, 128 * lc:128 * lc + 128],
                               GT[:, j, 512 * ocg:512 * ocg + 512],
                               start=(j == 0), stop=(j == NT - 1))
            k.s_mm.inc(mm)
            k.s_mm.wait(ve)
            k.s_v.inc(ve.tensor_scalar(MB[:, lc, 512 * ocg:512 * ocg + 512], pb,
                                       0.03125, None, op0=ALU.mult))
    k.s_v.wait(te)
    X1T = XTa
    x1r2 = h["x1_s"].rearrange("(j p) f -> p j f", p=128)
    for j in range(NT):
        k.s_st.wait(sy)
        k.s_lc.inc(sy.dma_start(out=X1T[:, j % 2, :], in_=x1r2[:, j, :]), 16)
        te.wait_ge(k.s_v.h, max(0, k.s_v.n - 1))
        for oc in range(2):
            pb = k.psB[:, 2 * (j % 2) + oc, :]
            for lc in range(2):
                mm = te.matmul(pb, TH[:, lc, 128 * j:128 * j + 128],
                               MB[:, lc, 512 * oc:512 * oc + 512],
                               start=(lc == 0), stop=(lc == 1))
            k.s_mm.inc(mm)
        k.s_mm.wait(ve)
        k.s_lc.wait(ve)
        for oc in range(2):
            ve.tensor_tensor(X1T[:, j % 2, 512 * oc:512 * oc + 512],
                             X1T[:, j % 2, 512 * oc:512 * oc + 512],
                             k.psB[:, 2 * (j % 2) + oc, :], op=ALU.add)
        k.s_v.inc(ve.drain())
        k.s_v.wait(sy)
        k.s_st.inc(sy.dma_start(out=x1r2[:, j, :], in_=X1T[:, j % 2, :]), 16)
    full_barrier(k)
    ar.release()

    if DEV_STAGE == "l1attn":
        ar.mark()
        XTb = ar.f32(2, D)
        outr = out_h.rearrange("(j p) f -> p j f", p=128)
        for j in range(NT):
            k.s_la.inc(sy.dma_start(out=XTb[:, j % 2, :], in_=x1r2[:, j, :]), 16)
            k.s_la.wait(sy)
            k.s_st.inc(sy.dma_start(out=outr[:, j, :], in_=XTb[:, j % 2, :]), 16)
            k.s_st.wait(sy)
        ar.release()
        return

    emit_l1_ffn(k, h, out_h, XINT)


def emit_l1_ffn(k, h, out_h, XINT):
    nc = k.nc
    te, ve, se, ge, sy = k.te, k.ve, k.se, k.ge, k.sy
    ar = k.ar

    CQ = ar.bf(T)
    SQ = ar.bf(T)
    CK = ar.bf(T)
    SK = ar.bf(T)

    P64R = k.sb("P64R", [128, 128], BF16)
    M01L = k.sb("M01L", [128, 128], BF16)
    PHB = k.sb("PHB", [128, 2], BF16)
    E2B = k.sb("E2B", [2, 128], BF16)
    E2Q = k.sb("E2Q", [2, 128], BF16)
    E2K = k.sb("E2K", [2, 128], BF16)
    QSR = k.sb("QSR", [2, 256], BF16)
    LNGC = k.sb("LNGC", [128, 8], F32)
    LNBC = k.sb("LNBC", [128, 8], F32)
    B1AC = k.sb("B1AC", [128, 22], F32)
    B1GC = k.sb("B1GC", [128, 22], F32)
    st4 = nc.dram_tensor("st4", [4, 2049], F32, kind="Internal").ap()

    full_barrier(k)

    PM32 = k.smallc[:, 8:9]
    LNBc = k.smallc[:, 9:10]
    INVF = k.smallc[:, 10:11]
    ZCOL = k.smallc[:, 15:16]
    ve.memset(ZCOL, 0.0)
    ve.tensor_scalar(PM32, k.pcol[:], -15.5, 0.03125, op0=ALU.add, op1=ALU.mult)
    ve.drain()
    ve.tensor_scalar(PM32, PM32, k.mcol[:, 0:1], k.mcol[:, 0:1], op0=ALU.add, op1=ALU.subtract)
    ve.drain()
    ve.scalar_tensor_tensor(PM32, PM32, -32.0, k.pcol[:], op0=ALU.mult, op1=ALU.add)
    ve.drain()
    ve.tensor_scalar(INVF, PM32, -0.03125, None, op0=ALU.mult)
    ve.drain()
    ve.tensor_scalar(INVF, INVF, k.ln4, None, op0=ALU.mult)
    ve.tensor_scalar(LNBc, PM32, 2.0, 25.6, op0=ALU.mult, op1=ALU.add)
    ve.drain()
    ve.tensor_scalar(LNBc, LNBc, 1.0 / 89.6, None, op0=ALU.mult)
    k.s_v.inc(ve.drain())
    k.s_v.wait(se)
    se.activation(INVF, INVF, AF.Exp)
    se.activation(LNBc, LNBc, AF.Ln)
    k.s_s.inc(se.drain())

    DIF = k.qA[:, 0:128]
    CM64 = k.qA[:, 128:256]
    S1 = k.qA[:, 256:384]
    S2 = k.qA[:, 384:512]
    k.s_s.wait(ge)
    ge.iota(DIF, pattern=[[1, 128]], base=0, channel_multiplier=-1,
            allow_small_or_imprecise_dtypes=True)
    ge.iota(CM64, pattern=[[0, 2], [1, 64]], base=0, channel_multiplier=0,
            allow_small_or_imprecise_dtypes=True)
    k.s_g.inc(ge.drain())
    k.s_g.wait(ve)
    k.s_s.wait(ve)
    ve.tensor_scalar(P64R[:], DIF, 32.0, None, op0=ALU.is_equal)
    ve.tensor_scalar(S1, DIF, -32.0, None, op0=ALU.is_equal)
    ve.tensor_scalar(S2, CM64, 32.0, None, op0=ALU.is_ge)
    ve.tensor_scalar(M01L[:], DIF, 0.0, None, op0=ALU.is_le)
    ve.drain()
    ve.tensor_tensor(P64R[:], P64R[:], S2, op=ALU.mult)
    ve.drain()
    ve.tensor_scalar(S2, S2, -1.0, 1.0, op0=ALU.mult, op1=ALU.add)
    ve.drain()
    ve.tensor_tensor(S1, S1, S2, op=ALU.mult)
    ve.drain()
    ve.tensor_tensor(P64R[:], P64R[:], S1, op=ALU.subtract)
    ve.memset(PHB[:], 0.0)
    ve.drain()
    ve.tensor_scalar(PHB[:, 0:1], k.pcol[:], 64.0, None, op0=ALU.is_lt)
    ve.tensor_scalar(PHB[:, 1:2], k.pcol[:], 64.0, None, op0=ALU.is_ge)
    k.s_v.inc(ve.drain())
    k.s_v.wait(te)
    pvE = k.psA[:, 0, :].bitcast(BF16)[0:2, 0:128]
    k.s_mm.inc(te.transpose(pvE, PHB[:], k.idb[:]))
    k.s_mm.wait(ve)
    k.s_v.inc(ve.tensor_copy(E2B[:], pvE))

    TG = k.qA[:, 0:T]
    ARG = k.qB[:, 0:T]
    RED = k.qC[:, 0:T]
    k.s_v.wait(ge)
    k.s_g.inc(ge.iota(TG, pattern=[[1, T]], base=0, channel_multiplier=0,
                      allow_small_or_imprecise_dtypes=True))
    k.s_g.wait(ve)

    def trig2(dst, use_hpi, pw_mult, finalmult):
        k.s_s.wait(ve)
        if use_hpi:
            ve.tensor_scalar(ARG, TG, INVF, k.hpi, op0=ALU.mult, op1=ALU.add)
        else:
            ve.tensor_scalar(ARG, TG, INVF, None, op0=ALU.mult)
        ve.drain()
        ve.tensor_scalar(RED, ARG, k.i2pi, None, op0=ALU.mult)
        ve.drain()
        ve.tensor_scalar(RED, RED, k.mcol[:, 0:1], k.mcol[:, 0:1],
                         op0=ALU.add, op1=ALU.subtract)
        ve.drain()
        ve.scalar_tensor_tensor(RED, RED, k.twopi, ARG, op0=ALU.mult, op1=ALU.subtract)
        ve.drain()
        ve.tensor_scalar(RED, RED, k.pin, k.pip, op0=ALU.max, op1=ALU.min)
        k.s_v.inc(ve.drain())
        k.s_v.wait(se)
        k.s_s.inc(se.activation(RED, RED, AF.Sin))
        k.s_s.wait(ve)
        ve.tensor_scalar(ARG, TG, -1024.0, pw_mult, op0=ALU.add, op1=ALU.mult)
        ve.drain()
        ve.tensor_scalar(ARG, ARG, LNBc, None, op0=ALU.mult)
        k.s_v.inc(ve.drain())
        k.s_v.wait(se)
        k.s_s.inc(se.activation(ARG, ARG, AF.Exp))
        k.s_s.wait(ve)
        ve.tensor_tensor(RED, RED, ARG, op=ALU.mult)
        ve.drain()
        ve.tensor_scalar(dst, RED, finalmult, None, op0=ALU.mult)
        k.s_v.inc(ve.drain())

    trig2(CQ, True, 0.00390625, 8.0)
    trig2(SQ, False, 0.00390625, 8.0)
    trig2(CK, True, -0.00390625, -1.0)
    trig2(SK, False, -0.00390625, -1.0)

    for l in range(2):
        emit_l1f_attn(k, h, out_h, XINT, l, CQ, SQ, CK, SK,
                      P64R, M01L, PHB, E2B, E2Q, E2K, QSR)
        if DEV_STAGE == f"l1f{l}a":
            dump_xint(k, out_h, XINT)
            return
        emit_l1f_mlp(k, h, out_h, XINT, l, LNGC, LNBC, B1AC, B1GC, st4)
        if DEV_STAGE == f"l1f{l}":
            dump_xint(k, out_h, XINT)
            return

    full_barrier(k)
    ar.mark()
    XO = ar.f32(2, D)
    x1r = h["x1_s"].rearrange("(j p) f -> p j f", p=128)
    outr = out_h.rearrange("(j p) f -> p j f", p=128)
    for j in range(NT):
        k.s_st.wait(sy)
        k.s_lc.inc(sy.dma_start(out=XO[:, j % 2, :], in_=x1r[:, j, :]), 16)
        k.s_lc.wait(ve)
        ve.tensor_tensor(XO[:, j % 2, :], XO[:, j % 2, :], XINT[:, j, :], op=ALU.add)
        k.s_v.inc(ve.drain())
        k.s_v.wait(sy)
        k.s_st.inc(sy.dma_start(out=outr[:, j, :], in_=XO[:, j % 2, :]), 16)
    k.s_st.wait(sy)
    k.s_st.wait(te)
    ar.release()


def dump_xint(k, out_h, XINT):
    full_barrier(k)
    outr = out_h.rearrange("(j p) f -> p j f", p=128)
    for j in range(NT):
        k.s_dmp.inc(k.ge.dma_start(out=outr[:, j, :], in_=XINT[:, j, :]), 16)
    k.s_dmp.wait(k.ge)
    k.s_dmp.wait(k.sy)


def emit_l1f_attn(k, h, out_h, XINT, l, CQ, SQ, CK, SK,
                  P64R, M01L, PHB, E2B, E2Q, E2K, QSR):
    te, ve, se, ge, sy = k.te, k.ve, k.se, k.ge, k.sy
    ar = k.ar
    M01 = k.M01

    full_barrier(k)
    ar.mark()
    QKVB = ar.bf(12, T)

    k.s_lg.inc(ge.dma_start(out=QSR[:, 0:128],
                            in_=_dc.replace(h["qs"][l], ap=[[0, 2], [1, 128]])), 16)
    k.s_lh.inc(ge.dma_start(out=QSR[:, 128:256],
                            in_=_dc.replace(h["ks"][l], ap=[[0, 2], [1, 128]])), 16)
    k.s_lg.wait(ve)
    k.s_lh.wait(ve)
    ve.tensor_tensor(E2Q[:], E2B[:], QSR[:, 0:128], op=ALU.mult)
    ve.tensor_tensor(E2K[:], E2B[:], QSR[:, 128:256], op=ALU.mult)
    k.s_v.inc(ve.drain())
    ge.wait_ge(k.s_v.h, k.s_v.n)

    ar.mark()
    XFM = ar.bf(8, 2056)
    WSL = ar.bf(2, 8, 128)
    ve.memset(XFM[:, :, 0:1], 0.0)
    k.s_v.inc(ve.drain())
    k.s_v.wait(te)
    for j in range(NT):
        pairs = [(XINT[:, j, 128 * fb:128 * fb + 128], XFM[:, fb, 1 + 128 * j:1 + 128 * j + 128])
                 for fb in range(8)]
        emit_transposes(k, pairs, k.idb[:])
    k.s_mm.wait(ve)

    k.s_v.wait(te)
    wsrc = h["wqkv"][l].rearrange("(a p) o -> p a o", p=128)
    lastuse = [0, 0]
    sems = (k.s_lg, k.s_lh)
    for oc in range(12):
        b = oc % 2
        ge.wait_ge(k.s_mm.h, lastuse[b])
        sems[b].inc(ge.dma_start(out=WSL[:, b, :, :], in_=wsrc[:, :, 128 * oc:128 * oc + 128]), 16)
        sems[b].wait(te)
        for tc in range(4):
            pb = k.psA[:, tc % 4, :]
            te.wait_ge(k.s_s.h, max(0, k.s_s.n - 3))
            for ic in range(8):
                soff = 1 if ic < 4 else 0
                mm = te.matmul(pb, WSL[:, b, ic, :],
                               XFM[:, ic, soff + 512 * tc: soff + 512 * tc + 512],
                               start=(ic == 0), stop=(ic == 7))
            k.s_mm.inc(mm)
            k.s_mm.wait(se)
            k.s_s.inc(se.activation(QKVB[:, oc, 512 * tc:512 * tc + 512], pb, AF.Copy, bias=0.0))
        lastuse[b] = k.s_mm.n
    full_barrier(k)
    ar.release()

    if DEV_TAP == f"f{l}qkv":
        dump_fm(k, QKVB, out_h, 12)
        return

    VT1 = ar.bf(4, NT, 2, 68)
    UTT = ar.bf(2, 8, 512)
    UT = ar.bf(2, 640)
    OE = ar.bf(2, 512)
    OTT = ar.bf(4, 2, 72)
    YC = ar.bf(4, 2, 64)
    RC = ar.f32(16)
    WO1b = ar.bf(4, D)

    ve.memset(VT1[:], 1.0)
    k.s_v.inc(ve.drain())
    k.s_v.wait(te)
    base_v = k.s_v.n
    i = 0
    for t4 in range(4):
        for h2 in range(2):
            fr = 64 * h2
            for j in range(NT):
                if i >= 4:
                    te.wait_ge(k.s_v.h, base_v + i - 3)
                pv = k.psA[:, i % 4, :].bitcast(BF16)[:, 0:64]
                idsl = k.idb[fr:fr + 64, fr:fr + 64]
                k.s_mm.inc(te.transpose(pv, QKVB[fr:fr + 64, 8 + t4, 128 * j:128 * j + 128], idsl))
                k.s_mm.wait(ve)
                k.s_v.inc(ve.tensor_copy(VT1[:, t4, j, h2, 0:64], pv))
                i += 1

    SQB = k.qAb[:, 0:T]
    RT = k.qAb[:, T:2 * T]
    NRM = k.qB[0:2, 0:T]
    INVB = k.qCb[0:2, 0:T]

    def normrope(oc, COS, SIN, E2X):
        k.s_v.wait(se)
        k.s_g.wait(se)
        k.s_s.inc(se.activation(SQB, QKVB[:, oc, :], AF.Square))
        k.s_s.wait(te)
        k.s_v.wait(te)
        for cc in range(4):
            k.s_mm.inc(te.matmul(k.psB[0:2, cc, :], PHB[:], SQB[:, 512 * cc:512 * cc + 512],
                                 start=True, stop=True))
        k.s_mm.wait(se)
        for cc in range(4):
            se.activation(NRM[:, 512 * cc:512 * cc + 512], k.psB[0:2, cc, :], AF.Sqrt)
        k.s_s.inc(se.drain())
        k.s_s.wait(ve)
        ve.tensor_scalar(NRM[:], NRM[:], 1e-12, None, op0=ALU.max)
        ve.drain()
        ve.reciprocal(NRM[:], NRM[:])
        ve.drain()
        ve.tensor_copy(INVB[:], NRM[:])
        k.s_v.inc(ve.drain())
        k.s_v.wait(te)
        for cc in range(4):
            k.s_mm.inc(te.matmul(k.psA[:, cc, :], E2X[:], INVB[:, 512 * cc:512 * cc + 512],
                                 start=True, stop=True))
        k.s_mm.wait(ve)
        for cc in range(4):
            ve.tensor_tensor(QKVB[:, oc, 512 * cc:512 * cc + 512],
                             QKVB[:, oc, 512 * cc:512 * cc + 512], k.psA[:, cc, :], op=ALU.mult)
        k.s_v.inc(ve.drain())
        k.s_v.wait(te)
        for cc in range(4):
            k.s_mm.inc(te.matmul(k.psB[:, cc, :], P64R[:], QKVB[:, oc, 512 * cc:512 * cc + 512],
                                 start=True, stop=True))
        ve.tensor_tensor(RT, QKVB[:, oc, :], COS, op=ALU.mult)
        k.s_mm.wait(ve)
        for cc in range(4):
            ve.tensor_tensor(k.psB[:, cc, :], k.psB[:, cc, :],
                             SIN[:, 512 * cc:512 * cc + 512], op=ALU.mult)
        ve.drain()
        for cc in range(4):
            ve.tensor_tensor(QKVB[:, oc, 512 * cc:512 * cc + 512],
                             RT[:, 512 * cc:512 * cc + 512], k.psB[:, cc, :], op=ALU.add)
        k.s_v.inc(ve.drain())

    for t4 in range(4):
        normrope(t4, CQ, SQ, E2Q)
    for t4 in range(4):
        normrope(4 + t4, CK, SK, E2K)
    full_barrier(k, "tvs")
    if DEV_TAP == f"f{l}q":
        dump_fm(k, QKVB[:, 0:4, :], out_h, 4)
        return
    if DEV_TAP == f"f{l}k":
        dump_fm(k, QKVB[:, 4:8, :], out_h, 4)
        return

    NM = k.stats[:, 44:46]
    for t4 in range(4):
        for w in range(4):
            for lq in range(4):
                qc = 4 * w + lq
                kts = list(range(max(0, qc - 4), qc + 1))
                nblk = len(kts)
                wcols = 128 * nblk
                te.wait_ge(k.s_s.h, k.s_s.n)
                te.wait_ge(k.s_v.h, k.s_v.n)
                for jj in range(2):
                    lhs = QKVB[64 * jj:64 * jj + 64, t4, 128 * qc:128 * qc + 128]
                    kbase = 128 * kts[0]
                    w0 = min(512, wcols)
                    mm = te.matmul(k.psA[:, 2 * jj, 0:w0], lhs,
                                   QKVB[64 * jj:64 * jj + 64, 4 + t4, kbase:kbase + w0],
                                   start=True, stop=True, tile_position=(64 * jj, 0))
                    if wcols > 512:
                        mm = te.matmul(k.psA[:, 2 * jj + 1, 0:wcols - 512], lhs,
                                       QKVB[64 * jj:64 * jj + 64, 4 + t4,
                                            kbase + 512:kbase + wcols],
                                       start=True, stop=True, tile_position=(64 * jj, 0))
                    k.s_mm.inc(mm)
                k.s_mm.wait(ve)
                ve.wait_ge(k.s_s.h, k.s_s.n)
                for jj in range(2):
                    flat = k.psA[:, 2 * jj:2 * jj + 2, :].rearrange("p a b -> p (a b)")
                    ve.tensor_reduce(NM[:, jj:jj + 1], flat[:, 0:wcols], axis=AX.X, op=ALU.min)
                k.s_v.inc(ve.drain())
                k.s_v.wait(se)
                for jj in range(2):
                    flat = k.psA[:, 2 * jj:2 * jj + 2, :].rearrange("p a b -> p (a b)")
                    se.activation(UT[:, jj, 0:wcols], flat[:, 0:wcols], AF.Exp,
                                  bias=NM[:, jj:jj + 1], scale=-1.0)
                k.s_s.inc(se.drain())
                k.s_s.wait(ve)
                for jj in range(2):
                    if qc >= 4:
                        ve.tensor_tensor(UT[:, jj, 0:128], UT[:, jj, 0:128], M01[:], op=ALU.mult)
                    ve.tensor_tensor(UT[:, jj, wcols - 128:wcols],
                                     UT[:, jj, wcols - 128:wcols], M01L[:], op=ALU.mult)
                k.s_v.inc(ve.drain())
                k.s_v.wait(te)
                vb2 = k.s_v.n
                i2 = 0
                for jj in range(2):
                    for b, kt in enumerate(kts):
                        ktr = kt - 4 * w + 4
                        if i2 >= 2:
                            te.wait_ge(k.s_v.h, vb2 + i2 - 1)
                        pvb = k.psB[:, 2 + i2 % 2, :].bitcast(BF16)[:, 0:128]
                        k.s_mm.inc(te.transpose(pvb, UT[:, jj, 128 * b:128 * b + 128], k.idb[:]))
                        k.s_mm.wait(ve)
                        k.s_v.inc(ve.tensor_copy(UTT[:, jj, ktr, 128 * lq:128 * lq + 128], pvb))
                        i2 += 1
            te.wait_ge(k.s_v.h, k.s_v.n)
            ktrs = [3, 4, 0, 1, 2, 5, 6, 7] if w > 0 else [4, 5, 6, 7]
            for jj in range(2):
                for idx, ktr in enumerate(ktrs):
                    kt = ktr + 4 * w - 4
                    c0 = 128 * max(0, ktr - 4)
                    c1 = 128 * (min(3, ktr) + 1)
                    mm = te.matmul(k.psB[0:65, jj, c0:c1], VT1[:, t4, kt, jj, 0:65],
                                   UTT[:, jj, ktr, c0:c1],
                                   start=(idx == 0), stop=(idx == len(ktrs) - 1))
                k.s_mm.inc(mm)
            k.s_mm.wait(ve)
            for jj in range(2):
                ve.tensor_copy(OE[0:65, jj, :], k.psB[0:65, jj, :])
            k.s_v.inc(ve.drain())
            k.s_v.wait(te)
            vb3 = k.s_v.n
            i3 = 0
            for cc in range(4):
                for jj in range(2):
                    if i3 >= 4:
                        te.wait_ge(k.s_v.h, vb3 + i3 - 3)
                    pv = k.psA[:, i3 % 4, :].bitcast(BF16)[:, 0:65]
                    k.s_mm.inc(te.transpose(pv, OE[0:65, jj, 128 * cc:128 * cc + 128],
                                            k.idb[0:65, 0:65]))
                    k.s_mm.wait(ve)
                    k.s_v.inc(ve.tensor_copy(OTT[:, cc, jj, 0:65], pv))
                    i3 += 1
            ve.drain()
            ve.tensor_copy(RC[:, 0:8].rearrange("p (b a) -> p b a", b=2),
                           OTT[:, :, :, 64:65].rearrange("p a b 1 -> p a b").transpose([0, 2, 1]))
            ve.drain()
            ve.reciprocal(RC[:, 0:8], RC[:, 0:8])
            ve.drain()
            for jj in range(2):
                rb = RC[:, 4 * jj:4 * jj + 4].unsqueeze(2).broadcast_to((128, 4, 64))
                ve.tensor_tensor(YC[:, :, jj, :], OTT[:, :, jj, 0:64], rb, op=ALU.mult)
            k.s_v.inc(ve.drain())
            k.s_v.wait(te)
            for cc in range(4):
                pv = k.psA[:, cc, :].bitcast(BF16)[:, 0:128]
                k.s_mm.inc(te.transpose(pv, YC[:, cc, :, :].rearrange("p a b -> p (a b)"),
                                        k.idb[:]))
            k.s_mm.wait(ve)
            for cc in range(4):
                ve.tensor_copy(QKVB[:, t4, 512 * w + 128 * cc:512 * w + 128 * cc + 128],
                               k.psA[:, cc, :].bitcast(BF16)[:, 0:128])
            k.s_v.inc(ve.drain())

    full_barrier(k, "tvs")
    k.s_lg.inc(ge.dma_start(out=WO1b, in_=h["wo1"][l].rearrange("(a p) o -> p a o", p=128)), 16)
    k.s_lg.wait(te)
    for j in range(NT):
        te.wait_ge(k.s_v.h, max(0, k.s_v.n - 1))
        for dc in range(2):
            pb = k.psB[:, 2 * (j % 2) + dc, :]
            for ft in range(4):
                mm = te.matmul(pb, QKVB[:, ft, 128 * j:128 * j + 128],
                               WO1b[:, ft, 512 * dc:512 * dc + 512],
                               start=(ft == 0), stop=(ft == 3))
            k.s_mm.inc(mm)
        k.s_mm.wait(ve)
        for dc in range(2):
            ve.tensor_tensor(XINT[:, j, 512 * dc:512 * dc + 512],
                             XINT[:, j, 512 * dc:512 * dc + 512],
                             k.psB[:, 2 * (j % 2) + dc, :], op=ALU.add)
        k.s_v.inc(ve.drain())
    full_barrier(k)
    ar.release()


def emit_l1f_mlp(k, h, out_h, XINT, l, LNGC, LNBC, B1AC, B1GC, st4):
    te, ve, se, ge, sy = k.te, k.ve, k.se, k.ge, k.sy
    ar = k.ar
    ar.mark()
    W2R = ar.bf(22, D)
    BCOL = ar.bf(8)
    ROWB = ar.bf(4, 2048)

    k.s_ld.inc(sy.dma_start(out=LNGC[:], in_=h["lngT"][l]), 16)
    k.s_ld.inc(sy.dma_start(out=LNBC[:], in_=h["lnbT"][l]), 16)
    k.s_ld.inc(sy.dma_start(out=B1AC[:], in_=h["b1ag"][l, 0]), 16)
    k.s_ld.inc(sy.dma_start(out=B1GC[:], in_=h["b1ag"][l, 1]), 16)
    w2src = h["w2"][l]
    k.s_lg.inc(ge.dma_start(out=W2R[:, 0:21, :],
                            in_=w2src[0:2688].rearrange("(a p) o -> p a o", p=128)), 16)
    k.s_lg.inc(ge.dma_start(out=W2R[0:42, 21, :], in_=w2src[2688:2730]), 16)

    sA = k.qA[:, 0:16]
    sB = k.qA[:, 16:32]
    qA_ = k.qA[:, 32:48]
    qB_ = k.qA[:, 48:64]
    SCAs = (k.qC[:, 1024:1536], k.qC[:, 1536:2048])
    SCBs = (k.qC[:, 2048:2560], k.qB[:, 2048:2560])
    sbase = k.s_s.n
    for j in range(NT):
        ve.tensor_reduce(sA[:, j:j + 1], XINT[:, j, 0:512], axis=AX.X, op=ALU.add)
        ve.tensor_reduce(sB[:, j:j + 1], XINT[:, j, 512:1024], axis=AX.X, op=ALU.add)
        if j:
            se.wait_ge(k.s_s.h, sbase + j)
        se.activation(SCAs[j % 2], XINT[:, j, 0:512], AF.Square, accum_out=qA_[:, j:j + 1])
        se.activation(SCBs[j % 2], XINT[:, j, 512:1024], AF.Square, accum_out=qB_[:, j:j + 1])
        k.s_s.inc(se.drain())
    k.s_v.inc(ve.drain())
    k.s_s.inc(se.drain())
    k.s_v.wait(te)
    k.s_s.wait(te)
    STG16s = (k.qC[0:16, 2048:2176], k.qC[0:16, 2176:2304])
    for st in range(4):
        k.s_st.inc(sy.dma_start(out=st4[st, 0:1], in_=k.smallc[0:1, 15:16]), 16)
        pv = k.psA[0:16, st % 4, 0:128]
        k.s_mm.inc(te.transpose(pv, k.qA[:, 16 * st:16 * st + 16], k.idf[:]))
        k.s_mm.wait(ve)
        k.s_st.wait(ve)
        k.s_v.inc(ve.tensor_copy(STG16s[st % 2], pv))
        k.s_v.wait(sy)
        k.s_st.inc(sy.dma_start(out=st4[st, 1:2049].rearrange("(j c) -> j c", c=128),
                                in_=STG16s[st % 2]), 16)
    k.s_st.wait(sy)
    RA = k.qA[0:1, 0:2048]
    RQA = k.qB[0:1, 0:2048]
    RC_ = k.qC[0:1, 0:2048]
    IVB = ROWB[0:1, 0, :]
    CHB = ROWB[0:1, 1, :]
    CLB = ROWB[0:1, 2, :]
    B2H = ROWB[0:1, 3, 0:1024]
    B2L = ROWB[0:1, 3, 1024:2048]
    k.s_ld.inc(sy.dma_start(out=RA, in_=st4[0, 1:2049].unsqueeze(0)), 16)
    k.s_ld.inc(sy.dma_start(out=RC_, in_=st4[1, 0:2048].unsqueeze(0)), 16)
    k.s_ld.wait(ve)
    ve.tensor_tensor(RA, RA, RC_, op=ALU.add)
    k.s_v.inc(ve.drain())
    k.s_v.wait(sy)
    k.s_ld.inc(sy.dma_start(out=RQA, in_=st4[2, 1:2049].unsqueeze(0)), 16)
    k.s_ld.inc(sy.dma_start(out=RC_, in_=st4[3, 0:2048].unsqueeze(0)), 16)
    k.s_ld.wait(ve)
    ve.tensor_tensor(RQA, RQA, RC_, op=ALU.add)
    ve.drain()
    ve.tensor_scalar(RA, RA, 1.0 / D, None, op0=ALU.mult)
    ve.tensor_scalar(RQA, RQA, 1.0 / D, None, op0=ALU.mult)
    ve.drain()
    ve.tensor_tensor(RC_, RA, RA, op=ALU.mult)
    ve.drain()
    ve.tensor_tensor(RQA, RQA, RC_, op=ALU.subtract)
    ve.drain()
    ve.tensor_scalar(RQA, RQA, 1e-5, None, op0=ALU.add)
    k.s_v.inc(ve.drain())
    k.s_v.wait(se)
    k.s_s.inc(se.activation(RQA, RQA, AF.Sqrt))
    k.s_s.wait(ve)
    ve.reciprocal(RQA, RQA)
    ve.drain()
    ve.tensor_tensor(RC_, RA, RQA, op=ALU.mult)
    ve.drain()
    ve.tensor_scalar(RC_, RC_, -1.0, None, op0=ALU.mult)
    ve.drain()
    ve.tensor_copy(IVB, RQA)
    ve.tensor_copy(CHB, RC_)
    ve.drain()
    ve.tensor_tensor(CLB, RC_, CHB, op=ALU.subtract)
    k.s_v.inc(ve.drain())
    b2f = h["b2"][l].rearrange("a b -> (a b)")
    B2F = k.qA[0:1, 0:1024]
    k.s_v.wait(sy)
    k.s_ld.inc(sy.dma_start(out=B2F, in_=b2f.unsqueeze(0)), 16)
    k.s_ld.wait(ve)
    ve.tensor_copy(B2H, B2F)
    ve.drain()
    ve.tensor_tensor(B2L, B2F, B2H, op=ALU.subtract)
    k.s_v.inc(ve.drain())

    w1src = h["w1"][l].rearrange("(a p) o -> p a o", p=128)
    GH = k.qCb[:, 4096:4608]
    AT = k.qCb[:, 4608:5120]

    for hf in range(2):
        ar.mark()
        XHh = ar.bf(8, 1024)
        ar.mark()
        XFMh = ar.bf(8, 1032)
        full_barrier(k, "tvs")
        k.s_ld.wait(se)
        k.s_ld.wait(ve)
        k.s_lg.wait(te)
        if hf == 0:
            ve.memset(XFMh[:, :, 0:1], 0.0)
        else:
            ve.tensor_copy(XFMh[:, :, 0:1].rearrange("p a 1 -> p a"), BCOL)
        k.s_v.inc(ve.drain())
        k.s_v.wait(te)
        for jj2 in range(8):
            j = 8 * hf + jj2
            pairs = [(XINT[:, j, 128 * fb:128 * fb + 128],
                      XFMh[:, fb, 1 + 128 * jj2:1 + 128 * jj2 + 128]) for fb in range(8)]
            emit_transposes(k, pairs, k.idb[:])
        k.s_mm.wait(ve)
        if hf == 0:
            ve.drain()
            k.s_v.inc(ve.tensor_copy(BCOL, XFMh[:, :, 1024:1025].rearrange("p a 1 -> p a")))
        for tc in range(2):
            gcol = 1024 * hf + 512 * tc
            te.wait_ge(k.s_v.h, k.s_v.n)
            te.wait_ge(k.s_s.h, k.s_s.n)
            k.s_mm.inc(te.matmul(k.psB[:, 2, :], k.ones1b[0:1, 0:128],
                                 IVB[0:1, gcol:gcol + 512], start=True, stop=True))
            te.matmul(k.psB[:, 3, :], k.ones1b[0:1, 0:128],
                      CHB[0:1, gcol:gcol + 512], start=True, stop=False)
            k.s_mm.inc(te.matmul(k.psB[:, 3, :], k.ones1b[0:1, 0:128],
                                 CLB[0:1, gcol:gcol + 512], start=False, stop=True))
            k.s_mm.wait(ve)
            for ft in range(8):
                soff = 1 if ft < 4 else 0
                ve.tensor_tensor(XHh[:, ft, 512 * tc:512 * tc + 512],
                                 XFMh[:, ft, soff + 512 * tc: soff + 512 * tc + 512],
                                 k.psB[:, 2, :], op=ALU.mult)
            ve.drain()
            for ft in range(8):
                ve.tensor_tensor(XHh[:, ft, 512 * tc:512 * tc + 512],
                                 XHh[:, ft, 512 * tc:512 * tc + 512],
                                 k.psB[:, 3, :], op=ALU.add)
            k.s_v.inc(ve.drain())
            k.s_v.wait(se)
            for ft in range(8):
                se.activation(XHh[:, ft, 512 * tc:512 * tc + 512],
                              XHh[:, ft, 512 * tc:512 * tc + 512], AF.Identity,
                              scale=LNGC[:, ft:ft + 1], bias=LNBC[:, ft:ft + 1])
            k.s_s.inc(se.drain())
            k.s_s.wait(ve)
            k.s_s.wait(te)
        full_barrier(k, "tvs")
        ar.release()
        AHq = ar.bf(22, 512)
        W1S = ar.bf(2, 2, 8, 128)

        for qt in range(2):
            toff = 512 * qt
            glast = [0, 0]
            wsems = (k.s_lg, k.s_lh)
            for hc in range(22):
                ocw = 128 if hc < 21 else 42
                b = hc % 2
                ge.wait_ge(k.s_mm.h, glast[b])
                wsems[b].inc(ge.dma_start(out=W1S[:, 0, b, :, 0:ocw],
                                          in_=w1src[:, :, 128 * hc:128 * hc + ocw]), 16)
                wsems[b].inc(ge.dma_start(out=W1S[:, 1, b, :, 0:ocw],
                                          in_=w1src[:, :, 2730 + 128 * hc:2730 + 128 * hc + ocw]), 16)
                wsems[b].wait(te)
                te.wait_ge(k.s_g.h, k.s_g.n)
                te.wait_ge(k.s_s.h, k.s_s.n)
                for ag in range(2):
                    for ic in range(8):
                        mm = te.matmul(k.psA[0:ocw, ag, :], W1S[:, ag, b, ic, 0:ocw],
                                       XHh[:, ic, toff:toff + 512],
                                       start=(ic == 0), stop=(ic == 7))
                    k.s_mm.inc(mm)
                glast[b] = k.s_mm.n
                k.s_mm.wait(se)
                k.s_g.wait(se)
                se.activation(GH[0:ocw, :], k.psA[0:ocw, 1, :], AF.Gelu,
                              bias=B1GC[0:ocw, hc:hc + 1])
                se.activation(AT[0:ocw, :], k.psA[0:ocw, 0, :], AF.Identity,
                              bias=B1AC[0:ocw, hc:hc + 1])
                k.s_s.inc(se.drain())
                k.s_s.wait(ge)
                k.s_g.inc(ge.tensor_tensor(AHq[0:ocw, hc, :], AT[0:ocw, :], GH[0:ocw, :],
                                           op=ALU.mult))
            k.s_g.wait(te)
            k.s_lg.wait(te)
            for tch in range(4):
                j = 4 * (2 * hf + qt) + tch
                te.wait_ge(k.s_v.h, max(0, k.s_v.n - 1))
                for dc in range(2):
                    pb = k.psB[:, 2 * (tch % 2) + dc, :]
                    te.matmul(pb, k.ones1b[0:1, 0:128], B2H[0:1, 512 * dc:512 * dc + 512],
                              start=True, stop=False)
                    te.matmul(pb, k.ones1b[0:1, 0:128], B2L[0:1, 512 * dc:512 * dc + 512],
                              start=False, stop=False)
                    for hc in range(22):
                        kcw = 128 if hc < 21 else 42
                        mm = te.matmul(pb, AHq[0:kcw, hc, 128 * tch:128 * tch + 128],
                                       W2R[0:kcw, hc, 512 * dc:512 * dc + 512],
                                       start=False, stop=(hc == 21))
                    k.s_mm.inc(mm)
                k.s_mm.wait(ve)
                for dc in range(2):
                    ve.tensor_tensor(XINT[:, j, 512 * dc:512 * dc + 512],
                                     XINT[:, j, 512 * dc:512 * dc + 512],
                                     k.psB[:, 2 * (tch % 2) + dc, :], op=ALU.add)
                k.s_v.inc(ve.drain())
        full_barrier(k)
        ar.release()
    ar.release()



def prep_host_tensors(inputs):
    t = {}
    perm = []
    for hh in range(16):
        for half in range(2):
            base = 64 * hh + 32 * half
            perm += [base + i for i in range(0, 32, 2)]
            perm += [base + i for i in range(1, 32, 2)]
    permq = np.array(perm, dtype=np.int64)
    kperm = permq[:256]
    Wq = np.asarray(inputs["L0_attn_Wq"], np.float32)
    Wkv = np.asarray(inputs["L0_attn_Wkv"], np.float32)
    t["wqT"] = np.ascontiguousarray(Wq.T[:, permq])
    wk = Wkv[0:256][kperm]
    wv = Wkv[256:512]
    t["wkvT"] = np.ascontiguousarray(np.concatenate([wk, wv], 0).T)
    t["woT"] = np.ascontiguousarray(np.asarray(inputs["L0_attn_Wout"], np.float32).T)
    t["lam"] = np.ascontiguousarray(np.asarray(inputs["L0_attn_lambda_init"], np.float32).reshape(16, 1))
    t["rmsw"] = np.ascontiguousarray(
        np.tile(np.asarray(inputs["L0_attn_rms_weight"], np.float32)[None, :], (128, 1)))
    t["wg0"] = np.ascontiguousarray(np.asarray(inputs["L0_ffn_w_gate"], np.float32))
    t["wu0"] = np.ascontiguousarray(np.asarray(inputs["L0_ffn_w_up"], np.float32))
    t["wd0"] = np.ascontiguousarray(np.asarray(inputs["L0_ffn_w_down"], np.float32))
    t["wthT"] = np.ascontiguousarray(np.asarray(inputs["L1_attn_W_theta"], np.float32).T)
    t["wphT"] = np.ascontiguousarray(np.asarray(inputs["L1_attn_W_phi"], np.float32).T)
    t["wgT"] = np.ascontiguousarray(np.asarray(inputs["L1_attn_W_g"], np.float32).T)
    t["wqkv"] = np.ascontiguousarray(np.asarray(inputs["L1_ffn_wqkv"], np.float32))
    t["qs"] = np.ascontiguousarray(np.tile(np.asarray(inputs["L1_ffn_q_scale"], np.float32), (1, 2)))
    t["ks"] = np.ascontiguousarray(np.tile(np.asarray(inputs["L1_ffn_k_scale"], np.float32), (1, 2)))
    t["wo1"] = np.ascontiguousarray(np.asarray(inputs["L1_ffn_wo"], np.float32))
    t["lngT"] = np.ascontiguousarray(np.asarray(inputs["L1_ffn_ln_g"], np.float32).reshape(2, 8, 128).transpose(0, 2, 1))
    t["lnbT"] = np.ascontiguousarray(np.asarray(inputs["L1_ffn_ln_b"], np.float32).reshape(2, 8, 128).transpose(0, 2, 1))
    t["w1"] = np.ascontiguousarray(np.asarray(inputs["L1_ffn_w1"], np.float32))
    b1 = np.asarray(inputs["L1_ffn_b1"], np.float32)
    b1p = np.zeros((2, 2, 2816), np.float32)
    b1p[:, 0, 0:2730] = b1[:, 0:2730]
    b1p[:, 1, 0:2730] = b1[:, 2730:5460]
    t["b1ag"] = np.ascontiguousarray(b1p.reshape(2, 2, 22, 128).transpose(0, 1, 3, 2))
    t["w2"] = np.ascontiguousarray(np.asarray(inputs["L1_ffn_w2"], np.float32))
    t["b2"] = np.ascontiguousarray(np.asarray(inputs["L1_ffn_b2"], np.float32).reshape(2, 8, 128))
    return t


def build_program():
    nc = bass.Bass("TRN2", target_bir_lowering=False)
    ctx = ExitStack()
    h = {}
    h["x_in"] = nc.dram_tensor("x_in", [T, D], F32, kind="ExternalInput").ap()
    out_h = nc.dram_tensor("out", [T, D], F32, kind="ExternalOutput").ap()
    for name, shape in [
        ("wqT", [D, D]), ("wkvT", [D, 512]), ("woT", [D, D]),
        ("lam", [16, 1]), ("rmsw", [128, 64]),
        ("wg0", [2816, D]), ("wu0", [2816, D]), ("wd0", [D, 2816]),
        ("wthT", [D, 256]), ("wphT", [D, 256]), ("wgT", [D, D]),
        ("wqkv", [2, D, 1536]), ("qs", [2, 128]), ("ks", [2, 128]),
        ("wo1", [2, 512, D]),
        ("w1", [2, D, 5460]), ("b1ag", [2, 2, 128, 22]), ("w2", [2, 2730, D]),
        ("b2", [2, 8, 128]), ("lngT", [2, 128, 8]), ("lnbT", [2, 128, 8]),
    ]:
        h[name] = nc.dram_tensor(name, shape, F32, kind="ExternalInput").ap()
    h["wqgT_s"] = nc.dram_tensor("wqgT_s", [D, 2816], BF16, kind="Internal").ap()
    h["wquT_s"] = nc.dram_tensor("wquT_s", [D, 2816], BF16, kind="Internal").ap()
    h["wdqT_s"] = nc.dram_tensor("wdqT_s", [2816, D], BF16, kind="Internal").ap()
    h["x1_s"] = nc.dram_tensor("x1_s", [T, D], F32, kind="Internal").ap()

    k = KB(nc, ctx)
    k.out_h = out_h
    blk = ctx.enter_context(nc.Block())
    k.psA = k.ps("psA", [128, 4, 512])
    k.psB = k.ps("psB", [128, 4, 512])
    k.qA = k.sb("qA", [128, 2816], F32)
    k.qB = k.sb("qB", [128, 2816], F32)
    k.qC = k.sb("qC", [128, 2816], F32)
    k.qAb = k.qA[:].bitcast(BF16)
    k.qBb = k.qB[:].bitcast(BF16)
    k.qCb = k.qC[:].bitcast(BF16)
    build_constants(k)
    k.ar = Arena(k, ARENA_N)
    k.ar.mark()
    k.arXQF = k.ar.bf(8, T)
    k.ar.mark()
    k.arQ = k.ar.bf(8, T)
    k.arK2 = k.ar.bf(2, T)
    k.arVT = k.ar.bf(NT, 4, 68)

    emit_l0(k, h, out_h)

    ctx.close()
    return nc


def kernel(**inputs):
    nb = 8
    x_full = np.ascontiguousarray(np.asarray(inputs["L0_attn_x"], np.float32))
    nc = build_program()
    host_tensors = prep_host_tensors(inputs)
    in_maps = []
    for c in range(nb):
        m = {"x_in": np.ascontiguousarray(x_full[c])}
        m.update(host_tensors)
        in_maps.append(m)
    res = run_bass_kernel_spmd(nc, in_maps, core_ids=list(range(nb)))
    return np.stack([r["out"] for r in res.results], axis=0)


# version 04
# speedup vs baseline: 1.5474x; 1.5474x over previous
import os
import dataclasses as _dc
from contextlib import ExitStack
import numpy as np
import ml_dtypes
import concourse.bass as bass
import concourse.mybir as mybir
from concourse.bass_utils import run_bass_kernel_spmd

F32 = mybir.dt.float32
BF16 = mybir.dt.bfloat16
AF = mybir.ActivationFunctionType
ALU = mybir.AluOpType
AX = mybir.AxisListType

T = 2048
NT = 16
D = 1024
MAGIC = 12582912.0
DEV_STAGE = os.environ.get("KDEV_STAGE", "full")
DEV_TAP = os.environ.get("KDEV_TAP", "")
ARENA_N = 85000


def _prod(xs):
    r = 1
    for x in xs:
        r *= x
    return r


class Sem:
    def __init__(self, nc, ctx, name):
        self.h = ctx.enter_context(nc.semaphore(name))
        self.n = 0

    def inc(self, instr, k=1):
        instr.then_inc(self.h, k)
        self.n += k
        return instr

    def wait(self, eng, val=None):
        eng.wait_ge(self.h, self.n if val is None else val)


class Arena:
    def __init__(self, k, n):
        self.t = k.nc.sbuf_tensor("ARENA", [128, n], BF16)
        self.th = k.ctx.enter_context(self.t)
        self.off = 0
        self.n = n
        self.stack = []

    def bf(self, *shape):
        m = _prod(shape)
        assert self.off + m <= self.n, f"arena overflow: {self.off}+{m} > {self.n}"
        ap = self.th[:, self.off:self.off + m]
        self.off += m
        if len(shape) > 1:
            names = " ".join(f"d{i}" for i in range(len(shape)))
            kw = {f"d{i}": s for i, s in enumerate(shape)}
            ap = ap.rearrange(f"p ({names}) -> p {names}", **kw)
        return ap

    def f32(self, *shape):
        m = _prod(shape)
        if self.off % 2:
            self.off += 1
        assert self.off + 2 * m <= self.n, f"arena overflow f32: {self.off}+{2*m} > {self.n}"
        ap = self.th[:, self.off:self.off + 2 * m].bitcast(F32)
        self.off += 2 * m
        if len(shape) > 1:
            names = " ".join(f"d{i}" for i in range(len(shape)))
            kw = {f"d{i}": s for i, s in enumerate(shape)}
            ap = ap.rearrange(f"p ({names}) -> p {names}", **kw)
        return ap

    def mark(self):
        self.stack.append(self.off)

    def release(self):
        self.off = self.stack.pop()


class KB:
    def __init__(self, nc, ctx):
        self.nc = nc
        self.ctx = ctx
        self.te, self.ve, self.se, self.ge, self.sy = \
            nc.tensor, nc.vector, nc.scalar, nc.gpsimd, nc.sync
        self.s_ld = Sem(nc, ctx, "s_ld")
        self.s_mm = Sem(nc, ctx, "s_mm")
        self.s_v = Sem(nc, ctx, "s_v")
        self.s_s = Sem(nc, ctx, "s_s")
        self.s_g = Sem(nc, ctx, "s_g")
        self.s_st = Sem(nc, ctx, "s_st")
        self.s_x = Sem(nc, ctx, "s_x")
        self.s_la = Sem(nc, ctx, "s_la")
        self.s_lb = Sem(nc, ctx, "s_lb")
        self.s_lg = Sem(nc, ctx, "s_lg")
        self.s_lh = Sem(nc, ctx, "s_lh")
        self.s_lc = Sem(nc, ctx, "s_lc")
        self.s_dmp = Sem(nc, ctx, "s_dmp")

    def sb(self, name, shape, dtype):
        return self.ctx.enter_context(self.nc.sbuf_tensor(name, shape, dtype))

    def ps(self, name, shape):
        return self.ctx.enter_context(self.nc.psum_tensor(name, shape, F32))


def dump_fm(k, buf, out_h, nt):
    full_barrier(k)
    ov = out_h.rearrange("(a p) f -> p a f", p=128)
    W = buf.shape[-1]
    for ft in range(nt):
        for hf in range(2):
            lo = 1024 * hf
            hi = min(lo + 1024, W)
            if lo >= hi:
                continue
            k.s_dmp.inc(k.ge.dma_start(out=ov[:, 2 * ft + hf, 0:hi - lo],
                                      in_=buf[:, ft, lo:hi]), 16)
    k.s_dmp.wait(k.ge)
    k.s_dmp.wait(k.sy)


def full_barrier(k, engines="tvsg", dma=True):
    s = k.s_x
    if "t" in engines:
        s.inc(k.te.drain())
    if "v" in engines:
        s.inc(k.ve.drain())
    if "s" in engines:
        s.inc(k.se.drain())
    if "g" in engines:
        s.inc(k.ge.drain())
    for e in (k.te, k.ve, k.se, k.ge, k.sy):
        e.wait_ge(s.h, s.n)
        if dma and engines == "tvsg":
            for sem in (k.s_ld, k.s_la, k.s_lb, k.s_lg, k.s_lh, k.s_lc, k.s_st, k.s_dmp):
                sem.wait(e)


def build_constants(k):
    k.mcol = k.sb("c_mcol", [128, 1], F32)
    k.rc6 = k.sb("c_rc6", [128, 1], F32)
    k.rc12 = k.sb("c_rc12", [128, 1], F32)
    k.idf = k.sb("c_idf", [128, 128], F32)
    k.idb = k.sb("c_idb", [128, 128], BF16)
    k.ones1b = k.sb("c_ones1b", [128, 128], BF16)
    k.M01 = k.sb("c_m01", [128, 128], BF16)
    k.P32 = k.sb("c_p32", [128, 128], BF16)
    k.P4 = k.sb("c_p4", [128, 128], BF16)
    k.qbm = k.sb("qbm", [128, 176], F32)
    k.qib6 = k.sb("qib6", [128, 176], F32)
    k.qbm12 = k.sb("qbm12", [128, 176], BF16)
    k.stats = k.sb("stats", [128, 48], F32)
    k.smallc = k.sb("smallc", [128, 16], F32)
    v, g = k.ve, k.ge
    v.memset(k.mcol[:], 1.0)
    v.drain()
    v.tensor_scalar(k.mcol[:], k.mcol[:], 2.0 ** -30, None, op0=ALU.add)
    v.drain()
    v.tensor_scalar(k.mcol[:], k.mcol[:], -1.0, None, op0=ALU.add)
    v.drain()
    v.tensor_scalar(k.mcol[:], k.mcol[:], 1.5 * 2.0 ** 82, MAGIC,
                    op0=ALU.mult, op1=ALU.add)
    v.memset(k.rc6[:], 6.0)
    v.memset(k.rc12[:], 12.0)
    v.memset(k.ones1b[:], 1.0)
    v.drain()
    v.reciprocal(k.rc6[:], k.rc6[:])
    v.reciprocal(k.rc12[:], k.rc12[:])
    k.s_g.inc(g.iota(k.idf[:], pattern=[[1, 128]], base=0, channel_multiplier=-1,
                     allow_small_or_imprecise_dtypes=True))
    k.s_g.wait(v)
    v.tensor_scalar(k.idb[:], k.idf[:], 0.0, None, op0=ALU.is_equal)
    v.tensor_scalar(k.M01[:], k.idf[:], 0.0, None, op0=ALU.is_ge)
    v.drain()
    k.s_v.inc(v.tensor_scalar(k.idf[:], k.idf[:], 0.0, None, op0=ALU.is_equal))
    DIFq = k.qA[:, 0:128]
    CM32 = k.qB[:, 0:128]
    SC2 = k.qC[:, 4:260]
    g.iota(DIFq, pattern=[[1, 128]], base=0, channel_multiplier=-1,
           allow_small_or_imprecise_dtypes=True)
    g.iota(CM32, pattern=[[0, 4], [1, 32]], base=0, channel_multiplier=0,
           allow_small_or_imprecise_dtypes=True)
    k.s_g.inc(g.drain())
    k.s_g.wait(v)
    k.pcol = k.sb("c_pcol", [128, 1], F32)
    k.pm16 = k.sb("c_pm16", [128, 1], F32)
    k.altc = k.sb("c_altc", [128, 1], F32)
    v.tensor_scalar(k.pcol[:], DIFq[:, 0:1], -1.0, None, op0=ALU.mult)
    v.drain()
    v.tensor_scalar(k.pm16[:], k.pcol[:], -7.5, 0.0625, op0=ALU.add, op1=ALU.mult)
    v.drain()
    v.tensor_scalar(k.pm16[:], k.pm16[:], k.mcol[:, 0:1], k.mcol[:, 0:1],
                    op0=ALU.add, op1=ALU.subtract)
    v.drain()
    v.scalar_tensor_tensor(k.pm16[:], k.pm16[:], -16.0, k.pcol[:],
                           op0=ALU.mult, op1=ALU.add)
    v.drain()
    v.tensor_scalar(k.altc[:], k.pcol[:], -15.5, 0.03125, op0=ALU.add, op1=ALU.mult)
    v.drain()
    v.tensor_scalar(k.altc[:], k.altc[:], k.mcol[:, 0:1], k.mcol[:, 0:1],
                    op0=ALU.add, op1=ALU.subtract)
    v.drain()
    v.scalar_tensor_tensor(k.altc[:], k.altc[:], -32.0, k.pcol[:],
                           op0=ALU.mult, op1=ALU.add)
    v.drain()
    v.tensor_scalar(k.altc[:], k.altc[:], 16.0, None, op0=ALU.is_ge)
    v.drain()
    v.tensor_scalar(k.altc[:], k.altc[:], -2.0, 1.0, op0=ALU.mult, op1=ALU.add)
    v.drain()
    CMOD = k.qA[:, 128:256]
    g.iota(CMOD, pattern=[[0, 4], [1, 32]], base=0, channel_multiplier=0,
           allow_small_or_imprecise_dtypes=True)
    k.s_g.inc(g.drain())
    k.s_g.wait(v)
    v.tensor_scalar(k.P32[:], DIFq, 16.0, None, op0=ALU.is_equal)
    v.tensor_scalar(SC2[:, 0:128], DIFq, -16.0, None, op0=ALU.is_equal)
    v.tensor_scalar(SC2[:, 128:256], CMOD, 16.0, None, op0=ALU.is_ge)
    v.drain()
    v.tensor_tensor(k.P32[:], k.P32[:], SC2[:, 128:256], op=ALU.mult)
    v.drain()
    v.tensor_scalar(SC2[:, 128:256], SC2[:, 128:256], -1.0, 1.0, op0=ALU.mult, op1=ALU.add)
    v.drain()
    v.tensor_tensor(SC2[:, 0:128], SC2[:, 0:128], SC2[:, 128:256], op=ALU.mult)
    v.drain()
    v.tensor_tensor(k.P32[:], k.P32[:], SC2[:, 0:128], op=ALU.add)
    v.drain()
    v.tensor_scalar(k.P32[:], k.P32[:], k.altc[:, 0:1], None, op0=ALU.mult)
    v.drain()
    v.tensor_scalar(SC2[:, 0:128], DIFq, 0.0, None, op0=ALU.is_le)
    v.tensor_scalar(SC2[:, 128:256], DIFq, -32.0, None, op0=ALU.is_gt)
    v.drain()
    v.tensor_scalar(CM32, CM32, 0.0, None, op0=ALU.is_equal)
    v.tensor_tensor(k.P4[:], SC2[:, 0:128], SC2[:, 128:256], op=ALU.mult)
    v.drain()
    k.s_v.inc(v.tensor_tensor(k.P4[:], k.P4[:], CM32, op=ALU.mult))
    ln4, pi4, twopi, i2pi, hpi, sig, pip, pin = (k.smallc[:, i:i + 1] for i in range(8))
    v.memset(ln4, 10000.0)
    v.memset(pi4, 1.0)
    v.memset(sig, 32.0)
    k.s_v.inc(v.drain())
    k.s_v.wait(k.se)
    k.se.activation(ln4, ln4, AF.Ln)
    k.se.activation(pi4, pi4, AF.Arctan)
    k.s_s.inc(k.se.drain())
    k.s_s.wait(v)
    v.tensor_scalar(twopi, pi4, 8.0, None, op0=ALU.mult)
    v.tensor_scalar(hpi, pi4, 2.0, None, op0=ALU.mult)
    v.tensor_scalar(pip, pi4, 4.0, None, op0=ALU.mult)
    v.tensor_scalar(pin, pi4, -4.0, None, op0=ALU.mult)
    v.reciprocal(sig, sig)
    v.drain()
    v.reciprocal(i2pi, twopi)
    k.s_v.inc(v.drain())
    k.s_v.wait(k.se)
    k.s_s.inc(k.se.activation(sig, sig, AF.Sqrt))
    k.ln4, k.pi4, k.twopi, k.i2pi, k.hpi, k.sig = ln4, pi4, twopi, i2pi, hpi, sig
    k.pip, k.pin = pip, pin


def emit_quant(k, S, Q, ge_q=True):
    v, g = k.ve, k.ge
    k.s_g.wait(v)
    W = S.shape[-1]
    nb = W // 16
    A, B, C = k.qA, k.qB, k.qC
    Ab, Bb, Cb = k.qAb, k.qBb, k.qCb
    bm, ib6, bm12 = k.qbm, k.qib6, k.qbm12
    S3 = S.rearrange("p (a b) -> p a b", b=16)
    v.tensor_reduce(bm[:, 0:nb], S3, axis=AX.X, op=ALU.max, apply_absolute_value=True)
    v.drain()
    v.tensor_scalar(bm[:, 0:nb], bm[:, 0:nb], 6e-12, None, op0=ALU.max)
    v.drain()
    v.reciprocal(ib6[:, 0:nb], bm[:, 0:nb])
    v.tensor_scalar(bm12[:, 0:nb], bm[:, 0:nb], k.rc6[:, 0:1], None, op0=ALU.mult)
    v.drain()
    v.tensor_scalar(ib6[:, 0:nb], ib6[:, 0:nb], 6.0, None, op0=ALU.mult)
    v.drain()
    ib6b = ib6[:, 0:nb].unsqueeze(2).broadcast_to((128, nb, 16))
    v.tensor_tensor(A[:, 0:W].rearrange("p (a b) -> p a b", b=16), S3, ib6b, op=ALU.mult)
    v.drain()
    v.tensor_scalar(B[:, 0:W], A[:, 0:W], -2.0, 2.0, op0=ALU.max, op1=ALU.min)
    v.tensor_scalar(C[:, 0:W], A[:, 0:W], -4.0, 4.0, op0=ALU.max, op1=ALU.min)
    k.s_v.inc(v.drain())
    k.s_v.wait(g)
    k.s_g.inc(g.tensor_tensor(C[:, 0:W], C[:, 0:W], A[:, 0:W], op=ALU.add))
    k.s_g.wait(v)
    v.scalar_tensor_tensor(A[:, 0:W], C[:, 0:W], 0.5, B[:, 0:W], op0=ALU.mult, op1=ALU.add)
    v.drain()
    z = Bb[:, 0:W]
    v.tensor_scalar(z, A[:, 0:W], k.mcol[:, 0:1], k.mcol[:, 0:1],
                    op0=ALU.add, op1=ALU.subtract)
    v.drain()
    m1, c6 = Cb[:, 0:W], Ab[:, 0:W]
    z2, s1 = Ab[:, W:2 * W], Cb[:, W:2 * W]
    v.tensor_scalar(m1, z, 0.5, -2.0, op0=ALU.mult, op1=ALU.max)
    v.tensor_scalar(c6, z, -6.0, 6.0, op0=ALU.max, op1=ALU.min)
    v.tensor_scalar(z2, z, 2.0, None, op0=ALU.mult)
    v.drain()
    v.tensor_scalar(m1, m1, 2.0, None, op0=ALU.min)
    v.drain()
    v.tensor_tensor(s1, m1, c6, op=ALU.add)
    v.drain()
    psi = Cb[:, 0:W]
    v.tensor_tensor(psi, z2, s1, op=ALU.subtract)
    k.s_v.inc(v.drain())
    bm12b = bm12[:, 0:nb].unsqueeze(2).broadcast_to((128, nb, 16))
    Q3 = Q.rearrange("p (a b) -> p a b", b=16)
    psi3 = psi.rearrange("p (a b) -> p a b", b=16)
    if ge_q:
        k.s_v.wait(g)
        k.s_g.inc(g.tensor_tensor(Q3, psi3, bm12b, op=ALU.mult))
    else:
        v.tensor_tensor(Q3, psi3, bm12b, op=ALU.mult)
        k.s_v.inc(v.drain())


def emit_transposes(k, pairs, identity):
    ve = k.ve
    if not hasattr(k, "_tp_i"):
        k._tp_i = 0
    for (src, dst) in pairs:
        k.te.wait_ge(k.s_v.h, max(0, k.s_v.n - 3))
        pv = k.psA[:, k._tp_i % 4, :].bitcast(BF16)[:, 0:dst.shape[-1]]
        k.s_mm.inc(k.te.transpose(pv, src, identity))
        k.s_mm.wait(ve)
        k.s_v.inc(ve.tensor_copy(dst, pv))
        k._tp_i += 1


def emit_ln_stats(k, xsrc_hbm, XTa, SCR, nt=NT):
    v, s, sy = k.ve, k.se, k.sy
    ssum = k.stats[:, 0:nt]
    ssq = k.stats[:, nt:2 * nt]
    inv = k.stats[:, 2 * nt:3 * nt]
    xr = xsrc_hbm.rearrange("(j p) f -> p j f", p=128)
    sems = (k.s_la, k.s_lb)
    for j in range(min(2, nt)):
        sems[j % 2].inc(sy.dma_start(out=XTa[:, j % 2, :], in_=xr[:, j, :]), 16)
    for j in range(nt):
        xt = XTa[:, j % 2, :]
        sems[j % 2].wait(v)
        sems[j % 2].wait(s)
        v.tensor_reduce(ssum[:, j:j + 1], xt, axis=AX.X, op=ALU.add)
        s.activation(SCR[:, (j % 2) * D:(j % 2) * D + D], xt, AF.Square,
                     accum_out=ssq[:, j:j + 1])
        if j + 2 < nt:
            k.s_s.inc(s.drain())
            k.s_v.inc(v.drain())
            k.s_s.wait(sy)
            k.s_v.wait(sy)
            sems[j % 2].inc(sy.dma_start(out=XTa[:, j % 2, :], in_=xr[:, j + 2, :]), 16)
    k.s_s.inc(s.drain())
    k.s_s.wait(v)
    v.drain()
    v.tensor_scalar(ssum[:], ssum[:], 1.0 / D, None, op0=ALU.mult)
    v.drain()
    v.tensor_tensor(inv[:], ssum[:], ssum[:], op=ALU.mult)
    v.tensor_scalar(ssq[:], ssq[:], 1.0 / D, None, op0=ALU.mult)
    v.drain()
    v.tensor_tensor(ssq[:], ssq[:], inv[:], op=ALU.subtract)
    v.drain()
    v.tensor_scalar(ssq[:], ssq[:], 1e-5, None, op0=ALU.add)
    v.drain()
    v.reciprocal(ssq[:], ssq[:])
    k.s_v.inc(v.drain())
    k.s_v.wait(s)
    s.activation(inv[:], ssq[:], AF.Sqrt)
    k.s_s.inc(s.drain())
    k.s_s.wait(v)
    k.s_s.wait(sy)
    k.s_v.wait(sy)



def emit_l0(k, h, out_h):
    te, ve, se, ge, sy = k.te, k.ve, k.se, k.ge, k.sy
    ar = k.ar

    XQF = k.arXQF
    ar.mark()
    FM0 = ar.bf(8, T)
    ar.mark()
    XTa = ar.f32(2, D)
    XNT = ar.f32(D)
    XB = ar.bf(D)
    XQT = ar.bf(D)
    WLD = ar.f32(2816)
    WQB = ar.bf(2816)

    emit_ln_stats(k, h["x_in"], XTa, k.qA[:, 0:2 * D])

    xr = h["x_in"].rearrange("(j p) f -> p j f", p=128)
    outr = out_h.rearrange("(j p) f -> p j f", p=128)
    ssum = k.stats[:, 0:NT]
    inv = k.stats[:, 2 * NT:3 * NT]
    sems = (k.s_la, k.s_lb)
    for j in range(min(2, NT)):
        sems[j % 2].inc(sy.dma_start(out=XTa[:, j % 2, :], in_=xr[:, j, :]), 16)
    for j in range(NT):
        xt = XTa[:, j % 2, :]
        sems[j % 2].wait(ve)
        k.s_st.wait(ve)
        k.s_s.wait(ve)
        ve.tensor_scalar(XNT, xt, ssum[:, j:j + 1], inv[:, j:j + 1],
                         op0=ALU.subtract, op1=ALU.mult)
        k.s_v.inc(ve.drain())
        if j + 2 < NT:
            k.s_v.wait(sy)
            sems[j % 2].inc(sy.dma_start(out=XTa[:, j % 2, :], in_=xr[:, j + 2, :]), 16)
        if DEV_STAGE == "ln":
            k.s_v.wait(sy)
            k.s_st.inc(sy.dma_start(out=outr[:, j, :], in_=XNT), 16)
            continue
        k.s_v.wait(se)
        k.s_s.inc(se.activation(XB, XNT, AF.Copy, bias=0.0))
        emit_quant(k, XNT, XQT)
        if DEV_STAGE == "xq":
            k.s_g.wait(ge)
            k.s_lg.inc(ge.dma_start(out=outr[:, j, :], in_=XQT), 16)
            k.s_lg.wait(ve)
            continue
        k.s_s.wait(te)
        k.s_v.wait(te)
        k.s_g.wait(te)
        pairs = [(XB[:, 128 * fb:128 * fb + 128], FM0[:, fb, 128 * j:128 * j + 128])
                 for fb in range(8)]
        pairs += [(XQT[:, 128 * fb:128 * fb + 128], XQF[:, fb, 128 * j:128 * j + 128])
                  for fb in range(8)]
        emit_transposes(k, pairs, k.idb[:])
        k.s_mm.wait(se)
        k.s_mm.wait(ve)
        k.s_mm.wait(ge)

    if DEV_STAGE in ("ln", "xq"):
        k.s_st.wait(sy)
        k.s_lg.wait(sy)
        return

    emit_l0_wquant(k, h, WLD, WQB, XTa.rearrange("p a b -> p (a b)").bitcast(BF16)[:, 0:2816])
    full_barrier(k)
    ar.release()
    emit_l0_attn(k, h, FM0, out_h)


def emit_l0_attn(k, h, FM0, out_h):
    te, ve, se, ge, sy = k.te, k.ve, k.se, k.ge, k.sy
    ar = k.ar
    Q = k.arQ
    VT = k.arVT
    K2 = k.arK2

    ar.mark()
    WQ = ar.bf(8, D)
    WKV = ar.bf(8, 512)
    VB = ar.bf(2, T)
    COSA = ar.bf(T)
    SINA = ar.bf(T)
    COSQ = ar.bf(T)
    SINQ = ar.bf(T)
    invc = ar.f32(1)

    k.s_lg.inc(ge.dma_start(out=WQ, in_=h["wqT"].rearrange("(a p) o -> p a o", p=128)), 16)
    k.s_lg.inc(ge.dma_start(out=WKV, in_=h["wkvT"].rearrange("(a p) o -> p a o", p=128)), 16)

    invf = invc
    ve.tensor_scalar(invf, k.pm16[:], -0.0625, None, op0=ALU.mult)
    ve.drain()
    ve.tensor_scalar(invf, invf, k.ln4, None, op0=ALU.mult)
    k.s_v.inc(ve.drain())
    k.s_v.wait(se)
    k.s_s.inc(se.activation(invf, invf, AF.Exp))

    TGRID = k.qA[:, 0:T]
    ARG = k.qB[:, 0:T]
    KQ = k.qC[:, 0:T]
    k.s_g.inc(ge.iota(TGRID, pattern=[[1, T]], base=0, channel_multiplier=0,
                      allow_small_or_imprecise_dtypes=True))
    k.s_g.wait(ve)
    k.s_s.wait(ve)

    def build_trig(dst, use_hpi):
        if use_hpi:
            ve.tensor_scalar(ARG, TGRID, invf, k.hpi, op0=ALU.mult, op1=ALU.add)
        else:
            ve.tensor_scalar(ARG, TGRID, invf, None, op0=ALU.mult)
        ve.drain()
        ve.tensor_scalar(KQ, ARG, k.i2pi, None, op0=ALU.mult)
        ve.drain()
        ve.tensor_scalar(KQ, KQ, k.mcol[:, 0:1], k.mcol[:, 0:1], op0=ALU.add, op1=ALU.subtract)
        ve.drain()
        ve.scalar_tensor_tensor(KQ, KQ, k.twopi, ARG, op0=ALU.mult, op1=ALU.subtract)
        ve.drain()
        ve.tensor_scalar(KQ, KQ, k.pin, k.pip, op0=ALU.max, op1=ALU.min)
        k.s_v.inc(ve.drain())
        k.s_v.wait(se)
        k.s_s.inc(se.activation(KQ, KQ, AF.Sin))
        k.s_s.wait(ve)
        ve.tensor_scalar(dst, KQ, -1.0, None, op0=ALU.mult)
        k.s_v.inc(ve.drain())

    build_trig(COSA, True)
    build_trig(SINA, False)
    k.s_v.wait(ve)
    ve.tensor_scalar(COSQ, COSA, k.sig, None, op0=ALU.mult)
    k.s_v.inc(ve.tensor_scalar(SINQ, SINA, k.sig, None, op0=ALU.mult))

    k.s_ld.wait(te)
    k.s_lg.wait(te)
    k.s_v.wait(te)

    def proj(WT, nout, dst, woff=0):
        for oc in range(nout):
            for tc in range(4):
                pb = k.psA[:, tc % 4, :]
                te.wait_ge(k.s_s.h, max(0, k.s_s.n - 3))
                for ic in range(8):
                    mm = te.matmul(pb, WT[:, ic, 128 * (oc + woff):128 * (oc + woff) + 128],
                                   FM0[:, ic, 512 * tc:512 * tc + 512],
                                   start=(ic == 0), stop=(ic == 7))
                k.s_mm.inc(mm)
                k.s_mm.wait(se)
                k.s_s.inc(se.activation(dst[:, oc, 512 * tc:512 * tc + 512], pb,
                                        AF.Copy, bias=0.0))

    proj(WQ, 8, Q)
    if DEV_TAP == "qpre":
        dump_fm(k, Q, k.out_h, 8)
        return
    proj(WKV, 2, K2)
    k.s_v.wait(ve)
    ve.memset(VT[:], 1.0)
    k.s_v.inc(ve.drain())
    proj(WKV, 2, VB.rearrange("p a t -> p a t"), woff=2)
    k.s_s.wait(te)
    k.s_v.wait(te)
    base_v = k.s_v.n
    i = 0
    for hk in range(4):
        fr, ft = (64 * hk) % 128, (64 * hk) // 128
        for j in range(NT):
            if i >= 4:
                te.wait_ge(k.s_v.h, base_v + i - 3)
            pv = k.psA[:, i % 4, :].bitcast(BF16)[:, 0:64]
            idsl = k.idb[fr:fr + 64, fr:fr + 64]
            k.s_mm.inc(te.transpose(pv, VB[fr:fr + 64, ft, 128 * j:128 * j + 128], idsl))
            k.s_mm.wait(ve)
            k.s_v.inc(ve.tensor_copy(VT[:, j, hk, 0:64], pv))
            i += 1

    RT1 = k.qA[:].bitcast(BF16)[:, 0:T]
    full_barrier(k, "tvs")

    def rope(SRC, n_tiles, COS, SIN):
        for ft in range(n_tiles):
            k.s_v.wait(te)
            for c in range(4):
                k.s_mm.inc(te.matmul(k.psB[:, c, :], k.P32[:], SRC[:, ft, 512 * c:512 * c + 512],
                                     start=True, stop=True))
            ve.tensor_tensor(RT1, SRC[:, ft, :], COS, op=ALU.mult)
            k.s_mm.wait(ve)
            for c in range(4):
                ve.tensor_tensor(k.psB[:, c, :], k.psB[:, c, :],
                                 SIN[:, 512 * c:512 * c + 512], op=ALU.mult)
            ve.drain()
            for c in range(4):
                ve.tensor_tensor(SRC[:, ft, 512 * c:512 * c + 512],
                                 RT1[:, 512 * c:512 * c + 512], k.psB[:, c, :], op=ALU.add)
            k.s_v.inc(ve.drain())

    rope(Q, 8, COSQ, SINQ)
    if DEV_TAP == "qrope":
        dump_fm(k, Q, k.out_h, 8)
        return
    rope(K2, 2, COSA, SINA)
    if DEV_TAP == "krope":
        dump_fm(k, K2, k.out_h, 2)
        return
    full_barrier(k, "tv")
    ar.release()
    emit_l0_scores(k, h, Q, K2, VT, out_h)


def emit_l0_scores(k, h, Q, K2, VT, out_h):
    nc = k.nc
    te, ve, se, ge, sy = k.te, k.ve, k.se, k.ge, k.sy
    ar = k.ar
    P4t = k.P4

    ar.mark()
    KARR = ar.bf(4, T)
    BT = ar.bf(T)
    UT = ar.bf(4, 512)
    OE = ar.bf(4, 512)
    OTT = ar.bf(4, 4, 72)
    YC = ar.bf(4, 2, 64)
    WROW = ar.bf(64)
    RC = ar.f32(24)
    KMG = ar.f32(2)
    KMC = ar.f32(4)
    NLAM = ar.f32(16)
    LAMC = ar.f32(1)

    k.s_v.wait(ve)
    for v_ in range(4):
        src = K2[64 * (v_ % 2):64 * (v_ % 2) + 64, v_ // 2, :]
        ve.tensor_copy(KARR[0:64, v_, :], src)
        ve.tensor_copy(KARR[64:128, v_, :], src)
    k.s_v.inc(ve.drain())

    lam_s = nc.dram_tensor("lam_scr", [16, 1], F32, kind="Internal").ap()
    km_s = nc.dram_tensor("km_scr", [128, 2], F32, kind="Internal").ap()
    k.s_ld.inc(sy.dma_start(out=LAMC[0:16, :], in_=h["lam"][:, :]), 16)
    k.s_ld.wait(se)
    se.activation(LAMC[0:16, :], LAMC[0:16, :], AF.Sigmoid)
    k.s_s.inc(se.drain())
    k.s_s.wait(ve)
    k.s_v.inc(ve.tensor_scalar(LAMC[0:16, :], LAMC[0:16, :], -1.0, None, op0=ALU.mult))
    k.s_v.wait(sy)
    k.s_st.inc(sy.dma_start(out=lam_s[:, :], in_=LAMC[0:16, :]), 16)
    k.s_st.wait(sy)
    nl_in = _dc.replace(lam_s, offset=0, ap=[[0, 128], [1, 16]])
    k.s_ld.inc(sy.dma_start(out=NLAM, in_=nl_in), 16)

    SQB = k.qA[:].bitcast(BF16)[:, 0:T]
    BTS = k.qB[:, 0:T]
    for kt2 in range(2):
        k.s_v.wait(se)
        k.s_g.wait(se)
        k.s_s.inc(se.activation(SQB, K2[:, kt2, :], AF.Square))
        k.s_s.wait(te)
        for cc in range(4):
            k.s_mm.inc(te.matmul(k.psB[:, cc, :], P4t[:], SQB[:, 512 * cc:512 * cc + 512],
                                 start=True, stop=True))
        k.s_mm.wait(ve)
        for cc in range(4):
            ve.tensor_reduce(KMG[:, kt2:kt2 + 1] if cc == 0 else BTS[:, cc:cc + 1],
                             k.psB[:, cc, :], axis=AX.X, op=ALU.max)
        ve.drain()
        ve.tensor_tensor(KMG[:, kt2:kt2 + 1], KMG[:, kt2:kt2 + 1], BTS[:, 1:2], op=ALU.max)
        ve.drain()
        ve.tensor_tensor(BTS[:, 2:3], BTS[:, 2:3], BTS[:, 3:4], op=ALU.max)
        ve.drain()
        ve.tensor_tensor(KMG[:, kt2:kt2 + 1], KMG[:, kt2:kt2 + 1], BTS[:, 2:3], op=ALU.max)
        k.s_v.inc(ve.drain())
    k.s_v.wait(se)
    k.s_s.inc(se.activation(KMG, KMG, AF.Sqrt))
    k.s_s.wait(sy)
    k.s_st.inc(sy.dma_start(out=km_s[:, :], in_=KMG), 16)
    k.s_st.wait(sy)
    for kv in range(4):
        for g4 in range(4):
            off = 128 * (kv % 2) + kv // 2 + 64 * (g4 % 2)
            src = _dc.replace(km_s, offset=off, ap=[[0, 32], [1, 1]])
            k.s_ld.inc(sy.dma_start(out=KMC[32 * g4:32 * g4 + 32, kv:kv + 1], in_=src), 16)

    k.s_lg.inc(ge.dma_start(out=WROW, in_=h["rmsw"][:, :]), 16)

    ones1b = k.ones1b
    M01 = k.M01
    k.s_ld.wait(ve)
    k.s_lg.wait(ve)
    k.s_ld.wait(te)
    for qt in range(8):
        kvv = qt // 2
        k.s_v.wait(se)
        k.s_g.wait(se)
        k.s_s.inc(se.activation(SQB, Q[:, qt, :], AF.Square))
        k.s_s.wait(te)
        k.s_v.wait(te)
        for cc in range(4):
            k.s_mm.inc(te.matmul(k.psB[:, cc, :], P4t[:], SQB[:, 512 * cc:512 * cc + 512],
                                 start=True, stop=True))
        k.s_mm.wait(se)
        for cc in range(4):
            se.activation(BTS[:, 512 * cc:512 * cc + 512], k.psB[:, cc, :], AF.Sqrt)
        k.s_s.inc(se.drain())
        k.s_s.wait(ve)
        k.s_v.inc(ve.tensor_scalar(BT, BTS, KMC[:, kvv:kvv + 1], -1.0,
                                   op0=ALU.mult, op1=ALU.mult))
        if DEV_TAP == "bt" and qt == 0:
            dump_fm(k, BT.unsqueeze(1), k.out_h, 1)
            return
        k.s_v.wait(te)

        for c in range(4):
            nkt = 4 * c + 4
            k.s_v.wait(te)
            for kt in range(nkt):
                masked = kt >= 4 * c
                off = 128 * (kt - 4 * c) if masked else 0
                w = 512 - off
                te.wait_ge(k.s_s.h, k.s_s.n)
                for jj in range(4):
                    te.matmul(k.psA[:, jj, off:512], ones1b[32 * jj:32 * jj + 1, 0:128],
                              BT[32 * jj:32 * jj + 1, 512 * c + off:512 * c + 512],
                              start=True, stop=False, tile_position=(32 * jj, 0))
                    mm = te.matmul(k.psA[:, jj, off:512],
                                   KARR[32 * jj:32 * jj + 32, kvv, 128 * kt:128 * kt + 128],
                                   Q[32 * jj:32 * jj + 32, qt, 512 * c + off:512 * c + 512],
                                   start=False, stop=True, tile_position=(32 * jj, 0))
                    k.s_mm.inc(mm)
                k.s_mm.wait(se)
                k.s_v.wait(se)
                for jj in range(4):
                    se.activation(UT[:, jj, off:512], k.psA[:, jj, off:512], AF.Exp)
                k.s_s.inc(se.drain())
                if masked:
                    k.s_s.wait(ve)
                    uview = UT[:, :, off:off + 128]
                    m01b = M01[:].unsqueeze(1).broadcast_to((128, 4, 128))
                    k.s_v.inc(ve.tensor_tensor(uview, uview, m01b, op=ALU.mult))
                if DEV_TAP == "ut" and qt == 0 and c == 3 and kt == 0:
                    full_barrier(k)
                    dump_fm(k, UT.rearrange("p a b -> p (a b)").unsqueeze(1), k.out_h, 1)
                    return
                k.s_s.wait(te)
                if masked:
                    k.s_v.wait(te)
                for jj in range(4):
                    k.s_mm.inc(te.matmul(k.psB[0:65, jj, off:512], VT[:, kt, kvv, 0:65],
                                         UT[:, jj, off:512], start=(kt == 0), stop=(kt == nkt - 1)))
            k.s_mm.wait(ve)
            for jj in range(4):
                ve.tensor_copy(OE[0:65, jj, :], k.psB[0:65, jj, :])
            k.s_v.inc(ve.drain())
            k.s_v.wait(te)
            i = 0
            for cc in range(4):
                for jj in range(4):
                    if i >= 4:
                        te.wait_ge(k.s_v.h, k.s_v.n - 3)
                    pv = k.psA[:, i % 4, :].bitcast(BF16)[:, 0:65]
                    k.s_mm.inc(te.transpose(pv, OE[0:65, jj, 128 * cc:128 * cc + 128],
                                            k.idb[0:65, 0:65]))
                    k.s_mm.wait(ve)
                    k.s_v.inc(ve.tensor_copy(OTT[:, cc, jj, 0:65], pv))
                    i += 1
            if DEV_TAP == "ott" and qt == 0 and c == 0:
                full_barrier(k)
                dump_fm(k, OTT.rearrange("p a b c -> p (a b c)").unsqueeze(1), k.out_h, 1)
                return
            ve.drain()
            ve.tensor_copy(RC[:, 0:16].rearrange("p (b a) -> p b a", b=4),
                           OTT[:, :, :, 64:65].rearrange("p a b 1 -> p a b").transpose([0, 2, 1]))
            ve.drain()
            ve.reciprocal(RC[:, 0:16], RC[:, 0:16])
            ve.drain()
            for hh2 in range(2):
                hglob = 2 * qt + hh2
                j1, j2 = 2 * hh2, 2 * hh2 + 1
                r1 = RC[:, 4 * j1:4 * j1 + 4].unsqueeze(2).broadcast_to((128, 4, 64))
                ve.tensor_tensor(YC[:, :, hh2, :], OTT[:, :, j1, 0:64], r1, op=ALU.mult)
                ve.tensor_scalar(RC[:, 16 + 4 * hh2:20 + 4 * hh2], RC[:, 4 * j2:4 * j2 + 4],
                                 NLAM[:, hglob:hglob + 1], None, op0=ALU.mult)
            ve.drain()
            for hh2 in range(2):
                j2 = 2 * hh2 + 1
                c2b = RC[:, 16 + 4 * hh2:20 + 4 * hh2].unsqueeze(2).broadcast_to((128, 4, 64))
                ve.tensor_tensor(OTT[:, :, j2, 0:64], OTT[:, :, j2, 0:64], c2b, op=ALU.mult)
            ve.drain()
            ve.tensor_tensor(YC[:], YC[:], OTT[:, :, 1::2, 0:64], op=ALU.add)
            k.s_v.inc(ve.drain())
            if DEV_TAP == "yc" and qt == 0 and c == 0:
                full_barrier(k)
                dump_fm(k, YC.rearrange("p a b c -> p (a b c)").unsqueeze(1), k.out_h, 1)
                return
            if DEV_TAP == "rc" and qt == 0 and c == 0:
                full_barrier(k)
                dump_fm(k, RC.bitcast(BF16)[:, 0:48].unsqueeze(1), k.out_h, 1)
                return
            k.s_v.wait(se)
            se.activation(BTS[:, 0:512], YC[:].rearrange("p a b d -> p (a b d)"), AF.Square)
            k.s_s.inc(se.drain())
            k.s_s.wait(ve)
            ve.tensor_reduce(RC[:, 0:8], BTS[:, 0:512].rearrange("p (a d) -> p a d", d=64),
                             axis=AX.X, op=ALU.add)
            ve.drain()
            ve.tensor_scalar(RC[:, 0:8], RC[:, 0:8], 1.0 / 64.0, 1e-6, op0=ALU.mult, op1=ALU.add)
            ve.drain()
            ve.reciprocal(RC[:, 0:8], RC[:, 0:8])
            k.s_v.inc(ve.drain())
            k.s_v.wait(se)
            se.activation(RC[:, 0:8], RC[:, 0:8], AF.Sqrt)
            k.s_s.inc(se.drain())
            k.s_s.wait(ve)
            rib = RC[:, 0:8].rearrange("p (a b) -> p a b", b=2).unsqueeze(3) \
                .broadcast_to((128, 4, 2, 64))
            ve.tensor_tensor(YC[:], YC[:], rib, op=ALU.mult)
            ve.drain()
            wrb = WROW.unsqueeze(1).unsqueeze(1).broadcast_to((128, 4, 2, 64))
            ve.tensor_tensor(YC[:], YC[:], wrb, op=ALU.mult)
            k.s_v.inc(ve.drain())
            if DEV_TAP == "yn" and qt == 0 and c == 0:
                full_barrier(k)
                dump_fm(k, YC.rearrange("p a b c -> p (a b c)").unsqueeze(1), k.out_h, 1)
                return
            k.s_v.wait(te)
            for cc in range(4):
                pv = k.psA[:, cc, :].bitcast(BF16)[:, 0:128]
                k.s_mm.inc(te.transpose(pv, YC[:, cc, :, :].rearrange("p a b -> p (a b)"),
                                        k.idb[:]))
            k.s_mm.wait(ve)
            for cc in range(4):
                ve.tensor_copy(Q[:, qt, 512 * c + 128 * cc:512 * c + 128 * cc + 128],
                               k.psA[:, cc, :].bitcast(BF16)[:, 0:128])
            k.s_v.inc(ve.drain())
            if DEV_TAP in ("y5", "y6", "y7", "y8", "y9", "yz") and qt == {"y5":2,"y6":3,"y7":4,"y8":5,"y9":6,"yz":7}[DEV_TAP] and c == 3:
                dump_fm(k, Q, k.out_h, 8)
                return
            if DEV_TAP == "y3" and qt == 0 and c == 3:
                dump_fm(k, Q, k.out_h, 8)
                return
            if DEV_TAP == "y4" and qt == 1 and c == 3:
                dump_fm(k, Q, k.out_h, 8)
                return
            if DEV_TAP == "y2" and qt == 0 and c == 0:
                dump_fm(k, Q, k.out_h, 8)
                return
            if DEV_TAP == "y1" and qt == 0 and c == 0:
                full_barrier(k)
                dump_fm(k, Q[:, 0:1, 0:512].rearrange("p a b -> p (a b)").unsqueeze(1), k.out_h, 1)
                return

    full_barrier(k)
    ar.release()
    ar.release()
    ar.mark()
    X1T = ar.f32(2, D)
    WOB = ar.bf(8, D)
    k.s_lg.inc(ge.dma_start(out=WOB, in_=h["woT"].rearrange("(a p) o -> p a o", p=128)), 16)
    k.s_ld.wait(te)
    k.s_lg.wait(te)
    x1r = h["x1_s"].rearrange("(j p) f -> p j f", p=128)
    outr2 = out_h.rearrange("(j p) f -> p j f", p=128)
    for j in range(NT):
        k.s_st.wait(ve)
        for oc in range(2):
            pb = k.psB[:, 2 * (j % 2) + oc, :]
            te.wait_ge(k.s_v.h, max(0, k.s_v.n - 3))
            for ft in range(8):
                mm = te.matmul(pb, Q[:, ft, 128 * j:128 * j + 128],
                               WOB[:, ft, 512 * oc:512 * oc + 512],
                               start=(ft == 0), stop=(ft == 7))
            k.s_mm.inc(mm)
            k.s_mm.wait(ve)
            k.s_v.inc(ve.tensor_copy(X1T[:, j % 2, 512 * oc:512 * oc + 512], pb))
        k.s_v.wait(sy)
        k.s_st.inc(sy.dma_start(out=x1r[:, j, :], in_=X1T[:, j % 2, :]), 16)
        if DEV_STAGE == "l0attn":
            k.s_st.inc(sy.dma_start(out=outr2[:, j, :], in_=X1T[:, j % 2, :]), 16)
    if DEV_STAGE == "l0attn":
        k.s_st.wait(sy)
        return
    full_barrier(k)
    ar.release()
    ar.release()
    emit_l0_ffn(k, h, out_h)


def emit_l0_wquant(k, h, WLD, WQB, STG):
    te, ve, se, ge, sy = k.te, k.ve, k.se, k.ge, k.sy
    specs = [(h["wg0"], h["wqgT_s"], 22, 1024, 2),
             (h["wu0"], h["wquT_s"], 22, 1024, 2),
             (h["wd0"], h["wdqT_s"], 8, 2816, 1)]
    for src, dst, nrt, W, PR in specs:
        srcr = src.rearrange("(r p) w -> p r w", p=128)
        dstr = dst.rearrange("(bb p) o -> p bb o", p=128)
        nb = W // 128
        for r0 in range(0, nrt, PR):
            WT = PR * W
            k.s_v.wait(sy)
            k.s_g.wait(sy)
            k.s_la.inc(sy.dma_start(out=WLD[:, 0:WT].rearrange("p (r w) -> p r w", r=PR),
                                    in_=srcr[:, r0:r0 + PR, :]), 16)
            k.s_la.wait(ve)
            k.s_la.wait(ge)
            emit_quant(k, WLD[:, 0:WT], WQB[:, 0:WT])
            k.s_g.wait(te)
            k.s_v.wait(te)
            k.s_st.wait(ve)
            pairs = [(WQB[:, 128 * b:128 * b + 128], STG[:, 128 * b:128 * b + 128])
                     for b in range(PR * nb)]
            emit_transposes(k, pairs, k.idb[:])
            k.s_mm.wait(ve)
            k.s_mm.wait(ge)
            k.s_v.wait(sy)
            for rr in range(PR):
                k.s_st.inc(sy.dma_start(
                    out=dstr[:, 0:nb, 128 * (r0 + rr):128 * (r0 + rr) + 128],
                    in_=STG[:, rr * W:(rr + 1) * W].rearrange("p (bb c) -> p bb c", c=128)), 16)


def emit_l0_ffn(k, h, out_h):
    te, ve, se, ge, sy = k.te, k.ve, k.se, k.ge, k.sy
    ar = k.ar
    XQF = k.arXQF
    ar.mark()
    WGS = ar.bf(8, 512)
    WUS = ar.bf(8, 512)
    WDS = ar.bf(22, 1024)
    HQ = [ar.bf(22, 128) for _ in range(4)]
    X1T = ar.f32(2, D)
    wds_src = h["wdqT_s"].rearrange("(c p) o -> p c o", p=128)
    k.s_ld.inc(sy.dma_start(out=WDS, in_=wds_src), 16)
    wg_src = h["wqgT_s"].rearrange("(c p) o -> p c o", p=128)
    wu_src = h["wquT_s"].rearrange("(c p) o -> p c o", p=128)
    xr = h["x_in"].rearrange("(j p) f -> p j f", p=128)
    x1r = h["x1_s"].rearrange("(j p) f -> p j f", p=128)
    outr = out_h.rearrange("(j p) f -> p j f", p=128)
    HGF = ar.f32(2048)
    HQB4 = ar.bf(2048)

    for jg in range(4):
        k.s_mm.wait(ve)
        k.s_mm.wait(se)
        for oc in range(6):
            ocw = 512 if oc < 5 else 256
            k.s_mm.wait(sy)
            k.s_la.inc(sy.dma_start(out=WGS[:, :, 0:ocw], in_=wg_src[:, :, 512 * oc:512 * oc + ocw]), 16)
            k.s_lb.inc(sy.dma_start(out=WUS[:, :, 0:ocw], in_=wu_src[:, :, 512 * oc:512 * oc + ocw]), 16)
            k.s_la.wait(te)
            k.s_lb.wait(te)
            for j4 in range(4):
                j = 4 * jg + j4
                te.wait_ge(k.s_s.h, k.s_s.n)
                te.wait_ge(k.s_v.h, k.s_v.n)
                for ic in range(8):
                    te.matmul(k.psA[:, 0, 0:ocw], XQF[:, ic, 128 * j:128 * j + 128],
                              WGS[:, ic, 0:ocw], start=(ic == 0), stop=(ic == 7))
                for ic in range(8):
                    mm = te.matmul(k.psA[:, 1, 0:ocw], XQF[:, ic, 128 * j:128 * j + 128],
                                   WUS[:, ic, 0:ocw], start=(ic == 0), stop=(ic == 7))
                k.s_mm.inc(mm)
                k.s_mm.wait(se)
                k.s_s.inc(se.activation(HGF[:, ocw * j4:ocw * j4 + ocw],
                                        k.psA[:, 0, 0:ocw], AF.Silu))
                k.s_s.wait(ve)
                k.s_v.inc(ve.tensor_tensor(HGF[:, ocw * j4:ocw * j4 + ocw],
                                           HGF[:, ocw * j4:ocw * j4 + ocw],
                                           k.psA[:, 1, 0:ocw], op=ALU.mult))
            ve.drain()
            emit_quant(k, HGF[:, 0:4 * ocw], HQB4[:, 0:4 * ocw])
            k.s_g.wait(te)
            k.s_v.wait(te)
            pairs = [(HQB4[:, ocw * j4 + 128 * b:ocw * j4 + 128 * b + 128],
                      HQ[j4][:, 4 * oc + b, :])
                     for j4 in range(4) for b in range(ocw // 128)]
            emit_transposes(k, pairs, k.idb[:])
            k.s_mm.wait(ve)
            k.s_mm.wait(ge)
            k.s_mm.wait(se)
        k.s_ld.wait(te)
        for j4 in range(4):
            j = 4 * jg + j4
            k.s_st.wait(sy)
            k.s_lc.inc(sy.dma_start(out=X1T[:, j4 % 2, :], in_=x1r[:, j, :]), 16)
            for dc in range(2):
                te.wait_ge(k.s_v.h, k.s_v.n)
                for c in range(22):
                    mm = te.matmul(k.psB[:, dc, :], HQ[j4][:, c, :],
                                   WDS[:, c, 512 * dc:512 * dc + 512],
                                   start=(c == 0), stop=(c == 21))
                k.s_mm.inc(mm)
            k.s_mm.wait(ve)
            k.s_lc.wait(ve)
            for dc in range(2):
                ve.tensor_tensor(X1T[:, j4 % 2, 512 * dc:512 * dc + 512],
                                 X1T[:, j4 % 2, 512 * dc:512 * dc + 512],
                                 k.psB[:, dc, :], op=ALU.add)
            k.s_v.inc(ve.drain())
            k.s_v.wait(ge)
            k.s_lg.inc(ge.dma_start(out=X1T[:, j4 % 2, :], in_=xr[:, j, :], accum_op=ALU.add), 16)
            k.s_lg.wait(sy)
            k.s_st.inc(sy.dma_start(out=x1r[:, j, :], in_=X1T[:, j4 % 2, :]), 16)
            if DEV_STAGE == "l0":
                k.s_st.inc(sy.dma_start(out=outr[:, j, :], in_=X1T[:, j4 % 2, :]), 16)
    full_barrier(k)
    ar.release()
    if DEV_STAGE == "l0":
        k.s_st.wait(sy)
        return
    ar.release()
    emit_l1(k, h, out_h)


def emit_l1(k, h, out_h):
    te, ve, se, ge, sy = k.te, k.ve, k.se, k.ge, k.sy
    ar = k.ar

    XINT = ar.bf(NT, D)
    ar.mark()
    XNF = ar.bf(8, T)
    XTa = ar.f32(2, D)

    emit_ln_stats(k, h["x1_s"], XTa, k.qA[:, 0:2 * D])
    xr = h["x1_s"].rearrange("(j p) f -> p j f", p=128)
    ssum = k.stats[:, 0:NT]
    inv = k.stats[:, 2 * NT:3 * NT]
    sems = (k.s_la, k.s_lb)
    for j in range(min(2, NT)):
        sems[j % 2].inc(sy.dma_start(out=XTa[:, j % 2, :], in_=xr[:, j, :]), 16)
    for j in range(NT):
        xt = XTa[:, j % 2, :]
        sems[j % 2].wait(ve)
        k.s_s.wait(ve)
        ve.tensor_scalar(XINT[:, j, :], xt, ssum[:, j:j + 1], inv[:, j:j + 1],
                         op0=ALU.subtract, op1=ALU.mult)
        k.s_v.inc(ve.drain())
        if j + 2 < NT:
            k.s_v.wait(sy)
            sems[j % 2].inc(sy.dma_start(out=XTa[:, j % 2, :], in_=xr[:, j + 2, :]), 16)
        k.s_v.wait(te)
        pairs = [(XINT[:, j, 128 * fb:128 * fb + 128], XNF[:, fb, 128 * j:128 * j + 128])
                 for fb in range(8)]
        emit_transposes(k, pairs, k.idb[:])
        k.s_mm.wait(ve)

    if DEV_STAGE == "l1ln":
        dump_xint(k, out_h, XINT)
        return

    TH = ar.bf(2, T)
    PHI = ar.bf(NT, 256)
    GT = ar.bf(NT, D)
    MB = ar.bf(2, D)
    WTH = ar.bf(8, 256)
    WPH = ar.bf(8, 256)
    WG = ar.bf(8, D)
    k.s_lg.inc(ge.dma_start(out=WTH, in_=h["wthT"].rearrange("(a p) o -> p a o", p=128)), 16)
    k.s_lg.inc(ge.dma_start(out=WPH, in_=h["wphT"].rearrange("(a p) o -> p a o", p=128)), 16)
    k.s_lg.inc(ge.dma_start(out=WG, in_=h["wgT"].rearrange("(a p) o -> p a o", p=128)), 16)
    full_barrier(k)

    for oc in range(2):
        for tc in range(4):
            pb = k.psA[:, tc % 4, :]
            te.wait_ge(k.s_s.h, max(0, k.s_s.n - 3))
            for ic in range(8):
                mm = te.matmul(pb, WTH[:, ic, 128 * oc:128 * oc + 128],
                               XNF[:, ic, 512 * tc:512 * tc + 512],
                               start=(ic == 0), stop=(ic == 7))
            k.s_mm.inc(mm)
            k.s_mm.wait(se)
            k.s_s.inc(se.activation(TH[:, oc, 512 * tc:512 * tc + 512], pb, AF.Copy, bias=0.0))
    def tproj(W, dst, ncol):
        for j in range(NT):
            noc = (ncol + 511) // 512
            for oc in range(noc):
                ow = min(512, ncol - 512 * oc)
                pb = k.psB[:, (j * noc + oc) % 4, :]
                te.wait_ge(k.s_s.h, max(0, k.s_s.n - 3))
                for ic in range(8):
                    mm = te.matmul(pb[:, 0:ow], XNF[:, ic, 128 * j:128 * j + 128],
                                   W[:, ic, 512 * oc:512 * oc + ow],
                                   start=(ic == 0), stop=(ic == 7))
                k.s_mm.inc(mm)
                k.s_mm.wait(se)
                k.s_s.inc(se.activation(dst[:, j, 512 * oc:512 * oc + ow], pb[:, 0:ow],
                                        AF.Copy, bias=0.0))
    tproj(WPH, PHI, 256)
    tproj(WG, GT, 1024)
    full_barrier(k)

    for lc in range(2):
        for ocg in range(2):
            pb = k.psA[:, 2 * lc + ocg, :]
            for j in range(NT):
                mm = te.matmul(pb, PHI[:, j, 128 * lc:128 * lc + 128],
                               GT[:, j, 512 * ocg:512 * ocg + 512],
                               start=(j == 0), stop=(j == NT - 1))
            k.s_mm.inc(mm)
            k.s_mm.wait(ve)
            k.s_v.inc(ve.tensor_scalar(MB[:, lc, 512 * ocg:512 * ocg + 512], pb,
                                       0.03125, None, op0=ALU.mult))
    k.s_v.wait(te)
    X1T = XTa
    x1r2 = h["x1_s"].rearrange("(j p) f -> p j f", p=128)
    for j in range(NT):
        k.s_st.wait(sy)
        k.s_lc.inc(sy.dma_start(out=X1T[:, j % 2, :], in_=x1r2[:, j, :]), 16)
        te.wait_ge(k.s_v.h, max(0, k.s_v.n - 1))
        for oc in range(2):
            pb = k.psB[:, 2 * (j % 2) + oc, :]
            for lc in range(2):
                mm = te.matmul(pb, TH[:, lc, 128 * j:128 * j + 128],
                               MB[:, lc, 512 * oc:512 * oc + 512],
                               start=(lc == 0), stop=(lc == 1))
            k.s_mm.inc(mm)
        k.s_mm.wait(ve)
        k.s_lc.wait(ve)
        for oc in range(2):
            ve.tensor_tensor(X1T[:, j % 2, 512 * oc:512 * oc + 512],
                             X1T[:, j % 2, 512 * oc:512 * oc + 512],
                             k.psB[:, 2 * (j % 2) + oc, :], op=ALU.add)
        k.s_v.inc(ve.drain())
        k.s_v.wait(sy)
        k.s_st.inc(sy.dma_start(out=x1r2[:, j, :], in_=X1T[:, j % 2, :]), 16)
    full_barrier(k)
    ar.release()

    if DEV_STAGE == "l1attn":
        ar.mark()
        XTb = ar.f32(2, D)
        outr = out_h.rearrange("(j p) f -> p j f", p=128)
        for j in range(NT):
            k.s_la.inc(sy.dma_start(out=XTb[:, j % 2, :], in_=x1r2[:, j, :]), 16)
            k.s_la.wait(sy)
            k.s_st.inc(sy.dma_start(out=outr[:, j, :], in_=XTb[:, j % 2, :]), 16)
            k.s_st.wait(sy)
        ar.release()
        return

    emit_l1_ffn(k, h, out_h, XINT)


def emit_l1_ffn(k, h, out_h, XINT):
    nc = k.nc
    te, ve, se, ge, sy = k.te, k.ve, k.se, k.ge, k.sy
    ar = k.ar

    CQ = ar.bf(T)
    SQ = ar.bf(T)
    CK = ar.bf(T)
    SK = ar.bf(T)

    P64R = k.sb("P64R", [128, 128], BF16)
    M01L = k.sb("M01L", [128, 128], BF16)
    PHB = k.sb("PHB", [128, 2], BF16)
    E2B = k.sb("E2B", [2, 128], BF16)
    E2Q = k.sb("E2Q", [2, 128], BF16)
    E2K = k.sb("E2K", [2, 128], BF16)
    QSR = k.sb("QSR", [2, 256], BF16)
    LNGC = k.sb("LNGC", [128, 8], F32)
    LNBC = k.sb("LNBC", [128, 8], F32)
    B1AC = k.sb("B1AC", [128, 22], F32)
    B1GC = k.sb("B1GC", [128, 22], F32)
    st4 = nc.dram_tensor("st4", [4, 2049], F32, kind="Internal").ap()

    full_barrier(k)

    PM32 = k.smallc[:, 8:9]
    LNBc = k.smallc[:, 9:10]
    INVF = k.smallc[:, 10:11]
    ZCOL = k.smallc[:, 15:16]
    ve.memset(ZCOL, 0.0)
    ve.tensor_scalar(PM32, k.pcol[:], -15.5, 0.03125, op0=ALU.add, op1=ALU.mult)
    ve.drain()
    ve.tensor_scalar(PM32, PM32, k.mcol[:, 0:1], k.mcol[:, 0:1], op0=ALU.add, op1=ALU.subtract)
    ve.drain()
    ve.scalar_tensor_tensor(PM32, PM32, -32.0, k.pcol[:], op0=ALU.mult, op1=ALU.add)
    ve.drain()
    ve.tensor_scalar(INVF, PM32, -0.03125, None, op0=ALU.mult)
    ve.drain()
    ve.tensor_scalar(INVF, INVF, k.ln4, None, op0=ALU.mult)
    ve.tensor_scalar(LNBc, PM32, 2.0, 25.6, op0=ALU.mult, op1=ALU.add)
    ve.drain()
    ve.tensor_scalar(LNBc, LNBc, 1.0 / 89.6, None, op0=ALU.mult)
    k.s_v.inc(ve.drain())
    k.s_v.wait(se)
    se.activation(INVF, INVF, AF.Exp)
    se.activation(LNBc, LNBc, AF.Ln)
    k.s_s.inc(se.drain())

    DIF = k.qA[:, 0:128]
    CM64 = k.qA[:, 128:256]
    S1 = k.qA[:, 256:384]
    S2 = k.qA[:, 384:512]
    k.s_s.wait(ge)
    ge.iota(DIF, pattern=[[1, 128]], base=0, channel_multiplier=-1,
            allow_small_or_imprecise_dtypes=True)
    ge.iota(CM64, pattern=[[0, 2], [1, 64]], base=0, channel_multiplier=0,
            allow_small_or_imprecise_dtypes=True)
    k.s_g.inc(ge.drain())
    k.s_g.wait(ve)
    k.s_s.wait(ve)
    ve.tensor_scalar(P64R[:], DIF, 32.0, None, op0=ALU.is_equal)
    ve.tensor_scalar(S1, DIF, -32.0, None, op0=ALU.is_equal)
    ve.tensor_scalar(S2, CM64, 32.0, None, op0=ALU.is_ge)
    ve.tensor_scalar(M01L[:], DIF, 0.0, None, op0=ALU.is_le)
    ve.drain()
    ve.tensor_tensor(P64R[:], P64R[:], S2, op=ALU.mult)
    ve.drain()
    ve.tensor_scalar(S2, S2, -1.0, 1.0, op0=ALU.mult, op1=ALU.add)
    ve.drain()
    ve.tensor_tensor(S1, S1, S2, op=ALU.mult)
    ve.drain()
    ve.tensor_tensor(P64R[:], P64R[:], S1, op=ALU.subtract)
    ve.memset(PHB[:], 0.0)
    ve.drain()
    ve.tensor_scalar(PHB[:, 0:1], k.pcol[:], 64.0, None, op0=ALU.is_lt)
    ve.tensor_scalar(PHB[:, 1:2], k.pcol[:], 64.0, None, op0=ALU.is_ge)
    k.s_v.inc(ve.drain())
    k.s_v.wait(te)
    pvE = k.psA[:, 0, :].bitcast(BF16)[0:2, 0:128]
    k.s_mm.inc(te.transpose(pvE, PHB[:], k.idb[:]))
    k.s_mm.wait(ve)
    k.s_v.inc(ve.tensor_copy(E2B[:], pvE))

    TG = k.qA[:, 0:T]
    ARG = k.qB[:, 0:T]
    RED = k.qC[:, 0:T]
    k.s_v.wait(ge)
    k.s_g.inc(ge.iota(TG, pattern=[[1, T]], base=0, channel_multiplier=0,
                      allow_small_or_imprecise_dtypes=True))
    k.s_g.wait(ve)

    def trig2(dst, use_hpi, pw_mult, finalmult):
        k.s_s.wait(ve)
        if use_hpi:
            ve.tensor_scalar(ARG, TG, INVF, k.hpi, op0=ALU.mult, op1=ALU.add)
        else:
            ve.tensor_scalar(ARG, TG, INVF, None, op0=ALU.mult)
        ve.drain()
        ve.tensor_scalar(RED, ARG, k.i2pi, None, op0=ALU.mult)
        ve.drain()
        ve.tensor_scalar(RED, RED, k.mcol[:, 0:1], k.mcol[:, 0:1],
                         op0=ALU.add, op1=ALU.subtract)
        ve.drain()
        ve.scalar_tensor_tensor(RED, RED, k.twopi, ARG, op0=ALU.mult, op1=ALU.subtract)
        ve.drain()
        ve.tensor_scalar(RED, RED, k.pin, k.pip, op0=ALU.max, op1=ALU.min)
        k.s_v.inc(ve.drain())
        k.s_v.wait(se)
        k.s_s.inc(se.activation(RED, RED, AF.Sin))
        k.s_s.wait(ve)
        ve.tensor_scalar(ARG, TG, -1024.0, pw_mult, op0=ALU.add, op1=ALU.mult)
        ve.drain()
        ve.tensor_scalar(ARG, ARG, LNBc, None, op0=ALU.mult)
        k.s_v.inc(ve.drain())
        k.s_v.wait(se)
        k.s_s.inc(se.activation(ARG, ARG, AF.Exp))
        k.s_s.wait(ve)
        ve.tensor_tensor(RED, RED, ARG, op=ALU.mult)
        ve.drain()
        ve.tensor_scalar(dst, RED, finalmult, None, op0=ALU.mult)
        k.s_v.inc(ve.drain())

    trig2(CQ, True, 0.00390625, 8.0)
    trig2(SQ, False, 0.00390625, 8.0)
    trig2(CK, True, -0.00390625, -1.0)
    trig2(SK, False, -0.00390625, -1.0)

    for l in range(2):
        emit_l1f_attn(k, h, out_h, XINT, l, CQ, SQ, CK, SK,
                      P64R, M01L, PHB, E2B, E2Q, E2K, QSR)
        if DEV_STAGE == f"l1f{l}a":
            dump_xint(k, out_h, XINT)
            return
        emit_l1f_mlp(k, h, out_h, XINT, l, LNGC, LNBC, B1AC, B1GC, st4)
        if DEV_STAGE == f"l1f{l}":
            dump_xint(k, out_h, XINT)
            return

    full_barrier(k)
    ar.mark()
    XO = ar.f32(2, D)
    x1r = h["x1_s"].rearrange("(j p) f -> p j f", p=128)
    outr = out_h.rearrange("(j p) f -> p j f", p=128)
    for j in range(NT):
        k.s_st.wait(sy)
        k.s_lc.inc(sy.dma_start(out=XO[:, j % 2, :], in_=x1r[:, j, :]), 16)
        k.s_lc.wait(ve)
        ve.tensor_tensor(XO[:, j % 2, :], XO[:, j % 2, :], XINT[:, j, :], op=ALU.add)
        k.s_v.inc(ve.drain())
        k.s_v.wait(sy)
        k.s_st.inc(sy.dma_start(out=outr[:, j, :], in_=XO[:, j % 2, :]), 16)
    k.s_st.wait(sy)
    k.s_st.wait(te)
    ar.release()


def dump_xint(k, out_h, XINT):
    full_barrier(k)
    outr = out_h.rearrange("(j p) f -> p j f", p=128)
    for j in range(NT):
        k.s_dmp.inc(k.ge.dma_start(out=outr[:, j, :], in_=XINT[:, j, :]), 16)
    k.s_dmp.wait(k.ge)
    k.s_dmp.wait(k.sy)


def emit_l1f_attn(k, h, out_h, XINT, l, CQ, SQ, CK, SK,
                  P64R, M01L, PHB, E2B, E2Q, E2K, QSR):
    te, ve, se, ge, sy = k.te, k.ve, k.se, k.ge, k.sy
    ar = k.ar
    M01 = k.M01

    full_barrier(k)
    ar.mark()
    QKVB = ar.bf(12, T)

    k.s_lg.inc(ge.dma_start(out=QSR[:, 0:128],
                            in_=_dc.replace(h["qs"][l], ap=[[0, 2], [1, 128]])), 16)
    k.s_lh.inc(ge.dma_start(out=QSR[:, 128:256],
                            in_=_dc.replace(h["ks"][l], ap=[[0, 2], [1, 128]])), 16)
    k.s_lg.wait(ve)
    k.s_lh.wait(ve)
    ve.tensor_tensor(E2Q[:], E2B[:], QSR[:, 0:128], op=ALU.mult)
    ve.tensor_tensor(E2K[:], E2B[:], QSR[:, 128:256], op=ALU.mult)
    k.s_v.inc(ve.drain())
    ge.wait_ge(k.s_v.h, k.s_v.n)

    ar.mark()
    XFM = ar.bf(8, 2056)
    WSL = ar.bf(2, 8, 128)
    ve.memset(XFM[:, :, 0:1], 0.0)
    k.s_v.inc(ve.drain())
    k.s_v.wait(te)
    for j in range(NT):
        pairs = [(XINT[:, j, 128 * fb:128 * fb + 128], XFM[:, fb, 1 + 128 * j:1 + 128 * j + 128])
                 for fb in range(8)]
        emit_transposes(k, pairs, k.idb[:])
    k.s_mm.wait(ve)

    k.s_v.wait(te)
    wsrc = h["wqkv"][l].rearrange("(a p) o -> p a o", p=128)
    lastuse = [0, 0]
    sems = (k.s_lg, k.s_lh)
    for oc in range(12):
        b = oc % 2
        ge.wait_ge(k.s_mm.h, lastuse[b])
        sems[b].inc(ge.dma_start(out=WSL[:, b, :, :], in_=wsrc[:, :, 128 * oc:128 * oc + 128]), 16)
        sems[b].wait(te)
        for tc in range(4):
            pb = k.psA[:, tc % 4, :]
            te.wait_ge(k.s_s.h, max(0, k.s_s.n - 3))
            for ic in range(8):
                soff = 1 if ic < 4 else 0
                mm = te.matmul(pb, WSL[:, b, ic, :],
                               XFM[:, ic, soff + 512 * tc: soff + 512 * tc + 512],
                               start=(ic == 0), stop=(ic == 7))
            k.s_mm.inc(mm)
            k.s_mm.wait(se)
            k.s_s.inc(se.activation(QKVB[:, oc, 512 * tc:512 * tc + 512], pb, AF.Copy, bias=0.0))
        lastuse[b] = k.s_mm.n
    full_barrier(k)
    ar.release()

    if DEV_TAP == f"f{l}qkv":
        dump_fm(k, QKVB, out_h, 12)
        return

    VT1 = ar.bf(4, NT, 2, 68)
    UTT = ar.bf(2, 8, 512)
    UT = ar.bf(2, 640)
    OE = ar.bf(2, 512)
    OTT = ar.bf(4, 2, 72)
    YC = ar.bf(4, 2, 64)
    RC = ar.f32(16)
    WO1b = ar.bf(4, D)

    ve.memset(VT1[:], 1.0)
    k.s_v.inc(ve.drain())
    k.s_v.wait(te)
    base_v = k.s_v.n
    i = 0
    for t4 in range(4):
        for h2 in range(2):
            fr = 64 * h2
            for j in range(NT):
                if i >= 4:
                    te.wait_ge(k.s_v.h, base_v + i - 3)
                pv = k.psA[:, i % 4, :].bitcast(BF16)[:, 0:64]
                idsl = k.idb[fr:fr + 64, fr:fr + 64]
                k.s_mm.inc(te.transpose(pv, QKVB[fr:fr + 64, 8 + t4, 128 * j:128 * j + 128], idsl))
                k.s_mm.wait(ve)
                k.s_v.inc(ve.tensor_copy(VT1[:, t4, j, h2, 0:64], pv))
                i += 1

    SQB = k.qAb[:, 0:T]
    RT = k.qAb[:, T:2 * T]
    NRM = k.qB[0:2, 0:T]
    INVB = k.qCb[0:2, 0:T]

    def normrope(oc, COS, SIN, E2X):
        k.s_v.wait(se)
        k.s_g.wait(se)
        k.s_s.inc(se.activation(SQB, QKVB[:, oc, :], AF.Square))
        k.s_s.wait(te)
        k.s_v.wait(te)
        for cc in range(4):
            k.s_mm.inc(te.matmul(k.psB[0:2, cc, :], PHB[:], SQB[:, 512 * cc:512 * cc + 512],
                                 start=True, stop=True))
        k.s_mm.wait(se)
        for cc in range(4):
            se.activation(NRM[:, 512 * cc:512 * cc + 512], k.psB[0:2, cc, :], AF.Sqrt)
        k.s_s.inc(se.drain())
        k.s_s.wait(ve)
        ve.tensor_scalar(NRM[:], NRM[:], 1e-12, None, op0=ALU.max)
        ve.drain()
        ve.reciprocal(NRM[:], NRM[:])
        ve.drain()
        ve.tensor_copy(INVB[:], NRM[:])
        k.s_v.inc(ve.drain())
        k.s_v.wait(te)
        for cc in range(4):
            k.s_mm.inc(te.matmul(k.psA[:, cc, :], E2X[:], INVB[:, 512 * cc:512 * cc + 512],
                                 start=True, stop=True))
        k.s_mm.wait(ve)
        for cc in range(4):
            ve.tensor_tensor(QKVB[:, oc, 512 * cc:512 * cc + 512],
                             QKVB[:, oc, 512 * cc:512 * cc + 512], k.psA[:, cc, :], op=ALU.mult)
        k.s_v.inc(ve.drain())
        k.s_v.wait(te)
        for cc in range(4):
            k.s_mm.inc(te.matmul(k.psB[:, cc, :], P64R[:], QKVB[:, oc, 512 * cc:512 * cc + 512],
                                 start=True, stop=True))
        ve.tensor_tensor(RT, QKVB[:, oc, :], COS, op=ALU.mult)
        k.s_mm.wait(ve)
        for cc in range(4):
            ve.tensor_tensor(k.psB[:, cc, :], k.psB[:, cc, :],
                             SIN[:, 512 * cc:512 * cc + 512], op=ALU.mult)
        ve.drain()
        for cc in range(4):
            ve.tensor_tensor(QKVB[:, oc, 512 * cc:512 * cc + 512],
                             RT[:, 512 * cc:512 * cc + 512], k.psB[:, cc, :], op=ALU.add)
        k.s_v.inc(ve.drain())

    for t4 in range(4):
        normrope(t4, CQ, SQ, E2Q)
    for t4 in range(4):
        normrope(4 + t4, CK, SK, E2K)
    full_barrier(k, "tvs")
    if DEV_TAP == f"f{l}q":
        dump_fm(k, QKVB[:, 0:4, :], out_h, 4)
        return
    if DEV_TAP == f"f{l}k":
        dump_fm(k, QKVB[:, 4:8, :], out_h, 4)
        return

    NM = k.stats[:, 44:46]
    for t4 in range(4):
        for w in range(4):
            for lq in range(4):
                qc = 4 * w + lq
                kts = list(range(max(0, qc - 4), qc + 1))
                nblk = len(kts)
                wcols = 128 * nblk
                te.wait_ge(k.s_s.h, k.s_s.n)
                te.wait_ge(k.s_v.h, k.s_v.n)
                for jj in range(2):
                    lhs = QKVB[64 * jj:64 * jj + 64, t4, 128 * qc:128 * qc + 128]
                    kbase = 128 * kts[0]
                    w0 = min(512, wcols)
                    mm = te.matmul(k.psA[:, 2 * jj, 0:w0], lhs,
                                   QKVB[64 * jj:64 * jj + 64, 4 + t4, kbase:kbase + w0],
                                   start=True, stop=True, tile_position=(64 * jj, 0))
                    if wcols > 512:
                        mm = te.matmul(k.psA[:, 2 * jj + 1, 0:wcols - 512], lhs,
                                       QKVB[64 * jj:64 * jj + 64, 4 + t4,
                                            kbase + 512:kbase + wcols],
                                       start=True, stop=True, tile_position=(64 * jj, 0))
                    k.s_mm.inc(mm)
                k.s_mm.wait(ve)
                ve.wait_ge(k.s_s.h, k.s_s.n)
                for jj in range(2):
                    flat = k.psA[:, 2 * jj:2 * jj + 2, :].rearrange("p a b -> p (a b)")
                    ve.tensor_reduce(NM[:, jj:jj + 1], flat[:, 0:wcols], axis=AX.X, op=ALU.min)
                k.s_v.inc(ve.drain())
                k.s_v.wait(se)
                for jj in range(2):
                    flat = k.psA[:, 2 * jj:2 * jj + 2, :].rearrange("p a b -> p (a b)")
                    se.activation(UT[:, jj, 0:wcols], flat[:, 0:wcols], AF.Exp,
                                  bias=NM[:, jj:jj + 1], scale=-1.0)
                k.s_s.inc(se.drain())
                k.s_s.wait(te)
                vb2 = k.s_v.n
                i2 = 0
                for jj in range(2):
                    for b, kt in enumerate(kts):
                        ktr = kt - 4 * w + 4
                        if i2 >= 2:
                            te.wait_ge(k.s_v.h, vb2 + i2 - 1)
                        pvb = k.psB[:, 2 + i2 % 2, :].bitcast(BF16)[:, 0:128]
                        k.s_mm.inc(te.transpose(pvb, UT[:, jj, 128 * b:128 * b + 128], k.idb[:]))
                        k.s_mm.wait(ve)
                        dst = UTT[:, jj, ktr, 128 * lq:128 * lq + 128]
                        if b == 0 and qc >= 4:
                            k.s_v.inc(ve.tensor_tensor(dst, pvb, M01L[:], op=ALU.mult))
                        elif b == nblk - 1:
                            k.s_v.inc(ve.tensor_tensor(dst, pvb, M01[:], op=ALU.mult))
                        else:
                            k.s_v.inc(ve.tensor_copy(dst, pvb))
                        i2 += 1
            te.wait_ge(k.s_v.h, k.s_v.n)
            ktrs = [3, 4, 0, 1, 2, 5, 6, 7] if w > 0 else [4, 5, 6, 7]
            for jj in range(2):
                for idx, ktr in enumerate(ktrs):
                    kt = ktr + 4 * w - 4
                    c0 = 128 * max(0, ktr - 4)
                    c1 = 128 * (min(3, ktr) + 1)
                    mm = te.matmul(k.psB[0:65, jj, c0:c1], VT1[:, t4, kt, jj, 0:65],
                                   UTT[:, jj, ktr, c0:c1],
                                   start=(idx == 0), stop=(idx == len(ktrs) - 1))
                k.s_mm.inc(mm)
            k.s_mm.wait(ve)
            for jj in range(2):
                ve.tensor_copy(OE[0:65, jj, :], k.psB[0:65, jj, :])
            k.s_v.inc(ve.drain())
            k.s_v.wait(te)
            vb3 = k.s_v.n
            i3 = 0
            for cc in range(4):
                for jj in range(2):
                    if i3 >= 4:
                        te.wait_ge(k.s_v.h, vb3 + i3 - 3)
                    pv = k.psA[:, i3 % 4, :].bitcast(BF16)[:, 0:65]
                    k.s_mm.inc(te.transpose(pv, OE[0:65, jj, 128 * cc:128 * cc + 128],
                                            k.idb[0:65, 0:65]))
                    k.s_mm.wait(ve)
                    k.s_v.inc(ve.tensor_copy(OTT[:, cc, jj, 0:65], pv))
                    i3 += 1
            ve.drain()
            ve.tensor_copy(RC[:, 0:8].rearrange("p (b a) -> p b a", b=2),
                           OTT[:, :, :, 64:65].rearrange("p a b 1 -> p a b").transpose([0, 2, 1]))
            ve.drain()
            ve.reciprocal(RC[:, 0:8], RC[:, 0:8])
            ve.drain()
            for jj in range(2):
                rb = RC[:, 4 * jj:4 * jj + 4].unsqueeze(2).broadcast_to((128, 4, 64))
                ve.tensor_tensor(YC[:, :, jj, :], OTT[:, :, jj, 0:64], rb, op=ALU.mult)
            k.s_v.inc(ve.drain())
            k.s_v.wait(te)
            for cc in range(4):
                pv = k.psA[:, cc, :].bitcast(BF16)[:, 0:128]
                k.s_mm.inc(te.transpose(pv, YC[:, cc, :, :].rearrange("p a b -> p (a b)"),
                                        k.idb[:]))
            k.s_mm.wait(ve)
            for cc in range(4):
                ve.tensor_copy(QKVB[:, t4, 512 * w + 128 * cc:512 * w + 128 * cc + 128],
                               k.psA[:, cc, :].bitcast(BF16)[:, 0:128])
            k.s_v.inc(ve.drain())

    full_barrier(k, "tvs")
    k.s_lg.inc(ge.dma_start(out=WO1b, in_=h["wo1"][l].rearrange("(a p) o -> p a o", p=128)), 16)
    k.s_lg.wait(te)
    for j in range(NT):
        te.wait_ge(k.s_v.h, max(0, k.s_v.n - 1))
        for dc in range(2):
            pb = k.psB[:, 2 * (j % 2) + dc, :]
            for ft in range(4):
                mm = te.matmul(pb, QKVB[:, ft, 128 * j:128 * j + 128],
                               WO1b[:, ft, 512 * dc:512 * dc + 512],
                               start=(ft == 0), stop=(ft == 3))
            k.s_mm.inc(mm)
        k.s_mm.wait(ve)
        for dc in range(2):
            ve.tensor_tensor(XINT[:, j, 512 * dc:512 * dc + 512],
                             XINT[:, j, 512 * dc:512 * dc + 512],
                             k.psB[:, 2 * (j % 2) + dc, :], op=ALU.add)
        k.s_v.inc(ve.drain())
    full_barrier(k)
    ar.release()


def emit_l1f_mlp(k, h, out_h, XINT, l, LNGC, LNBC, B1AC, B1GC, st4):
    te, ve, se, ge, sy = k.te, k.ve, k.se, k.ge, k.sy
    ar = k.ar
    ar.mark()
    W2R = ar.bf(22, D)
    BCOL = ar.bf(8)
    ROWB = ar.bf(4, 2048)

    k.s_ld.inc(sy.dma_start(out=LNGC[:], in_=h["lngT"][l]), 16)
    k.s_ld.inc(sy.dma_start(out=LNBC[:], in_=h["lnbT"][l]), 16)
    k.s_ld.inc(sy.dma_start(out=B1AC[:], in_=h["b1ag"][l, 0]), 16)
    k.s_ld.inc(sy.dma_start(out=B1GC[:], in_=h["b1ag"][l, 1]), 16)
    w2src = h["w2"][l]
    k.s_lg.inc(ge.dma_start(out=W2R[:, 0:21, :],
                            in_=w2src[0:2688].rearrange("(a p) o -> p a o", p=128)), 16)
    k.s_lg.inc(ge.dma_start(out=W2R[0:42, 21, :], in_=w2src[2688:2730]), 16)

    sA = k.qA[:, 0:16]
    sB = k.qA[:, 16:32]
    qA_ = k.qA[:, 32:48]
    qB_ = k.qA[:, 48:64]
    SCAs = (k.qC[:, 1024:1536], k.qC[:, 1536:2048])
    SCBs = (k.qC[:, 2048:2560], k.qB[:, 2048:2560])
    sbase = k.s_s.n
    for j in range(NT):
        ve.tensor_reduce(sA[:, j:j + 1], XINT[:, j, 0:512], axis=AX.X, op=ALU.add)
        ve.tensor_reduce(sB[:, j:j + 1], XINT[:, j, 512:1024], axis=AX.X, op=ALU.add)
        if j:
            se.wait_ge(k.s_s.h, sbase + j)
        se.activation(SCAs[j % 2], XINT[:, j, 0:512], AF.Square, accum_out=qA_[:, j:j + 1])
        se.activation(SCBs[j % 2], XINT[:, j, 512:1024], AF.Square, accum_out=qB_[:, j:j + 1])
        k.s_s.inc(se.drain())
    k.s_v.inc(ve.drain())
    k.s_s.inc(se.drain())
    k.s_v.wait(te)
    k.s_s.wait(te)
    STG16s = (k.qC[0:16, 2048:2176], k.qC[0:16, 2176:2304])
    for st in range(4):
        k.s_st.inc(sy.dma_start(out=st4[st, 0:1], in_=k.smallc[0:1, 15:16]), 16)
        pv = k.psA[0:16, st % 4, 0:128]
        k.s_mm.inc(te.transpose(pv, k.qA[:, 16 * st:16 * st + 16], k.idf[:]))
        k.s_mm.wait(ve)
        k.s_st.wait(ve)
        k.s_v.inc(ve.tensor_copy(STG16s[st % 2], pv))
        k.s_v.wait(sy)
        k.s_st.inc(sy.dma_start(out=st4[st, 1:2049].rearrange("(j c) -> j c", c=128),
                                in_=STG16s[st % 2]), 16)
    k.s_st.wait(sy)
    RA = k.qA[0:1, 0:2048]
    RQA = k.qB[0:1, 0:2048]
    RC_ = k.qC[0:1, 0:2048]
    IVB = ROWB[0:1, 0, :]
    CHB = ROWB[0:1, 1, :]
    CLB = ROWB[0:1, 2, :]
    B2H = ROWB[0:1, 3, 0:1024]
    B2L = ROWB[0:1, 3, 1024:2048]
    k.s_ld.inc(sy.dma_start(out=RA, in_=st4[0, 1:2049].unsqueeze(0)), 16)
    k.s_ld.inc(sy.dma_start(out=RC_, in_=st4[1, 0:2048].unsqueeze(0)), 16)
    k.s_ld.wait(ve)
    ve.tensor_tensor(RA, RA, RC_, op=ALU.add)
    k.s_v.inc(ve.drain())
    k.s_v.wait(sy)
    k.s_ld.inc(sy.dma_start(out=RQA, in_=st4[2, 1:2049].unsqueeze(0)), 16)
    k.s_ld.inc(sy.dma_start(out=RC_, in_=st4[3, 0:2048].unsqueeze(0)), 16)
    k.s_ld.wait(ve)
    ve.tensor_tensor(RQA, RQA, RC_, op=ALU.add)
    ve.drain()
    ve.tensor_scalar(RA, RA, 1.0 / D, None, op0=ALU.mult)
    ve.tensor_scalar(RQA, RQA, 1.0 / D, None, op0=ALU.mult)
    ve.drain()
    ve.tensor_tensor(RC_, RA, RA, op=ALU.mult)
    ve.drain()
    ve.tensor_tensor(RQA, RQA, RC_, op=ALU.subtract)
    ve.drain()
    ve.tensor_scalar(RQA, RQA, 1e-5, None, op0=ALU.add)
    k.s_v.inc(ve.drain())
    k.s_v.wait(se)
    k.s_s.inc(se.activation(RQA, RQA, AF.Sqrt))
    k.s_s.wait(ve)
    ve.reciprocal(RQA, RQA)
    ve.drain()
    ve.tensor_tensor(RC_, RA, RQA, op=ALU.mult)
    ve.drain()
    ve.tensor_scalar(RC_, RC_, -1.0, None, op0=ALU.mult)
    ve.drain()
    ve.tensor_copy(IVB, RQA)
    ve.tensor_copy(CHB, RC_)
    ve.drain()
    ve.tensor_tensor(CLB, RC_, CHB, op=ALU.subtract)
    k.s_v.inc(ve.drain())
    b2f = h["b2"][l].rearrange("a b -> (a b)")
    B2F = k.qA[0:1, 0:1024]
    k.s_v.wait(sy)
    k.s_ld.inc(sy.dma_start(out=B2F, in_=b2f.unsqueeze(0)), 16)
    k.s_ld.wait(ve)
    ve.tensor_copy(B2H, B2F)
    ve.drain()
    ve.tensor_tensor(B2L, B2F, B2H, op=ALU.subtract)
    k.s_v.inc(ve.drain())

    w1src = h["w1"][l].rearrange("(a p) o -> p a o", p=128)
    GH = k.qCb[:, 4096:4608]
    AT = k.qCb[:, 4608:5120]

    for hf in range(2):
        ar.mark()
        XHh = ar.bf(8, 1024)
        ar.mark()
        XFMh = ar.bf(8, 1032)
        full_barrier(k, "tvs")
        k.s_ld.wait(se)
        k.s_ld.wait(ve)
        k.s_lg.wait(te)
        if hf == 0:
            ve.memset(XFMh[:, :, 0:1], 0.0)
        else:
            ve.tensor_copy(XFMh[:, :, 0:1].rearrange("p a 1 -> p a"), BCOL)
        k.s_v.inc(ve.drain())
        k.s_v.wait(te)
        for jj2 in range(8):
            j = 8 * hf + jj2
            pairs = [(XINT[:, j, 128 * fb:128 * fb + 128],
                      XFMh[:, fb, 1 + 128 * jj2:1 + 128 * jj2 + 128]) for fb in range(8)]
            emit_transposes(k, pairs, k.idb[:])
        k.s_mm.wait(ve)
        if hf == 0:
            ve.drain()
            k.s_v.inc(ve.tensor_copy(BCOL, XFMh[:, :, 1024:1025].rearrange("p a 1 -> p a")))
        for tc in range(2):
            gcol = 1024 * hf + 512 * tc
            te.wait_ge(k.s_v.h, k.s_v.n)
            te.wait_ge(k.s_s.h, k.s_s.n)
            k.s_mm.inc(te.matmul(k.psB[:, 2, :], k.ones1b[0:1, 0:128],
                                 IVB[0:1, gcol:gcol + 512], start=True, stop=True))
            te.matmul(k.psB[:, 3, :], k.ones1b[0:1, 0:128],
                      CHB[0:1, gcol:gcol + 512], start=True, stop=False)
            k.s_mm.inc(te.matmul(k.psB[:, 3, :], k.ones1b[0:1, 0:128],
                                 CLB[0:1, gcol:gcol + 512], start=False, stop=True))
            k.s_mm.wait(ve)
            for ft in range(8):
                soff = 1 if ft < 4 else 0
                ve.tensor_tensor(XHh[:, ft, 512 * tc:512 * tc + 512],
                                 XFMh[:, ft, soff + 512 * tc: soff + 512 * tc + 512],
                                 k.psB[:, 2, :], op=ALU.mult)
            ve.drain()
            for ft in range(8):
                ve.tensor_tensor(XHh[:, ft, 512 * tc:512 * tc + 512],
                                 XHh[:, ft, 512 * tc:512 * tc + 512],
                                 k.psB[:, 3, :], op=ALU.add)
            k.s_v.inc(ve.drain())
            k.s_v.wait(se)
            for ft in range(8):
                se.activation(XHh[:, ft, 512 * tc:512 * tc + 512],
                              XHh[:, ft, 512 * tc:512 * tc + 512], AF.Identity,
                              scale=LNGC[:, ft:ft + 1], bias=LNBC[:, ft:ft + 1])
            k.s_s.inc(se.drain())
            k.s_s.wait(ve)
            k.s_s.wait(te)
        full_barrier(k, "tvs")
        ar.release()
        AHq = ar.bf(22, 512)
        W1S = ar.bf(2, 2, 8, 128)

        for qt in range(2):
            toff = 512 * qt
            glast = [0, 0]
            wsems = (k.s_lg, k.s_lh)
            for hc in range(22):
                ocw = 128 if hc < 21 else 42
                b = hc % 2
                ge.wait_ge(k.s_mm.h, glast[b])
                wsems[b].inc(ge.dma_start(out=W1S[:, 0, b, :, 0:ocw],
                                          in_=w1src[:, :, 128 * hc:128 * hc + ocw]), 16)
                wsems[b].inc(ge.dma_start(out=W1S[:, 1, b, :, 0:ocw],
                                          in_=w1src[:, :, 2730 + 128 * hc:2730 + 128 * hc + ocw]), 16)
                wsems[b].wait(te)
                te.wait_ge(k.s_g.h, k.s_g.n)
                te.wait_ge(k.s_s.h, k.s_s.n)
                for ag in range(2):
                    for ic in range(8):
                        mm = te.matmul(k.psA[0:ocw, ag, :], W1S[:, ag, b, ic, 0:ocw],
                                       XHh[:, ic, toff:toff + 512],
                                       start=(ic == 0), stop=(ic == 7))
                    k.s_mm.inc(mm)
                glast[b] = k.s_mm.n
                k.s_mm.wait(se)
                k.s_g.wait(se)
                se.activation(GH[0:ocw, :], k.psA[0:ocw, 1, :], AF.Gelu,
                              bias=B1GC[0:ocw, hc:hc + 1])
                se.activation(AT[0:ocw, :], k.psA[0:ocw, 0, :], AF.Identity,
                              bias=B1AC[0:ocw, hc:hc + 1])
                k.s_s.inc(se.drain())
                k.s_s.wait(ge)
                k.s_g.inc(ge.tensor_tensor(AHq[0:ocw, hc, :], AT[0:ocw, :], GH[0:ocw, :],
                                           op=ALU.mult))
            k.s_g.wait(te)
            k.s_lg.wait(te)
            for tch in range(4):
                j = 4 * (2 * hf + qt) + tch
                te.wait_ge(k.s_v.h, max(0, k.s_v.n - 1))
                for dc in range(2):
                    pb = k.psB[:, 2 * (tch % 2) + dc, :]
                    te.matmul(pb, k.ones1b[0:1, 0:128], B2H[0:1, 512 * dc:512 * dc + 512],
                              start=True, stop=False)
                    te.matmul(pb, k.ones1b[0:1, 0:128], B2L[0:1, 512 * dc:512 * dc + 512],
                              start=False, stop=False)
                    for hc in range(22):
                        kcw = 128 if hc < 21 else 42
                        mm = te.matmul(pb, AHq[0:kcw, hc, 128 * tch:128 * tch + 128],
                                       W2R[0:kcw, hc, 512 * dc:512 * dc + 512],
                                       start=False, stop=(hc == 21))
                    k.s_mm.inc(mm)
                k.s_mm.wait(ve)
                for dc in range(2):
                    ve.tensor_tensor(XINT[:, j, 512 * dc:512 * dc + 512],
                                     XINT[:, j, 512 * dc:512 * dc + 512],
                                     k.psB[:, 2 * (tch % 2) + dc, :], op=ALU.add)
                k.s_v.inc(ve.drain())
        full_barrier(k)
        ar.release()
    ar.release()



def prep_host_tensors(inputs):
    t = {}
    perm = []
    for hh in range(16):
        for half in range(2):
            base = 64 * hh + 32 * half
            perm += [base + i for i in range(0, 32, 2)]
            perm += [base + i for i in range(1, 32, 2)]
    permq = np.array(perm, dtype=np.int64)
    kperm = permq[:256]
    Wq = np.asarray(inputs["L0_attn_Wq"], np.float32)
    Wkv = np.asarray(inputs["L0_attn_Wkv"], np.float32)
    t["wqT"] = np.ascontiguousarray(Wq.T[:, permq])
    wk = Wkv[0:256][kperm]
    wv = Wkv[256:512]
    t["wkvT"] = np.ascontiguousarray(np.concatenate([wk, wv], 0).T)
    t["woT"] = np.ascontiguousarray(np.asarray(inputs["L0_attn_Wout"], np.float32).T)
    t["lam"] = np.ascontiguousarray(np.asarray(inputs["L0_attn_lambda_init"], np.float32).reshape(16, 1))
    t["rmsw"] = np.ascontiguousarray(
        np.tile(np.asarray(inputs["L0_attn_rms_weight"], np.float32)[None, :], (128, 1)))
    t["wg0"] = np.ascontiguousarray(np.asarray(inputs["L0_ffn_w_gate"], np.float32))
    t["wu0"] = np.ascontiguousarray(np.asarray(inputs["L0_ffn_w_up"], np.float32))
    t["wd0"] = np.ascontiguousarray(np.asarray(inputs["L0_ffn_w_down"], np.float32))
    t["wthT"] = np.ascontiguousarray(np.asarray(inputs["L1_attn_W_theta"], np.float32).T)
    t["wphT"] = np.ascontiguousarray(np.asarray(inputs["L1_attn_W_phi"], np.float32).T)
    t["wgT"] = np.ascontiguousarray(np.asarray(inputs["L1_attn_W_g"], np.float32).T)
    t["wqkv"] = np.ascontiguousarray(np.asarray(inputs["L1_ffn_wqkv"], np.float32))
    t["qs"] = np.ascontiguousarray(np.tile(np.asarray(inputs["L1_ffn_q_scale"], np.float32), (1, 2)))
    t["ks"] = np.ascontiguousarray(np.tile(np.asarray(inputs["L1_ffn_k_scale"], np.float32), (1, 2)))
    t["wo1"] = np.ascontiguousarray(np.asarray(inputs["L1_ffn_wo"], np.float32))
    t["lngT"] = np.ascontiguousarray(np.asarray(inputs["L1_ffn_ln_g"], np.float32).reshape(2, 8, 128).transpose(0, 2, 1))
    t["lnbT"] = np.ascontiguousarray(np.asarray(inputs["L1_ffn_ln_b"], np.float32).reshape(2, 8, 128).transpose(0, 2, 1))
    t["w1"] = np.ascontiguousarray(np.asarray(inputs["L1_ffn_w1"], np.float32))
    b1 = np.asarray(inputs["L1_ffn_b1"], np.float32)
    b1p = np.zeros((2, 2, 2816), np.float32)
    b1p[:, 0, 0:2730] = b1[:, 0:2730]
    b1p[:, 1, 0:2730] = b1[:, 2730:5460]
    t["b1ag"] = np.ascontiguousarray(b1p.reshape(2, 2, 22, 128).transpose(0, 1, 3, 2))
    t["w2"] = np.ascontiguousarray(np.asarray(inputs["L1_ffn_w2"], np.float32))
    t["b2"] = np.ascontiguousarray(np.asarray(inputs["L1_ffn_b2"], np.float32).reshape(2, 8, 128))
    return t


def build_program():
    nc = bass.Bass("TRN2", target_bir_lowering=False)
    ctx = ExitStack()
    h = {}
    h["x_in"] = nc.dram_tensor("x_in", [T, D], F32, kind="ExternalInput").ap()
    out_h = nc.dram_tensor("out", [T, D], F32, kind="ExternalOutput").ap()
    for name, shape in [
        ("wqT", [D, D]), ("wkvT", [D, 512]), ("woT", [D, D]),
        ("lam", [16, 1]), ("rmsw", [128, 64]),
        ("wg0", [2816, D]), ("wu0", [2816, D]), ("wd0", [D, 2816]),
        ("wthT", [D, 256]), ("wphT", [D, 256]), ("wgT", [D, D]),
        ("wqkv", [2, D, 1536]), ("qs", [2, 128]), ("ks", [2, 128]),
        ("wo1", [2, 512, D]),
        ("w1", [2, D, 5460]), ("b1ag", [2, 2, 128, 22]), ("w2", [2, 2730, D]),
        ("b2", [2, 8, 128]), ("lngT", [2, 128, 8]), ("lnbT", [2, 128, 8]),
    ]:
        h[name] = nc.dram_tensor(name, shape, F32, kind="ExternalInput").ap()
    h["wqgT_s"] = nc.dram_tensor("wqgT_s", [D, 2816], BF16, kind="Internal").ap()
    h["wquT_s"] = nc.dram_tensor("wquT_s", [D, 2816], BF16, kind="Internal").ap()
    h["wdqT_s"] = nc.dram_tensor("wdqT_s", [2816, D], BF16, kind="Internal").ap()
    h["x1_s"] = nc.dram_tensor("x1_s", [T, D], F32, kind="Internal").ap()

    k = KB(nc, ctx)
    k.out_h = out_h
    blk = ctx.enter_context(nc.Block())
    k.psA = k.ps("psA", [128, 4, 512])
    k.psB = k.ps("psB", [128, 4, 512])
    k.qA = k.sb("qA", [128, 2816], F32)
    k.qB = k.sb("qB", [128, 2816], F32)
    k.qC = k.sb("qC", [128, 2816], F32)
    k.qAb = k.qA[:].bitcast(BF16)
    k.qBb = k.qB[:].bitcast(BF16)
    k.qCb = k.qC[:].bitcast(BF16)
    build_constants(k)
    k.ar = Arena(k, ARENA_N)
    k.ar.mark()
    k.arXQF = k.ar.bf(8, T)
    k.ar.mark()
    k.arQ = k.ar.bf(8, T)
    k.arK2 = k.ar.bf(2, T)
    k.arVT = k.ar.bf(NT, 4, 68)

    emit_l0(k, h, out_h)

    ctx.close()
    return nc


def kernel(**inputs):
    nb = 8
    x_full = np.ascontiguousarray(np.asarray(inputs["L0_attn_x"], np.float32))
    nc = build_program()
    host_tensors = prep_host_tensors(inputs)
    in_maps = []
    for c in range(nb):
        m = {"x_in": np.ascontiguousarray(x_full[c])}
        m.update(host_tensors)
        in_maps.append(m)
    res = run_bass_kernel_spmd(nc, in_maps, core_ids=list(range(nb)))
    return np.stack([r["out"] for r in res.results], axis=0)


# version 05
# speedup vs baseline: 1.5502x; 1.0018x over previous
import os
import dataclasses as _dc
from contextlib import ExitStack
import numpy as np
import ml_dtypes
import concourse.bass as bass
import concourse.mybir as mybir
from concourse.bass_utils import run_bass_kernel_spmd

F32 = mybir.dt.float32
BF16 = mybir.dt.bfloat16
AF = mybir.ActivationFunctionType
ALU = mybir.AluOpType
AX = mybir.AxisListType

T = 2048
NT = 16
D = 1024
MAGIC = 12582912.0
DEV_STAGE = os.environ.get("KDEV_STAGE", "full")
DEV_TAP = os.environ.get("KDEV_TAP", "")
ARENA_N = 85000


def _prod(xs):
    r = 1
    for x in xs:
        r *= x
    return r


class Sem:
    def __init__(self, nc, ctx, name):
        self.h = ctx.enter_context(nc.semaphore(name))
        self.n = 0

    def inc(self, instr, k=1):
        instr.then_inc(self.h, k)
        self.n += k
        return instr

    def wait(self, eng, val=None):
        eng.wait_ge(self.h, self.n if val is None else val)


class Arena:
    def __init__(self, k, n):
        self.t = k.nc.sbuf_tensor("ARENA", [128, n], BF16)
        self.th = k.ctx.enter_context(self.t)
        self.off = 0
        self.n = n
        self.stack = []

    def bf(self, *shape):
        m = _prod(shape)
        assert self.off + m <= self.n, f"arena overflow: {self.off}+{m} > {self.n}"
        ap = self.th[:, self.off:self.off + m]
        self.off += m
        if len(shape) > 1:
            names = " ".join(f"d{i}" for i in range(len(shape)))
            kw = {f"d{i}": s for i, s in enumerate(shape)}
            ap = ap.rearrange(f"p ({names}) -> p {names}", **kw)
        return ap

    def f32(self, *shape):
        m = _prod(shape)
        if self.off % 2:
            self.off += 1
        assert self.off + 2 * m <= self.n, f"arena overflow f32: {self.off}+{2*m} > {self.n}"
        ap = self.th[:, self.off:self.off + 2 * m].bitcast(F32)
        self.off += 2 * m
        if len(shape) > 1:
            names = " ".join(f"d{i}" for i in range(len(shape)))
            kw = {f"d{i}": s for i, s in enumerate(shape)}
            ap = ap.rearrange(f"p ({names}) -> p {names}", **kw)
        return ap

    def mark(self):
        self.stack.append(self.off)

    def release(self):
        self.off = self.stack.pop()


class KB:
    def __init__(self, nc, ctx):
        self.nc = nc
        self.ctx = ctx
        self.te, self.ve, self.se, self.ge, self.sy = \
            nc.tensor, nc.vector, nc.scalar, nc.gpsimd, nc.sync
        self.s_ld = Sem(nc, ctx, "s_ld")
        self.s_mm = Sem(nc, ctx, "s_mm")
        self.s_v = Sem(nc, ctx, "s_v")
        self.s_s = Sem(nc, ctx, "s_s")
        self.s_g = Sem(nc, ctx, "s_g")
        self.s_st = Sem(nc, ctx, "s_st")
        self.s_x = Sem(nc, ctx, "s_x")
        self.s_la = Sem(nc, ctx, "s_la")
        self.s_lb = Sem(nc, ctx, "s_lb")
        self.s_lg = Sem(nc, ctx, "s_lg")
        self.s_lh = Sem(nc, ctx, "s_lh")
        self.s_lc = Sem(nc, ctx, "s_lc")
        self.s_dmp = Sem(nc, ctx, "s_dmp")

    def sb(self, name, shape, dtype):
        return self.ctx.enter_context(self.nc.sbuf_tensor(name, shape, dtype))

    def ps(self, name, shape):
        return self.ctx.enter_context(self.nc.psum_tensor(name, shape, F32))


def dump_fm(k, buf, out_h, nt):
    full_barrier(k)
    ov = out_h.rearrange("(a p) f -> p a f", p=128)
    W = buf.shape[-1]
    for ft in range(nt):
        for hf in range(2):
            lo = 1024 * hf
            hi = min(lo + 1024, W)
            if lo >= hi:
                continue
            k.s_dmp.inc(k.ge.dma_start(out=ov[:, 2 * ft + hf, 0:hi - lo],
                                      in_=buf[:, ft, lo:hi]), 16)
    k.s_dmp.wait(k.ge)
    k.s_dmp.wait(k.sy)


def full_barrier(k, engines="tvsg", dma=True):
    s = k.s_x
    if "t" in engines:
        s.inc(k.te.drain())
    if "v" in engines:
        s.inc(k.ve.drain())
    if "s" in engines:
        s.inc(k.se.drain())
    if "g" in engines:
        s.inc(k.ge.drain())
    for e in (k.te, k.ve, k.se, k.ge, k.sy):
        e.wait_ge(s.h, s.n)
        if dma and engines == "tvsg":
            for sem in (k.s_ld, k.s_la, k.s_lb, k.s_lg, k.s_lh, k.s_lc, k.s_st, k.s_dmp):
                sem.wait(e)


def build_constants(k):
    k.mcol = k.sb("c_mcol", [128, 1], F32)
    k.rc6 = k.sb("c_rc6", [128, 1], F32)
    k.rc12 = k.sb("c_rc12", [128, 1], F32)
    k.idf = k.sb("c_idf", [128, 128], F32)
    k.idb = k.sb("c_idb", [128, 128], BF16)
    k.ones1b = k.sb("c_ones1b", [128, 128], BF16)
    k.M01 = k.sb("c_m01", [128, 128], BF16)
    k.P32 = k.sb("c_p32", [128, 128], BF16)
    k.P4 = k.sb("c_p4", [128, 128], BF16)
    k.qbm = k.sb("qbm", [128, 176], F32)
    k.qib6 = k.sb("qib6", [128, 176], F32)
    k.qbm12 = k.sb("qbm12", [128, 176], BF16)
    k.stats = k.sb("stats", [128, 48], F32)
    k.smallc = k.sb("smallc", [128, 16], F32)
    v, g = k.ve, k.ge
    v.memset(k.mcol[:], 1.0)
    v.drain()
    v.tensor_scalar(k.mcol[:], k.mcol[:], 2.0 ** -30, None, op0=ALU.add)
    v.drain()
    v.tensor_scalar(k.mcol[:], k.mcol[:], -1.0, None, op0=ALU.add)
    v.drain()
    v.tensor_scalar(k.mcol[:], k.mcol[:], 1.5 * 2.0 ** 82, MAGIC,
                    op0=ALU.mult, op1=ALU.add)
    v.memset(k.rc6[:], 6.0)
    v.memset(k.rc12[:], 12.0)
    v.memset(k.ones1b[:], 1.0)
    v.drain()
    v.reciprocal(k.rc6[:], k.rc6[:])
    v.reciprocal(k.rc12[:], k.rc12[:])
    k.s_g.inc(g.iota(k.idf[:], pattern=[[1, 128]], base=0, channel_multiplier=-1,
                     allow_small_or_imprecise_dtypes=True))
    k.s_g.wait(v)
    v.tensor_scalar(k.idb[:], k.idf[:], 0.0, None, op0=ALU.is_equal)
    v.tensor_scalar(k.M01[:], k.idf[:], 0.0, None, op0=ALU.is_ge)
    v.drain()
    k.s_v.inc(v.tensor_scalar(k.idf[:], k.idf[:], 0.0, None, op0=ALU.is_equal))
    DIFq = k.qA[:, 0:128]
    CM32 = k.qB[:, 0:128]
    SC2 = k.qC[:, 4:260]
    g.iota(DIFq, pattern=[[1, 128]], base=0, channel_multiplier=-1,
           allow_small_or_imprecise_dtypes=True)
    g.iota(CM32, pattern=[[0, 4], [1, 32]], base=0, channel_multiplier=0,
           allow_small_or_imprecise_dtypes=True)
    k.s_g.inc(g.drain())
    k.s_g.wait(v)
    k.pcol = k.sb("c_pcol", [128, 1], F32)
    k.pm16 = k.sb("c_pm16", [128, 1], F32)
    k.altc = k.sb("c_altc", [128, 1], F32)
    v.tensor_scalar(k.pcol[:], DIFq[:, 0:1], -1.0, None, op0=ALU.mult)
    v.drain()
    v.tensor_scalar(k.pm16[:], k.pcol[:], -7.5, 0.0625, op0=ALU.add, op1=ALU.mult)
    v.drain()
    v.tensor_scalar(k.pm16[:], k.pm16[:], k.mcol[:, 0:1], k.mcol[:, 0:1],
                    op0=ALU.add, op1=ALU.subtract)
    v.drain()
    v.scalar_tensor_tensor(k.pm16[:], k.pm16[:], -16.0, k.pcol[:],
                           op0=ALU.mult, op1=ALU.add)
    v.drain()
    v.tensor_scalar(k.altc[:], k.pcol[:], -15.5, 0.03125, op0=ALU.add, op1=ALU.mult)
    v.drain()
    v.tensor_scalar(k.altc[:], k.altc[:], k.mcol[:, 0:1], k.mcol[:, 0:1],
                    op0=ALU.add, op1=ALU.subtract)
    v.drain()
    v.scalar_tensor_tensor(k.altc[:], k.altc[:], -32.0, k.pcol[:],
                           op0=ALU.mult, op1=ALU.add)
    v.drain()
    v.tensor_scalar(k.altc[:], k.altc[:], 16.0, None, op0=ALU.is_ge)
    v.drain()
    v.tensor_scalar(k.altc[:], k.altc[:], -2.0, 1.0, op0=ALU.mult, op1=ALU.add)
    v.drain()
    CMOD = k.qA[:, 128:256]
    g.iota(CMOD, pattern=[[0, 4], [1, 32]], base=0, channel_multiplier=0,
           allow_small_or_imprecise_dtypes=True)
    k.s_g.inc(g.drain())
    k.s_g.wait(v)
    v.tensor_scalar(k.P32[:], DIFq, 16.0, None, op0=ALU.is_equal)
    v.tensor_scalar(SC2[:, 0:128], DIFq, -16.0, None, op0=ALU.is_equal)
    v.tensor_scalar(SC2[:, 128:256], CMOD, 16.0, None, op0=ALU.is_ge)
    v.drain()
    v.tensor_tensor(k.P32[:], k.P32[:], SC2[:, 128:256], op=ALU.mult)
    v.drain()
    v.tensor_scalar(SC2[:, 128:256], SC2[:, 128:256], -1.0, 1.0, op0=ALU.mult, op1=ALU.add)
    v.drain()
    v.tensor_tensor(SC2[:, 0:128], SC2[:, 0:128], SC2[:, 128:256], op=ALU.mult)
    v.drain()
    v.tensor_tensor(k.P32[:], k.P32[:], SC2[:, 0:128], op=ALU.add)
    v.drain()
    v.tensor_scalar(k.P32[:], k.P32[:], k.altc[:, 0:1], None, op0=ALU.mult)
    v.drain()
    v.tensor_scalar(SC2[:, 0:128], DIFq, 0.0, None, op0=ALU.is_le)
    v.tensor_scalar(SC2[:, 128:256], DIFq, -32.0, None, op0=ALU.is_gt)
    v.drain()
    v.tensor_scalar(CM32, CM32, 0.0, None, op0=ALU.is_equal)
    v.tensor_tensor(k.P4[:], SC2[:, 0:128], SC2[:, 128:256], op=ALU.mult)
    v.drain()
    k.s_v.inc(v.tensor_tensor(k.P4[:], k.P4[:], CM32, op=ALU.mult))
    ln4, pi4, twopi, i2pi, hpi, sig, pip, pin = (k.smallc[:, i:i + 1] for i in range(8))
    v.memset(ln4, 10000.0)
    v.memset(pi4, 1.0)
    v.memset(sig, 32.0)
    k.s_v.inc(v.drain())
    k.s_v.wait(k.se)
    k.se.activation(ln4, ln4, AF.Ln)
    k.se.activation(pi4, pi4, AF.Arctan)
    k.s_s.inc(k.se.drain())
    k.s_s.wait(v)
    v.tensor_scalar(twopi, pi4, 8.0, None, op0=ALU.mult)
    v.tensor_scalar(hpi, pi4, 2.0, None, op0=ALU.mult)
    v.tensor_scalar(pip, pi4, 4.0, None, op0=ALU.mult)
    v.tensor_scalar(pin, pi4, -4.0, None, op0=ALU.mult)
    v.reciprocal(sig, sig)
    v.drain()
    v.reciprocal(i2pi, twopi)
    k.s_v.inc(v.drain())
    k.s_v.wait(k.se)
    k.s_s.inc(k.se.activation(sig, sig, AF.Sqrt))
    k.ln4, k.pi4, k.twopi, k.i2pi, k.hpi, k.sig = ln4, pi4, twopi, i2pi, hpi, sig
    k.pip, k.pin = pip, pin


def emit_quant(k, S, Q, ge_q=True):
    v, g = k.ve, k.ge
    k.s_g.wait(v)
    W = S.shape[-1]
    nb = W // 16
    A, B, C = k.qA, k.qB, k.qC
    Ab, Bb, Cb = k.qAb, k.qBb, k.qCb
    bm, ib6, bm12 = k.qbm, k.qib6, k.qbm12
    S3 = S.rearrange("p (a b) -> p a b", b=16)
    v.tensor_reduce(bm[:, 0:nb], S3, axis=AX.X, op=ALU.max, apply_absolute_value=True)
    v.drain()
    v.tensor_scalar(bm[:, 0:nb], bm[:, 0:nb], 6e-12, None, op0=ALU.max)
    v.drain()
    v.reciprocal(ib6[:, 0:nb], bm[:, 0:nb])
    v.tensor_scalar(bm12[:, 0:nb], bm[:, 0:nb], k.rc6[:, 0:1], None, op0=ALU.mult)
    v.drain()
    v.tensor_scalar(ib6[:, 0:nb], ib6[:, 0:nb], 6.0, None, op0=ALU.mult)
    v.drain()
    ib6b = ib6[:, 0:nb].unsqueeze(2).broadcast_to((128, nb, 16))
    v.tensor_tensor(A[:, 0:W].rearrange("p (a b) -> p a b", b=16), S3, ib6b, op=ALU.mult)
    v.drain()
    v.tensor_scalar(B[:, 0:W], A[:, 0:W], -2.0, 2.0, op0=ALU.max, op1=ALU.min)
    v.tensor_scalar(C[:, 0:W], A[:, 0:W], -4.0, 4.0, op0=ALU.max, op1=ALU.min)
    k.s_v.inc(v.drain())
    k.s_v.wait(g)
    k.s_g.inc(g.tensor_tensor(C[:, 0:W], C[:, 0:W], A[:, 0:W], op=ALU.add))
    k.s_g.wait(v)
    v.scalar_tensor_tensor(A[:, 0:W], C[:, 0:W], 0.5, B[:, 0:W], op0=ALU.mult, op1=ALU.add)
    v.drain()
    z = Bb[:, 0:W]
    v.tensor_scalar(z, A[:, 0:W], k.mcol[:, 0:1], k.mcol[:, 0:1],
                    op0=ALU.add, op1=ALU.subtract)
    v.drain()
    m1, c6 = Cb[:, 0:W], Ab[:, 0:W]
    z2, s1 = Ab[:, W:2 * W], Cb[:, W:2 * W]
    v.tensor_scalar(m1, z, 0.5, -2.0, op0=ALU.mult, op1=ALU.max)
    v.tensor_scalar(c6, z, -6.0, 6.0, op0=ALU.max, op1=ALU.min)
    v.tensor_scalar(z2, z, 2.0, None, op0=ALU.mult)
    v.drain()
    v.tensor_scalar(m1, m1, 2.0, None, op0=ALU.min)
    v.drain()
    v.tensor_tensor(s1, m1, c6, op=ALU.add)
    v.drain()
    psi = Cb[:, 0:W]
    v.tensor_tensor(psi, z2, s1, op=ALU.subtract)
    k.s_v.inc(v.drain())
    bm12b = bm12[:, 0:nb].unsqueeze(2).broadcast_to((128, nb, 16))
    Q3 = Q.rearrange("p (a b) -> p a b", b=16)
    psi3 = psi.rearrange("p (a b) -> p a b", b=16)
    if ge_q:
        k.s_v.wait(g)
        k.s_g.inc(g.tensor_tensor(Q3, psi3, bm12b, op=ALU.mult))
    else:
        v.tensor_tensor(Q3, psi3, bm12b, op=ALU.mult)
        k.s_v.inc(v.drain())


def emit_transposes(k, pairs, identity, deep=False):
    ve = k.ve
    if not hasattr(k, "_tp_i"):
        k._tp_i = 0
    nb_ = 8 if deep else 4
    for (src, dst) in pairs:
        k.te.wait_ge(k.s_v.h, max(0, k.s_v.n - (nb_ - 1)))
        i = k._tp_i % nb_
        pst = k.psA if i < 4 else k.psB
        pv = pst[:, i % 4, :].bitcast(BF16)[:, 0:dst.shape[-1]]
        k.s_mm.inc(k.te.transpose(pv, src, identity))
        k.s_mm.wait(ve)
        k.s_v.inc(ve.tensor_copy(dst, pv))
        k._tp_i += 1


def emit_ln_stats(k, xsrc_hbm, XTa, SCR, nt=NT):
    v, s, sy = k.ve, k.se, k.sy
    ssum = k.stats[:, 0:nt]
    ssq = k.stats[:, nt:2 * nt]
    inv = k.stats[:, 2 * nt:3 * nt]
    xr = xsrc_hbm.rearrange("(j p) f -> p j f", p=128)
    sems = (k.s_la, k.s_lb)
    for j in range(min(2, nt)):
        sems[j % 2].inc(sy.dma_start(out=XTa[:, j % 2, :], in_=xr[:, j, :]), 16)
    for j in range(nt):
        xt = XTa[:, j % 2, :]
        sems[j % 2].wait(v)
        sems[j % 2].wait(s)
        v.tensor_reduce(ssum[:, j:j + 1], xt, axis=AX.X, op=ALU.add)
        s.activation(SCR[:, (j % 2) * D:(j % 2) * D + D], xt, AF.Square,
                     accum_out=ssq[:, j:j + 1])
        if j + 2 < nt:
            k.s_s.inc(s.drain())
            k.s_v.inc(v.drain())
            k.s_s.wait(sy)
            k.s_v.wait(sy)
            sems[j % 2].inc(sy.dma_start(out=XTa[:, j % 2, :], in_=xr[:, j + 2, :]), 16)
    k.s_s.inc(s.drain())
    k.s_s.wait(v)
    v.drain()
    v.tensor_scalar(ssum[:], ssum[:], 1.0 / D, None, op0=ALU.mult)
    v.drain()
    v.tensor_tensor(inv[:], ssum[:], ssum[:], op=ALU.mult)
    v.tensor_scalar(ssq[:], ssq[:], 1.0 / D, None, op0=ALU.mult)
    v.drain()
    v.tensor_tensor(ssq[:], ssq[:], inv[:], op=ALU.subtract)
    v.drain()
    v.tensor_scalar(ssq[:], ssq[:], 1e-5, None, op0=ALU.add)
    v.drain()
    v.reciprocal(ssq[:], ssq[:])
    k.s_v.inc(v.drain())
    k.s_v.wait(s)
    s.activation(inv[:], ssq[:], AF.Sqrt)
    k.s_s.inc(s.drain())
    k.s_s.wait(v)
    k.s_s.wait(sy)
    k.s_v.wait(sy)



def emit_l0(k, h, out_h):
    te, ve, se, ge, sy = k.te, k.ve, k.se, k.ge, k.sy
    ar = k.ar

    XQF = k.arXQF
    ar.mark()
    FM0 = ar.bf(8, T)
    ar.mark()
    XTa = ar.f32(2, D)
    XNT = ar.f32(D)
    XB = ar.bf(D)
    XQT = ar.bf(D)
    WLD = ar.f32(2816)
    WQB = ar.bf(2816)

    emit_ln_stats(k, h["x_in"], XTa, k.qA[:, 0:2 * D])

    xr = h["x_in"].rearrange("(j p) f -> p j f", p=128)
    outr = out_h.rearrange("(j p) f -> p j f", p=128)
    ssum = k.stats[:, 0:NT]
    inv = k.stats[:, 2 * NT:3 * NT]
    sems = (k.s_la, k.s_lb)
    for j in range(min(2, NT)):
        sems[j % 2].inc(sy.dma_start(out=XTa[:, j % 2, :], in_=xr[:, j, :]), 16)
    for j in range(NT):
        xt = XTa[:, j % 2, :]
        sems[j % 2].wait(ve)
        k.s_st.wait(ve)
        k.s_s.wait(ve)
        ve.tensor_scalar(XNT, xt, ssum[:, j:j + 1], inv[:, j:j + 1],
                         op0=ALU.subtract, op1=ALU.mult)
        k.s_v.inc(ve.drain())
        if j + 2 < NT:
            k.s_v.wait(sy)
            sems[j % 2].inc(sy.dma_start(out=XTa[:, j % 2, :], in_=xr[:, j + 2, :]), 16)
        if DEV_STAGE == "ln":
            k.s_v.wait(sy)
            k.s_st.inc(sy.dma_start(out=outr[:, j, :], in_=XNT), 16)
            continue
        k.s_v.wait(se)
        k.s_s.inc(se.activation(XB, XNT, AF.Copy, bias=0.0))
        emit_quant(k, XNT, XQT)
        if DEV_STAGE == "xq":
            k.s_g.wait(ge)
            k.s_lg.inc(ge.dma_start(out=outr[:, j, :], in_=XQT), 16)
            k.s_lg.wait(ve)
            continue
        k.s_s.wait(te)
        k.s_v.wait(te)
        k.s_g.wait(te)
        pairs = [(XB[:, 128 * fb:128 * fb + 128], FM0[:, fb, 128 * j:128 * j + 128])
                 for fb in range(8)]
        pairs += [(XQT[:, 128 * fb:128 * fb + 128], XQF[:, fb, 128 * j:128 * j + 128])
                  for fb in range(8)]
        emit_transposes(k, pairs, k.idb[:], deep=True)
        k.s_mm.wait(se)
        k.s_mm.wait(ve)
        k.s_mm.wait(ge)

    if DEV_STAGE in ("ln", "xq"):
        k.s_st.wait(sy)
        k.s_lg.wait(sy)
        return

    emit_l0_wquant(k, h, WLD, WQB, XTa.rearrange("p a b -> p (a b)").bitcast(BF16)[:, 0:2816])
    full_barrier(k)
    ar.release()
    emit_l0_attn(k, h, FM0, out_h)


def emit_l0_attn(k, h, FM0, out_h):
    te, ve, se, ge, sy = k.te, k.ve, k.se, k.ge, k.sy
    ar = k.ar
    Q = k.arQ
    VT = k.arVT
    K2 = k.arK2

    ar.mark()
    WQ = ar.bf(8, D)
    WKV = ar.bf(8, 512)
    VB = ar.bf(2, T)
    COSA = ar.bf(T)
    SINA = ar.bf(T)
    COSQ = ar.bf(T)
    SINQ = ar.bf(T)
    invc = ar.f32(1)

    k.s_lg.inc(ge.dma_start(out=WQ, in_=h["wqT"].rearrange("(a p) o -> p a o", p=128)), 16)
    k.s_lg.inc(ge.dma_start(out=WKV, in_=h["wkvT"].rearrange("(a p) o -> p a o", p=128)), 16)

    invf = invc
    ve.tensor_scalar(invf, k.pm16[:], -0.0625, None, op0=ALU.mult)
    ve.drain()
    ve.tensor_scalar(invf, invf, k.ln4, None, op0=ALU.mult)
    k.s_v.inc(ve.drain())
    k.s_v.wait(se)
    k.s_s.inc(se.activation(invf, invf, AF.Exp))

    TGRID = k.qA[:, 0:T]
    ARG = k.qB[:, 0:T]
    KQ = k.qC[:, 0:T]
    k.s_g.inc(ge.iota(TGRID, pattern=[[1, T]], base=0, channel_multiplier=0,
                      allow_small_or_imprecise_dtypes=True))
    k.s_g.wait(ve)
    k.s_s.wait(ve)

    def build_trig(dst, use_hpi):
        if use_hpi:
            ve.tensor_scalar(ARG, TGRID, invf, k.hpi, op0=ALU.mult, op1=ALU.add)
        else:
            ve.tensor_scalar(ARG, TGRID, invf, None, op0=ALU.mult)
        ve.drain()
        ve.tensor_scalar(KQ, ARG, k.i2pi, None, op0=ALU.mult)
        ve.drain()
        ve.tensor_scalar(KQ, KQ, k.mcol[:, 0:1], k.mcol[:, 0:1], op0=ALU.add, op1=ALU.subtract)
        ve.drain()
        ve.scalar_tensor_tensor(KQ, KQ, k.twopi, ARG, op0=ALU.mult, op1=ALU.subtract)
        ve.drain()
        ve.tensor_scalar(KQ, KQ, k.pin, k.pip, op0=ALU.max, op1=ALU.min)
        k.s_v.inc(ve.drain())
        k.s_v.wait(se)
        k.s_s.inc(se.activation(KQ, KQ, AF.Sin))
        k.s_s.wait(ve)
        ve.tensor_scalar(dst, KQ, -1.0, None, op0=ALU.mult)
        k.s_v.inc(ve.drain())

    build_trig(COSA, True)
    build_trig(SINA, False)
    k.s_v.wait(ve)
    ve.tensor_scalar(COSQ, COSA, k.sig, None, op0=ALU.mult)
    k.s_v.inc(ve.tensor_scalar(SINQ, SINA, k.sig, None, op0=ALU.mult))

    k.s_ld.wait(te)
    k.s_lg.wait(te)
    k.s_v.wait(te)

    def proj(WT, nout, dst, woff=0):
        for oc in range(nout):
            for tc in range(4):
                pb = k.psA[:, tc % 4, :]
                te.wait_ge(k.s_s.h, max(0, k.s_s.n - 3))
                for ic in range(8):
                    mm = te.matmul(pb, WT[:, ic, 128 * (oc + woff):128 * (oc + woff) + 128],
                                   FM0[:, ic, 512 * tc:512 * tc + 512],
                                   start=(ic == 0), stop=(ic == 7))
                k.s_mm.inc(mm)
                k.s_mm.wait(se)
                k.s_s.inc(se.activation(dst[:, oc, 512 * tc:512 * tc + 512], pb,
                                        AF.Copy, bias=0.0))

    proj(WQ, 8, Q)
    if DEV_TAP == "qpre":
        dump_fm(k, Q, k.out_h, 8)
        return
    proj(WKV, 2, K2)
    k.s_v.wait(ve)
    ve.memset(VT[:], 1.0)
    k.s_v.inc(ve.drain())
    proj(WKV, 2, VB.rearrange("p a t -> p a t"), woff=2)
    k.s_s.wait(te)
    k.s_v.wait(te)
    base_v = k.s_v.n
    i = 0
    for hk in range(4):
        fr, ft = (64 * hk) % 128, (64 * hk) // 128
        for j in range(NT):
            if i >= 4:
                te.wait_ge(k.s_v.h, base_v + i - 3)
            pv = k.psA[:, i % 4, :].bitcast(BF16)[:, 0:64]
            idsl = k.idb[fr:fr + 64, fr:fr + 64]
            k.s_mm.inc(te.transpose(pv, VB[fr:fr + 64, ft, 128 * j:128 * j + 128], idsl))
            k.s_mm.wait(ve)
            k.s_v.inc(ve.tensor_copy(VT[:, j, hk, 0:64], pv))
            i += 1

    RT1 = k.qA[:].bitcast(BF16)[:, 0:T]
    full_barrier(k, "tvs")

    def rope(SRC, n_tiles, COS, SIN):
        for ft in range(n_tiles):
            k.s_v.wait(te)
            for c in range(4):
                k.s_mm.inc(te.matmul(k.psB[:, c, :], k.P32[:], SRC[:, ft, 512 * c:512 * c + 512],
                                     start=True, stop=True))
            ve.tensor_tensor(RT1, SRC[:, ft, :], COS, op=ALU.mult)
            k.s_mm.wait(ve)
            for c in range(4):
                ve.tensor_tensor(k.psB[:, c, :], k.psB[:, c, :],
                                 SIN[:, 512 * c:512 * c + 512], op=ALU.mult)
            ve.drain()
            for c in range(4):
                ve.tensor_tensor(SRC[:, ft, 512 * c:512 * c + 512],
                                 RT1[:, 512 * c:512 * c + 512], k.psB[:, c, :], op=ALU.add)
            k.s_v.inc(ve.drain())

    rope(Q, 8, COSQ, SINQ)
    if DEV_TAP == "qrope":
        dump_fm(k, Q, k.out_h, 8)
        return
    rope(K2, 2, COSA, SINA)
    if DEV_TAP == "krope":
        dump_fm(k, K2, k.out_h, 2)
        return
    full_barrier(k, "tv")
    ar.release()
    emit_l0_scores(k, h, Q, K2, VT, out_h)


def emit_l0_scores(k, h, Q, K2, VT, out_h):
    nc = k.nc
    te, ve, se, ge, sy = k.te, k.ve, k.se, k.ge, k.sy
    ar = k.ar
    P4t = k.P4

    ar.mark()
    KARR = ar.bf(4, T)
    BT = ar.bf(T)
    UT = ar.bf(4, 512)
    OE = ar.bf(4, 512)
    OTT = ar.bf(4, 4, 72)
    YC = ar.bf(4, 2, 64)
    WROW = ar.bf(64)
    RC = ar.f32(24)
    KMG = ar.f32(2)
    KMC = ar.f32(4)
    NLAM = ar.f32(16)
    LAMC = ar.f32(1)

    k.s_v.wait(ve)
    for v_ in range(4):
        src = K2[64 * (v_ % 2):64 * (v_ % 2) + 64, v_ // 2, :]
        ve.tensor_copy(KARR[0:64, v_, :], src)
        ve.tensor_copy(KARR[64:128, v_, :], src)
    k.s_v.inc(ve.drain())

    lam_s = nc.dram_tensor("lam_scr", [16, 1], F32, kind="Internal").ap()
    km_s = nc.dram_tensor("km_scr", [128, 2], F32, kind="Internal").ap()
    k.s_ld.inc(sy.dma_start(out=LAMC[0:16, :], in_=h["lam"][:, :]), 16)
    k.s_ld.wait(se)
    se.activation(LAMC[0:16, :], LAMC[0:16, :], AF.Sigmoid)
    k.s_s.inc(se.drain())
    k.s_s.wait(ve)
    k.s_v.inc(ve.tensor_scalar(LAMC[0:16, :], LAMC[0:16, :], -1.0, None, op0=ALU.mult))
    k.s_v.wait(sy)
    k.s_st.inc(sy.dma_start(out=lam_s[:, :], in_=LAMC[0:16, :]), 16)
    k.s_st.wait(sy)
    nl_in = _dc.replace(lam_s, offset=0, ap=[[0, 128], [1, 16]])
    k.s_ld.inc(sy.dma_start(out=NLAM, in_=nl_in), 16)

    SQB = k.qA[:].bitcast(BF16)[:, 0:T]
    BTS = k.qB[:, 0:T]
    for kt2 in range(2):
        k.s_v.wait(se)
        k.s_g.wait(se)
        k.s_s.inc(se.activation(SQB, K2[:, kt2, :], AF.Square))
        k.s_s.wait(te)
        for cc in range(4):
            k.s_mm.inc(te.matmul(k.psB[:, cc, :], P4t[:], SQB[:, 512 * cc:512 * cc + 512],
                                 start=True, stop=True))
        k.s_mm.wait(ve)
        for cc in range(4):
            ve.tensor_reduce(KMG[:, kt2:kt2 + 1] if cc == 0 else BTS[:, cc:cc + 1],
                             k.psB[:, cc, :], axis=AX.X, op=ALU.max)
        ve.drain()
        ve.tensor_tensor(KMG[:, kt2:kt2 + 1], KMG[:, kt2:kt2 + 1], BTS[:, 1:2], op=ALU.max)
        ve.drain()
        ve.tensor_tensor(BTS[:, 2:3], BTS[:, 2:3], BTS[:, 3:4], op=ALU.max)
        ve.drain()
        ve.tensor_tensor(KMG[:, kt2:kt2 + 1], KMG[:, kt2:kt2 + 1], BTS[:, 2:3], op=ALU.max)
        k.s_v.inc(ve.drain())
    k.s_v.wait(se)
    k.s_s.inc(se.activation(KMG, KMG, AF.Sqrt))
    k.s_s.wait(sy)
    k.s_st.inc(sy.dma_start(out=km_s[:, :], in_=KMG), 16)
    k.s_st.wait(sy)
    for kv in range(4):
        for g4 in range(4):
            off = 128 * (kv % 2) + kv // 2 + 64 * (g4 % 2)
            src = _dc.replace(km_s, offset=off, ap=[[0, 32], [1, 1]])
            k.s_ld.inc(sy.dma_start(out=KMC[32 * g4:32 * g4 + 32, kv:kv + 1], in_=src), 16)

    k.s_lg.inc(ge.dma_start(out=WROW, in_=h["rmsw"][:, :]), 16)

    ones1b = k.ones1b
    M01 = k.M01
    k.s_ld.wait(ve)
    k.s_lg.wait(ve)
    k.s_ld.wait(te)
    for qt in range(8):
        kvv = qt // 2
        k.s_v.wait(se)
        k.s_g.wait(se)
        k.s_s.inc(se.activation(SQB, Q[:, qt, :], AF.Square))
        k.s_s.wait(te)
        k.s_v.wait(te)
        for cc in range(4):
            k.s_mm.inc(te.matmul(k.psB[:, cc, :], P4t[:], SQB[:, 512 * cc:512 * cc + 512],
                                 start=True, stop=True))
        k.s_mm.wait(se)
        for cc in range(4):
            se.activation(BTS[:, 512 * cc:512 * cc + 512], k.psB[:, cc, :], AF.Sqrt)
        k.s_s.inc(se.drain())
        k.s_s.wait(ve)
        k.s_v.inc(ve.tensor_scalar(BT, BTS, KMC[:, kvv:kvv + 1], -1.0,
                                   op0=ALU.mult, op1=ALU.mult))
        if DEV_TAP == "bt" and qt == 0:
            dump_fm(k, BT.unsqueeze(1), k.out_h, 1)
            return
        k.s_v.wait(te)

        for c in range(4):
            nkt = 4 * c + 4
            k.s_v.wait(te)
            for kt in range(nkt):
                masked = kt >= 4 * c
                off = 128 * (kt - 4 * c) if masked else 0
                w = 512 - off
                te.wait_ge(k.s_s.h, k.s_s.n)
                for jj in range(4):
                    te.matmul(k.psA[:, jj, off:512], ones1b[32 * jj:32 * jj + 1, 0:128],
                              BT[32 * jj:32 * jj + 1, 512 * c + off:512 * c + 512],
                              start=True, stop=False, tile_position=(32 * jj, 0))
                    mm = te.matmul(k.psA[:, jj, off:512],
                                   KARR[32 * jj:32 * jj + 32, kvv, 128 * kt:128 * kt + 128],
                                   Q[32 * jj:32 * jj + 32, qt, 512 * c + off:512 * c + 512],
                                   start=False, stop=True, tile_position=(32 * jj, 0))
                    k.s_mm.inc(mm)
                k.s_mm.wait(se)
                k.s_v.wait(se)
                for jj in range(4):
                    se.activation(UT[:, jj, off:512], k.psA[:, jj, off:512], AF.Exp)
                k.s_s.inc(se.drain())
                if masked:
                    k.s_s.wait(ve)
                    uview = UT[:, :, off:off + 128]
                    m01b = M01[:].unsqueeze(1).broadcast_to((128, 4, 128))
                    k.s_v.inc(ve.tensor_tensor(uview, uview, m01b, op=ALU.mult))
                if DEV_TAP == "ut" and qt == 0 and c == 3 and kt == 0:
                    full_barrier(k)
                    dump_fm(k, UT.rearrange("p a b -> p (a b)").unsqueeze(1), k.out_h, 1)
                    return
                k.s_s.wait(te)
                if masked:
                    k.s_v.wait(te)
                for jj in range(4):
                    k.s_mm.inc(te.matmul(k.psB[0:65, jj, off:512], VT[:, kt, kvv, 0:65],
                                         UT[:, jj, off:512], start=(kt == 0), stop=(kt == nkt - 1)))
            k.s_mm.wait(ve)
            for jj in range(4):
                ve.tensor_copy(OE[0:65, jj, :], k.psB[0:65, jj, :])
            k.s_v.inc(ve.drain())
            k.s_v.wait(te)
            i = 0
            for cc in range(4):
                for jj in range(4):
                    if i >= 4:
                        te.wait_ge(k.s_v.h, k.s_v.n - 3)
                    pv = k.psA[:, i % 4, :].bitcast(BF16)[:, 0:65]
                    k.s_mm.inc(te.transpose(pv, OE[0:65, jj, 128 * cc:128 * cc + 128],
                                            k.idb[0:65, 0:65]))
                    k.s_mm.wait(ve)
                    k.s_v.inc(ve.tensor_copy(OTT[:, cc, jj, 0:65], pv))
                    i += 1
            if DEV_TAP == "ott" and qt == 0 and c == 0:
                full_barrier(k)
                dump_fm(k, OTT.rearrange("p a b c -> p (a b c)").unsqueeze(1), k.out_h, 1)
                return
            ve.drain()
            ve.tensor_copy(RC[:, 0:16].rearrange("p (b a) -> p b a", b=4),
                           OTT[:, :, :, 64:65].rearrange("p a b 1 -> p a b").transpose([0, 2, 1]))
            ve.drain()
            ve.reciprocal(RC[:, 0:16], RC[:, 0:16])
            ve.drain()
            for hh2 in range(2):
                hglob = 2 * qt + hh2
                j1, j2 = 2 * hh2, 2 * hh2 + 1
                r1 = RC[:, 4 * j1:4 * j1 + 4].unsqueeze(2).broadcast_to((128, 4, 64))
                ve.tensor_tensor(YC[:, :, hh2, :], OTT[:, :, j1, 0:64], r1, op=ALU.mult)
                ve.tensor_scalar(RC[:, 16 + 4 * hh2:20 + 4 * hh2], RC[:, 4 * j2:4 * j2 + 4],
                                 NLAM[:, hglob:hglob + 1], None, op0=ALU.mult)
            ve.drain()
            for hh2 in range(2):
                j2 = 2 * hh2 + 1
                c2b = RC[:, 16 + 4 * hh2:20 + 4 * hh2].unsqueeze(2).broadcast_to((128, 4, 64))
                ve.tensor_tensor(OTT[:, :, j2, 0:64], OTT[:, :, j2, 0:64], c2b, op=ALU.mult)
            ve.drain()
            ve.tensor_tensor(YC[:], YC[:], OTT[:, :, 1::2, 0:64], op=ALU.add)
            k.s_v.inc(ve.drain())
            if DEV_TAP == "yc" and qt == 0 and c == 0:
                full_barrier(k)
                dump_fm(k, YC.rearrange("p a b c -> p (a b c)").unsqueeze(1), k.out_h, 1)
                return
            if DEV_TAP == "rc" and qt == 0 and c == 0:
                full_barrier(k)
                dump_fm(k, RC.bitcast(BF16)[:, 0:48].unsqueeze(1), k.out_h, 1)
                return
            k.s_v.wait(se)
            se.activation(BTS[:, 0:512], YC[:].rearrange("p a b d -> p (a b d)"), AF.Square)
            k.s_s.inc(se.drain())
            k.s_s.wait(ve)
            ve.tensor_reduce(RC[:, 0:8], BTS[:, 0:512].rearrange("p (a d) -> p a d", d=64),
                             axis=AX.X, op=ALU.add)
            ve.drain()
            ve.tensor_scalar(RC[:, 0:8], RC[:, 0:8], 1.0 / 64.0, 1e-6, op0=ALU.mult, op1=ALU.add)
            ve.drain()
            ve.reciprocal(RC[:, 0:8], RC[:, 0:8])
            k.s_v.inc(ve.drain())
            k.s_v.wait(se)
            se.activation(RC[:, 0:8], RC[:, 0:8], AF.Sqrt)
            k.s_s.inc(se.drain())
            k.s_s.wait(ve)
            rib = RC[:, 0:8].rearrange("p (a b) -> p a b", b=2).unsqueeze(3) \
                .broadcast_to((128, 4, 2, 64))
            ve.tensor_tensor(YC[:], YC[:], rib, op=ALU.mult)
            ve.drain()
            wrb = WROW.unsqueeze(1).unsqueeze(1).broadcast_to((128, 4, 2, 64))
            ve.tensor_tensor(YC[:], YC[:], wrb, op=ALU.mult)
            k.s_v.inc(ve.drain())
            if DEV_TAP == "yn" and qt == 0 and c == 0:
                full_barrier(k)
                dump_fm(k, YC.rearrange("p a b c -> p (a b c)").unsqueeze(1), k.out_h, 1)
                return
            k.s_v.wait(te)
            for cc in range(4):
                pv = k.psA[:, cc, :].bitcast(BF16)[:, 0:128]
                k.s_mm.inc(te.transpose(pv, YC[:, cc, :, :].rearrange("p a b -> p (a b)"),
                                        k.idb[:]))
            k.s_mm.wait(ve)
            for cc in range(4):
                ve.tensor_copy(Q[:, qt, 512 * c + 128 * cc:512 * c + 128 * cc + 128],
                               k.psA[:, cc, :].bitcast(BF16)[:, 0:128])
            k.s_v.inc(ve.drain())
            if DEV_TAP in ("y5", "y6", "y7", "y8", "y9", "yz") and qt == {"y5":2,"y6":3,"y7":4,"y8":5,"y9":6,"yz":7}[DEV_TAP] and c == 3:
                dump_fm(k, Q, k.out_h, 8)
                return
            if DEV_TAP == "y3" and qt == 0 and c == 3:
                dump_fm(k, Q, k.out_h, 8)
                return
            if DEV_TAP == "y4" and qt == 1 and c == 3:
                dump_fm(k, Q, k.out_h, 8)
                return
            if DEV_TAP == "y2" and qt == 0 and c == 0:
                dump_fm(k, Q, k.out_h, 8)
                return
            if DEV_TAP == "y1" and qt == 0 and c == 0:
                full_barrier(k)
                dump_fm(k, Q[:, 0:1, 0:512].rearrange("p a b -> p (a b)").unsqueeze(1), k.out_h, 1)
                return

    full_barrier(k)
    ar.release()
    ar.release()
    ar.mark()
    X1T = ar.f32(2, D)
    WOB = ar.bf(8, D)
    k.s_lg.inc(ge.dma_start(out=WOB, in_=h["woT"].rearrange("(a p) o -> p a o", p=128)), 16)
    k.s_ld.wait(te)
    k.s_lg.wait(te)
    x1r = h["x1_s"].rearrange("(j p) f -> p j f", p=128)
    outr2 = out_h.rearrange("(j p) f -> p j f", p=128)
    for j in range(NT):
        k.s_st.wait(ve)
        for oc in range(2):
            pb = k.psB[:, 2 * (j % 2) + oc, :]
            te.wait_ge(k.s_v.h, max(0, k.s_v.n - 3))
            for ft in range(8):
                mm = te.matmul(pb, Q[:, ft, 128 * j:128 * j + 128],
                               WOB[:, ft, 512 * oc:512 * oc + 512],
                               start=(ft == 0), stop=(ft == 7))
            k.s_mm.inc(mm)
            k.s_mm.wait(ve)
            k.s_v.inc(ve.tensor_copy(X1T[:, j % 2, 512 * oc:512 * oc + 512], pb))
        k.s_v.wait(sy)
        k.s_st.inc(sy.dma_start(out=x1r[:, j, :], in_=X1T[:, j % 2, :]), 16)
        if DEV_STAGE == "l0attn":
            k.s_st.inc(sy.dma_start(out=outr2[:, j, :], in_=X1T[:, j % 2, :]), 16)
    if DEV_STAGE == "l0attn":
        k.s_st.wait(sy)
        return
    full_barrier(k)
    ar.release()
    ar.release()
    emit_l0_ffn(k, h, out_h)


def emit_l0_wquant(k, h, WLD, WQB, STG):
    te, ve, se, ge, sy = k.te, k.ve, k.se, k.ge, k.sy
    specs = [(h["wg0"], h["wqgT_s"], 22, 1024, 2),
             (h["wu0"], h["wquT_s"], 22, 1024, 2),
             (h["wd0"], h["wdqT_s"], 8, 2816, 1)]
    for src, dst, nrt, W, PR in specs:
        srcr = src.rearrange("(r p) w -> p r w", p=128)
        dstr = dst.rearrange("(bb p) o -> p bb o", p=128)
        nb = W // 128
        for r0 in range(0, nrt, PR):
            WT = PR * W
            k.s_v.wait(sy)
            k.s_g.wait(sy)
            k.s_la.inc(sy.dma_start(out=WLD[:, 0:WT].rearrange("p (r w) -> p r w", r=PR),
                                    in_=srcr[:, r0:r0 + PR, :]), 16)
            k.s_la.wait(ve)
            k.s_la.wait(ge)
            emit_quant(k, WLD[:, 0:WT], WQB[:, 0:WT])
            k.s_g.wait(te)
            k.s_v.wait(te)
            k.s_st.wait(ve)
            pairs = [(WQB[:, 128 * b:128 * b + 128], STG[:, 128 * b:128 * b + 128])
                     for b in range(PR * nb)]
            emit_transposes(k, pairs, k.idb[:], deep=True)
            k.s_mm.wait(ve)
            k.s_mm.wait(ge)
            k.s_v.wait(sy)
            for rr in range(PR):
                k.s_st.inc(sy.dma_start(
                    out=dstr[:, 0:nb, 128 * (r0 + rr):128 * (r0 + rr) + 128],
                    in_=STG[:, rr * W:(rr + 1) * W].rearrange("p (bb c) -> p bb c", c=128)), 16)


def emit_l0_ffn(k, h, out_h):
    te, ve, se, ge, sy = k.te, k.ve, k.se, k.ge, k.sy
    ar = k.ar
    XQF = k.arXQF
    ar.mark()
    WGS = ar.bf(8, 512)
    WUS = ar.bf(8, 512)
    WDS = ar.bf(22, 1024)
    HQ = [ar.bf(22, 128) for _ in range(4)]
    X1T = ar.f32(2, D)
    wds_src = h["wdqT_s"].rearrange("(c p) o -> p c o", p=128)
    k.s_ld.inc(sy.dma_start(out=WDS, in_=wds_src), 16)
    wg_src = h["wqgT_s"].rearrange("(c p) o -> p c o", p=128)
    wu_src = h["wquT_s"].rearrange("(c p) o -> p c o", p=128)
    xr = h["x_in"].rearrange("(j p) f -> p j f", p=128)
    x1r = h["x1_s"].rearrange("(j p) f -> p j f", p=128)
    outr = out_h.rearrange("(j p) f -> p j f", p=128)
    HGF = ar.f32(2048)
    HQB4 = ar.bf(2048)

    for jg in range(4):
        k.s_mm.wait(ve)
        k.s_mm.wait(se)
        for oc in range(6):
            ocw = 512 if oc < 5 else 256
            k.s_mm.wait(sy)
            k.s_la.inc(sy.dma_start(out=WGS[:, :, 0:ocw], in_=wg_src[:, :, 512 * oc:512 * oc + ocw]), 16)
            k.s_lb.inc(sy.dma_start(out=WUS[:, :, 0:ocw], in_=wu_src[:, :, 512 * oc:512 * oc + ocw]), 16)
            k.s_la.wait(te)
            k.s_lb.wait(te)
            for j4 in range(4):
                j = 4 * jg + j4
                te.wait_ge(k.s_s.h, k.s_s.n)
                te.wait_ge(k.s_v.h, k.s_v.n)
                for ic in range(8):
                    te.matmul(k.psA[:, 0, 0:ocw], XQF[:, ic, 128 * j:128 * j + 128],
                              WGS[:, ic, 0:ocw], start=(ic == 0), stop=(ic == 7))
                for ic in range(8):
                    mm = te.matmul(k.psA[:, 1, 0:ocw], XQF[:, ic, 128 * j:128 * j + 128],
                                   WUS[:, ic, 0:ocw], start=(ic == 0), stop=(ic == 7))
                k.s_mm.inc(mm)
                k.s_mm.wait(se)
                k.s_s.inc(se.activation(HGF[:, ocw * j4:ocw * j4 + ocw],
                                        k.psA[:, 0, 0:ocw], AF.Silu))
                k.s_s.wait(ve)
                k.s_v.inc(ve.tensor_tensor(HGF[:, ocw * j4:ocw * j4 + ocw],
                                           HGF[:, ocw * j4:ocw * j4 + ocw],
                                           k.psA[:, 1, 0:ocw], op=ALU.mult))
            ve.drain()
            emit_quant(k, HGF[:, 0:4 * ocw], HQB4[:, 0:4 * ocw])
            k.s_g.wait(te)
            k.s_v.wait(te)
            pairs = [(HQB4[:, ocw * j4 + 128 * b:ocw * j4 + 128 * b + 128],
                      HQ[j4][:, 4 * oc + b, :])
                     for j4 in range(4) for b in range(ocw // 128)]
            emit_transposes(k, pairs, k.idb[:], deep=True)
            k.s_mm.wait(ve)
            k.s_mm.wait(ge)
            k.s_mm.wait(se)
        k.s_ld.wait(te)
        for j4 in range(4):
            j = 4 * jg + j4
            k.s_st.wait(sy)
            k.s_lc.inc(sy.dma_start(out=X1T[:, j4 % 2, :], in_=x1r[:, j, :]), 16)
            for dc in range(2):
                te.wait_ge(k.s_v.h, k.s_v.n)
                for c in range(22):
                    mm = te.matmul(k.psB[:, dc, :], HQ[j4][:, c, :],
                                   WDS[:, c, 512 * dc:512 * dc + 512],
                                   start=(c == 0), stop=(c == 21))
                k.s_mm.inc(mm)
            k.s_mm.wait(ve)
            k.s_lc.wait(ve)
            for dc in range(2):
                ve.tensor_tensor(X1T[:, j4 % 2, 512 * dc:512 * dc + 512],
                                 X1T[:, j4 % 2, 512 * dc:512 * dc + 512],
                                 k.psB[:, dc, :], op=ALU.add)
            k.s_v.inc(ve.drain())
            k.s_v.wait(ge)
            k.s_lg.inc(ge.dma_start(out=X1T[:, j4 % 2, :], in_=xr[:, j, :], accum_op=ALU.add), 16)
            k.s_lg.wait(sy)
            k.s_st.inc(sy.dma_start(out=x1r[:, j, :], in_=X1T[:, j4 % 2, :]), 16)
            if DEV_STAGE == "l0":
                k.s_st.inc(sy.dma_start(out=outr[:, j, :], in_=X1T[:, j4 % 2, :]), 16)
    full_barrier(k)
    ar.release()
    if DEV_STAGE == "l0":
        k.s_st.wait(sy)
        return
    ar.release()
    emit_l1(k, h, out_h)


def emit_l1(k, h, out_h):
    te, ve, se, ge, sy = k.te, k.ve, k.se, k.ge, k.sy
    ar = k.ar

    XINT = ar.bf(NT, D)
    ar.mark()
    XNF = ar.bf(8, T)
    XTa = ar.f32(2, D)

    emit_ln_stats(k, h["x1_s"], XTa, k.qA[:, 0:2 * D])
    xr = h["x1_s"].rearrange("(j p) f -> p j f", p=128)
    ssum = k.stats[:, 0:NT]
    inv = k.stats[:, 2 * NT:3 * NT]
    sems = (k.s_la, k.s_lb)
    for j in range(min(2, NT)):
        sems[j % 2].inc(sy.dma_start(out=XTa[:, j % 2, :], in_=xr[:, j, :]), 16)
    for j in range(NT):
        xt = XTa[:, j % 2, :]
        sems[j % 2].wait(ve)
        k.s_s.wait(ve)
        ve.tensor_scalar(XINT[:, j, :], xt, ssum[:, j:j + 1], inv[:, j:j + 1],
                         op0=ALU.subtract, op1=ALU.mult)
        k.s_v.inc(ve.drain())
        if j + 2 < NT:
            k.s_v.wait(sy)
            sems[j % 2].inc(sy.dma_start(out=XTa[:, j % 2, :], in_=xr[:, j + 2, :]), 16)
        k.s_v.wait(te)
        pairs = [(XINT[:, j, 128 * fb:128 * fb + 128], XNF[:, fb, 128 * j:128 * j + 128])
                 for fb in range(8)]
        emit_transposes(k, pairs, k.idb[:], deep=True)
        k.s_mm.wait(ve)

    if DEV_STAGE == "l1ln":
        dump_xint(k, out_h, XINT)
        return

    TH = ar.bf(2, T)
    PHI = ar.bf(NT, 256)
    GT = ar.bf(NT, D)
    MB = ar.bf(2, D)
    WTH = ar.bf(8, 256)
    WPH = ar.bf(8, 256)
    WG = ar.bf(8, D)
    k.s_lg.inc(ge.dma_start(out=WTH, in_=h["wthT"].rearrange("(a p) o -> p a o", p=128)), 16)
    k.s_lg.inc(ge.dma_start(out=WPH, in_=h["wphT"].rearrange("(a p) o -> p a o", p=128)), 16)
    k.s_lg.inc(ge.dma_start(out=WG, in_=h["wgT"].rearrange("(a p) o -> p a o", p=128)), 16)
    full_barrier(k)

    for oc in range(2):
        for tc in range(4):
            pb = k.psA[:, tc % 4, :]
            te.wait_ge(k.s_s.h, max(0, k.s_s.n - 3))
            for ic in range(8):
                mm = te.matmul(pb, WTH[:, ic, 128 * oc:128 * oc + 128],
                               XNF[:, ic, 512 * tc:512 * tc + 512],
                               start=(ic == 0), stop=(ic == 7))
            k.s_mm.inc(mm)
            k.s_mm.wait(se)
            k.s_s.inc(se.activation(TH[:, oc, 512 * tc:512 * tc + 512], pb, AF.Copy, bias=0.0))
    def tproj(W, dst, ncol):
        for j in range(NT):
            noc = (ncol + 511) // 512
            for oc in range(noc):
                ow = min(512, ncol - 512 * oc)
                pb = k.psB[:, (j * noc + oc) % 4, :]
                te.wait_ge(k.s_s.h, max(0, k.s_s.n - 3))
                for ic in range(8):
                    mm = te.matmul(pb[:, 0:ow], XNF[:, ic, 128 * j:128 * j + 128],
                                   W[:, ic, 512 * oc:512 * oc + ow],
                                   start=(ic == 0), stop=(ic == 7))
                k.s_mm.inc(mm)
                k.s_mm.wait(se)
                k.s_s.inc(se.activation(dst[:, j, 512 * oc:512 * oc + ow], pb[:, 0:ow],
                                        AF.Copy, bias=0.0))
    tproj(WPH, PHI, 256)
    tproj(WG, GT, 1024)
    full_barrier(k)

    for lc in range(2):
        for ocg in range(2):
            pb = k.psA[:, 2 * lc + ocg, :]
            for j in range(NT):
                mm = te.matmul(pb, PHI[:, j, 128 * lc:128 * lc + 128],
                               GT[:, j, 512 * ocg:512 * ocg + 512],
                               start=(j == 0), stop=(j == NT - 1))
            k.s_mm.inc(mm)
            k.s_mm.wait(ve)
            k.s_v.inc(ve.tensor_scalar(MB[:, lc, 512 * ocg:512 * ocg + 512], pb,
                                       0.03125, None, op0=ALU.mult))
    k.s_v.wait(te)
    X1T = XTa
    x1r2 = h["x1_s"].rearrange("(j p) f -> p j f", p=128)
    for j in range(NT):
        k.s_st.wait(sy)
        k.s_lc.inc(sy.dma_start(out=X1T[:, j % 2, :], in_=x1r2[:, j, :]), 16)
        te.wait_ge(k.s_v.h, max(0, k.s_v.n - 1))
        for oc in range(2):
            pb = k.psB[:, 2 * (j % 2) + oc, :]
            for lc in range(2):
                mm = te.matmul(pb, TH[:, lc, 128 * j:128 * j + 128],
                               MB[:, lc, 512 * oc:512 * oc + 512],
                               start=(lc == 0), stop=(lc == 1))
            k.s_mm.inc(mm)
        k.s_mm.wait(ve)
        k.s_lc.wait(ve)
        for oc in range(2):
            ve.tensor_tensor(X1T[:, j % 2, 512 * oc:512 * oc + 512],
                             X1T[:, j % 2, 512 * oc:512 * oc + 512],
                             k.psB[:, 2 * (j % 2) + oc, :], op=ALU.add)
        k.s_v.inc(ve.drain())
        k.s_v.wait(sy)
        k.s_st.inc(sy.dma_start(out=x1r2[:, j, :], in_=X1T[:, j % 2, :]), 16)
    full_barrier(k)
    ar.release()

    if DEV_STAGE == "l1attn":
        ar.mark()
        XTb = ar.f32(2, D)
        outr = out_h.rearrange("(j p) f -> p j f", p=128)
        for j in range(NT):
            k.s_la.inc(sy.dma_start(out=XTb[:, j % 2, :], in_=x1r2[:, j, :]), 16)
            k.s_la.wait(sy)
            k.s_st.inc(sy.dma_start(out=outr[:, j, :], in_=XTb[:, j % 2, :]), 16)
            k.s_st.wait(sy)
        ar.release()
        return

    emit_l1_ffn(k, h, out_h, XINT)


def emit_l1_ffn(k, h, out_h, XINT):
    nc = k.nc
    te, ve, se, ge, sy = k.te, k.ve, k.se, k.ge, k.sy
    ar = k.ar

    CQ = ar.bf(T)
    SQ = ar.bf(T)
    CK = ar.bf(T)
    SK = ar.bf(T)

    P64R = k.sb("P64R", [128, 128], BF16)
    M01L = k.sb("M01L", [128, 128], BF16)
    PHB = k.sb("PHB", [128, 2], BF16)
    E2B = k.sb("E2B", [2, 128], BF16)
    E2Q = k.sb("E2Q", [2, 128], BF16)
    E2K = k.sb("E2K", [2, 128], BF16)
    QSR = k.sb("QSR", [2, 256], BF16)
    LNGC = k.sb("LNGC", [128, 8], F32)
    LNBC = k.sb("LNBC", [128, 8], F32)
    B1AC = k.sb("B1AC", [128, 22], F32)
    B1GC = k.sb("B1GC", [128, 22], F32)
    st4 = nc.dram_tensor("st4", [4, 2049], F32, kind="Internal").ap()

    full_barrier(k)

    PM32 = k.smallc[:, 8:9]
    LNBc = k.smallc[:, 9:10]
    INVF = k.smallc[:, 10:11]
    ZCOL = k.smallc[:, 15:16]
    ve.memset(ZCOL, 0.0)
    ve.tensor_scalar(PM32, k.pcol[:], -15.5, 0.03125, op0=ALU.add, op1=ALU.mult)
    ve.drain()
    ve.tensor_scalar(PM32, PM32, k.mcol[:, 0:1], k.mcol[:, 0:1], op0=ALU.add, op1=ALU.subtract)
    ve.drain()
    ve.scalar_tensor_tensor(PM32, PM32, -32.0, k.pcol[:], op0=ALU.mult, op1=ALU.add)
    ve.drain()
    ve.tensor_scalar(INVF, PM32, -0.03125, None, op0=ALU.mult)
    ve.drain()
    ve.tensor_scalar(INVF, INVF, k.ln4, None, op0=ALU.mult)
    ve.tensor_scalar(LNBc, PM32, 2.0, 25.6, op0=ALU.mult, op1=ALU.add)
    ve.drain()
    ve.tensor_scalar(LNBc, LNBc, 1.0 / 89.6, None, op0=ALU.mult)
    k.s_v.inc(ve.drain())
    k.s_v.wait(se)
    se.activation(INVF, INVF, AF.Exp)
    se.activation(LNBc, LNBc, AF.Ln)
    k.s_s.inc(se.drain())

    DIF = k.qA[:, 0:128]
    CM64 = k.qA[:, 128:256]
    S1 = k.qA[:, 256:384]
    S2 = k.qA[:, 384:512]
    k.s_s.wait(ge)
    ge.iota(DIF, pattern=[[1, 128]], base=0, channel_multiplier=-1,
            allow_small_or_imprecise_dtypes=True)
    ge.iota(CM64, pattern=[[0, 2], [1, 64]], base=0, channel_multiplier=0,
            allow_small_or_imprecise_dtypes=True)
    k.s_g.inc(ge.drain())
    k.s_g.wait(ve)
    k.s_s.wait(ve)
    ve.tensor_scalar(P64R[:], DIF, 32.0, None, op0=ALU.is_equal)
    ve.tensor_scalar(S1, DIF, -32.0, None, op0=ALU.is_equal)
    ve.tensor_scalar(S2, CM64, 32.0, None, op0=ALU.is_ge)
    ve.tensor_scalar(M01L[:], DIF, 0.0, None, op0=ALU.is_le)
    ve.drain()
    ve.tensor_tensor(P64R[:], P64R[:], S2, op=ALU.mult)
    ve.drain()
    ve.tensor_scalar(S2, S2, -1.0, 1.0, op0=ALU.mult, op1=ALU.add)
    ve.drain()
    ve.tensor_tensor(S1, S1, S2, op=ALU.mult)
    ve.drain()
    ve.tensor_tensor(P64R[:], P64R[:], S1, op=ALU.subtract)
    ve.memset(PHB[:], 0.0)
    ve.drain()
    ve.tensor_scalar(PHB[:, 0:1], k.pcol[:], 64.0, None, op0=ALU.is_lt)
    ve.tensor_scalar(PHB[:, 1:2], k.pcol[:], 64.0, None, op0=ALU.is_ge)
    k.s_v.inc(ve.drain())
    k.s_v.wait(te)
    pvE = k.psA[:, 0, :].bitcast(BF16)[0:2, 0:128]
    k.s_mm.inc(te.transpose(pvE, PHB[:], k.idb[:]))
    k.s_mm.wait(ve)
    k.s_v.inc(ve.tensor_copy(E2B[:], pvE))

    TG = k.qA[:, 0:T]
    ARG = k.qB[:, 0:T]
    RED = k.qC[:, 0:T]
    k.s_v.wait(ge)
    k.s_g.inc(ge.iota(TG, pattern=[[1, T]], base=0, channel_multiplier=0,
                      allow_small_or_imprecise_dtypes=True))
    k.s_g.wait(ve)

    def trig2(dst, use_hpi, pw_mult, finalmult):
        k.s_s.wait(ve)
        if use_hpi:
            ve.tensor_scalar(ARG, TG, INVF, k.hpi, op0=ALU.mult, op1=ALU.add)
        else:
            ve.tensor_scalar(ARG, TG, INVF, None, op0=ALU.mult)
        ve.drain()
        ve.tensor_scalar(RED, ARG, k.i2pi, None, op0=ALU.mult)
        ve.drain()
        ve.tensor_scalar(RED, RED, k.mcol[:, 0:1], k.mcol[:, 0:1],
                         op0=ALU.add, op1=ALU.subtract)
        ve.drain()
        ve.scalar_tensor_tensor(RED, RED, k.twopi, ARG, op0=ALU.mult, op1=ALU.subtract)
        ve.drain()
        ve.tensor_scalar(RED, RED, k.pin, k.pip, op0=ALU.max, op1=ALU.min)
        k.s_v.inc(ve.drain())
        k.s_v.wait(se)
        k.s_s.inc(se.activation(RED, RED, AF.Sin))
        k.s_s.wait(ve)
        ve.tensor_scalar(ARG, TG, -1024.0, pw_mult, op0=ALU.add, op1=ALU.mult)
        ve.drain()
        ve.tensor_scalar(ARG, ARG, LNBc, None, op0=ALU.mult)
        k.s_v.inc(ve.drain())
        k.s_v.wait(se)
        k.s_s.inc(se.activation(ARG, ARG, AF.Exp))
        k.s_s.wait(ve)
        ve.tensor_tensor(RED, RED, ARG, op=ALU.mult)
        ve.drain()
        ve.tensor_scalar(dst, RED, finalmult, None, op0=ALU.mult)
        k.s_v.inc(ve.drain())

    trig2(CQ, True, 0.00390625, 8.0)
    trig2(SQ, False, 0.00390625, 8.0)
    trig2(CK, True, -0.00390625, -1.0)
    trig2(SK, False, -0.00390625, -1.0)

    for l in range(2):
        emit_l1f_attn(k, h, out_h, XINT, l, CQ, SQ, CK, SK,
                      P64R, M01L, PHB, E2B, E2Q, E2K, QSR)
        if DEV_STAGE == f"l1f{l}a":
            dump_xint(k, out_h, XINT)
            return
        emit_l1f_mlp(k, h, out_h, XINT, l, LNGC, LNBC, B1AC, B1GC, st4)
        if DEV_STAGE == f"l1f{l}":
            dump_xint(k, out_h, XINT)
            return

    full_barrier(k)
    ar.mark()
    XO = ar.f32(2, D)
    x1r = h["x1_s"].rearrange("(j p) f -> p j f", p=128)
    outr = out_h.rearrange("(j p) f -> p j f", p=128)
    for j in range(NT):
        k.s_st.wait(sy)
        k.s_lc.inc(sy.dma_start(out=XO[:, j % 2, :], in_=x1r[:, j, :]), 16)
        k.s_lc.wait(ve)
        ve.tensor_tensor(XO[:, j % 2, :], XO[:, j % 2, :], XINT[:, j, :], op=ALU.add)
        k.s_v.inc(ve.drain())
        k.s_v.wait(sy)
        k.s_st.inc(sy.dma_start(out=outr[:, j, :], in_=XO[:, j % 2, :]), 16)
    k.s_st.wait(sy)
    k.s_st.wait(te)
    ar.release()


def dump_xint(k, out_h, XINT):
    full_barrier(k)
    outr = out_h.rearrange("(j p) f -> p j f", p=128)
    for j in range(NT):
        k.s_dmp.inc(k.ge.dma_start(out=outr[:, j, :], in_=XINT[:, j, :]), 16)
    k.s_dmp.wait(k.ge)
    k.s_dmp.wait(k.sy)


def emit_l1f_attn(k, h, out_h, XINT, l, CQ, SQ, CK, SK,
                  P64R, M01L, PHB, E2B, E2Q, E2K, QSR):
    te, ve, se, ge, sy = k.te, k.ve, k.se, k.ge, k.sy
    ar = k.ar
    M01 = k.M01

    full_barrier(k)
    ar.mark()
    QKVB = ar.bf(12, T)

    k.s_lg.inc(ge.dma_start(out=QSR[:, 0:128],
                            in_=_dc.replace(h["qs"][l], ap=[[0, 2], [1, 128]])), 16)
    k.s_lh.inc(ge.dma_start(out=QSR[:, 128:256],
                            in_=_dc.replace(h["ks"][l], ap=[[0, 2], [1, 128]])), 16)
    k.s_lg.wait(ve)
    k.s_lh.wait(ve)
    ve.tensor_tensor(E2Q[:], E2B[:], QSR[:, 0:128], op=ALU.mult)
    ve.tensor_tensor(E2K[:], E2B[:], QSR[:, 128:256], op=ALU.mult)
    k.s_v.inc(ve.drain())
    ge.wait_ge(k.s_v.h, k.s_v.n)

    ar.mark()
    XFM = ar.bf(8, 2056)
    WSL = ar.bf(2, 8, 128)
    ve.memset(XFM[:, :, 0:1], 0.0)
    k.s_v.inc(ve.drain())
    k.s_v.wait(te)
    for j in range(NT):
        pairs = [(XINT[:, j, 128 * fb:128 * fb + 128], XFM[:, fb, 1 + 128 * j:1 + 128 * j + 128])
                 for fb in range(8)]
        emit_transposes(k, pairs, k.idb[:], deep=True)
    k.s_mm.wait(ve)

    k.s_v.wait(te)
    wsrc = h["wqkv"][l].rearrange("(a p) o -> p a o", p=128)
    lastuse = [0, 0]
    sems = (k.s_lg, k.s_lh)
    for oc in range(12):
        b = oc % 2
        ge.wait_ge(k.s_mm.h, lastuse[b])
        sems[b].inc(ge.dma_start(out=WSL[:, b, :, :], in_=wsrc[:, :, 128 * oc:128 * oc + 128]), 16)
        sems[b].wait(te)
        for tc in range(4):
            pb = k.psA[:, tc % 4, :]
            te.wait_ge(k.s_s.h, max(0, k.s_s.n - 3))
            for ic in range(8):
                soff = 1 if ic < 4 else 0
                mm = te.matmul(pb, WSL[:, b, ic, :],
                               XFM[:, ic, soff + 512 * tc: soff + 512 * tc + 512],
                               start=(ic == 0), stop=(ic == 7))
            k.s_mm.inc(mm)
            k.s_mm.wait(se)
            k.s_s.inc(se.activation(QKVB[:, oc, 512 * tc:512 * tc + 512], pb, AF.Copy, bias=0.0))
        lastuse[b] = k.s_mm.n
    full_barrier(k)
    ar.release()

    if DEV_TAP == f"f{l}qkv":
        dump_fm(k, QKVB, out_h, 12)
        return

    VT1 = ar.bf(4, NT, 2, 68)
    UTT = ar.bf(2, 8, 512)
    UT = ar.bf(2, 640)
    OE = ar.bf(2, 512)
    OTT = ar.bf(4, 2, 72)
    YC = ar.bf(4, 2, 64)
    RC = ar.f32(16)
    WO1b = ar.bf(4, D)

    ve.memset(VT1[:], 1.0)
    k.s_v.inc(ve.drain())
    k.s_v.wait(te)
    base_v = k.s_v.n
    i = 0
    for t4 in range(4):
        for h2 in range(2):
            fr = 64 * h2
            for j in range(NT):
                if i >= 4:
                    te.wait_ge(k.s_v.h, base_v + i - 3)
                pv = k.psA[:, i % 4, :].bitcast(BF16)[:, 0:64]
                idsl = k.idb[fr:fr + 64, fr:fr + 64]
                k.s_mm.inc(te.transpose(pv, QKVB[fr:fr + 64, 8 + t4, 128 * j:128 * j + 128], idsl))
                k.s_mm.wait(ve)
                k.s_v.inc(ve.tensor_copy(VT1[:, t4, j, h2, 0:64], pv))
                i += 1

    SQB = k.qAb[:, 0:T]
    RT = k.qAb[:, T:2 * T]
    NRM = k.qB[0:2, 0:T]
    INVB = k.qCb[0:2, 0:T]

    def normrope(oc, COS, SIN, E2X):
        k.s_v.wait(se)
        k.s_g.wait(se)
        k.s_s.inc(se.activation(SQB, QKVB[:, oc, :], AF.Square))
        k.s_s.wait(te)
        k.s_v.wait(te)
        for cc in range(4):
            k.s_mm.inc(te.matmul(k.psB[0:2, cc, :], PHB[:], SQB[:, 512 * cc:512 * cc + 512],
                                 start=True, stop=True))
        k.s_mm.wait(se)
        for cc in range(4):
            se.activation(NRM[:, 512 * cc:512 * cc + 512], k.psB[0:2, cc, :], AF.Sqrt)
        k.s_s.inc(se.drain())
        k.s_s.wait(ve)
        ve.tensor_scalar(NRM[:], NRM[:], 1e-12, None, op0=ALU.max)
        ve.drain()
        ve.reciprocal(NRM[:], NRM[:])
        ve.drain()
        ve.tensor_copy(INVB[:], NRM[:])
        k.s_v.inc(ve.drain())
        k.s_v.wait(te)
        for cc in range(4):
            k.s_mm.inc(te.matmul(k.psA[:, cc, :], E2X[:], INVB[:, 512 * cc:512 * cc + 512],
                                 start=True, stop=True))
        k.s_mm.wait(ve)
        for cc in range(4):
            ve.tensor_tensor(QKVB[:, oc, 512 * cc:512 * cc + 512],
                             QKVB[:, oc, 512 * cc:512 * cc + 512], k.psA[:, cc, :], op=ALU.mult)
        k.s_v.inc(ve.drain())
        k.s_v.wait(te)
        for cc in range(4):
            k.s_mm.inc(te.matmul(k.psB[:, cc, :], P64R[:], QKVB[:, oc, 512 * cc:512 * cc + 512],
                                 start=True, stop=True))
        ve.tensor_tensor(RT, QKVB[:, oc, :], COS, op=ALU.mult)
        k.s_mm.wait(ve)
        for cc in range(4):
            ve.tensor_tensor(k.psB[:, cc, :], k.psB[:, cc, :],
                             SIN[:, 512 * cc:512 * cc + 512], op=ALU.mult)
        ve.drain()
        for cc in range(4):
            ve.tensor_tensor(QKVB[:, oc, 512 * cc:512 * cc + 512],
                             RT[:, 512 * cc:512 * cc + 512], k.psB[:, cc, :], op=ALU.add)
        k.s_v.inc(ve.drain())

    for t4 in range(4):
        normrope(t4, CQ, SQ, E2Q)
    for t4 in range(4):
        normrope(4 + t4, CK, SK, E2K)
    full_barrier(k, "tvs")
    if DEV_TAP == f"f{l}q":
        dump_fm(k, QKVB[:, 0:4, :], out_h, 4)
        return
    if DEV_TAP == f"f{l}k":
        dump_fm(k, QKVB[:, 4:8, :], out_h, 4)
        return

    NM = k.stats[:, 44:46]
    for t4 in range(4):
        for w in range(4):
            for lq in range(4):
                qc = 4 * w + lq
                kts = list(range(max(0, qc - 4), qc + 1))
                nblk = len(kts)
                wcols = 128 * nblk
                te.wait_ge(k.s_s.h, k.s_s.n)
                te.wait_ge(k.s_v.h, k.s_v.n)
                for jj in range(2):
                    lhs = QKVB[64 * jj:64 * jj + 64, t4, 128 * qc:128 * qc + 128]
                    kbase = 128 * kts[0]
                    w0 = min(512, wcols)
                    mm = te.matmul(k.psA[:, 2 * jj, 0:w0], lhs,
                                   QKVB[64 * jj:64 * jj + 64, 4 + t4, kbase:kbase + w0],
                                   start=True, stop=True, tile_position=(64 * jj, 0))
                    if wcols > 512:
                        mm = te.matmul(k.psA[:, 2 * jj + 1, 0:wcols - 512], lhs,
                                       QKVB[64 * jj:64 * jj + 64, 4 + t4,
                                            kbase + 512:kbase + wcols],
                                       start=True, stop=True, tile_position=(64 * jj, 0))
                    k.s_mm.inc(mm)
                k.s_mm.wait(ve)
                ve.wait_ge(k.s_s.h, k.s_s.n)
                for jj in range(2):
                    flat = k.psA[:, 2 * jj:2 * jj + 2, :].rearrange("p a b -> p (a b)")
                    ve.tensor_reduce(NM[:, jj:jj + 1], flat[:, 0:wcols], axis=AX.X, op=ALU.min)
                k.s_v.inc(ve.drain())
                k.s_v.wait(se)
                for jj in range(2):
                    flat = k.psA[:, 2 * jj:2 * jj + 2, :].rearrange("p a b -> p (a b)")
                    se.activation(UT[:, jj, 0:wcols], flat[:, 0:wcols], AF.Exp,
                                  bias=NM[:, jj:jj + 1], scale=-1.0)
                k.s_s.inc(se.drain())
                k.s_s.wait(te)
                vb2 = k.s_v.n
                i2 = 0
                for jj in range(2):
                    for b, kt in enumerate(kts):
                        ktr = kt - 4 * w + 4
                        if i2 >= 2:
                            te.wait_ge(k.s_v.h, vb2 + i2 - 1)
                        pvb = k.psB[:, 2 + i2 % 2, :].bitcast(BF16)[:, 0:128]
                        k.s_mm.inc(te.transpose(pvb, UT[:, jj, 128 * b:128 * b + 128], k.idb[:]))
                        k.s_mm.wait(ve)
                        dst = UTT[:, jj, ktr, 128 * lq:128 * lq + 128]
                        if b == 0 and qc >= 4:
                            k.s_v.inc(ve.tensor_tensor(dst, pvb, M01L[:], op=ALU.mult))
                        elif b == nblk - 1:
                            k.s_v.inc(ve.tensor_tensor(dst, pvb, M01[:], op=ALU.mult))
                        else:
                            k.s_v.inc(ve.tensor_copy(dst, pvb))
                        i2 += 1
            te.wait_ge(k.s_v.h, k.s_v.n)
            ktrs = [3, 4, 0, 1, 2, 5, 6, 7] if w > 0 else [4, 5, 6, 7]
            for jj in range(2):
                for idx, ktr in enumerate(ktrs):
                    kt = ktr + 4 * w - 4
                    c0 = 128 * max(0, ktr - 4)
                    c1 = 128 * (min(3, ktr) + 1)
                    mm = te.matmul(k.psB[0:65, jj, c0:c1], VT1[:, t4, kt, jj, 0:65],
                                   UTT[:, jj, ktr, c0:c1],
                                   start=(idx == 0), stop=(idx == len(ktrs) - 1))
                k.s_mm.inc(mm)
            k.s_mm.wait(ve)
            for jj in range(2):
                ve.tensor_copy(OE[0:65, jj, :], k.psB[0:65, jj, :])
            k.s_v.inc(ve.drain())
            k.s_v.wait(te)
            vb3 = k.s_v.n
            i3 = 0
            for cc in range(4):
                for jj in range(2):
                    if i3 >= 4:
                        te.wait_ge(k.s_v.h, vb3 + i3 - 3)
                    pv = k.psA[:, i3 % 4, :].bitcast(BF16)[:, 0:65]
                    k.s_mm.inc(te.transpose(pv, OE[0:65, jj, 128 * cc:128 * cc + 128],
                                            k.idb[0:65, 0:65]))
                    k.s_mm.wait(ve)
                    k.s_v.inc(ve.tensor_copy(OTT[:, cc, jj, 0:65], pv))
                    i3 += 1
            ve.drain()
            ve.tensor_copy(RC[:, 0:8].rearrange("p (b a) -> p b a", b=2),
                           OTT[:, :, :, 64:65].rearrange("p a b 1 -> p a b").transpose([0, 2, 1]))
            ve.drain()
            ve.reciprocal(RC[:, 0:8], RC[:, 0:8])
            ve.drain()
            for jj in range(2):
                rb = RC[:, 4 * jj:4 * jj + 4].unsqueeze(2).broadcast_to((128, 4, 64))
                ve.tensor_tensor(YC[:, :, jj, :], OTT[:, :, jj, 0:64], rb, op=ALU.mult)
            k.s_v.inc(ve.drain())
            k.s_v.wait(te)
            for cc in range(4):
                pv = k.psA[:, cc, :].bitcast(BF16)[:, 0:128]
                k.s_mm.inc(te.transpose(pv, YC[:, cc, :, :].rearrange("p a b -> p (a b)"),
                                        k.idb[:]))
            k.s_mm.wait(ve)
            for cc in range(4):
                ve.tensor_copy(QKVB[:, t4, 512 * w + 128 * cc:512 * w + 128 * cc + 128],
                               k.psA[:, cc, :].bitcast(BF16)[:, 0:128])
            k.s_v.inc(ve.drain())

    full_barrier(k, "tvs")
    k.s_lg.inc(ge.dma_start(out=WO1b, in_=h["wo1"][l].rearrange("(a p) o -> p a o", p=128)), 16)
    k.s_lg.wait(te)
    for j in range(NT):
        te.wait_ge(k.s_v.h, max(0, k.s_v.n - 1))
        for dc in range(2):
            pb = k.psB[:, 2 * (j % 2) + dc, :]
            for ft in range(4):
                mm = te.matmul(pb, QKVB[:, ft, 128 * j:128 * j + 128],
                               WO1b[:, ft, 512 * dc:512 * dc + 512],
                               start=(ft == 0), stop=(ft == 3))
            k.s_mm.inc(mm)
        k.s_mm.wait(ve)
        for dc in range(2):
            ve.tensor_tensor(XINT[:, j, 512 * dc:512 * dc + 512],
                             XINT[:, j, 512 * dc:512 * dc + 512],
                             k.psB[:, 2 * (j % 2) + dc, :], op=ALU.add)
        k.s_v.inc(ve.drain())
    full_barrier(k)
    ar.release()


def emit_l1f_mlp(k, h, out_h, XINT, l, LNGC, LNBC, B1AC, B1GC, st4):
    te, ve, se, ge, sy = k.te, k.ve, k.se, k.ge, k.sy
    ar = k.ar
    ar.mark()
    W2R = ar.bf(22, D)
    BCOL = ar.bf(8)
    ROWB = ar.bf(4, 2048)

    k.s_ld.inc(sy.dma_start(out=LNGC[:], in_=h["lngT"][l]), 16)
    k.s_ld.inc(sy.dma_start(out=LNBC[:], in_=h["lnbT"][l]), 16)
    k.s_ld.inc(sy.dma_start(out=B1AC[:], in_=h["b1ag"][l, 0]), 16)
    k.s_ld.inc(sy.dma_start(out=B1GC[:], in_=h["b1ag"][l, 1]), 16)
    w2src = h["w2"][l]
    k.s_lg.inc(ge.dma_start(out=W2R[:, 0:21, :],
                            in_=w2src[0:2688].rearrange("(a p) o -> p a o", p=128)), 16)
    k.s_lg.inc(ge.dma_start(out=W2R[0:42, 21, :], in_=w2src[2688:2730]), 16)

    sA = k.qA[:, 0:16]
    sB = k.qA[:, 16:32]
    qA_ = k.qA[:, 32:48]
    qB_ = k.qA[:, 48:64]
    SCAs = (k.qC[:, 1024:1536], k.qC[:, 1536:2048])
    SCBs = (k.qC[:, 2048:2560], k.qB[:, 2048:2560])
    sbase = k.s_s.n
    for j in range(NT):
        ve.tensor_reduce(sA[:, j:j + 1], XINT[:, j, 0:512], axis=AX.X, op=ALU.add)
        ve.tensor_reduce(sB[:, j:j + 1], XINT[:, j, 512:1024], axis=AX.X, op=ALU.add)
        if j:
            se.wait_ge(k.s_s.h, sbase + j)
        se.activation(SCAs[j % 2], XINT[:, j, 0:512], AF.Square, accum_out=qA_[:, j:j + 1])
        se.activation(SCBs[j % 2], XINT[:, j, 512:1024], AF.Square, accum_out=qB_[:, j:j + 1])
        k.s_s.inc(se.drain())
    k.s_v.inc(ve.drain())
    k.s_s.inc(se.drain())
    k.s_v.wait(te)
    k.s_s.wait(te)
    STG16s = (k.qC[0:16, 2048:2176], k.qC[0:16, 2176:2304])
    for st in range(4):
        k.s_st.inc(sy.dma_start(out=st4[st, 0:1], in_=k.smallc[0:1, 15:16]), 16)
        pv = k.psA[0:16, st % 4, 0:128]
        k.s_mm.inc(te.transpose(pv, k.qA[:, 16 * st:16 * st + 16], k.idf[:]))
        k.s_mm.wait(ve)
        k.s_st.wait(ve)
        k.s_v.inc(ve.tensor_copy(STG16s[st % 2], pv))
        k.s_v.wait(sy)
        k.s_st.inc(sy.dma_start(out=st4[st, 1:2049].rearrange("(j c) -> j c", c=128),
                                in_=STG16s[st % 2]), 16)
    k.s_st.wait(sy)
    RA = k.qA[0:1, 0:2048]
    RQA = k.qB[0:1, 0:2048]
    RC_ = k.qC[0:1, 0:2048]
    IVB = ROWB[0:1, 0, :]
    CHB = ROWB[0:1, 1, :]
    CLB = ROWB[0:1, 2, :]
    B2H = ROWB[0:1, 3, 0:1024]
    B2L = ROWB[0:1, 3, 1024:2048]
    k.s_ld.inc(sy.dma_start(out=RA, in_=st4[0, 1:2049].unsqueeze(0)), 16)
    k.s_ld.inc(sy.dma_start(out=RC_, in_=st4[1, 0:2048].unsqueeze(0)), 16)
    k.s_ld.wait(ve)
    ve.tensor_tensor(RA, RA, RC_, op=ALU.add)
    k.s_v.inc(ve.drain())
    k.s_v.wait(sy)
    k.s_ld.inc(sy.dma_start(out=RQA, in_=st4[2, 1:2049].unsqueeze(0)), 16)
    k.s_ld.inc(sy.dma_start(out=RC_, in_=st4[3, 0:2048].unsqueeze(0)), 16)
    k.s_ld.wait(ve)
    ve.tensor_tensor(RQA, RQA, RC_, op=ALU.add)
    ve.drain()
    ve.tensor_scalar(RA, RA, 1.0 / D, None, op0=ALU.mult)
    ve.tensor_scalar(RQA, RQA, 1.0 / D, None, op0=ALU.mult)
    ve.drain()
    ve.tensor_tensor(RC_, RA, RA, op=ALU.mult)
    ve.drain()
    ve.tensor_tensor(RQA, RQA, RC_, op=ALU.subtract)
    ve.drain()
    ve.tensor_scalar(RQA, RQA, 1e-5, None, op0=ALU.add)
    k.s_v.inc(ve.drain())
    k.s_v.wait(se)
    k.s_s.inc(se.activation(RQA, RQA, AF.Sqrt))
    k.s_s.wait(ve)
    ve.reciprocal(RQA, RQA)
    ve.drain()
    ve.tensor_tensor(RC_, RA, RQA, op=ALU.mult)
    ve.drain()
    ve.tensor_scalar(RC_, RC_, -1.0, None, op0=ALU.mult)
    ve.drain()
    ve.tensor_copy(IVB, RQA)
    ve.tensor_copy(CHB, RC_)
    ve.drain()
    ve.tensor_tensor(CLB, RC_, CHB, op=ALU.subtract)
    k.s_v.inc(ve.drain())
    b2f = h["b2"][l].rearrange("a b -> (a b)")
    B2F = k.qA[0:1, 0:1024]
    k.s_v.wait(sy)
    k.s_ld.inc(sy.dma_start(out=B2F, in_=b2f.unsqueeze(0)), 16)
    k.s_ld.wait(ve)
    ve.tensor_copy(B2H, B2F)
    ve.drain()
    ve.tensor_tensor(B2L, B2F, B2H, op=ALU.subtract)
    k.s_v.inc(ve.drain())

    w1src = h["w1"][l].rearrange("(a p) o -> p a o", p=128)
    GH = k.qCb[:, 4096:4608]
    AT = k.qCb[:, 4608:5120]

    for hf in range(2):
        ar.mark()
        XHh = ar.bf(8, 1024)
        ar.mark()
        XFMh = ar.bf(8, 1032)
        full_barrier(k, "tvs")
        k.s_ld.wait(se)
        k.s_ld.wait(ve)
        k.s_lg.wait(te)
        if hf == 0:
            ve.memset(XFMh[:, :, 0:1], 0.0)
        else:
            ve.tensor_copy(XFMh[:, :, 0:1].rearrange("p a 1 -> p a"), BCOL)
        k.s_v.inc(ve.drain())
        k.s_v.wait(te)
        for jj2 in range(8):
            j = 8 * hf + jj2
            pairs = [(XINT[:, j, 128 * fb:128 * fb + 128],
                      XFMh[:, fb, 1 + 128 * jj2:1 + 128 * jj2 + 128]) for fb in range(8)]
            emit_transposes(k, pairs, k.idb[:], deep=True)
        k.s_mm.wait(ve)
        if hf == 0:
            ve.drain()
            k.s_v.inc(ve.tensor_copy(BCOL, XFMh[:, :, 1024:1025].rearrange("p a 1 -> p a")))
        for tc in range(2):
            gcol = 1024 * hf + 512 * tc
            te.wait_ge(k.s_v.h, k.s_v.n)
            te.wait_ge(k.s_s.h, k.s_s.n)
            k.s_mm.inc(te.matmul(k.psB[:, 2, :], k.ones1b[0:1, 0:128],
                                 IVB[0:1, gcol:gcol + 512], start=True, stop=True))
            te.matmul(k.psB[:, 3, :], k.ones1b[0:1, 0:128],
                      CHB[0:1, gcol:gcol + 512], start=True, stop=False)
            k.s_mm.inc(te.matmul(k.psB[:, 3, :], k.ones1b[0:1, 0:128],
                                 CLB[0:1, gcol:gcol + 512], start=False, stop=True))
            k.s_mm.wait(ve)
            for ft in range(8):
                soff = 1 if ft < 4 else 0
                ve.tensor_tensor(XHh[:, ft, 512 * tc:512 * tc + 512],
                                 XFMh[:, ft, soff + 512 * tc: soff + 512 * tc + 512],
                                 k.psB[:, 2, :], op=ALU.mult)
            ve.drain()
            for ft in range(8):
                ve.tensor_tensor(XHh[:, ft, 512 * tc:512 * tc + 512],
                                 XHh[:, ft, 512 * tc:512 * tc + 512],
                                 k.psB[:, 3, :], op=ALU.add)
            k.s_v.inc(ve.drain())
            k.s_v.wait(se)
            for ft in range(8):
                se.activation(XHh[:, ft, 512 * tc:512 * tc + 512],
                              XHh[:, ft, 512 * tc:512 * tc + 512], AF.Identity,
                              scale=LNGC[:, ft:ft + 1], bias=LNBC[:, ft:ft + 1])
            k.s_s.inc(se.drain())
            k.s_s.wait(ve)
            k.s_s.wait(te)
        full_barrier(k, "tvs")
        ar.release()
        AHq = ar.bf(22, 512)
        W1S = ar.bf(2, 2, 8, 128)

        for qt in range(2):
            toff = 512 * qt
            glast = [0, 0]
            wsems = (k.s_lg, k.s_lh)
            for hc in range(22):
                ocw = 128 if hc < 21 else 42
                b = hc % 2
                ge.wait_ge(k.s_mm.h, glast[b])
                wsems[b].inc(ge.dma_start(out=W1S[:, 0, b, :, 0:ocw],
                                          in_=w1src[:, :, 128 * hc:128 * hc + ocw]), 16)
                wsems[b].inc(ge.dma_start(out=W1S[:, 1, b, :, 0:ocw],
                                          in_=w1src[:, :, 2730 + 128 * hc:2730 + 128 * hc + ocw]), 16)
                wsems[b].wait(te)
                te.wait_ge(k.s_g.h, k.s_g.n)
                te.wait_ge(k.s_s.h, k.s_s.n)
                for ag in range(2):
                    for ic in range(8):
                        mm = te.matmul(k.psA[0:ocw, ag, :], W1S[:, ag, b, ic, 0:ocw],
                                       XHh[:, ic, toff:toff + 512],
                                       start=(ic == 0), stop=(ic == 7))
                    k.s_mm.inc(mm)
                glast[b] = k.s_mm.n
                k.s_mm.wait(se)
                k.s_g.wait(se)
                se.activation(GH[0:ocw, :], k.psA[0:ocw, 1, :], AF.Gelu,
                              bias=B1GC[0:ocw, hc:hc + 1])
                se.activation(AT[0:ocw, :], k.psA[0:ocw, 0, :], AF.Identity,
                              bias=B1AC[0:ocw, hc:hc + 1])
                k.s_s.inc(se.drain())
                k.s_s.wait(ge)
                k.s_g.inc(ge.tensor_tensor(AHq[0:ocw, hc, :], AT[0:ocw, :], GH[0:ocw, :],
                                           op=ALU.mult))
            k.s_g.wait(te)
            k.s_lg.wait(te)
            for tch in range(4):
                j = 4 * (2 * hf + qt) + tch
                te.wait_ge(k.s_v.h, max(0, k.s_v.n - 1))
                for dc in range(2):
                    pb = k.psB[:, 2 * (tch % 2) + dc, :]
                    te.matmul(pb, k.ones1b[0:1, 0:128], B2H[0:1, 512 * dc:512 * dc + 512],
                              start=True, stop=False)
                    te.matmul(pb, k.ones1b[0:1, 0:128], B2L[0:1, 512 * dc:512 * dc + 512],
                              start=False, stop=False)
                    for hc in range(22):
                        kcw = 128 if hc < 21 else 42
                        mm = te.matmul(pb, AHq[0:kcw, hc, 128 * tch:128 * tch + 128],
                                       W2R[0:kcw, hc, 512 * dc:512 * dc + 512],
                                       start=False, stop=(hc == 21))
                    k.s_mm.inc(mm)
                k.s_mm.wait(ve)
                for dc in range(2):
                    ve.tensor_tensor(XINT[:, j, 512 * dc:512 * dc + 512],
                                     XINT[:, j, 512 * dc:512 * dc + 512],
                                     k.psB[:, 2 * (tch % 2) + dc, :], op=ALU.add)
                k.s_v.inc(ve.drain())
        full_barrier(k)
        ar.release()
    ar.release()



def prep_host_tensors(inputs):
    t = {}
    perm = []
    for hh in range(16):
        for half in range(2):
            base = 64 * hh + 32 * half
            perm += [base + i for i in range(0, 32, 2)]
            perm += [base + i for i in range(1, 32, 2)]
    permq = np.array(perm, dtype=np.int64)
    kperm = permq[:256]
    Wq = np.asarray(inputs["L0_attn_Wq"], np.float32)
    Wkv = np.asarray(inputs["L0_attn_Wkv"], np.float32)
    t["wqT"] = np.ascontiguousarray(Wq.T[:, permq])
    wk = Wkv[0:256][kperm]
    wv = Wkv[256:512]
    t["wkvT"] = np.ascontiguousarray(np.concatenate([wk, wv], 0).T)
    t["woT"] = np.ascontiguousarray(np.asarray(inputs["L0_attn_Wout"], np.float32).T)
    t["lam"] = np.ascontiguousarray(np.asarray(inputs["L0_attn_lambda_init"], np.float32).reshape(16, 1))
    t["rmsw"] = np.ascontiguousarray(
        np.tile(np.asarray(inputs["L0_attn_rms_weight"], np.float32)[None, :], (128, 1)))
    t["wg0"] = np.ascontiguousarray(np.asarray(inputs["L0_ffn_w_gate"], np.float32))
    t["wu0"] = np.ascontiguousarray(np.asarray(inputs["L0_ffn_w_up"], np.float32))
    t["wd0"] = np.ascontiguousarray(np.asarray(inputs["L0_ffn_w_down"], np.float32))
    t["wthT"] = np.ascontiguousarray(np.asarray(inputs["L1_attn_W_theta"], np.float32).T)
    t["wphT"] = np.ascontiguousarray(np.asarray(inputs["L1_attn_W_phi"], np.float32).T)
    t["wgT"] = np.ascontiguousarray(np.asarray(inputs["L1_attn_W_g"], np.float32).T)
    t["wqkv"] = np.ascontiguousarray(np.asarray(inputs["L1_ffn_wqkv"], np.float32))
    t["qs"] = np.ascontiguousarray(np.tile(np.asarray(inputs["L1_ffn_q_scale"], np.float32), (1, 2)))
    t["ks"] = np.ascontiguousarray(np.tile(np.asarray(inputs["L1_ffn_k_scale"], np.float32), (1, 2)))
    t["wo1"] = np.ascontiguousarray(np.asarray(inputs["L1_ffn_wo"], np.float32))
    t["lngT"] = np.ascontiguousarray(np.asarray(inputs["L1_ffn_ln_g"], np.float32).reshape(2, 8, 128).transpose(0, 2, 1))
    t["lnbT"] = np.ascontiguousarray(np.asarray(inputs["L1_ffn_ln_b"], np.float32).reshape(2, 8, 128).transpose(0, 2, 1))
    t["w1"] = np.ascontiguousarray(np.asarray(inputs["L1_ffn_w1"], np.float32))
    b1 = np.asarray(inputs["L1_ffn_b1"], np.float32)
    b1p = np.zeros((2, 2, 2816), np.float32)
    b1p[:, 0, 0:2730] = b1[:, 0:2730]
    b1p[:, 1, 0:2730] = b1[:, 2730:5460]
    t["b1ag"] = np.ascontiguousarray(b1p.reshape(2, 2, 22, 128).transpose(0, 1, 3, 2))
    t["w2"] = np.ascontiguousarray(np.asarray(inputs["L1_ffn_w2"], np.float32))
    t["b2"] = np.ascontiguousarray(np.asarray(inputs["L1_ffn_b2"], np.float32).reshape(2, 8, 128))
    return t


def build_program():
    nc = bass.Bass("TRN2", target_bir_lowering=False)
    ctx = ExitStack()
    h = {}
    h["x_in"] = nc.dram_tensor("x_in", [T, D], F32, kind="ExternalInput").ap()
    out_h = nc.dram_tensor("out", [T, D], F32, kind="ExternalOutput").ap()
    for name, shape in [
        ("wqT", [D, D]), ("wkvT", [D, 512]), ("woT", [D, D]),
        ("lam", [16, 1]), ("rmsw", [128, 64]),
        ("wg0", [2816, D]), ("wu0", [2816, D]), ("wd0", [D, 2816]),
        ("wthT", [D, 256]), ("wphT", [D, 256]), ("wgT", [D, D]),
        ("wqkv", [2, D, 1536]), ("qs", [2, 128]), ("ks", [2, 128]),
        ("wo1", [2, 512, D]),
        ("w1", [2, D, 5460]), ("b1ag", [2, 2, 128, 22]), ("w2", [2, 2730, D]),
        ("b2", [2, 8, 128]), ("lngT", [2, 128, 8]), ("lnbT", [2, 128, 8]),
    ]:
        h[name] = nc.dram_tensor(name, shape, F32, kind="ExternalInput").ap()
    h["wqgT_s"] = nc.dram_tensor("wqgT_s", [D, 2816], BF16, kind="Internal").ap()
    h["wquT_s"] = nc.dram_tensor("wquT_s", [D, 2816], BF16, kind="Internal").ap()
    h["wdqT_s"] = nc.dram_tensor("wdqT_s", [2816, D], BF16, kind="Internal").ap()
    h["x1_s"] = nc.dram_tensor("x1_s", [T, D], F32, kind="Internal").ap()

    k = KB(nc, ctx)
    k.out_h = out_h
    blk = ctx.enter_context(nc.Block())
    k.psA = k.ps("psA", [128, 4, 512])
    k.psB = k.ps("psB", [128, 4, 512])
    k.qA = k.sb("qA", [128, 2816], F32)
    k.qB = k.sb("qB", [128, 2816], F32)
    k.qC = k.sb("qC", [128, 2816], F32)
    k.qAb = k.qA[:].bitcast(BF16)
    k.qBb = k.qB[:].bitcast(BF16)
    k.qCb = k.qC[:].bitcast(BF16)
    build_constants(k)
    k.ar = Arena(k, ARENA_N)
    k.ar.mark()
    k.arXQF = k.ar.bf(8, T)
    k.ar.mark()
    k.arQ = k.ar.bf(8, T)
    k.arK2 = k.ar.bf(2, T)
    k.arVT = k.ar.bf(NT, 4, 68)

    emit_l0(k, h, out_h)

    ctx.close()
    return nc


def kernel(**inputs):
    nb = 8
    x_full = np.ascontiguousarray(np.asarray(inputs["L0_attn_x"], np.float32))
    nc = build_program()
    host_tensors = prep_host_tensors(inputs)
    in_maps = []
    for c in range(nb):
        m = {"x_in": np.ascontiguousarray(x_full[c])}
        m.update(host_tensors)
        in_maps.append(m)
    res = run_bass_kernel_spmd(nc, in_maps, core_ids=list(range(nb)))
    return np.stack([r["out"] for r in res.results], axis=0)


# version 06
# speedup vs baseline: 1.6040x; 1.0347x over previous
import os
import dataclasses as _dc
from contextlib import ExitStack
import numpy as np
import ml_dtypes
import concourse.bass as bass
import concourse.mybir as mybir
from concourse.bass_utils import run_bass_kernel_spmd

F32 = mybir.dt.float32
BF16 = mybir.dt.bfloat16
AF = mybir.ActivationFunctionType
ALU = mybir.AluOpType
AX = mybir.AxisListType

T = 2048
NT = 16
D = 1024
MAGIC = 12582912.0
DEV_STAGE = os.environ.get("KDEV_STAGE", "full")
DEV_TAP = os.environ.get("KDEV_TAP", "")
ARENA_N = 85000


def _prod(xs):
    r = 1
    for x in xs:
        r *= x
    return r


class Sem:
    def __init__(self, nc, ctx, name):
        self.h = ctx.enter_context(nc.semaphore(name))
        self.n = 0

    def inc(self, instr, k=1):
        instr.then_inc(self.h, k)
        self.n += k
        return instr

    def wait(self, eng, val=None):
        eng.wait_ge(self.h, self.n if val is None else val)


class Arena:
    def __init__(self, k, n):
        self.t = k.nc.sbuf_tensor("ARENA", [128, n], BF16)
        self.th = k.ctx.enter_context(self.t)
        self.off = 0
        self.n = n
        self.stack = []

    def bf(self, *shape):
        m = _prod(shape)
        assert self.off + m <= self.n, f"arena overflow: {self.off}+{m} > {self.n}"
        ap = self.th[:, self.off:self.off + m]
        self.off += m
        if len(shape) > 1:
            names = " ".join(f"d{i}" for i in range(len(shape)))
            kw = {f"d{i}": s for i, s in enumerate(shape)}
            ap = ap.rearrange(f"p ({names}) -> p {names}", **kw)
        return ap

    def f32(self, *shape):
        m = _prod(shape)
        if self.off % 2:
            self.off += 1
        assert self.off + 2 * m <= self.n, f"arena overflow f32: {self.off}+{2*m} > {self.n}"
        ap = self.th[:, self.off:self.off + 2 * m].bitcast(F32)
        self.off += 2 * m
        if len(shape) > 1:
            names = " ".join(f"d{i}" for i in range(len(shape)))
            kw = {f"d{i}": s for i, s in enumerate(shape)}
            ap = ap.rearrange(f"p ({names}) -> p {names}", **kw)
        return ap

    def mark(self):
        self.stack.append(self.off)

    def release(self):
        self.off = self.stack.pop()


class KB:
    def __init__(self, nc, ctx):
        self.nc = nc
        self.ctx = ctx
        self.te, self.ve, self.se, self.ge, self.sy = \
            nc.tensor, nc.vector, nc.scalar, nc.gpsimd, nc.sync
        self.s_ld = Sem(nc, ctx, "s_ld")
        self.s_mm = Sem(nc, ctx, "s_mm")
        self.s_v = Sem(nc, ctx, "s_v")
        self.s_s = Sem(nc, ctx, "s_s")
        self.s_g = Sem(nc, ctx, "s_g")
        self.s_st = Sem(nc, ctx, "s_st")
        self.s_x = Sem(nc, ctx, "s_x")
        self.s_la = Sem(nc, ctx, "s_la")
        self.s_lb = Sem(nc, ctx, "s_lb")
        self.s_lg = Sem(nc, ctx, "s_lg")
        self.s_lh = Sem(nc, ctx, "s_lh")
        self.s_lc = Sem(nc, ctx, "s_lc")
        self.s_dmp = Sem(nc, ctx, "s_dmp")

    def sb(self, name, shape, dtype):
        return self.ctx.enter_context(self.nc.sbuf_tensor(name, shape, dtype))

    def ps(self, name, shape):
        return self.ctx.enter_context(self.nc.psum_tensor(name, shape, F32))


def dump_fm(k, buf, out_h, nt):
    full_barrier(k)
    ov = out_h.rearrange("(a p) f -> p a f", p=128)
    W = buf.shape[-1]
    for ft in range(nt):
        for hf in range(2):
            lo = 1024 * hf
            hi = min(lo + 1024, W)
            if lo >= hi:
                continue
            k.s_dmp.inc(k.ge.dma_start(out=ov[:, 2 * ft + hf, 0:hi - lo],
                                      in_=buf[:, ft, lo:hi]), 16)
    k.s_dmp.wait(k.ge)
    k.s_dmp.wait(k.sy)


def full_barrier(k, engines="tvsg", dma=True):
    s = k.s_x
    if "t" in engines:
        s.inc(k.te.drain())
    if "v" in engines:
        s.inc(k.ve.drain())
    if "s" in engines:
        s.inc(k.se.drain())
    if "g" in engines:
        s.inc(k.ge.drain())
    for e in (k.te, k.ve, k.se, k.ge, k.sy):
        e.wait_ge(s.h, s.n)
        if dma and engines == "tvsg":
            for sem in (k.s_ld, k.s_la, k.s_lb, k.s_lg, k.s_lh, k.s_lc, k.s_st, k.s_dmp):
                sem.wait(e)


def build_constants(k):
    k.mcol = k.sb("c_mcol", [128, 1], F32)
    k.rc6 = k.sb("c_rc6", [128, 1], F32)
    k.rc12 = k.sb("c_rc12", [128, 1], F32)
    k.idf = k.sb("c_idf", [128, 128], F32)
    k.idb = k.sb("c_idb", [128, 128], BF16)
    k.ones1b = k.sb("c_ones1b", [128, 128], BF16)
    k.M01 = k.sb("c_m01", [128, 128], BF16)
    k.P32 = k.sb("c_p32", [128, 128], BF16)
    k.P4 = k.sb("c_p4", [128, 128], BF16)
    k.qbm = k.sb("qbm", [128, 176], F32)
    k.qib6 = k.sb("qib6", [128, 176], F32)
    k.qbm12 = k.sb("qbm12", [128, 176], BF16)
    k.stats = k.sb("stats", [128, 48], F32)
    k.smallc = k.sb("smallc", [128, 16], F32)
    v, g = k.ve, k.ge
    v.memset(k.mcol[:], 1.0)
    v.drain()
    v.tensor_scalar(k.mcol[:], k.mcol[:], 2.0 ** -30, None, op0=ALU.add)
    v.drain()
    v.tensor_scalar(k.mcol[:], k.mcol[:], -1.0, None, op0=ALU.add)
    v.drain()
    v.tensor_scalar(k.mcol[:], k.mcol[:], 1.5 * 2.0 ** 82, MAGIC,
                    op0=ALU.mult, op1=ALU.add)
    v.memset(k.rc6[:], 6.0)
    v.memset(k.rc12[:], 12.0)
    v.memset(k.ones1b[:], 1.0)
    v.drain()
    v.reciprocal(k.rc6[:], k.rc6[:])
    v.reciprocal(k.rc12[:], k.rc12[:])
    k.s_g.inc(g.iota(k.idf[:], pattern=[[1, 128]], base=0, channel_multiplier=-1,
                     allow_small_or_imprecise_dtypes=True))
    k.s_g.wait(v)
    v.tensor_scalar(k.idb[:], k.idf[:], 0.0, None, op0=ALU.is_equal)
    v.tensor_scalar(k.M01[:], k.idf[:], 0.0, None, op0=ALU.is_ge)
    v.drain()
    k.s_v.inc(v.tensor_scalar(k.idf[:], k.idf[:], 0.0, None, op0=ALU.is_equal))
    DIFq = k.qA[:, 0:128]
    CM32 = k.qB[:, 0:128]
    SC2 = k.qC[:, 4:260]
    g.iota(DIFq, pattern=[[1, 128]], base=0, channel_multiplier=-1,
           allow_small_or_imprecise_dtypes=True)
    g.iota(CM32, pattern=[[0, 4], [1, 32]], base=0, channel_multiplier=0,
           allow_small_or_imprecise_dtypes=True)
    k.s_g.inc(g.drain())
    k.s_g.wait(v)
    k.pcol = k.sb("c_pcol", [128, 1], F32)
    k.pm16 = k.sb("c_pm16", [128, 1], F32)
    k.altc = k.sb("c_altc", [128, 1], F32)
    v.tensor_scalar(k.pcol[:], DIFq[:, 0:1], -1.0, None, op0=ALU.mult)
    v.drain()
    v.tensor_scalar(k.pm16[:], k.pcol[:], -7.5, 0.0625, op0=ALU.add, op1=ALU.mult)
    v.drain()
    v.tensor_scalar(k.pm16[:], k.pm16[:], k.mcol[:, 0:1], k.mcol[:, 0:1],
                    op0=ALU.add, op1=ALU.subtract)
    v.drain()
    v.scalar_tensor_tensor(k.pm16[:], k.pm16[:], -16.0, k.pcol[:],
                           op0=ALU.mult, op1=ALU.add)
    v.drain()
    v.tensor_scalar(k.altc[:], k.pcol[:], -15.5, 0.03125, op0=ALU.add, op1=ALU.mult)
    v.drain()
    v.tensor_scalar(k.altc[:], k.altc[:], k.mcol[:, 0:1], k.mcol[:, 0:1],
                    op0=ALU.add, op1=ALU.subtract)
    v.drain()
    v.scalar_tensor_tensor(k.altc[:], k.altc[:], -32.0, k.pcol[:],
                           op0=ALU.mult, op1=ALU.add)
    v.drain()
    v.tensor_scalar(k.altc[:], k.altc[:], 16.0, None, op0=ALU.is_ge)
    v.drain()
    v.tensor_scalar(k.altc[:], k.altc[:], -2.0, 1.0, op0=ALU.mult, op1=ALU.add)
    v.drain()
    CMOD = k.qA[:, 128:256]
    g.iota(CMOD, pattern=[[0, 4], [1, 32]], base=0, channel_multiplier=0,
           allow_small_or_imprecise_dtypes=True)
    k.s_g.inc(g.drain())
    k.s_g.wait(v)
    v.tensor_scalar(k.P32[:], DIFq, 16.0, None, op0=ALU.is_equal)
    v.tensor_scalar(SC2[:, 0:128], DIFq, -16.0, None, op0=ALU.is_equal)
    v.tensor_scalar(SC2[:, 128:256], CMOD, 16.0, None, op0=ALU.is_ge)
    v.drain()
    v.tensor_tensor(k.P32[:], k.P32[:], SC2[:, 128:256], op=ALU.mult)
    v.drain()
    v.tensor_scalar(SC2[:, 128:256], SC2[:, 128:256], -1.0, 1.0, op0=ALU.mult, op1=ALU.add)
    v.drain()
    v.tensor_tensor(SC2[:, 0:128], SC2[:, 0:128], SC2[:, 128:256], op=ALU.mult)
    v.drain()
    v.tensor_tensor(k.P32[:], k.P32[:], SC2[:, 0:128], op=ALU.add)
    v.drain()
    v.tensor_scalar(k.P32[:], k.P32[:], k.altc[:, 0:1], None, op0=ALU.mult)
    v.drain()
    v.tensor_scalar(SC2[:, 0:128], DIFq, 0.0, None, op0=ALU.is_le)
    v.tensor_scalar(SC2[:, 128:256], DIFq, -32.0, None, op0=ALU.is_gt)
    v.drain()
    v.tensor_scalar(CM32, CM32, 0.0, None, op0=ALU.is_equal)
    v.tensor_tensor(k.P4[:], SC2[:, 0:128], SC2[:, 128:256], op=ALU.mult)
    v.drain()
    k.s_v.inc(v.tensor_tensor(k.P4[:], k.P4[:], CM32, op=ALU.mult))
    ln4, pi4, twopi, i2pi, hpi, sig, pip, pin = (k.smallc[:, i:i + 1] for i in range(8))
    v.memset(ln4, 10000.0)
    v.memset(pi4, 1.0)
    v.memset(sig, 32.0)
    k.s_v.inc(v.drain())
    k.s_v.wait(k.se)
    k.se.activation(ln4, ln4, AF.Ln)
    k.se.activation(pi4, pi4, AF.Arctan)
    k.s_s.inc(k.se.drain())
    k.s_s.wait(v)
    v.tensor_scalar(twopi, pi4, 8.0, None, op0=ALU.mult)
    v.tensor_scalar(hpi, pi4, 2.0, None, op0=ALU.mult)
    v.tensor_scalar(pip, pi4, 4.0, None, op0=ALU.mult)
    v.tensor_scalar(pin, pi4, -4.0, None, op0=ALU.mult)
    v.reciprocal(sig, sig)
    v.drain()
    v.reciprocal(i2pi, twopi)
    k.s_v.inc(v.drain())
    k.s_v.wait(k.se)
    k.s_s.inc(k.se.activation(sig, sig, AF.Sqrt))
    k.ln4, k.pi4, k.twopi, k.i2pi, k.hpi, k.sig = ln4, pi4, twopi, i2pi, hpi, sig
    k.pip, k.pin = pip, pin


def emit_quant(k, S, Q, ge_q=True):
    v, g = k.ve, k.ge
    k.s_g.wait(v)
    W = S.shape[-1]
    nb = W // 16
    A, B, C = k.qA, k.qB, k.qC
    Ab, Bb, Cb = k.qAb, k.qBb, k.qCb
    bm, ib6, bm12 = k.qbm, k.qib6, k.qbm12
    S3 = S.rearrange("p (a b) -> p a b", b=16)
    v.tensor_reduce(bm[:, 0:nb], S3, axis=AX.X, op=ALU.max, apply_absolute_value=True)
    v.drain()
    v.tensor_scalar(bm[:, 0:nb], bm[:, 0:nb], 6e-12, None, op0=ALU.max)
    v.drain()
    v.reciprocal(ib6[:, 0:nb], bm[:, 0:nb])
    v.tensor_scalar(bm12[:, 0:nb], bm[:, 0:nb], k.rc6[:, 0:1], None, op0=ALU.mult)
    v.drain()
    v.tensor_scalar(ib6[:, 0:nb], ib6[:, 0:nb], 6.0, None, op0=ALU.mult)
    v.drain()
    ib6b = ib6[:, 0:nb].unsqueeze(2).broadcast_to((128, nb, 16))
    v.tensor_tensor(A[:, 0:W].rearrange("p (a b) -> p a b", b=16), S3, ib6b, op=ALU.mult)
    v.drain()
    v.tensor_scalar(B[:, 0:W], A[:, 0:W], -2.0, 2.0, op0=ALU.max, op1=ALU.min)
    v.tensor_scalar(C[:, 0:W], A[:, 0:W], -4.0, 4.0, op0=ALU.max, op1=ALU.min)
    k.s_v.inc(v.drain())
    k.s_v.wait(g)
    k.s_g.inc(g.tensor_tensor(C[:, 0:W], C[:, 0:W], A[:, 0:W], op=ALU.add))
    k.s_g.wait(v)
    v.scalar_tensor_tensor(A[:, 0:W], C[:, 0:W], 0.5, B[:, 0:W], op0=ALU.mult, op1=ALU.add)
    v.drain()
    z = Bb[:, 0:W]
    v.tensor_scalar(z, A[:, 0:W], k.mcol[:, 0:1], k.mcol[:, 0:1],
                    op0=ALU.add, op1=ALU.subtract)
    v.drain()
    m1, c6 = Cb[:, 0:W], Ab[:, 0:W]
    z2, s1 = Ab[:, W:2 * W], Cb[:, W:2 * W]
    v.tensor_scalar(m1, z, 0.5, -2.0, op0=ALU.mult, op1=ALU.max)
    v.tensor_scalar(c6, z, -6.0, 6.0, op0=ALU.max, op1=ALU.min)
    v.tensor_scalar(z2, z, 2.0, None, op0=ALU.mult)
    v.drain()
    v.tensor_scalar(m1, m1, 2.0, None, op0=ALU.min)
    v.drain()
    v.tensor_tensor(s1, m1, c6, op=ALU.add)
    v.drain()
    psi = Cb[:, 0:W]
    v.tensor_tensor(psi, z2, s1, op=ALU.subtract)
    k.s_v.inc(v.drain())
    bm12b = bm12[:, 0:nb].unsqueeze(2).broadcast_to((128, nb, 16))
    Q3 = Q.rearrange("p (a b) -> p a b", b=16)
    psi3 = psi.rearrange("p (a b) -> p a b", b=16)
    if ge_q:
        k.s_v.wait(g)
        k.s_g.inc(g.tensor_tensor(Q3, psi3, bm12b, op=ALU.mult))
    else:
        v.tensor_tensor(Q3, psi3, bm12b, op=ALU.mult)
        k.s_v.inc(v.drain())


def emit_transposes(k, pairs, identity, deep=False):
    ve = k.ve
    if not hasattr(k, "_tp_i"):
        k._tp_i = 0
    nb_ = 8 if deep else 4
    for (src, dst) in pairs:
        k.te.wait_ge(k.s_v.h, max(0, k.s_v.n - (nb_ - 1)))
        i = k._tp_i % nb_
        pst = k.psA if i < 4 else k.psB
        pv = pst[:, i % 4, :].bitcast(BF16)[:, 0:dst.shape[-1]]
        k.s_mm.inc(k.te.transpose(pv, src, identity))
        k.s_mm.wait(ve)
        k.s_v.inc(ve.tensor_copy(dst, pv))
        k._tp_i += 1


def emit_ln_stats(k, xsrc_hbm, XTa, SCR, nt=NT):
    v, s, sy = k.ve, k.se, k.sy
    ssum = k.stats[:, 0:nt]
    ssq = k.stats[:, nt:2 * nt]
    inv = k.stats[:, 2 * nt:3 * nt]
    xr = xsrc_hbm.rearrange("(j p) f -> p j f", p=128)
    sems = (k.s_la, k.s_lb)
    for j in range(min(2, nt)):
        sems[j % 2].inc(sy.dma_start(out=XTa[:, j % 2, :], in_=xr[:, j, :]), 16)
    for j in range(nt):
        xt = XTa[:, j % 2, :]
        sems[j % 2].wait(v)
        sems[j % 2].wait(s)
        v.tensor_reduce(ssum[:, j:j + 1], xt, axis=AX.X, op=ALU.add)
        s.activation(SCR[:, (j % 2) * D:(j % 2) * D + D], xt, AF.Square,
                     accum_out=ssq[:, j:j + 1])
        if j + 2 < nt:
            k.s_s.inc(s.drain())
            k.s_v.inc(v.drain())
            k.s_s.wait(sy)
            k.s_v.wait(sy)
            sems[j % 2].inc(sy.dma_start(out=XTa[:, j % 2, :], in_=xr[:, j + 2, :]), 16)
    k.s_s.inc(s.drain())
    k.s_s.wait(v)
    v.drain()
    v.tensor_scalar(ssum[:], ssum[:], 1.0 / D, None, op0=ALU.mult)
    v.drain()
    v.tensor_tensor(inv[:], ssum[:], ssum[:], op=ALU.mult)
    v.tensor_scalar(ssq[:], ssq[:], 1.0 / D, None, op0=ALU.mult)
    v.drain()
    v.tensor_tensor(ssq[:], ssq[:], inv[:], op=ALU.subtract)
    v.drain()
    v.tensor_scalar(ssq[:], ssq[:], 1e-5, None, op0=ALU.add)
    v.drain()
    v.reciprocal(ssq[:], ssq[:])
    k.s_v.inc(v.drain())
    k.s_v.wait(s)
    s.activation(inv[:], ssq[:], AF.Sqrt)
    k.s_s.inc(s.drain())
    k.s_s.wait(v)
    k.s_s.wait(sy)
    k.s_v.wait(sy)



def emit_l0(k, h, out_h):
    te, ve, se, ge, sy = k.te, k.ve, k.se, k.ge, k.sy
    ar = k.ar

    XQF = k.arXQF
    ar.mark()
    FM0 = ar.bf(8, T)
    ar.mark()
    XTa = ar.f32(2, D)
    XNT = ar.f32(D)
    XB = ar.bf(D)
    XQT = ar.bf(D)
    WLD = ar.f32(2816)
    WQB = ar.bf(2816)

    emit_ln_stats(k, h["x_in"], XTa, k.qA[:, 0:2 * D])

    xr = h["x_in"].rearrange("(j p) f -> p j f", p=128)
    outr = out_h.rearrange("(j p) f -> p j f", p=128)
    ssum = k.stats[:, 0:NT]
    inv = k.stats[:, 2 * NT:3 * NT]
    sems = (k.s_la, k.s_lb)
    for j in range(min(2, NT)):
        sems[j % 2].inc(sy.dma_start(out=XTa[:, j % 2, :], in_=xr[:, j, :]), 16)
    for j in range(NT):
        xt = XTa[:, j % 2, :]
        sems[j % 2].wait(ve)
        k.s_st.wait(ve)
        k.s_s.wait(ve)
        ve.tensor_scalar(XNT, xt, ssum[:, j:j + 1], inv[:, j:j + 1],
                         op0=ALU.subtract, op1=ALU.mult)
        k.s_v.inc(ve.drain())
        if j + 2 < NT:
            k.s_v.wait(sy)
            sems[j % 2].inc(sy.dma_start(out=XTa[:, j % 2, :], in_=xr[:, j + 2, :]), 16)
        if DEV_STAGE == "ln":
            k.s_v.wait(sy)
            k.s_st.inc(sy.dma_start(out=outr[:, j, :], in_=XNT), 16)
            continue
        k.s_v.wait(se)
        k.s_s.inc(se.activation(XB, XNT, AF.Copy, bias=0.0))
        emit_quant(k, XNT, XQT)
        if DEV_STAGE == "xq":
            k.s_g.wait(ge)
            k.s_lg.inc(ge.dma_start(out=outr[:, j, :], in_=XQT), 16)
            k.s_lg.wait(ve)
            continue
        k.s_s.wait(te)
        k.s_v.wait(te)
        k.s_g.wait(te)
        pairs = [(XB[:, 128 * fb:128 * fb + 128], FM0[:, fb, 128 * j:128 * j + 128])
                 for fb in range(8)]
        pairs += [(XQT[:, 128 * fb:128 * fb + 128], XQF[:, fb, 128 * j:128 * j + 128])
                  for fb in range(8)]
        emit_transposes(k, pairs, k.idb[:], deep=True)
        k.s_mm.wait(se)
        k.s_mm.wait(ve)
        k.s_mm.wait(ge)

    if DEV_STAGE in ("ln", "xq"):
        k.s_st.wait(sy)
        k.s_lg.wait(sy)
        return

    emit_l0_wquant(k, h, WLD, WQB, XTa.rearrange("p a b -> p (a b)").bitcast(BF16)[:, 0:2816])
    full_barrier(k)
    ar.release()
    emit_l0_attn(k, h, FM0, out_h)


def emit_l0_attn(k, h, FM0, out_h):
    te, ve, se, ge, sy = k.te, k.ve, k.se, k.ge, k.sy
    ar = k.ar
    Q = k.arQ
    VT = k.arVT
    K2 = k.arK2

    ar.mark()
    WQ = ar.bf(8, D)
    WKV = ar.bf(8, 512)
    VB = ar.bf(2, T)
    COSA = ar.bf(T)
    SINA = ar.bf(T)
    COSQ = ar.bf(T)
    SINQ = ar.bf(T)
    invc = ar.f32(1)

    k.s_lg.inc(ge.dma_start(out=WQ, in_=h["wqT"].rearrange("(a p) o -> p a o", p=128)), 16)
    k.s_lg.inc(ge.dma_start(out=WKV, in_=h["wkvT"].rearrange("(a p) o -> p a o", p=128)), 16)

    invf = invc
    ve.tensor_scalar(invf, k.pm16[:], -0.0625, None, op0=ALU.mult)
    ve.drain()
    ve.tensor_scalar(invf, invf, k.ln4, None, op0=ALU.mult)
    k.s_v.inc(ve.drain())
    k.s_v.wait(se)
    k.s_s.inc(se.activation(invf, invf, AF.Exp))

    TGRID = k.qA[:, 0:T]
    ARG = k.qB[:, 0:T]
    KQ = k.qC[:, 0:T]
    k.s_g.inc(ge.iota(TGRID, pattern=[[1, T]], base=0, channel_multiplier=0,
                      allow_small_or_imprecise_dtypes=True))
    k.s_g.wait(ve)
    k.s_s.wait(ve)

    def build_trig(dst, use_hpi):
        if use_hpi:
            ve.tensor_scalar(ARG, TGRID, invf, k.hpi, op0=ALU.mult, op1=ALU.add)
        else:
            ve.tensor_scalar(ARG, TGRID, invf, None, op0=ALU.mult)
        ve.drain()
        ve.tensor_scalar(KQ, ARG, k.i2pi, None, op0=ALU.mult)
        ve.drain()
        ve.tensor_scalar(KQ, KQ, k.mcol[:, 0:1], k.mcol[:, 0:1], op0=ALU.add, op1=ALU.subtract)
        ve.drain()
        ve.scalar_tensor_tensor(KQ, KQ, k.twopi, ARG, op0=ALU.mult, op1=ALU.subtract)
        ve.drain()
        ve.tensor_scalar(KQ, KQ, k.pin, k.pip, op0=ALU.max, op1=ALU.min)
        k.s_v.inc(ve.drain())
        k.s_v.wait(se)
        k.s_s.inc(se.activation(KQ, KQ, AF.Sin))
        k.s_s.wait(ve)
        ve.tensor_scalar(dst, KQ, -1.0, None, op0=ALU.mult)
        k.s_v.inc(ve.drain())

    build_trig(COSA, True)
    build_trig(SINA, False)
    k.s_v.wait(ve)
    ve.tensor_scalar(COSQ, COSA, k.sig, None, op0=ALU.mult)
    k.s_v.inc(ve.tensor_scalar(SINQ, SINA, k.sig, None, op0=ALU.mult))

    k.s_ld.wait(te)
    k.s_lg.wait(te)
    k.s_v.wait(te)

    def proj(WT, nout, dst, woff=0):
        for oc in range(nout):
            for tc in range(4):
                pb = k.psA[:, tc % 4, :]
                te.wait_ge(k.s_s.h, max(0, k.s_s.n - 3))
                for ic in range(8):
                    mm = te.matmul(pb, WT[:, ic, 128 * (oc + woff):128 * (oc + woff) + 128],
                                   FM0[:, ic, 512 * tc:512 * tc + 512],
                                   start=(ic == 0), stop=(ic == 7))
                k.s_mm.inc(mm)
                k.s_mm.wait(se)
                k.s_s.inc(se.activation(dst[:, oc, 512 * tc:512 * tc + 512], pb,
                                        AF.Copy, bias=0.0))

    proj(WQ, 8, Q)
    if DEV_TAP == "qpre":
        dump_fm(k, Q, k.out_h, 8)
        return
    proj(WKV, 2, K2)
    k.s_v.wait(ve)
    ve.memset(VT[:], 1.0)
    k.s_v.inc(ve.drain())
    proj(WKV, 2, VB.rearrange("p a t -> p a t"), woff=2)
    k.s_s.wait(te)
    k.s_v.wait(te)
    base_v = k.s_v.n
    i = 0
    for hk in range(4):
        fr, ft = (64 * hk) % 128, (64 * hk) // 128
        for j in range(NT):
            if i >= 4:
                te.wait_ge(k.s_v.h, base_v + i - 3)
            pv = k.psA[:, i % 4, :].bitcast(BF16)[:, 0:64]
            idsl = k.idb[fr:fr + 64, fr:fr + 64]
            k.s_mm.inc(te.transpose(pv, VB[fr:fr + 64, ft, 128 * j:128 * j + 128], idsl))
            k.s_mm.wait(ve)
            k.s_v.inc(ve.tensor_copy(VT[:, j, hk, 0:64], pv))
            i += 1

    RT1 = k.qA[:].bitcast(BF16)[:, 0:T]
    full_barrier(k, "tvs")

    def rope(SRC, n_tiles, COS, SIN):
        for ft in range(n_tiles):
            k.s_v.wait(te)
            for c in range(4):
                k.s_mm.inc(te.matmul(k.psB[:, c, :], k.P32[:], SRC[:, ft, 512 * c:512 * c + 512],
                                     start=True, stop=True))
            ve.tensor_tensor(RT1, SRC[:, ft, :], COS, op=ALU.mult)
            k.s_mm.wait(ve)
            for c in range(4):
                ve.tensor_tensor(k.psB[:, c, :], k.psB[:, c, :],
                                 SIN[:, 512 * c:512 * c + 512], op=ALU.mult)
            ve.drain()
            for c in range(4):
                ve.tensor_tensor(SRC[:, ft, 512 * c:512 * c + 512],
                                 RT1[:, 512 * c:512 * c + 512], k.psB[:, c, :], op=ALU.add)
            k.s_v.inc(ve.drain())

    rope(Q, 8, COSQ, SINQ)
    if DEV_TAP == "qrope":
        dump_fm(k, Q, k.out_h, 8)
        return
    rope(K2, 2, COSA, SINA)
    if DEV_TAP == "krope":
        dump_fm(k, K2, k.out_h, 2)
        return
    full_barrier(k, "tv")
    ar.release()
    emit_l0_scores(k, h, Q, K2, VT, out_h)


def emit_l0_scores(k, h, Q, K2, VT, out_h):
    nc = k.nc
    te, ve, se, ge, sy = k.te, k.ve, k.se, k.ge, k.sy
    ar = k.ar
    P4t = k.P4

    ar.mark()
    KARR = ar.bf(4, T)
    BT = ar.bf(T)
    UT = ar.bf(4, 512)
    OE = ar.bf(4, 512)
    OTT = ar.bf(4, 4, 72)
    YC = ar.bf(4, 2, 64)
    WROW = ar.bf(64)
    RC = ar.f32(24)
    KMG = ar.f32(2)
    KMC = ar.f32(4)
    NLAM = ar.f32(16)
    LAMC = ar.f32(1)

    k.s_v.wait(ve)
    for v_ in range(4):
        src = K2[64 * (v_ % 2):64 * (v_ % 2) + 64, v_ // 2, :]
        ve.tensor_copy(KARR[0:64, v_, :], src)
        ve.tensor_copy(KARR[64:128, v_, :], src)
    k.s_v.inc(ve.drain())

    lam_s = nc.dram_tensor("lam_scr", [16, 1], F32, kind="Internal").ap()
    km_s = nc.dram_tensor("km_scr", [128, 2], F32, kind="Internal").ap()
    k.s_ld.inc(sy.dma_start(out=LAMC[0:16, :], in_=h["lam"][:, :]), 16)
    k.s_ld.wait(se)
    se.activation(LAMC[0:16, :], LAMC[0:16, :], AF.Sigmoid)
    k.s_s.inc(se.drain())
    k.s_s.wait(ve)
    k.s_v.inc(ve.tensor_scalar(LAMC[0:16, :], LAMC[0:16, :], -1.0, None, op0=ALU.mult))
    k.s_v.wait(sy)
    k.s_st.inc(sy.dma_start(out=lam_s[:, :], in_=LAMC[0:16, :]), 16)
    k.s_st.wait(sy)
    nl_in = _dc.replace(lam_s, offset=0, ap=[[0, 128], [1, 16]])
    k.s_ld.inc(sy.dma_start(out=NLAM, in_=nl_in), 16)

    SQB = k.qA[:].bitcast(BF16)[:, 0:T]
    BTS = k.qB[:, 0:T]
    for kt2 in range(2):
        k.s_v.wait(se)
        k.s_g.wait(se)
        k.s_s.inc(se.activation(SQB, K2[:, kt2, :], AF.Square))
        k.s_s.wait(te)
        for cc in range(4):
            k.s_mm.inc(te.matmul(k.psB[:, cc, :], P4t[:], SQB[:, 512 * cc:512 * cc + 512],
                                 start=True, stop=True))
        k.s_mm.wait(ve)
        for cc in range(4):
            ve.tensor_reduce(KMG[:, kt2:kt2 + 1] if cc == 0 else BTS[:, cc:cc + 1],
                             k.psB[:, cc, :], axis=AX.X, op=ALU.max)
        ve.drain()
        ve.tensor_tensor(KMG[:, kt2:kt2 + 1], KMG[:, kt2:kt2 + 1], BTS[:, 1:2], op=ALU.max)
        ve.drain()
        ve.tensor_tensor(BTS[:, 2:3], BTS[:, 2:3], BTS[:, 3:4], op=ALU.max)
        ve.drain()
        ve.tensor_tensor(KMG[:, kt2:kt2 + 1], KMG[:, kt2:kt2 + 1], BTS[:, 2:3], op=ALU.max)
        k.s_v.inc(ve.drain())
    k.s_v.wait(se)
    k.s_s.inc(se.activation(KMG, KMG, AF.Sqrt))
    k.s_s.wait(sy)
    k.s_st.inc(sy.dma_start(out=km_s[:, :], in_=KMG), 16)
    k.s_st.wait(sy)
    for kv in range(4):
        for g4 in range(4):
            off = 128 * (kv % 2) + kv // 2 + 64 * (g4 % 2)
            src = _dc.replace(km_s, offset=off, ap=[[0, 32], [1, 1]])
            k.s_ld.inc(sy.dma_start(out=KMC[32 * g4:32 * g4 + 32, kv:kv + 1], in_=src), 16)

    k.s_lg.inc(ge.dma_start(out=WROW, in_=h["rmsw"][:, :]), 16)

    ones1b = k.ones1b
    M01 = k.M01
    k.s_ld.wait(ve)
    k.s_lg.wait(ve)
    k.s_ld.wait(te)
    for qt in range(8):
        kvv = qt // 2
        k.s_v.wait(se)
        k.s_g.wait(se)
        k.s_s.inc(se.activation(SQB, Q[:, qt, :], AF.Square))
        k.s_s.wait(te)
        k.s_v.wait(te)
        for cc in range(4):
            k.s_mm.inc(te.matmul(k.psB[:, cc, :], P4t[:], SQB[:, 512 * cc:512 * cc + 512],
                                 start=True, stop=True))
        k.s_mm.wait(se)
        for cc in range(4):
            se.activation(BTS[:, 512 * cc:512 * cc + 512], k.psB[:, cc, :], AF.Sqrt)
        k.s_s.inc(se.drain())
        k.s_s.wait(ve)
        k.s_v.inc(ve.tensor_scalar(BT, BTS, KMC[:, kvv:kvv + 1], -1.0,
                                   op0=ALU.mult, op1=ALU.mult))
        if DEV_TAP == "bt" and qt == 0:
            dump_fm(k, BT.unsqueeze(1), k.out_h, 1)
            return
        k.s_v.wait(te)

        for c in range(4):
            nkt = 4 * c + 4
            k.s_v.wait(te)
            for kt in range(nkt):
                masked = kt >= 4 * c
                off = 128 * (kt - 4 * c) if masked else 0
                w = 512 - off
                te.wait_ge(k.s_s.h, k.s_s.n)
                for jj in range(4):
                    te.matmul(k.psA[:, jj, off:512], ones1b[32 * jj:32 * jj + 1, 0:128],
                              BT[32 * jj:32 * jj + 1, 512 * c + off:512 * c + 512],
                              start=True, stop=False, tile_position=(32 * jj, 0))
                    mm = te.matmul(k.psA[:, jj, off:512],
                                   KARR[32 * jj:32 * jj + 32, kvv, 128 * kt:128 * kt + 128],
                                   Q[32 * jj:32 * jj + 32, qt, 512 * c + off:512 * c + 512],
                                   start=False, stop=True, tile_position=(32 * jj, 0))
                    k.s_mm.inc(mm)
                k.s_mm.wait(se)
                k.s_v.wait(se)
                m01b2 = M01[:].unsqueeze(1).broadcast_to((128, 2, 128))
                for half in range(2):
                    jjs = (0, 1) if half == 0 else (2, 3)
                    se.activation(UT[:, jjs[0], off:512], k.psA[:, jjs[0], off:512], AF.Exp)
                    k.s_s.inc(se.activation(UT[:, jjs[1], off:512],
                                            k.psA[:, jjs[1], off:512], AF.Exp))
                    if masked:
                        k.s_s.wait(ve)
                        uview = UT[:, jjs[0]:jjs[0] + 2, off:off + 128]
                        k.s_v.inc(ve.tensor_tensor(uview, uview, m01b2, op=ALU.mult))
                    k.s_s.wait(te)
                    if masked:
                        k.s_v.wait(te)
                    for jj in jjs:
                        k.s_mm.inc(te.matmul(k.psB[0:65, jj, off:512], VT[:, kt, kvv, 0:65],
                                             UT[:, jj, off:512],
                                             start=(kt == 0), stop=(kt == nkt - 1)))
                if DEV_TAP == "ut" and qt == 0 and c == 3 and kt == 0:
                    full_barrier(k)
                    dump_fm(k, UT.rearrange("p a b -> p (a b)").unsqueeze(1), k.out_h, 1)
                    return
            k.s_mm.wait(ve)
            for jj in range(4):
                ve.tensor_copy(OE[0:65, jj, :], k.psB[0:65, jj, :])
            k.s_v.inc(ve.drain())
            k.s_v.wait(te)
            i = 0
            for cc in range(4):
                for jj in range(4):
                    if i >= 4:
                        te.wait_ge(k.s_v.h, k.s_v.n - 3)
                    pv = k.psA[:, i % 4, :].bitcast(BF16)[:, 0:65]
                    k.s_mm.inc(te.transpose(pv, OE[0:65, jj, 128 * cc:128 * cc + 128],
                                            k.idb[0:65, 0:65]))
                    k.s_mm.wait(ve)
                    k.s_v.inc(ve.tensor_copy(OTT[:, cc, jj, 0:65], pv))
                    i += 1
            if DEV_TAP == "ott" and qt == 0 and c == 0:
                full_barrier(k)
                dump_fm(k, OTT.rearrange("p a b c -> p (a b c)").unsqueeze(1), k.out_h, 1)
                return
            ve.drain()
            ve.tensor_copy(RC[:, 0:16].rearrange("p (b a) -> p b a", b=4),
                           OTT[:, :, :, 64:65].rearrange("p a b 1 -> p a b").transpose([0, 2, 1]))
            ve.drain()
            ve.reciprocal(RC[:, 0:16], RC[:, 0:16])
            ve.drain()
            for hh2 in range(2):
                hglob = 2 * qt + hh2
                j1, j2 = 2 * hh2, 2 * hh2 + 1
                r1 = RC[:, 4 * j1:4 * j1 + 4].unsqueeze(2).broadcast_to((128, 4, 64))
                ve.tensor_tensor(YC[:, :, hh2, :], OTT[:, :, j1, 0:64], r1, op=ALU.mult)
                ve.tensor_scalar(RC[:, 16 + 4 * hh2:20 + 4 * hh2], RC[:, 4 * j2:4 * j2 + 4],
                                 NLAM[:, hglob:hglob + 1], None, op0=ALU.mult)
            ve.drain()
            for hh2 in range(2):
                j2 = 2 * hh2 + 1
                c2b = RC[:, 16 + 4 * hh2:20 + 4 * hh2].unsqueeze(2).broadcast_to((128, 4, 64))
                ve.tensor_tensor(OTT[:, :, j2, 0:64], OTT[:, :, j2, 0:64], c2b, op=ALU.mult)
            ve.drain()
            ve.tensor_tensor(YC[:], YC[:], OTT[:, :, 1::2, 0:64], op=ALU.add)
            k.s_v.inc(ve.drain())
            if DEV_TAP == "yc" and qt == 0 and c == 0:
                full_barrier(k)
                dump_fm(k, YC.rearrange("p a b c -> p (a b c)").unsqueeze(1), k.out_h, 1)
                return
            if DEV_TAP == "rc" and qt == 0 and c == 0:
                full_barrier(k)
                dump_fm(k, RC.bitcast(BF16)[:, 0:48].unsqueeze(1), k.out_h, 1)
                return
            k.s_v.wait(se)
            se.activation(BTS[:, 0:512], YC[:].rearrange("p a b d -> p (a b d)"), AF.Square)
            k.s_s.inc(se.drain())
            k.s_s.wait(ve)
            ve.tensor_reduce(RC[:, 0:8], BTS[:, 0:512].rearrange("p (a d) -> p a d", d=64),
                             axis=AX.X, op=ALU.add)
            ve.drain()
            ve.tensor_scalar(RC[:, 0:8], RC[:, 0:8], 1.0 / 64.0, 1e-6, op0=ALU.mult, op1=ALU.add)
            ve.drain()
            ve.reciprocal(RC[:, 0:8], RC[:, 0:8])
            k.s_v.inc(ve.drain())
            k.s_v.wait(se)
            se.activation(RC[:, 0:8], RC[:, 0:8], AF.Sqrt)
            k.s_s.inc(se.drain())
            k.s_s.wait(ve)
            rib = RC[:, 0:8].rearrange("p (a b) -> p a b", b=2).unsqueeze(3) \
                .broadcast_to((128, 4, 2, 64))
            ve.tensor_tensor(YC[:], YC[:], rib, op=ALU.mult)
            ve.drain()
            wrb = WROW.unsqueeze(1).unsqueeze(1).broadcast_to((128, 4, 2, 64))
            ve.tensor_tensor(YC[:], YC[:], wrb, op=ALU.mult)
            k.s_v.inc(ve.drain())
            if DEV_TAP == "yn" and qt == 0 and c == 0:
                full_barrier(k)
                dump_fm(k, YC.rearrange("p a b c -> p (a b c)").unsqueeze(1), k.out_h, 1)
                return
            k.s_v.wait(te)
            for cc in range(4):
                pv = k.psA[:, cc, :].bitcast(BF16)[:, 0:128]
                k.s_mm.inc(te.transpose(pv, YC[:, cc, :, :].rearrange("p a b -> p (a b)"),
                                        k.idb[:]))
            k.s_mm.wait(ve)
            for cc in range(4):
                ve.tensor_copy(Q[:, qt, 512 * c + 128 * cc:512 * c + 128 * cc + 128],
                               k.psA[:, cc, :].bitcast(BF16)[:, 0:128])
            k.s_v.inc(ve.drain())
            if DEV_TAP in ("y5", "y6", "y7", "y8", "y9", "yz") and qt == {"y5":2,"y6":3,"y7":4,"y8":5,"y9":6,"yz":7}[DEV_TAP] and c == 3:
                dump_fm(k, Q, k.out_h, 8)
                return
            if DEV_TAP == "y3" and qt == 0 and c == 3:
                dump_fm(k, Q, k.out_h, 8)
                return
            if DEV_TAP == "y4" and qt == 1 and c == 3:
                dump_fm(k, Q, k.out_h, 8)
                return
            if DEV_TAP == "y2" and qt == 0 and c == 0:
                dump_fm(k, Q, k.out_h, 8)
                return
            if DEV_TAP == "y1" and qt == 0 and c == 0:
                full_barrier(k)
                dump_fm(k, Q[:, 0:1, 0:512].rearrange("p a b -> p (a b)").unsqueeze(1), k.out_h, 1)
                return

    full_barrier(k)
    ar.release()
    ar.release()
    ar.mark()
    X1T = ar.f32(2, D)
    WOB = ar.bf(8, D)
    k.s_lg.inc(ge.dma_start(out=WOB, in_=h["woT"].rearrange("(a p) o -> p a o", p=128)), 16)
    k.s_ld.wait(te)
    k.s_lg.wait(te)
    x1r = h["x1_s"].rearrange("(j p) f -> p j f", p=128)
    outr2 = out_h.rearrange("(j p) f -> p j f", p=128)
    for j in range(NT):
        k.s_st.wait(ve)
        for oc in range(2):
            pb = k.psB[:, 2 * (j % 2) + oc, :]
            te.wait_ge(k.s_v.h, max(0, k.s_v.n - 3))
            for ft in range(8):
                mm = te.matmul(pb, Q[:, ft, 128 * j:128 * j + 128],
                               WOB[:, ft, 512 * oc:512 * oc + 512],
                               start=(ft == 0), stop=(ft == 7))
            k.s_mm.inc(mm)
            k.s_mm.wait(ve)
            k.s_v.inc(ve.tensor_copy(X1T[:, j % 2, 512 * oc:512 * oc + 512], pb))
        k.s_v.wait(sy)
        k.s_st.inc(sy.dma_start(out=x1r[:, j, :], in_=X1T[:, j % 2, :]), 16)
        if DEV_STAGE == "l0attn":
            k.s_st.inc(sy.dma_start(out=outr2[:, j, :], in_=X1T[:, j % 2, :]), 16)
    if DEV_STAGE == "l0attn":
        k.s_st.wait(sy)
        return
    full_barrier(k)
    ar.release()
    ar.release()
    emit_l0_ffn(k, h, out_h)


def emit_l0_wquant(k, h, WLD, WQB, STG):
    te, ve, se, ge, sy = k.te, k.ve, k.se, k.ge, k.sy
    specs = [(h["wg0"], h["wqgT_s"], 22, 1024, 2),
             (h["wu0"], h["wquT_s"], 22, 1024, 2),
             (h["wd0"], h["wdqT_s"], 8, 2816, 1)]
    for src, dst, nrt, W, PR in specs:
        srcr = src.rearrange("(r p) w -> p r w", p=128)
        dstr = dst.rearrange("(bb p) o -> p bb o", p=128)
        nb = W // 128
        for r0 in range(0, nrt, PR):
            WT = PR * W
            k.s_v.wait(sy)
            k.s_g.wait(sy)
            k.s_la.inc(sy.dma_start(out=WLD[:, 0:WT].rearrange("p (r w) -> p r w", r=PR),
                                    in_=srcr[:, r0:r0 + PR, :]), 16)
            k.s_la.wait(ve)
            k.s_la.wait(ge)
            emit_quant(k, WLD[:, 0:WT], WQB[:, 0:WT])
            k.s_g.wait(te)
            k.s_v.wait(te)
            k.s_st.wait(ve)
            pairs = [(WQB[:, 128 * b:128 * b + 128], STG[:, 128 * b:128 * b + 128])
                     for b in range(PR * nb)]
            emit_transposes(k, pairs, k.idb[:], deep=True)
            k.s_mm.wait(ve)
            k.s_mm.wait(ge)
            k.s_v.wait(sy)
            for rr in range(PR):
                k.s_st.inc(sy.dma_start(
                    out=dstr[:, 0:nb, 128 * (r0 + rr):128 * (r0 + rr) + 128],
                    in_=STG[:, rr * W:(rr + 1) * W].rearrange("p (bb c) -> p bb c", c=128)), 16)


def emit_l0_ffn(k, h, out_h):
    te, ve, se, ge, sy = k.te, k.ve, k.se, k.ge, k.sy
    ar = k.ar
    XQF = k.arXQF
    ar.mark()
    WGS = ar.bf(8, 512)
    WUS = ar.bf(8, 512)
    WDS = ar.bf(22, 1024)
    HQ = [ar.bf(22, 128) for _ in range(4)]
    X1T = ar.f32(2, D)
    wds_src = h["wdqT_s"].rearrange("(c p) o -> p c o", p=128)
    k.s_ld.inc(sy.dma_start(out=WDS, in_=wds_src), 16)
    wg_src = h["wqgT_s"].rearrange("(c p) o -> p c o", p=128)
    wu_src = h["wquT_s"].rearrange("(c p) o -> p c o", p=128)
    xr = h["x_in"].rearrange("(j p) f -> p j f", p=128)
    x1r = h["x1_s"].rearrange("(j p) f -> p j f", p=128)
    outr = out_h.rearrange("(j p) f -> p j f", p=128)
    HGF = ar.f32(2048)
    HQB4 = ar.bf(2048)

    for jg in range(4):
        k.s_mm.wait(ve)
        k.s_mm.wait(se)
        for oc in range(6):
            ocw = 512 if oc < 5 else 256
            k.s_mm.wait(sy)
            k.s_la.inc(sy.dma_start(out=WGS[:, :, 0:ocw], in_=wg_src[:, :, 512 * oc:512 * oc + ocw]), 16)
            k.s_lb.inc(sy.dma_start(out=WUS[:, :, 0:ocw], in_=wu_src[:, :, 512 * oc:512 * oc + ocw]), 16)
            k.s_la.wait(te)
            k.s_lb.wait(te)
            for j4 in range(4):
                j = 4 * jg + j4
                te.wait_ge(k.s_s.h, k.s_s.n)
                te.wait_ge(k.s_v.h, k.s_v.n)
                for ic in range(8):
                    te.matmul(k.psA[:, 0, 0:ocw], XQF[:, ic, 128 * j:128 * j + 128],
                              WGS[:, ic, 0:ocw], start=(ic == 0), stop=(ic == 7))
                for ic in range(8):
                    mm = te.matmul(k.psA[:, 1, 0:ocw], XQF[:, ic, 128 * j:128 * j + 128],
                                   WUS[:, ic, 0:ocw], start=(ic == 0), stop=(ic == 7))
                k.s_mm.inc(mm)
                k.s_mm.wait(se)
                k.s_s.inc(se.activation(HGF[:, ocw * j4:ocw * j4 + ocw],
                                        k.psA[:, 0, 0:ocw], AF.Silu))
                k.s_s.wait(ve)
                k.s_v.inc(ve.tensor_tensor(HGF[:, ocw * j4:ocw * j4 + ocw],
                                           HGF[:, ocw * j4:ocw * j4 + ocw],
                                           k.psA[:, 1, 0:ocw], op=ALU.mult))
            ve.drain()
            emit_quant(k, HGF[:, 0:4 * ocw], HQB4[:, 0:4 * ocw])
            k.s_g.wait(te)
            k.s_v.wait(te)
            pairs = [(HQB4[:, ocw * j4 + 128 * b:ocw * j4 + 128 * b + 128],
                      HQ[j4][:, 4 * oc + b, :])
                     for j4 in range(4) for b in range(ocw // 128)]
            emit_transposes(k, pairs, k.idb[:], deep=True)
            k.s_mm.wait(ve)
            k.s_mm.wait(ge)
            k.s_mm.wait(se)
        k.s_ld.wait(te)
        for j4 in range(4):
            j = 4 * jg + j4
            k.s_st.wait(sy)
            k.s_lc.inc(sy.dma_start(out=X1T[:, j4 % 2, :], in_=x1r[:, j, :]), 16)
            for dc in range(2):
                te.wait_ge(k.s_v.h, k.s_v.n)
                for c in range(22):
                    mm = te.matmul(k.psB[:, dc, :], HQ[j4][:, c, :],
                                   WDS[:, c, 512 * dc:512 * dc + 512],
                                   start=(c == 0), stop=(c == 21))
                k.s_mm.inc(mm)
            k.s_mm.wait(ve)
            k.s_lc.wait(ve)
            for dc in range(2):
                ve.tensor_tensor(X1T[:, j4 % 2, 512 * dc:512 * dc + 512],
                                 X1T[:, j4 % 2, 512 * dc:512 * dc + 512],
                                 k.psB[:, dc, :], op=ALU.add)
            k.s_v.inc(ve.drain())
            k.s_v.wait(ge)
            k.s_lg.inc(ge.dma_start(out=X1T[:, j4 % 2, :], in_=xr[:, j, :], accum_op=ALU.add), 16)
            k.s_lg.wait(sy)
            k.s_st.inc(sy.dma_start(out=x1r[:, j, :], in_=X1T[:, j4 % 2, :]), 16)
            if DEV_STAGE == "l0":
                k.s_st.inc(sy.dma_start(out=outr[:, j, :], in_=X1T[:, j4 % 2, :]), 16)
    full_barrier(k)
    ar.release()
    if DEV_STAGE == "l0":
        k.s_st.wait(sy)
        return
    ar.release()
    emit_l1(k, h, out_h)


def emit_l1(k, h, out_h):
    te, ve, se, ge, sy = k.te, k.ve, k.se, k.ge, k.sy
    ar = k.ar

    XINT = ar.bf(NT, D)
    ar.mark()
    XNF = ar.bf(8, T)
    XTa = ar.f32(2, D)

    emit_ln_stats(k, h["x1_s"], XTa, k.qA[:, 0:2 * D])
    xr = h["x1_s"].rearrange("(j p) f -> p j f", p=128)
    ssum = k.stats[:, 0:NT]
    inv = k.stats[:, 2 * NT:3 * NT]
    sems = (k.s_la, k.s_lb)
    for j in range(min(2, NT)):
        sems[j % 2].inc(sy.dma_start(out=XTa[:, j % 2, :], in_=xr[:, j, :]), 16)
    for j in range(NT):
        xt = XTa[:, j % 2, :]
        sems[j % 2].wait(ve)
        k.s_s.wait(ve)
        ve.tensor_scalar(XINT[:, j, :], xt, ssum[:, j:j + 1], inv[:, j:j + 1],
                         op0=ALU.subtract, op1=ALU.mult)
        k.s_v.inc(ve.drain())
        if j + 2 < NT:
            k.s_v.wait(sy)
            sems[j % 2].inc(sy.dma_start(out=XTa[:, j % 2, :], in_=xr[:, j + 2, :]), 16)
        k.s_v.wait(te)
        pairs = [(XINT[:, j, 128 * fb:128 * fb + 128], XNF[:, fb, 128 * j:128 * j + 128])
                 for fb in range(8)]
        emit_transposes(k, pairs, k.idb[:], deep=True)
        k.s_mm.wait(ve)

    if DEV_STAGE == "l1ln":
        dump_xint(k, out_h, XINT)
        return

    TH = ar.bf(2, T)
    PHI = ar.bf(NT, 256)
    GT = ar.bf(NT, D)
    MB = ar.bf(2, D)
    WTH = ar.bf(8, 256)
    WPH = ar.bf(8, 256)
    WG = ar.bf(8, D)
    k.s_lg.inc(ge.dma_start(out=WTH, in_=h["wthT"].rearrange("(a p) o -> p a o", p=128)), 16)
    k.s_lg.inc(ge.dma_start(out=WPH, in_=h["wphT"].rearrange("(a p) o -> p a o", p=128)), 16)
    k.s_lg.inc(ge.dma_start(out=WG, in_=h["wgT"].rearrange("(a p) o -> p a o", p=128)), 16)
    full_barrier(k)

    for oc in range(2):
        for tc in range(4):
            pb = k.psA[:, tc % 4, :]
            te.wait_ge(k.s_s.h, max(0, k.s_s.n - 3))
            for ic in range(8):
                mm = te.matmul(pb, WTH[:, ic, 128 * oc:128 * oc + 128],
                               XNF[:, ic, 512 * tc:512 * tc + 512],
                               start=(ic == 0), stop=(ic == 7))
            k.s_mm.inc(mm)
            k.s_mm.wait(se)
            k.s_s.inc(se.activation(TH[:, oc, 512 * tc:512 * tc + 512], pb, AF.Copy, bias=0.0))
    def tproj(W, dst, ncol):
        for j in range(NT):
            noc = (ncol + 511) // 512
            for oc in range(noc):
                ow = min(512, ncol - 512 * oc)
                pb = k.psB[:, (j * noc + oc) % 4, :]
                te.wait_ge(k.s_s.h, max(0, k.s_s.n - 3))
                for ic in range(8):
                    mm = te.matmul(pb[:, 0:ow], XNF[:, ic, 128 * j:128 * j + 128],
                                   W[:, ic, 512 * oc:512 * oc + ow],
                                   start=(ic == 0), stop=(ic == 7))
                k.s_mm.inc(mm)
                k.s_mm.wait(se)
                k.s_s.inc(se.activation(dst[:, j, 512 * oc:512 * oc + ow], pb[:, 0:ow],
                                        AF.Copy, bias=0.0))
    tproj(WPH, PHI, 256)
    tproj(WG, GT, 1024)
    full_barrier(k)

    for lc in range(2):
        for ocg in range(2):
            pb = k.psA[:, 2 * lc + ocg, :]
            for j in range(NT):
                mm = te.matmul(pb, PHI[:, j, 128 * lc:128 * lc + 128],
                               GT[:, j, 512 * ocg:512 * ocg + 512],
                               start=(j == 0), stop=(j == NT - 1))
            k.s_mm.inc(mm)
            k.s_mm.wait(ve)
            k.s_v.inc(ve.tensor_scalar(MB[:, lc, 512 * ocg:512 * ocg + 512], pb,
                                       0.03125, None, op0=ALU.mult))
    k.s_v.wait(te)
    X1T = XTa
    x1r2 = h["x1_s"].rearrange("(j p) f -> p j f", p=128)
    for j in range(NT):
        k.s_st.wait(sy)
        k.s_lc.inc(sy.dma_start(out=X1T[:, j % 2, :], in_=x1r2[:, j, :]), 16)
        te.wait_ge(k.s_v.h, max(0, k.s_v.n - 1))
        for oc in range(2):
            pb = k.psB[:, 2 * (j % 2) + oc, :]
            for lc in range(2):
                mm = te.matmul(pb, TH[:, lc, 128 * j:128 * j + 128],
                               MB[:, lc, 512 * oc:512 * oc + 512],
                               start=(lc == 0), stop=(lc == 1))
            k.s_mm.inc(mm)
        k.s_mm.wait(ve)
        k.s_lc.wait(ve)
        for oc in range(2):
            ve.tensor_tensor(X1T[:, j % 2, 512 * oc:512 * oc + 512],
                             X1T[:, j % 2, 512 * oc:512 * oc + 512],
                             k.psB[:, 2 * (j % 2) + oc, :], op=ALU.add)
        k.s_v.inc(ve.drain())
        k.s_v.wait(sy)
        k.s_st.inc(sy.dma_start(out=x1r2[:, j, :], in_=X1T[:, j % 2, :]), 16)
    full_barrier(k)
    ar.release()

    if DEV_STAGE == "l1attn":
        ar.mark()
        XTb = ar.f32(2, D)
        outr = out_h.rearrange("(j p) f -> p j f", p=128)
        for j in range(NT):
            k.s_la.inc(sy.dma_start(out=XTb[:, j % 2, :], in_=x1r2[:, j, :]), 16)
            k.s_la.wait(sy)
            k.s_st.inc(sy.dma_start(out=outr[:, j, :], in_=XTb[:, j % 2, :]), 16)
            k.s_st.wait(sy)
        ar.release()
        return

    emit_l1_ffn(k, h, out_h, XINT)


def emit_l1_ffn(k, h, out_h, XINT):
    nc = k.nc
    te, ve, se, ge, sy = k.te, k.ve, k.se, k.ge, k.sy
    ar = k.ar

    CQ = ar.bf(T)
    SQ = ar.bf(T)
    CK = ar.bf(T)
    SK = ar.bf(T)

    P64R = k.sb("P64R", [128, 128], BF16)
    M01L = k.sb("M01L", [128, 128], BF16)
    PHB = k.sb("PHB", [128, 2], BF16)
    E2B = k.sb("E2B", [2, 128], BF16)
    E2Q = k.sb("E2Q", [2, 128], BF16)
    E2K = k.sb("E2K", [2, 128], BF16)
    QSR = k.sb("QSR", [2, 256], BF16)
    LNGC = k.sb("LNGC", [128, 8], F32)
    LNBC = k.sb("LNBC", [128, 8], F32)
    B1AC = k.sb("B1AC", [128, 22], F32)
    B1GC = k.sb("B1GC", [128, 22], F32)
    st4 = nc.dram_tensor("st4", [4, 2049], F32, kind="Internal").ap()

    full_barrier(k)

    PM32 = k.smallc[:, 8:9]
    LNBc = k.smallc[:, 9:10]
    INVF = k.smallc[:, 10:11]
    ZCOL = k.smallc[:, 15:16]
    ve.memset(ZCOL, 0.0)
    ve.tensor_scalar(PM32, k.pcol[:], -15.5, 0.03125, op0=ALU.add, op1=ALU.mult)
    ve.drain()
    ve.tensor_scalar(PM32, PM32, k.mcol[:, 0:1], k.mcol[:, 0:1], op0=ALU.add, op1=ALU.subtract)
    ve.drain()
    ve.scalar_tensor_tensor(PM32, PM32, -32.0, k.pcol[:], op0=ALU.mult, op1=ALU.add)
    ve.drain()
    ve.tensor_scalar(INVF, PM32, -0.03125, None, op0=ALU.mult)
    ve.drain()
    ve.tensor_scalar(INVF, INVF, k.ln4, None, op0=ALU.mult)
    ve.tensor_scalar(LNBc, PM32, 2.0, 25.6, op0=ALU.mult, op1=ALU.add)
    ve.drain()
    ve.tensor_scalar(LNBc, LNBc, 1.0 / 89.6, None, op0=ALU.mult)
    k.s_v.inc(ve.drain())
    k.s_v.wait(se)
    se.activation(INVF, INVF, AF.Exp)
    se.activation(LNBc, LNBc, AF.Ln)
    k.s_s.inc(se.drain())

    DIF = k.qA[:, 0:128]
    CM64 = k.qA[:, 128:256]
    S1 = k.qA[:, 256:384]
    S2 = k.qA[:, 384:512]
    k.s_s.wait(ge)
    ge.iota(DIF, pattern=[[1, 128]], base=0, channel_multiplier=-1,
            allow_small_or_imprecise_dtypes=True)
    ge.iota(CM64, pattern=[[0, 2], [1, 64]], base=0, channel_multiplier=0,
            allow_small_or_imprecise_dtypes=True)
    k.s_g.inc(ge.drain())
    k.s_g.wait(ve)
    k.s_s.wait(ve)
    ve.tensor_scalar(P64R[:], DIF, 32.0, None, op0=ALU.is_equal)
    ve.tensor_scalar(S1, DIF, -32.0, None, op0=ALU.is_equal)
    ve.tensor_scalar(S2, CM64, 32.0, None, op0=ALU.is_ge)
    ve.tensor_scalar(M01L[:], DIF, 0.0, None, op0=ALU.is_le)
    ve.drain()
    ve.tensor_tensor(P64R[:], P64R[:], S2, op=ALU.mult)
    ve.drain()
    ve.tensor_scalar(S2, S2, -1.0, 1.0, op0=ALU.mult, op1=ALU.add)
    ve.drain()
    ve.tensor_tensor(S1, S1, S2, op=ALU.mult)
    ve.drain()
    ve.tensor_tensor(P64R[:], P64R[:], S1, op=ALU.subtract)
    ve.memset(PHB[:], 0.0)
    ve.drain()
    ve.tensor_scalar(PHB[:, 0:1], k.pcol[:], 64.0, None, op0=ALU.is_lt)
    ve.tensor_scalar(PHB[:, 1:2], k.pcol[:], 64.0, None, op0=ALU.is_ge)
    k.s_v.inc(ve.drain())
    k.s_v.wait(te)
    pvE = k.psA[:, 0, :].bitcast(BF16)[0:2, 0:128]
    k.s_mm.inc(te.transpose(pvE, PHB[:], k.idb[:]))
    k.s_mm.wait(ve)
    k.s_v.inc(ve.tensor_copy(E2B[:], pvE))

    TG = k.qA[:, 0:T]
    ARG = k.qB[:, 0:T]
    RED = k.qC[:, 0:T]
    k.s_v.wait(ge)
    k.s_g.inc(ge.iota(TG, pattern=[[1, T]], base=0, channel_multiplier=0,
                      allow_small_or_imprecise_dtypes=True))
    k.s_g.wait(ve)

    def trig2(dst, use_hpi, pw_mult, finalmult):
        k.s_s.wait(ve)
        if use_hpi:
            ve.tensor_scalar(ARG, TG, INVF, k.hpi, op0=ALU.mult, op1=ALU.add)
        else:
            ve.tensor_scalar(ARG, TG, INVF, None, op0=ALU.mult)
        ve.drain()
        ve.tensor_scalar(RED, ARG, k.i2pi, None, op0=ALU.mult)
        ve.drain()
        ve.tensor_scalar(RED, RED, k.mcol[:, 0:1], k.mcol[:, 0:1],
                         op0=ALU.add, op1=ALU.subtract)
        ve.drain()
        ve.scalar_tensor_tensor(RED, RED, k.twopi, ARG, op0=ALU.mult, op1=ALU.subtract)
        ve.drain()
        ve.tensor_scalar(RED, RED, k.pin, k.pip, op0=ALU.max, op1=ALU.min)
        k.s_v.inc(ve.drain())
        k.s_v.wait(se)
        k.s_s.inc(se.activation(RED, RED, AF.Sin))
        k.s_s.wait(ve)
        ve.tensor_scalar(ARG, TG, -1024.0, pw_mult, op0=ALU.add, op1=ALU.mult)
        ve.drain()
        ve.tensor_scalar(ARG, ARG, LNBc, None, op0=ALU.mult)
        k.s_v.inc(ve.drain())
        k.s_v.wait(se)
        k.s_s.inc(se.activation(ARG, ARG, AF.Exp))
        k.s_s.wait(ve)
        ve.tensor_tensor(RED, RED, ARG, op=ALU.mult)
        ve.drain()
        ve.tensor_scalar(dst, RED, finalmult, None, op0=ALU.mult)
        k.s_v.inc(ve.drain())

    trig2(CQ, True, 0.00390625, 8.0)
    trig2(SQ, False, 0.00390625, 8.0)
    trig2(CK, True, -0.00390625, -1.0)
    trig2(SK, False, -0.00390625, -1.0)

    for l in range(2):
        emit_l1f_attn(k, h, out_h, XINT, l, CQ, SQ, CK, SK,
                      P64R, M01L, PHB, E2B, E2Q, E2K, QSR)
        if DEV_STAGE == f"l1f{l}a":
            dump_xint(k, out_h, XINT)
            return
        emit_l1f_mlp(k, h, out_h, XINT, l, LNGC, LNBC, B1AC, B1GC, st4)
        if DEV_STAGE == f"l1f{l}":
            dump_xint(k, out_h, XINT)
            return

    full_barrier(k)
    ar.mark()
    XO = ar.f32(2, D)
    x1r = h["x1_s"].rearrange("(j p) f -> p j f", p=128)
    outr = out_h.rearrange("(j p) f -> p j f", p=128)
    for j in range(NT):
        k.s_st.wait(sy)
        k.s_lc.inc(sy.dma_start(out=XO[:, j % 2, :], in_=x1r[:, j, :]), 16)
        k.s_lc.wait(ve)
        ve.tensor_tensor(XO[:, j % 2, :], XO[:, j % 2, :], XINT[:, j, :], op=ALU.add)
        k.s_v.inc(ve.drain())
        k.s_v.wait(sy)
        k.s_st.inc(sy.dma_start(out=outr[:, j, :], in_=XO[:, j % 2, :]), 16)
    k.s_st.wait(sy)
    k.s_st.wait(te)
    ar.release()


def dump_xint(k, out_h, XINT):
    full_barrier(k)
    outr = out_h.rearrange("(j p) f -> p j f", p=128)
    for j in range(NT):
        k.s_dmp.inc(k.ge.dma_start(out=outr[:, j, :], in_=XINT[:, j, :]), 16)
    k.s_dmp.wait(k.ge)
    k.s_dmp.wait(k.sy)


def emit_l1f_attn(k, h, out_h, XINT, l, CQ, SQ, CK, SK,
                  P64R, M01L, PHB, E2B, E2Q, E2K, QSR):
    te, ve, se, ge, sy = k.te, k.ve, k.se, k.ge, k.sy
    ar = k.ar
    M01 = k.M01

    full_barrier(k)
    ar.mark()
    QKVB = ar.bf(12, T)

    k.s_lg.inc(ge.dma_start(out=QSR[:, 0:128],
                            in_=_dc.replace(h["qs"][l], ap=[[0, 2], [1, 128]])), 16)
    k.s_lh.inc(ge.dma_start(out=QSR[:, 128:256],
                            in_=_dc.replace(h["ks"][l], ap=[[0, 2], [1, 128]])), 16)
    k.s_lg.wait(ve)
    k.s_lh.wait(ve)
    ve.tensor_tensor(E2Q[:], E2B[:], QSR[:, 0:128], op=ALU.mult)
    ve.tensor_tensor(E2K[:], E2B[:], QSR[:, 128:256], op=ALU.mult)
    k.s_v.inc(ve.drain())
    ge.wait_ge(k.s_v.h, k.s_v.n)

    ar.mark()
    XFM = ar.bf(8, 2056)
    WSL = ar.bf(2, 8, 128)
    ve.memset(XFM[:, :, 0:1], 0.0)
    k.s_v.inc(ve.drain())
    k.s_v.wait(te)
    for j in range(NT):
        pairs = [(XINT[:, j, 128 * fb:128 * fb + 128], XFM[:, fb, 1 + 128 * j:1 + 128 * j + 128])
                 for fb in range(8)]
        emit_transposes(k, pairs, k.idb[:], deep=True)
    k.s_mm.wait(ve)

    k.s_v.wait(te)
    wsrc = h["wqkv"][l].rearrange("(a p) o -> p a o", p=128)
    lastuse = [0, 0]
    sems = (k.s_lg, k.s_lh)
    for oc in range(12):
        b = oc % 2
        ge.wait_ge(k.s_mm.h, lastuse[b])
        sems[b].inc(ge.dma_start(out=WSL[:, b, :, :], in_=wsrc[:, :, 128 * oc:128 * oc + 128]), 16)
        sems[b].wait(te)
        for tc in range(4):
            pb = k.psA[:, tc % 4, :]
            te.wait_ge(k.s_s.h, max(0, k.s_s.n - 3))
            for ic in range(8):
                soff = 1 if ic < 4 else 0
                mm = te.matmul(pb, WSL[:, b, ic, :],
                               XFM[:, ic, soff + 512 * tc: soff + 512 * tc + 512],
                               start=(ic == 0), stop=(ic == 7))
            k.s_mm.inc(mm)
            k.s_mm.wait(se)
            k.s_s.inc(se.activation(QKVB[:, oc, 512 * tc:512 * tc + 512], pb, AF.Copy, bias=0.0))
        lastuse[b] = k.s_mm.n
    full_barrier(k)
    ar.release()

    if DEV_TAP == f"f{l}qkv":
        dump_fm(k, QKVB, out_h, 12)
        return

    VT1 = ar.bf(4, NT, 2, 68)
    UTT = ar.bf(2, 8, 512)
    UT = ar.bf(2, 640)
    OE = ar.bf(2, 512)
    OTT = ar.bf(4, 2, 72)
    YC = ar.bf(4, 2, 64)
    RC = ar.f32(16)
    WO1b = ar.bf(4, D)

    ve.memset(VT1[:], 1.0)
    k.s_v.inc(ve.drain())
    k.s_v.wait(te)
    base_v = k.s_v.n
    i = 0
    for t4 in range(4):
        for h2 in range(2):
            fr = 64 * h2
            for j in range(NT):
                if i >= 4:
                    te.wait_ge(k.s_v.h, base_v + i - 3)
                pv = k.psA[:, i % 4, :].bitcast(BF16)[:, 0:64]
                idsl = k.idb[fr:fr + 64, fr:fr + 64]
                k.s_mm.inc(te.transpose(pv, QKVB[fr:fr + 64, 8 + t4, 128 * j:128 * j + 128], idsl))
                k.s_mm.wait(ve)
                k.s_v.inc(ve.tensor_copy(VT1[:, t4, j, h2, 0:64], pv))
                i += 1

    SQB = k.qAb[:, 0:T]
    RT = k.qAb[:, T:2 * T]
    NRM = k.qB[0:2, 0:T]
    INVB = k.qCb[0:2, 0:T]

    def normrope(oc, COS, SIN, E2X):
        k.s_v.wait(se)
        k.s_g.wait(se)
        k.s_s.inc(se.activation(SQB, QKVB[:, oc, :], AF.Square))
        k.s_s.wait(te)
        k.s_v.wait(te)
        for cc in range(4):
            k.s_mm.inc(te.matmul(k.psB[0:2, cc, :], PHB[:], SQB[:, 512 * cc:512 * cc + 512],
                                 start=True, stop=True))
        k.s_mm.wait(se)
        for cc in range(4):
            se.activation(NRM[:, 512 * cc:512 * cc + 512], k.psB[0:2, cc, :], AF.Sqrt)
        k.s_s.inc(se.drain())
        k.s_s.wait(ve)
        ve.tensor_scalar(NRM[:], NRM[:], 1e-12, None, op0=ALU.max)
        ve.drain()
        ve.reciprocal(NRM[:], NRM[:])
        ve.drain()
        ve.tensor_copy(INVB[:], NRM[:])
        k.s_v.inc(ve.drain())
        k.s_v.wait(te)
        for cc in range(4):
            k.s_mm.inc(te.matmul(k.psA[:, cc, :], E2X[:], INVB[:, 512 * cc:512 * cc + 512],
                                 start=True, stop=True))
        k.s_mm.wait(ve)
        for cc in range(4):
            ve.tensor_tensor(QKVB[:, oc, 512 * cc:512 * cc + 512],
                             QKVB[:, oc, 512 * cc:512 * cc + 512], k.psA[:, cc, :], op=ALU.mult)
        k.s_v.inc(ve.drain())
        k.s_v.wait(te)
        for cc in range(4):
            k.s_mm.inc(te.matmul(k.psB[:, cc, :], P64R[:], QKVB[:, oc, 512 * cc:512 * cc + 512],
                                 start=True, stop=True))
        ve.tensor_tensor(RT, QKVB[:, oc, :], COS, op=ALU.mult)
        k.s_mm.wait(ve)
        for cc in range(4):
            ve.tensor_tensor(k.psB[:, cc, :], k.psB[:, cc, :],
                             SIN[:, 512 * cc:512 * cc + 512], op=ALU.mult)
        ve.drain()
        for cc in range(4):
            ve.tensor_tensor(QKVB[:, oc, 512 * cc:512 * cc + 512],
                             RT[:, 512 * cc:512 * cc + 512], k.psB[:, cc, :], op=ALU.add)
        k.s_v.inc(ve.drain())

    for t4 in range(4):
        normrope(t4, CQ, SQ, E2Q)
    for t4 in range(4):
        normrope(4 + t4, CK, SK, E2K)
    full_barrier(k, "tvs")
    if DEV_TAP == f"f{l}q":
        dump_fm(k, QKVB[:, 0:4, :], out_h, 4)
        return
    if DEV_TAP == f"f{l}k":
        dump_fm(k, QKVB[:, 4:8, :], out_h, 4)
        return

    NM = k.stats[:, 44:46]
    for t4 in range(4):
        for w in range(4):
            for lq in range(4):
                qc = 4 * w + lq
                kts = list(range(max(0, qc - 4), qc + 1))
                nblk = len(kts)
                wcols = 128 * nblk
                te.wait_ge(k.s_s.h, k.s_s.n)
                te.wait_ge(k.s_v.h, k.s_v.n)
                for jj in range(2):
                    lhs = QKVB[64 * jj:64 * jj + 64, t4, 128 * qc:128 * qc + 128]
                    kbase = 128 * kts[0]
                    w0 = min(512, wcols)
                    mm = te.matmul(k.psA[:, 2 * jj, 0:w0], lhs,
                                   QKVB[64 * jj:64 * jj + 64, 4 + t4, kbase:kbase + w0],
                                   start=True, stop=True, tile_position=(64 * jj, 0))
                    if wcols > 512:
                        mm = te.matmul(k.psA[:, 2 * jj + 1, 0:wcols - 512], lhs,
                                       QKVB[64 * jj:64 * jj + 64, 4 + t4,
                                            kbase + 512:kbase + wcols],
                                       start=True, stop=True, tile_position=(64 * jj, 0))
                    k.s_mm.inc(mm)
                k.s_mm.wait(ve)
                ve.wait_ge(k.s_s.h, k.s_s.n)
                ecnt = []
                for jj in range(2):
                    flat = k.psA[:, 2 * jj:2 * jj + 2, :].rearrange("p a b -> p (a b)")
                    k.s_v.inc(ve.tensor_reduce(NM[:, jj:jj + 1], flat[:, 0:wcols],
                                               axis=AX.X, op=ALU.min))
                    se.wait_ge(k.s_v.h, k.s_v.n)
                    flat2 = k.psA[:, 2 * jj:2 * jj + 2, :].rearrange("p a b -> p (a b)")
                    k.s_s.inc(se.activation(UT[:, jj, 0:wcols], flat2[:, 0:wcols], AF.Exp,
                                            bias=NM[:, jj:jj + 1], scale=-1.0))
                    ecnt.append(k.s_s.n)
                vb2 = k.s_v.n
                i2 = 0
                for jj in range(2):
                    te.wait_ge(k.s_s.h, ecnt[jj])
                    for b, kt in enumerate(kts):
                        ktr = kt - 4 * w + 4
                        if i2 >= 2:
                            te.wait_ge(k.s_v.h, vb2 + i2 - 1)
                        pvb = k.psB[:, 2 + i2 % 2, :].bitcast(BF16)[:, 0:128]
                        k.s_mm.inc(te.transpose(pvb, UT[:, jj, 128 * b:128 * b + 128], k.idb[:]))
                        k.s_mm.wait(ve)
                        dst = UTT[:, jj, ktr, 128 * lq:128 * lq + 128]
                        if b == 0 and qc >= 4:
                            k.s_v.inc(ve.tensor_tensor(dst, pvb, M01L[:], op=ALU.mult))
                        elif b == nblk - 1:
                            k.s_v.inc(ve.tensor_tensor(dst, pvb, M01[:], op=ALU.mult))
                        else:
                            k.s_v.inc(ve.tensor_copy(dst, pvb))
                        i2 += 1
            te.wait_ge(k.s_v.h, k.s_v.n)
            ktrs = [3, 4, 0, 1, 2, 5, 6, 7] if w > 0 else [4, 5, 6, 7]
            for jj in range(2):
                for idx, ktr in enumerate(ktrs):
                    kt = ktr + 4 * w - 4
                    c0 = 128 * max(0, ktr - 4)
                    c1 = 128 * (min(3, ktr) + 1)
                    mm = te.matmul(k.psB[0:65, jj, c0:c1], VT1[:, t4, kt, jj, 0:65],
                                   UTT[:, jj, ktr, c0:c1],
                                   start=(idx == 0), stop=(idx == len(ktrs) - 1))
                k.s_mm.inc(mm)
            k.s_mm.wait(ve)
            for jj in range(2):
                ve.tensor_copy(OE[0:65, jj, :], k.psB[0:65, jj, :])
            k.s_v.inc(ve.drain())
            k.s_v.wait(te)
            vb3 = k.s_v.n
            i3 = 0
            for cc in range(4):
                for jj in range(2):
                    if i3 >= 4:
                        te.wait_ge(k.s_v.h, vb3 + i3 - 3)
                    pv = k.psA[:, i3 % 4, :].bitcast(BF16)[:, 0:65]
                    k.s_mm.inc(te.transpose(pv, OE[0:65, jj, 128 * cc:128 * cc + 128],
                                            k.idb[0:65, 0:65]))
                    k.s_mm.wait(ve)
                    k.s_v.inc(ve.tensor_copy(OTT[:, cc, jj, 0:65], pv))
                    i3 += 1
            ve.drain()
            ve.tensor_copy(RC[:, 0:8].rearrange("p (b a) -> p b a", b=2),
                           OTT[:, :, :, 64:65].rearrange("p a b 1 -> p a b").transpose([0, 2, 1]))
            ve.drain()
            ve.reciprocal(RC[:, 0:8], RC[:, 0:8])
            ve.drain()
            for jj in range(2):
                rb = RC[:, 4 * jj:4 * jj + 4].unsqueeze(2).broadcast_to((128, 4, 64))
                ve.tensor_tensor(YC[:, :, jj, :], OTT[:, :, jj, 0:64], rb, op=ALU.mult)
            k.s_v.inc(ve.drain())
            k.s_v.wait(te)
            for cc in range(4):
                pv = k.psA[:, cc, :].bitcast(BF16)[:, 0:128]
                k.s_mm.inc(te.transpose(pv, YC[:, cc, :, :].rearrange("p a b -> p (a b)"),
                                        k.idb[:]))
            k.s_mm.wait(ve)
            for cc in range(4):
                ve.tensor_copy(QKVB[:, t4, 512 * w + 128 * cc:512 * w + 128 * cc + 128],
                               k.psA[:, cc, :].bitcast(BF16)[:, 0:128])
            k.s_v.inc(ve.drain())

    full_barrier(k, "tvs")
    k.s_lg.inc(ge.dma_start(out=WO1b, in_=h["wo1"][l].rearrange("(a p) o -> p a o", p=128)), 16)
    k.s_lg.wait(te)
    for j in range(NT):
        te.wait_ge(k.s_v.h, max(0, k.s_v.n - 1))
        for dc in range(2):
            pb = k.psB[:, 2 * (j % 2) + dc, :]
            for ft in range(4):
                mm = te.matmul(pb, QKVB[:, ft, 128 * j:128 * j + 128],
                               WO1b[:, ft, 512 * dc:512 * dc + 512],
                               start=(ft == 0), stop=(ft == 3))
            k.s_mm.inc(mm)
        k.s_mm.wait(ve)
        for dc in range(2):
            ve.tensor_tensor(XINT[:, j, 512 * dc:512 * dc + 512],
                             XINT[:, j, 512 * dc:512 * dc + 512],
                             k.psB[:, 2 * (j % 2) + dc, :], op=ALU.add)
        k.s_v.inc(ve.drain())
    full_barrier(k)
    ar.release()


def emit_l1f_mlp(k, h, out_h, XINT, l, LNGC, LNBC, B1AC, B1GC, st4):
    te, ve, se, ge, sy = k.te, k.ve, k.se, k.ge, k.sy
    ar = k.ar
    ar.mark()
    W2R = ar.bf(22, D)
    BCOL = ar.bf(8)
    ROWB = ar.bf(4, 2048)

    k.s_ld.inc(sy.dma_start(out=LNGC[:], in_=h["lngT"][l]), 16)
    k.s_ld.inc(sy.dma_start(out=LNBC[:], in_=h["lnbT"][l]), 16)
    k.s_ld.inc(sy.dma_start(out=B1AC[:], in_=h["b1ag"][l, 0]), 16)
    k.s_ld.inc(sy.dma_start(out=B1GC[:], in_=h["b1ag"][l, 1]), 16)
    w2src = h["w2"][l]
    k.s_lg.inc(ge.dma_start(out=W2R[:, 0:21, :],
                            in_=w2src[0:2688].rearrange("(a p) o -> p a o", p=128)), 16)
    k.s_lg.inc(ge.dma_start(out=W2R[0:42, 21, :], in_=w2src[2688:2730]), 16)

    sA = k.qA[:, 0:16]
    sB = k.qA[:, 16:32]
    qA_ = k.qA[:, 32:48]
    qB_ = k.qA[:, 48:64]
    SCAs = (k.qC[:, 1024:1536], k.qC[:, 1536:2048])
    SCBs = (k.qC[:, 2048:2560], k.qB[:, 2048:2560])
    sbase = k.s_s.n
    for j in range(NT):
        ve.tensor_reduce(sA[:, j:j + 1], XINT[:, j, 0:512], axis=AX.X, op=ALU.add)
        ve.tensor_reduce(sB[:, j:j + 1], XINT[:, j, 512:1024], axis=AX.X, op=ALU.add)
        if j:
            se.wait_ge(k.s_s.h, sbase + j)
        se.activation(SCAs[j % 2], XINT[:, j, 0:512], AF.Square, accum_out=qA_[:, j:j + 1])
        se.activation(SCBs[j % 2], XINT[:, j, 512:1024], AF.Square, accum_out=qB_[:, j:j + 1])
        k.s_s.inc(se.drain())
    k.s_v.inc(ve.drain())
    k.s_s.inc(se.drain())
    k.s_v.wait(te)
    k.s_s.wait(te)
    STG16s = (k.qC[0:16, 2048:2176], k.qC[0:16, 2176:2304])
    for st in range(4):
        k.s_st.inc(sy.dma_start(out=st4[st, 0:1], in_=k.smallc[0:1, 15:16]), 16)
        pv = k.psA[0:16, st % 4, 0:128]
        k.s_mm.inc(te.transpose(pv, k.qA[:, 16 * st:16 * st + 16], k.idf[:]))
        k.s_mm.wait(ve)
        k.s_st.wait(ve)
        k.s_v.inc(ve.tensor_copy(STG16s[st % 2], pv))
        k.s_v.wait(sy)
        k.s_st.inc(sy.dma_start(out=st4[st, 1:2049].rearrange("(j c) -> j c", c=128),
                                in_=STG16s[st % 2]), 16)
    k.s_st.wait(sy)
    RA = k.qA[0:1, 0:2048]
    RQA = k.qB[0:1, 0:2048]
    RC_ = k.qC[0:1, 0:2048]
    IVB = ROWB[0:1, 0, :]
    CHB = ROWB[0:1, 1, :]
    CLB = ROWB[0:1, 2, :]
    B2H = ROWB[0:1, 3, 0:1024]
    B2L = ROWB[0:1, 3, 1024:2048]
    k.s_ld.inc(sy.dma_start(out=RA, in_=st4[0, 1:2049].unsqueeze(0)), 16)
    k.s_ld.inc(sy.dma_start(out=RC_, in_=st4[1, 0:2048].unsqueeze(0)), 16)
    k.s_ld.wait(ve)
    ve.tensor_tensor(RA, RA, RC_, op=ALU.add)
    k.s_v.inc(ve.drain())
    k.s_v.wait(sy)
    k.s_ld.inc(sy.dma_start(out=RQA, in_=st4[2, 1:2049].unsqueeze(0)), 16)
    k.s_ld.inc(sy.dma_start(out=RC_, in_=st4[3, 0:2048].unsqueeze(0)), 16)
    k.s_ld.wait(ve)
    ve.tensor_tensor(RQA, RQA, RC_, op=ALU.add)
    ve.drain()
    ve.tensor_scalar(RA, RA, 1.0 / D, None, op0=ALU.mult)
    ve.tensor_scalar(RQA, RQA, 1.0 / D, None, op0=ALU.mult)
    ve.drain()
    ve.tensor_tensor(RC_, RA, RA, op=ALU.mult)
    ve.drain()
    ve.tensor_tensor(RQA, RQA, RC_, op=ALU.subtract)
    ve.drain()
    ve.tensor_scalar(RQA, RQA, 1e-5, None, op0=ALU.add)
    k.s_v.inc(ve.drain())
    k.s_v.wait(se)
    k.s_s.inc(se.activation(RQA, RQA, AF.Sqrt))
    k.s_s.wait(ve)
    ve.reciprocal(RQA, RQA)
    ve.drain()
    ve.tensor_tensor(RC_, RA, RQA, op=ALU.mult)
    ve.drain()
    ve.tensor_scalar(RC_, RC_, -1.0, None, op0=ALU.mult)
    ve.drain()
    ve.tensor_copy(IVB, RQA)
    ve.tensor_copy(CHB, RC_)
    ve.drain()
    ve.tensor_tensor(CLB, RC_, CHB, op=ALU.subtract)
    k.s_v.inc(ve.drain())
    b2f = h["b2"][l].rearrange("a b -> (a b)")
    B2F = k.qA[0:1, 0:1024]
    k.s_v.wait(sy)
    k.s_ld.inc(sy.dma_start(out=B2F, in_=b2f.unsqueeze(0)), 16)
    k.s_ld.wait(ve)
    ve.tensor_copy(B2H, B2F)
    ve.drain()
    ve.tensor_tensor(B2L, B2F, B2H, op=ALU.subtract)
    k.s_v.inc(ve.drain())

    w1src = h["w1"][l].rearrange("(a p) o -> p a o", p=128)
    GH = k.qCb[:, 4096:4608]
    AT = k.qCb[:, 4608:5120]

    for hf in range(2):
        ar.mark()
        XHh = ar.bf(8, 1024)
        ar.mark()
        XFMh = ar.bf(8, 1032)
        full_barrier(k, "tvs")
        k.s_ld.wait(se)
        k.s_ld.wait(ve)
        k.s_lg.wait(te)
        if hf == 0:
            ve.memset(XFMh[:, :, 0:1], 0.0)
        else:
            ve.tensor_copy(XFMh[:, :, 0:1].rearrange("p a 1 -> p a"), BCOL)
        k.s_v.inc(ve.drain())
        k.s_v.wait(te)
        for jj2 in range(8):
            j = 8 * hf + jj2
            pairs = [(XINT[:, j, 128 * fb:128 * fb + 128],
                      XFMh[:, fb, 1 + 128 * jj2:1 + 128 * jj2 + 128]) for fb in range(8)]
            emit_transposes(k, pairs, k.idb[:], deep=True)
        k.s_mm.wait(ve)
        if hf == 0:
            ve.drain()
            k.s_v.inc(ve.tensor_copy(BCOL, XFMh[:, :, 1024:1025].rearrange("p a 1 -> p a")))
        for tc in range(2):
            gcol = 1024 * hf + 512 * tc
            te.wait_ge(k.s_v.h, k.s_v.n)
            te.wait_ge(k.s_s.h, k.s_s.n)
            k.s_mm.inc(te.matmul(k.psB[:, 2, :], k.ones1b[0:1, 0:128],
                                 IVB[0:1, gcol:gcol + 512], start=True, stop=True))
            te.matmul(k.psB[:, 3, :], k.ones1b[0:1, 0:128],
                      CHB[0:1, gcol:gcol + 512], start=True, stop=False)
            k.s_mm.inc(te.matmul(k.psB[:, 3, :], k.ones1b[0:1, 0:128],
                                 CLB[0:1, gcol:gcol + 512], start=False, stop=True))
            k.s_mm.wait(ve)
            for ft in range(8):
                soff = 1 if ft < 4 else 0
                ve.tensor_tensor(XHh[:, ft, 512 * tc:512 * tc + 512],
                                 XFMh[:, ft, soff + 512 * tc: soff + 512 * tc + 512],
                                 k.psB[:, 2, :], op=ALU.mult)
            ve.drain()
            for ft in range(8):
                ve.tensor_tensor(XHh[:, ft, 512 * tc:512 * tc + 512],
                                 XHh[:, ft, 512 * tc:512 * tc + 512],
                                 k.psB[:, 3, :], op=ALU.add)
            k.s_v.inc(ve.drain())
            k.s_v.wait(se)
            for ft in range(8):
                se.activation(XHh[:, ft, 512 * tc:512 * tc + 512],
                              XHh[:, ft, 512 * tc:512 * tc + 512], AF.Identity,
                              scale=LNGC[:, ft:ft + 1], bias=LNBC[:, ft:ft + 1])
            k.s_s.inc(se.drain())
            k.s_s.wait(ve)
            k.s_s.wait(te)
        full_barrier(k, "tvs")
        ar.release()
        AHq = ar.bf(22, 512)
        W1S = ar.bf(2, 2, 8, 128)

        for qt in range(2):
            toff = 512 * qt
            glast = [0, 0]
            wsems = (k.s_lg, k.s_lh)
            for hc in range(22):
                ocw = 128 if hc < 21 else 42
                b = hc % 2
                ge.wait_ge(k.s_mm.h, glast[b])
                wsems[b].inc(ge.dma_start(out=W1S[:, 0, b, :, 0:ocw],
                                          in_=w1src[:, :, 128 * hc:128 * hc + ocw]), 16)
                wsems[b].inc(ge.dma_start(out=W1S[:, 1, b, :, 0:ocw],
                                          in_=w1src[:, :, 2730 + 128 * hc:2730 + 128 * hc + ocw]), 16)
                wsems[b].wait(te)
                te.wait_ge(k.s_g.h, k.s_g.n)
                te.wait_ge(k.s_s.h, k.s_s.n)
                for ag in range(2):
                    for ic in range(8):
                        mm = te.matmul(k.psA[0:ocw, ag, :], W1S[:, ag, b, ic, 0:ocw],
                                       XHh[:, ic, toff:toff + 512],
                                       start=(ic == 0), stop=(ic == 7))
                    k.s_mm.inc(mm)
                glast[b] = k.s_mm.n
                k.s_mm.wait(se)
                k.s_g.wait(se)
                se.activation(GH[0:ocw, :], k.psA[0:ocw, 1, :], AF.Gelu,
                              bias=B1GC[0:ocw, hc:hc + 1])
                se.activation(AT[0:ocw, :], k.psA[0:ocw, 0, :], AF.Identity,
                              bias=B1AC[0:ocw, hc:hc + 1])
                k.s_s.inc(se.drain())
                k.s_s.wait(ge)
                k.s_g.inc(ge.tensor_tensor(AHq[0:ocw, hc, :], AT[0:ocw, :], GH[0:ocw, :],
                                           op=ALU.mult))
            k.s_g.wait(te)
            k.s_lg.wait(te)
            for tch in range(4):
                j = 4 * (2 * hf + qt) + tch
                te.wait_ge(k.s_v.h, max(0, k.s_v.n - 1))
                for dc in range(2):
                    pb = k.psB[:, 2 * (tch % 2) + dc, :]
                    te.matmul(pb, k.ones1b[0:1, 0:128], B2H[0:1, 512 * dc:512 * dc + 512],
                              start=True, stop=False)
                    te.matmul(pb, k.ones1b[0:1, 0:128], B2L[0:1, 512 * dc:512 * dc + 512],
                              start=False, stop=False)
                    for hc in range(22):
                        kcw = 128 if hc < 21 else 42
                        mm = te.matmul(pb, AHq[0:kcw, hc, 128 * tch:128 * tch + 128],
                                       W2R[0:kcw, hc, 512 * dc:512 * dc + 512],
                                       start=False, stop=(hc == 21))
                    k.s_mm.inc(mm)
                k.s_mm.wait(ve)
                for dc in range(2):
                    ve.tensor_tensor(XINT[:, j, 512 * dc:512 * dc + 512],
                                     XINT[:, j, 512 * dc:512 * dc + 512],
                                     k.psB[:, 2 * (tch % 2) + dc, :], op=ALU.add)
                k.s_v.inc(ve.drain())
        full_barrier(k)
        ar.release()
    ar.release()



def prep_host_tensors(inputs):
    t = {}
    perm = []
    for hh in range(16):
        for half in range(2):
            base = 64 * hh + 32 * half
            perm += [base + i for i in range(0, 32, 2)]
            perm += [base + i for i in range(1, 32, 2)]
    permq = np.array(perm, dtype=np.int64)
    kperm = permq[:256]
    Wq = np.asarray(inputs["L0_attn_Wq"], np.float32)
    Wkv = np.asarray(inputs["L0_attn_Wkv"], np.float32)
    t["wqT"] = np.ascontiguousarray(Wq.T[:, permq])
    wk = Wkv[0:256][kperm]
    wv = Wkv[256:512]
    t["wkvT"] = np.ascontiguousarray(np.concatenate([wk, wv], 0).T)
    t["woT"] = np.ascontiguousarray(np.asarray(inputs["L0_attn_Wout"], np.float32).T)
    t["lam"] = np.ascontiguousarray(np.asarray(inputs["L0_attn_lambda_init"], np.float32).reshape(16, 1))
    t["rmsw"] = np.ascontiguousarray(
        np.tile(np.asarray(inputs["L0_attn_rms_weight"], np.float32)[None, :], (128, 1)))
    t["wg0"] = np.ascontiguousarray(np.asarray(inputs["L0_ffn_w_gate"], np.float32))
    t["wu0"] = np.ascontiguousarray(np.asarray(inputs["L0_ffn_w_up"], np.float32))
    t["wd0"] = np.ascontiguousarray(np.asarray(inputs["L0_ffn_w_down"], np.float32))
    t["wthT"] = np.ascontiguousarray(np.asarray(inputs["L1_attn_W_theta"], np.float32).T)
    t["wphT"] = np.ascontiguousarray(np.asarray(inputs["L1_attn_W_phi"], np.float32).T)
    t["wgT"] = np.ascontiguousarray(np.asarray(inputs["L1_attn_W_g"], np.float32).T)
    t["wqkv"] = np.ascontiguousarray(np.asarray(inputs["L1_ffn_wqkv"], np.float32))
    t["qs"] = np.ascontiguousarray(np.tile(np.asarray(inputs["L1_ffn_q_scale"], np.float32), (1, 2)))
    t["ks"] = np.ascontiguousarray(np.tile(np.asarray(inputs["L1_ffn_k_scale"], np.float32), (1, 2)))
    t["wo1"] = np.ascontiguousarray(np.asarray(inputs["L1_ffn_wo"], np.float32))
    t["lngT"] = np.ascontiguousarray(np.asarray(inputs["L1_ffn_ln_g"], np.float32).reshape(2, 8, 128).transpose(0, 2, 1))
    t["lnbT"] = np.ascontiguousarray(np.asarray(inputs["L1_ffn_ln_b"], np.float32).reshape(2, 8, 128).transpose(0, 2, 1))
    t["w1"] = np.ascontiguousarray(np.asarray(inputs["L1_ffn_w1"], np.float32))
    b1 = np.asarray(inputs["L1_ffn_b1"], np.float32)
    b1p = np.zeros((2, 2, 2816), np.float32)
    b1p[:, 0, 0:2730] = b1[:, 0:2730]
    b1p[:, 1, 0:2730] = b1[:, 2730:5460]
    t["b1ag"] = np.ascontiguousarray(b1p.reshape(2, 2, 22, 128).transpose(0, 1, 3, 2))
    t["w2"] = np.ascontiguousarray(np.asarray(inputs["L1_ffn_w2"], np.float32))
    t["b2"] = np.ascontiguousarray(np.asarray(inputs["L1_ffn_b2"], np.float32).reshape(2, 8, 128))
    return t


def build_program():
    nc = bass.Bass("TRN2", target_bir_lowering=False)
    ctx = ExitStack()
    h = {}
    h["x_in"] = nc.dram_tensor("x_in", [T, D], F32, kind="ExternalInput").ap()
    out_h = nc.dram_tensor("out", [T, D], F32, kind="ExternalOutput").ap()
    for name, shape in [
        ("wqT", [D, D]), ("wkvT", [D, 512]), ("woT", [D, D]),
        ("lam", [16, 1]), ("rmsw", [128, 64]),
        ("wg0", [2816, D]), ("wu0", [2816, D]), ("wd0", [D, 2816]),
        ("wthT", [D, 256]), ("wphT", [D, 256]), ("wgT", [D, D]),
        ("wqkv", [2, D, 1536]), ("qs", [2, 128]), ("ks", [2, 128]),
        ("wo1", [2, 512, D]),
        ("w1", [2, D, 5460]), ("b1ag", [2, 2, 128, 22]), ("w2", [2, 2730, D]),
        ("b2", [2, 8, 128]), ("lngT", [2, 128, 8]), ("lnbT", [2, 128, 8]),
    ]:
        h[name] = nc.dram_tensor(name, shape, F32, kind="ExternalInput").ap()
    h["wqgT_s"] = nc.dram_tensor("wqgT_s", [D, 2816], BF16, kind="Internal").ap()
    h["wquT_s"] = nc.dram_tensor("wquT_s", [D, 2816], BF16, kind="Internal").ap()
    h["wdqT_s"] = nc.dram_tensor("wdqT_s", [2816, D], BF16, kind="Internal").ap()
    h["x1_s"] = nc.dram_tensor("x1_s", [T, D], F32, kind="Internal").ap()

    k = KB(nc, ctx)
    k.out_h = out_h
    blk = ctx.enter_context(nc.Block())
    k.psA = k.ps("psA", [128, 4, 512])
    k.psB = k.ps("psB", [128, 4, 512])
    k.qA = k.sb("qA", [128, 2816], F32)
    k.qB = k.sb("qB", [128, 2816], F32)
    k.qC = k.sb("qC", [128, 2816], F32)
    k.qAb = k.qA[:].bitcast(BF16)
    k.qBb = k.qB[:].bitcast(BF16)
    k.qCb = k.qC[:].bitcast(BF16)
    build_constants(k)
    k.ar = Arena(k, ARENA_N)
    k.ar.mark()
    k.arXQF = k.ar.bf(8, T)
    k.ar.mark()
    k.arQ = k.ar.bf(8, T)
    k.arK2 = k.ar.bf(2, T)
    k.arVT = k.ar.bf(NT, 4, 68)

    emit_l0(k, h, out_h)

    ctx.close()
    return nc


def kernel(**inputs):
    nb = 8
    x_full = np.ascontiguousarray(np.asarray(inputs["L0_attn_x"], np.float32))
    nc = build_program()
    host_tensors = prep_host_tensors(inputs)
    in_maps = []
    for c in range(nb):
        m = {"x_in": np.ascontiguousarray(x_full[c])}
        m.update(host_tensors)
        in_maps.append(m)
    res = run_bass_kernel_spmd(nc, in_maps, core_ids=list(range(nb)))
    return np.stack([r["out"] for r in res.results], axis=0)


# version 07
# speedup vs baseline: 1.6135x; 1.0059x over previous
import os
import dataclasses as _dc
from contextlib import ExitStack
import numpy as np
import ml_dtypes
import concourse.bass as bass
import concourse.mybir as mybir
from concourse.bass_utils import run_bass_kernel_spmd

F32 = mybir.dt.float32
BF16 = mybir.dt.bfloat16
AF = mybir.ActivationFunctionType
ALU = mybir.AluOpType
AX = mybir.AxisListType

T = 2048
NT = 16
D = 1024
MAGIC = 12582912.0
DEV_STAGE = os.environ.get("KDEV_STAGE", "full")
DEV_TAP = os.environ.get("KDEV_TAP", "")
ARENA_N = 85000


def _prod(xs):
    r = 1
    for x in xs:
        r *= x
    return r


class Sem:
    def __init__(self, nc, ctx, name):
        self.h = ctx.enter_context(nc.semaphore(name))
        self.n = 0

    def inc(self, instr, k=1):
        instr.then_inc(self.h, k)
        self.n += k
        return instr

    def wait(self, eng, val=None):
        eng.wait_ge(self.h, self.n if val is None else val)


class Arena:
    def __init__(self, k, n):
        self.t = k.nc.sbuf_tensor("ARENA", [128, n], BF16)
        self.th = k.ctx.enter_context(self.t)
        self.off = 0
        self.n = n
        self.stack = []

    def bf(self, *shape):
        m = _prod(shape)
        assert self.off + m <= self.n, f"arena overflow: {self.off}+{m} > {self.n}"
        ap = self.th[:, self.off:self.off + m]
        self.off += m
        if len(shape) > 1:
            names = " ".join(f"d{i}" for i in range(len(shape)))
            kw = {f"d{i}": s for i, s in enumerate(shape)}
            ap = ap.rearrange(f"p ({names}) -> p {names}", **kw)
        return ap

    def f32(self, *shape):
        m = _prod(shape)
        if self.off % 2:
            self.off += 1
        assert self.off + 2 * m <= self.n, f"arena overflow f32: {self.off}+{2*m} > {self.n}"
        ap = self.th[:, self.off:self.off + 2 * m].bitcast(F32)
        self.off += 2 * m
        if len(shape) > 1:
            names = " ".join(f"d{i}" for i in range(len(shape)))
            kw = {f"d{i}": s for i, s in enumerate(shape)}
            ap = ap.rearrange(f"p ({names}) -> p {names}", **kw)
        return ap

    def mark(self):
        self.stack.append(self.off)

    def release(self):
        self.off = self.stack.pop()


class KB:
    def __init__(self, nc, ctx):
        self.nc = nc
        self.ctx = ctx
        self.te, self.ve, self.se, self.ge, self.sy = \
            nc.tensor, nc.vector, nc.scalar, nc.gpsimd, nc.sync
        self.s_ld = Sem(nc, ctx, "s_ld")
        self.s_mm = Sem(nc, ctx, "s_mm")
        self.s_v = Sem(nc, ctx, "s_v")
        self.s_s = Sem(nc, ctx, "s_s")
        self.s_g = Sem(nc, ctx, "s_g")
        self.s_st = Sem(nc, ctx, "s_st")
        self.s_x = Sem(nc, ctx, "s_x")
        self.s_la = Sem(nc, ctx, "s_la")
        self.s_lb = Sem(nc, ctx, "s_lb")
        self.s_lg = Sem(nc, ctx, "s_lg")
        self.s_lh = Sem(nc, ctx, "s_lh")
        self.s_lc = Sem(nc, ctx, "s_lc")
        self.s_dmp = Sem(nc, ctx, "s_dmp")

    def sb(self, name, shape, dtype):
        return self.ctx.enter_context(self.nc.sbuf_tensor(name, shape, dtype))

    def ps(self, name, shape):
        return self.ctx.enter_context(self.nc.psum_tensor(name, shape, F32))


def dump_fm(k, buf, out_h, nt):
    full_barrier(k)
    ov = out_h.rearrange("(a p) f -> p a f", p=128)
    W = buf.shape[-1]
    for ft in range(nt):
        for hf in range(2):
            lo = 1024 * hf
            hi = min(lo + 1024, W)
            if lo >= hi:
                continue
            k.s_dmp.inc(k.ge.dma_start(out=ov[:, 2 * ft + hf, 0:hi - lo],
                                      in_=buf[:, ft, lo:hi]), 16)
    k.s_dmp.wait(k.ge)
    k.s_dmp.wait(k.sy)


def full_barrier(k, engines="tvsg", dma=True):
    s = k.s_x
    if "t" in engines:
        s.inc(k.te.drain())
    if "v" in engines:
        s.inc(k.ve.drain())
    if "s" in engines:
        s.inc(k.se.drain())
    if "g" in engines:
        s.inc(k.ge.drain())
    for e in (k.te, k.ve, k.se, k.ge, k.sy):
        e.wait_ge(s.h, s.n)
        if dma and engines == "tvsg":
            for sem in (k.s_ld, k.s_la, k.s_lb, k.s_lg, k.s_lh, k.s_lc, k.s_st, k.s_dmp):
                sem.wait(e)


def build_constants(k):
    k.mcol = k.sb("c_mcol", [128, 1], F32)
    k.rc6 = k.sb("c_rc6", [128, 1], F32)
    k.rc12 = k.sb("c_rc12", [128, 1], F32)
    k.idf = k.sb("c_idf", [128, 128], F32)
    k.idb = k.sb("c_idb", [128, 128], BF16)
    k.ones1b = k.sb("c_ones1b", [128, 128], BF16)
    k.M01 = k.sb("c_m01", [128, 128], BF16)
    k.P32 = k.sb("c_p32", [128, 128], BF16)
    k.P4 = k.sb("c_p4", [128, 128], BF16)
    k.qbm = k.sb("qbm", [128, 176], F32)
    k.qib6 = k.sb("qib6", [128, 176], F32)
    k.qbm12 = k.sb("qbm12", [128, 176], BF16)
    k.stats = k.sb("stats", [128, 48], F32)
    k.smallc = k.sb("smallc", [128, 16], F32)
    v, g = k.ve, k.ge
    v.memset(k.mcol[:], 1.0)
    v.drain()
    v.tensor_scalar(k.mcol[:], k.mcol[:], 2.0 ** -30, None, op0=ALU.add)
    v.drain()
    v.tensor_scalar(k.mcol[:], k.mcol[:], -1.0, None, op0=ALU.add)
    v.drain()
    v.tensor_scalar(k.mcol[:], k.mcol[:], 1.5 * 2.0 ** 82, MAGIC,
                    op0=ALU.mult, op1=ALU.add)
    v.memset(k.rc6[:], 6.0)
    v.memset(k.rc12[:], 12.0)
    v.memset(k.ones1b[:], 1.0)
    v.drain()
    v.reciprocal(k.rc6[:], k.rc6[:])
    v.reciprocal(k.rc12[:], k.rc12[:])
    k.s_g.inc(g.iota(k.idf[:], pattern=[[1, 128]], base=0, channel_multiplier=-1,
                     allow_small_or_imprecise_dtypes=True))
    k.s_g.wait(v)
    v.tensor_scalar(k.idb[:], k.idf[:], 0.0, None, op0=ALU.is_equal)
    v.tensor_scalar(k.M01[:], k.idf[:], 0.0, None, op0=ALU.is_ge)
    v.drain()
    k.s_v.inc(v.tensor_scalar(k.idf[:], k.idf[:], 0.0, None, op0=ALU.is_equal))
    DIFq = k.qA[:, 0:128]
    CM32 = k.qB[:, 0:128]
    SC2 = k.qC[:, 4:260]
    g.iota(DIFq, pattern=[[1, 128]], base=0, channel_multiplier=-1,
           allow_small_or_imprecise_dtypes=True)
    g.iota(CM32, pattern=[[0, 4], [1, 32]], base=0, channel_multiplier=0,
           allow_small_or_imprecise_dtypes=True)
    k.s_g.inc(g.drain())
    k.s_g.wait(v)
    k.pcol = k.sb("c_pcol", [128, 1], F32)
    k.pm16 = k.sb("c_pm16", [128, 1], F32)
    k.altc = k.sb("c_altc", [128, 1], F32)
    v.tensor_scalar(k.pcol[:], DIFq[:, 0:1], -1.0, None, op0=ALU.mult)
    v.drain()
    v.tensor_scalar(k.pm16[:], k.pcol[:], -7.5, 0.0625, op0=ALU.add, op1=ALU.mult)
    v.drain()
    v.tensor_scalar(k.pm16[:], k.pm16[:], k.mcol[:, 0:1], k.mcol[:, 0:1],
                    op0=ALU.add, op1=ALU.subtract)
    v.drain()
    v.scalar_tensor_tensor(k.pm16[:], k.pm16[:], -16.0, k.pcol[:],
                           op0=ALU.mult, op1=ALU.add)
    v.drain()
    v.tensor_scalar(k.altc[:], k.pcol[:], -15.5, 0.03125, op0=ALU.add, op1=ALU.mult)
    v.drain()
    v.tensor_scalar(k.altc[:], k.altc[:], k.mcol[:, 0:1], k.mcol[:, 0:1],
                    op0=ALU.add, op1=ALU.subtract)
    v.drain()
    v.scalar_tensor_tensor(k.altc[:], k.altc[:], -32.0, k.pcol[:],
                           op0=ALU.mult, op1=ALU.add)
    v.drain()
    v.tensor_scalar(k.altc[:], k.altc[:], 16.0, None, op0=ALU.is_ge)
    v.drain()
    v.tensor_scalar(k.altc[:], k.altc[:], -2.0, 1.0, op0=ALU.mult, op1=ALU.add)
    v.drain()
    CMOD = k.qA[:, 128:256]
    g.iota(CMOD, pattern=[[0, 4], [1, 32]], base=0, channel_multiplier=0,
           allow_small_or_imprecise_dtypes=True)
    k.s_g.inc(g.drain())
    k.s_g.wait(v)
    v.tensor_scalar(k.P32[:], DIFq, 16.0, None, op0=ALU.is_equal)
    v.tensor_scalar(SC2[:, 0:128], DIFq, -16.0, None, op0=ALU.is_equal)
    v.tensor_scalar(SC2[:, 128:256], CMOD, 16.0, None, op0=ALU.is_ge)
    v.drain()
    v.tensor_tensor(k.P32[:], k.P32[:], SC2[:, 128:256], op=ALU.mult)
    v.drain()
    v.tensor_scalar(SC2[:, 128:256], SC2[:, 128:256], -1.0, 1.0, op0=ALU.mult, op1=ALU.add)
    v.drain()
    v.tensor_tensor(SC2[:, 0:128], SC2[:, 0:128], SC2[:, 128:256], op=ALU.mult)
    v.drain()
    v.tensor_tensor(k.P32[:], k.P32[:], SC2[:, 0:128], op=ALU.add)
    v.drain()
    v.tensor_scalar(k.P32[:], k.P32[:], k.altc[:, 0:1], None, op0=ALU.mult)
    v.drain()
    v.tensor_scalar(SC2[:, 0:128], DIFq, 0.0, None, op0=ALU.is_le)
    v.tensor_scalar(SC2[:, 128:256], DIFq, -32.0, None, op0=ALU.is_gt)
    v.drain()
    v.tensor_scalar(CM32, CM32, 0.0, None, op0=ALU.is_equal)
    v.tensor_tensor(k.P4[:], SC2[:, 0:128], SC2[:, 128:256], op=ALU.mult)
    v.drain()
    k.s_v.inc(v.tensor_tensor(k.P4[:], k.P4[:], CM32, op=ALU.mult))
    ln4, pi4, twopi, i2pi, hpi, sig, pip, pin = (k.smallc[:, i:i + 1] for i in range(8))
    v.memset(ln4, 10000.0)
    v.memset(pi4, 1.0)
    v.memset(sig, 32.0)
    k.s_v.inc(v.drain())
    k.s_v.wait(k.se)
    k.se.activation(ln4, ln4, AF.Ln)
    k.se.activation(pi4, pi4, AF.Arctan)
    k.s_s.inc(k.se.drain())
    k.s_s.wait(v)
    v.tensor_scalar(twopi, pi4, 8.0, None, op0=ALU.mult)
    v.tensor_scalar(hpi, pi4, 2.0, None, op0=ALU.mult)
    v.tensor_scalar(pip, pi4, 4.0, None, op0=ALU.mult)
    v.tensor_scalar(pin, pi4, -4.0, None, op0=ALU.mult)
    v.reciprocal(sig, sig)
    v.drain()
    v.reciprocal(i2pi, twopi)
    k.s_v.inc(v.drain())
    k.s_v.wait(k.se)
    k.s_s.inc(k.se.activation(sig, sig, AF.Sqrt))
    k.ln4, k.pi4, k.twopi, k.i2pi, k.hpi, k.sig = ln4, pi4, twopi, i2pi, hpi, sig
    k.pip, k.pin = pip, pin


def emit_quant(k, S, Q, ge_q=True):
    v, g = k.ve, k.ge
    k.s_g.wait(v)
    W = S.shape[-1]
    nb = W // 16
    A, B, C = k.qA, k.qB, k.qC
    Ab, Bb, Cb = k.qAb, k.qBb, k.qCb
    bm, ib6, bm12 = k.qbm, k.qib6, k.qbm12
    S3 = S.rearrange("p (a b) -> p a b", b=16)
    v.tensor_reduce(bm[:, 0:nb], S3, axis=AX.X, op=ALU.max, apply_absolute_value=True)
    v.drain()
    v.tensor_scalar(bm[:, 0:nb], bm[:, 0:nb], 6e-12, None, op0=ALU.max)
    v.drain()
    v.reciprocal(ib6[:, 0:nb], bm[:, 0:nb])
    v.tensor_scalar(bm12[:, 0:nb], bm[:, 0:nb], k.rc6[:, 0:1], None, op0=ALU.mult)
    v.drain()
    v.tensor_scalar(ib6[:, 0:nb], ib6[:, 0:nb], 6.0, None, op0=ALU.mult)
    v.drain()
    ib6b = ib6[:, 0:nb].unsqueeze(2).broadcast_to((128, nb, 16))
    v.tensor_tensor(A[:, 0:W].rearrange("p (a b) -> p a b", b=16), S3, ib6b, op=ALU.mult)
    v.drain()
    v.tensor_scalar(B[:, 0:W], A[:, 0:W], -2.0, 2.0, op0=ALU.max, op1=ALU.min)
    v.tensor_scalar(C[:, 0:W], A[:, 0:W], -4.0, 4.0, op0=ALU.max, op1=ALU.min)
    k.s_v.inc(v.drain())
    k.s_v.wait(g)
    k.s_g.inc(g.tensor_tensor(C[:, 0:W], C[:, 0:W], A[:, 0:W], op=ALU.add))
    k.s_g.wait(v)
    v.scalar_tensor_tensor(A[:, 0:W], C[:, 0:W], 0.5, B[:, 0:W], op0=ALU.mult, op1=ALU.add)
    v.drain()
    z = Bb[:, 0:W]
    v.tensor_scalar(z, A[:, 0:W], k.mcol[:, 0:1], k.mcol[:, 0:1],
                    op0=ALU.add, op1=ALU.subtract)
    v.drain()
    m1, c6 = Cb[:, 0:W], Ab[:, 0:W]
    z2, s1 = Ab[:, W:2 * W], Cb[:, W:2 * W]
    v.tensor_scalar(m1, z, 0.5, -2.0, op0=ALU.mult, op1=ALU.max)
    v.tensor_scalar(c6, z, -6.0, 6.0, op0=ALU.max, op1=ALU.min)
    v.tensor_scalar(z2, z, 2.0, None, op0=ALU.mult)
    v.drain()
    v.tensor_scalar(m1, m1, 2.0, None, op0=ALU.min)
    v.drain()
    v.tensor_tensor(s1, m1, c6, op=ALU.add)
    v.drain()
    psi = Cb[:, 0:W]
    v.tensor_tensor(psi, z2, s1, op=ALU.subtract)
    k.s_v.inc(v.drain())
    bm12b = bm12[:, 0:nb].unsqueeze(2).broadcast_to((128, nb, 16))
    Q3 = Q.rearrange("p (a b) -> p a b", b=16)
    psi3 = psi.rearrange("p (a b) -> p a b", b=16)
    if ge_q:
        k.s_v.wait(g)
        k.s_g.inc(g.tensor_tensor(Q3, psi3, bm12b, op=ALU.mult))
    else:
        v.tensor_tensor(Q3, psi3, bm12b, op=ALU.mult)
        k.s_v.inc(v.drain())


def emit_transposes(k, pairs, identity, deep=False):
    ve = k.ve
    if not hasattr(k, "_tp_i"):
        k._tp_i = 0
    nb_ = 8 if deep else 4
    for (src, dst) in pairs:
        k.te.wait_ge(k.s_v.h, max(0, k.s_v.n - (nb_ - 1)))
        i = k._tp_i % nb_
        pst = k.psA if i < 4 else k.psB
        pv = pst[:, i % 4, :].bitcast(BF16)[:, 0:dst.shape[-1]]
        k.s_mm.inc(k.te.transpose(pv, src, identity))
        k.s_mm.wait(ve)
        k.s_v.inc(ve.tensor_copy(dst, pv))
        k._tp_i += 1


def emit_ln_stats(k, xsrc_hbm, XTa, SCR, nt=NT):
    v, s, sy = k.ve, k.se, k.sy
    ssum = k.stats[:, 0:nt]
    ssq = k.stats[:, nt:2 * nt]
    inv = k.stats[:, 2 * nt:3 * nt]
    xr = xsrc_hbm.rearrange("(j p) f -> p j f", p=128)
    sems = (k.s_la, k.s_lb)
    for j in range(min(2, nt)):
        sems[j % 2].inc(sy.dma_start(out=XTa[:, j % 2, :], in_=xr[:, j, :]), 16)
    for j in range(nt):
        xt = XTa[:, j % 2, :]
        sems[j % 2].wait(v)
        sems[j % 2].wait(s)
        v.tensor_reduce(ssum[:, j:j + 1], xt, axis=AX.X, op=ALU.add)
        s.activation(SCR[:, (j % 2) * D:(j % 2) * D + D], xt, AF.Square,
                     accum_out=ssq[:, j:j + 1])
        if j + 2 < nt:
            k.s_s.inc(s.drain())
            k.s_v.inc(v.drain())
            k.s_s.wait(sy)
            k.s_v.wait(sy)
            sems[j % 2].inc(sy.dma_start(out=XTa[:, j % 2, :], in_=xr[:, j + 2, :]), 16)
    k.s_s.inc(s.drain())
    k.s_s.wait(v)
    v.drain()
    v.tensor_scalar(ssum[:], ssum[:], 1.0 / D, None, op0=ALU.mult)
    v.drain()
    v.tensor_tensor(inv[:], ssum[:], ssum[:], op=ALU.mult)
    v.tensor_scalar(ssq[:], ssq[:], 1.0 / D, None, op0=ALU.mult)
    v.drain()
    v.tensor_tensor(ssq[:], ssq[:], inv[:], op=ALU.subtract)
    v.drain()
    v.tensor_scalar(ssq[:], ssq[:], 1e-5, None, op0=ALU.add)
    v.drain()
    v.reciprocal(ssq[:], ssq[:])
    k.s_v.inc(v.drain())
    k.s_v.wait(s)
    s.activation(inv[:], ssq[:], AF.Sqrt)
    k.s_s.inc(s.drain())
    k.s_s.wait(v)
    k.s_s.wait(sy)
    k.s_v.wait(sy)



def emit_l0(k, h, out_h):
    te, ve, se, ge, sy = k.te, k.ve, k.se, k.ge, k.sy
    ar = k.ar

    XQF = k.arXQF
    ar.mark()
    FM0 = ar.bf(8, T)
    ar.mark()
    XTa = ar.f32(2, D)
    XNT = ar.f32(2, D)
    XB = ar.bf(2, D)
    XQT = ar.bf(2, D)
    WLD = ar.f32(2816)
    WQB = ar.bf(2816)

    emit_ln_stats(k, h["x_in"], XTa, k.qA[:, 0:2 * D])

    xr = h["x_in"].rearrange("(j p) f -> p j f", p=128)
    outr = out_h.rearrange("(j p) f -> p j f", p=128)
    ssum = k.stats[:, 0:NT]
    inv = k.stats[:, 2 * NT:3 * NT]
    sems = (k.s_la, k.s_lb)
    for j in range(min(2, NT)):
        sems[j % 2].inc(sy.dma_start(out=XTa[:, j % 2, :], in_=xr[:, j, :]), 16)
    for j0 in range(0, NT, 2):
        k.s_st.wait(ve)
        k.s_s.wait(ve)
        for j in (j0, j0 + 1):
            xt = XTa[:, j % 2, :]
            sems[j % 2].wait(ve)
            ve.tensor_scalar(XNT[:, j % 2, :], xt, ssum[:, j:j + 1], inv[:, j:j + 1],
                             op0=ALU.subtract, op1=ALU.mult)
            k.s_v.inc(ve.drain())
            if j + 2 < NT:
                k.s_v.wait(sy)
                sems[j % 2].inc(sy.dma_start(out=XTa[:, j % 2, :], in_=xr[:, j + 2, :]), 16)
            if DEV_STAGE == "ln":
                k.s_v.wait(sy)
                k.s_st.inc(sy.dma_start(out=outr[:, j, :], in_=XNT[:, j % 2, :]), 16)
                continue
            k.s_v.wait(se)
            k.s_s.inc(se.activation(XB[:, j % 2, :], XNT[:, j % 2, :], AF.Copy, bias=0.0))
        if DEV_STAGE == "ln":
            continue
        emit_quant(k, XNT.rearrange("p a b -> p (a b)"), XQT.rearrange("p a b -> p (a b)"))
        if DEV_STAGE == "xq":
            k.s_g.wait(ge)
            for j in (j0, j0 + 1):
                k.s_lg.inc(ge.dma_start(out=outr[:, j, :], in_=XQT[:, j % 2, :]), 16)
            k.s_lg.wait(ve)
            continue
        k.s_s.wait(te)
        k.s_v.wait(te)
        k.s_g.wait(te)
        pairs = []
        for j in (j0, j0 + 1):
            pairs += [(XB[:, j % 2, 128 * fb:128 * fb + 128], FM0[:, fb, 128 * j:128 * j + 128])
                      for fb in range(8)]
            pairs += [(XQT[:, j % 2, 128 * fb:128 * fb + 128], XQF[:, fb, 128 * j:128 * j + 128])
                      for fb in range(8)]
        emit_transposes(k, pairs, k.idb[:], deep=True)
        k.s_mm.wait(se)
        k.s_mm.wait(ve)
        k.s_mm.wait(ge)

    if DEV_STAGE in ("ln", "xq"):
        k.s_st.wait(sy)
        k.s_lg.wait(sy)
        return

    emit_l0_wquant(k, h, WLD, WQB, XTa.rearrange("p a b -> p (a b)").bitcast(BF16)[:, 0:2816])
    full_barrier(k)
    ar.release()
    emit_l0_attn(k, h, FM0, out_h)


def emit_l0_attn(k, h, FM0, out_h):
    te, ve, se, ge, sy = k.te, k.ve, k.se, k.ge, k.sy
    ar = k.ar
    Q = k.arQ
    VT = k.arVT
    K2 = k.arK2

    ar.mark()
    WQ = ar.bf(8, D)
    WKV = ar.bf(8, 512)
    VB = ar.bf(2, T)
    COSA = ar.bf(T)
    SINA = ar.bf(T)
    COSQ = ar.bf(T)
    SINQ = ar.bf(T)
    invc = ar.f32(1)

    k.s_lg.inc(ge.dma_start(out=WQ, in_=h["wqT"].rearrange("(a p) o -> p a o", p=128)), 16)
    k.s_lg.inc(ge.dma_start(out=WKV, in_=h["wkvT"].rearrange("(a p) o -> p a o", p=128)), 16)

    invf = invc
    ve.tensor_scalar(invf, k.pm16[:], -0.0625, None, op0=ALU.mult)
    ve.drain()
    ve.tensor_scalar(invf, invf, k.ln4, None, op0=ALU.mult)
    k.s_v.inc(ve.drain())
    k.s_v.wait(se)
    k.s_s.inc(se.activation(invf, invf, AF.Exp))

    TGRID = k.qA[:, 0:T]
    ARG = k.qB[:, 0:T]
    KQ = k.qC[:, 0:T]
    k.s_g.inc(ge.iota(TGRID, pattern=[[1, T]], base=0, channel_multiplier=0,
                      allow_small_or_imprecise_dtypes=True))
    k.s_g.wait(ve)
    k.s_s.wait(ve)

    def build_trig(dst, use_hpi):
        if use_hpi:
            ve.tensor_scalar(ARG, TGRID, invf, k.hpi, op0=ALU.mult, op1=ALU.add)
        else:
            ve.tensor_scalar(ARG, TGRID, invf, None, op0=ALU.mult)
        ve.drain()
        ve.tensor_scalar(KQ, ARG, k.i2pi, None, op0=ALU.mult)
        ve.drain()
        ve.tensor_scalar(KQ, KQ, k.mcol[:, 0:1], k.mcol[:, 0:1], op0=ALU.add, op1=ALU.subtract)
        ve.drain()
        ve.scalar_tensor_tensor(KQ, KQ, k.twopi, ARG, op0=ALU.mult, op1=ALU.subtract)
        ve.drain()
        ve.tensor_scalar(KQ, KQ, k.pin, k.pip, op0=ALU.max, op1=ALU.min)
        k.s_v.inc(ve.drain())
        k.s_v.wait(se)
        k.s_s.inc(se.activation(KQ, KQ, AF.Sin))
        k.s_s.wait(ve)
        ve.tensor_scalar(dst, KQ, -1.0, None, op0=ALU.mult)
        k.s_v.inc(ve.drain())

    build_trig(COSA, True)
    build_trig(SINA, False)
    k.s_v.wait(ve)
    ve.tensor_scalar(COSQ, COSA, k.sig, None, op0=ALU.mult)
    k.s_v.inc(ve.tensor_scalar(SINQ, SINA, k.sig, None, op0=ALU.mult))

    k.s_ld.wait(te)
    k.s_lg.wait(te)
    k.s_v.wait(te)

    def proj(WT, nout, dst, woff=0):
        for oc in range(nout):
            for tc in range(4):
                pb = k.psA[:, tc % 4, :]
                te.wait_ge(k.s_s.h, max(0, k.s_s.n - 3))
                for ic in range(8):
                    mm = te.matmul(pb, WT[:, ic, 128 * (oc + woff):128 * (oc + woff) + 128],
                                   FM0[:, ic, 512 * tc:512 * tc + 512],
                                   start=(ic == 0), stop=(ic == 7))
                k.s_mm.inc(mm)
                k.s_mm.wait(se)
                k.s_s.inc(se.activation(dst[:, oc, 512 * tc:512 * tc + 512], pb,
                                        AF.Copy, bias=0.0))

    proj(WQ, 8, Q)
    if DEV_TAP == "qpre":
        dump_fm(k, Q, k.out_h, 8)
        return
    proj(WKV, 2, K2)
    k.s_v.wait(ve)
    ve.memset(VT[:], 1.0)
    k.s_v.inc(ve.drain())
    proj(WKV, 2, VB.rearrange("p a t -> p a t"), woff=2)
    k.s_s.wait(te)
    k.s_v.wait(te)
    base_v = k.s_v.n
    i = 0
    for hk in range(4):
        fr, ft = (64 * hk) % 128, (64 * hk) // 128
        for j in range(NT):
            if i >= 4:
                te.wait_ge(k.s_v.h, base_v + i - 3)
            pv = k.psA[:, i % 4, :].bitcast(BF16)[:, 0:64]
            idsl = k.idb[fr:fr + 64, fr:fr + 64]
            k.s_mm.inc(te.transpose(pv, VB[fr:fr + 64, ft, 128 * j:128 * j + 128], idsl))
            k.s_mm.wait(ve)
            k.s_v.inc(ve.tensor_copy(VT[:, j, hk, 0:64], pv))
            i += 1

    RT1 = k.qA[:].bitcast(BF16)[:, 0:T]
    full_barrier(k, "tvs")

    def rope(SRC, n_tiles, COS, SIN):
        for ft in range(n_tiles):
            k.s_v.wait(te)
            for c in range(4):
                k.s_mm.inc(te.matmul(k.psB[:, c, :], k.P32[:], SRC[:, ft, 512 * c:512 * c + 512],
                                     start=True, stop=True))
            ve.tensor_tensor(RT1, SRC[:, ft, :], COS, op=ALU.mult)
            k.s_mm.wait(ve)
            for c in range(4):
                ve.tensor_tensor(k.psB[:, c, :], k.psB[:, c, :],
                                 SIN[:, 512 * c:512 * c + 512], op=ALU.mult)
            ve.drain()
            for c in range(4):
                ve.tensor_tensor(SRC[:, ft, 512 * c:512 * c + 512],
                                 RT1[:, 512 * c:512 * c + 512], k.psB[:, c, :], op=ALU.add)
            k.s_v.inc(ve.drain())

    rope(Q, 8, COSQ, SINQ)
    if DEV_TAP == "qrope":
        dump_fm(k, Q, k.out_h, 8)
        return
    rope(K2, 2, COSA, SINA)
    if DEV_TAP == "krope":
        dump_fm(k, K2, k.out_h, 2)
        return
    full_barrier(k, "tv")
    ar.release()
    emit_l0_scores(k, h, Q, K2, VT, out_h)


def emit_l0_scores(k, h, Q, K2, VT, out_h):
    nc = k.nc
    te, ve, se, ge, sy = k.te, k.ve, k.se, k.ge, k.sy
    ar = k.ar
    P4t = k.P4

    ar.mark()
    KARR = ar.bf(4, T)
    BT = ar.bf(T)
    UT = ar.bf(4, 512)
    OE = ar.bf(4, 512)
    OTT = ar.bf(4, 4, 72)
    YC = ar.bf(4, 2, 64)
    WROW = ar.bf(64)
    RC = ar.f32(24)
    KMG = ar.f32(2)
    KMC = ar.f32(4)
    NLAM = ar.f32(16)
    LAMC = ar.f32(1)

    k.s_v.wait(ve)
    for v_ in range(4):
        src = K2[64 * (v_ % 2):64 * (v_ % 2) + 64, v_ // 2, :]
        ve.tensor_copy(KARR[0:64, v_, :], src)
        ve.tensor_copy(KARR[64:128, v_, :], src)
    k.s_v.inc(ve.drain())

    lam_s = nc.dram_tensor("lam_scr", [16, 1], F32, kind="Internal").ap()
    km_s = nc.dram_tensor("km_scr", [128, 2], F32, kind="Internal").ap()
    k.s_ld.inc(sy.dma_start(out=LAMC[0:16, :], in_=h["lam"][:, :]), 16)
    k.s_ld.wait(se)
    se.activation(LAMC[0:16, :], LAMC[0:16, :], AF.Sigmoid)
    k.s_s.inc(se.drain())
    k.s_s.wait(ve)
    k.s_v.inc(ve.tensor_scalar(LAMC[0:16, :], LAMC[0:16, :], -1.0, None, op0=ALU.mult))
    k.s_v.wait(sy)
    k.s_st.inc(sy.dma_start(out=lam_s[:, :], in_=LAMC[0:16, :]), 16)
    k.s_st.wait(sy)
    nl_in = _dc.replace(lam_s, offset=0, ap=[[0, 128], [1, 16]])
    k.s_ld.inc(sy.dma_start(out=NLAM, in_=nl_in), 16)

    SQB = k.qA[:].bitcast(BF16)[:, 0:T]
    BTS = k.qB[:, 0:T]
    for kt2 in range(2):
        k.s_v.wait(se)
        k.s_g.wait(se)
        k.s_s.inc(se.activation(SQB, K2[:, kt2, :], AF.Square))
        k.s_s.wait(te)
        for cc in range(4):
            k.s_mm.inc(te.matmul(k.psB[:, cc, :], P4t[:], SQB[:, 512 * cc:512 * cc + 512],
                                 start=True, stop=True))
        k.s_mm.wait(ve)
        for cc in range(4):
            ve.tensor_reduce(KMG[:, kt2:kt2 + 1] if cc == 0 else BTS[:, cc:cc + 1],
                             k.psB[:, cc, :], axis=AX.X, op=ALU.max)
        ve.drain()
        ve.tensor_tensor(KMG[:, kt2:kt2 + 1], KMG[:, kt2:kt2 + 1], BTS[:, 1:2], op=ALU.max)
        ve.drain()
        ve.tensor_tensor(BTS[:, 2:3], BTS[:, 2:3], BTS[:, 3:4], op=ALU.max)
        ve.drain()
        ve.tensor_tensor(KMG[:, kt2:kt2 + 1], KMG[:, kt2:kt2 + 1], BTS[:, 2:3], op=ALU.max)
        k.s_v.inc(ve.drain())
    k.s_v.wait(se)
    k.s_s.inc(se.activation(KMG, KMG, AF.Sqrt))
    k.s_s.wait(sy)
    k.s_st.inc(sy.dma_start(out=km_s[:, :], in_=KMG), 16)
    k.s_st.wait(sy)
    for kv in range(4):
        for g4 in range(4):
            off = 128 * (kv % 2) + kv // 2 + 64 * (g4 % 2)
            src = _dc.replace(km_s, offset=off, ap=[[0, 32], [1, 1]])
            k.s_ld.inc(sy.dma_start(out=KMC[32 * g4:32 * g4 + 32, kv:kv + 1], in_=src), 16)

    k.s_lg.inc(ge.dma_start(out=WROW, in_=h["rmsw"][:, :]), 16)

    ones1b = k.ones1b
    M01 = k.M01
    k.s_ld.wait(ve)
    k.s_lg.wait(ve)
    k.s_ld.wait(te)
    for qt in range(8):
        kvv = qt // 2
        k.s_v.wait(se)
        k.s_g.wait(se)
        k.s_s.inc(se.activation(SQB, Q[:, qt, :], AF.Square))
        k.s_s.wait(te)
        k.s_v.wait(te)
        for cc in range(4):
            k.s_mm.inc(te.matmul(k.psB[:, cc, :], P4t[:], SQB[:, 512 * cc:512 * cc + 512],
                                 start=True, stop=True))
        k.s_mm.wait(se)
        for cc in range(4):
            se.activation(BTS[:, 512 * cc:512 * cc + 512], k.psB[:, cc, :], AF.Sqrt)
        k.s_s.inc(se.drain())
        k.s_s.wait(ve)
        k.s_v.inc(ve.tensor_scalar(BT, BTS, KMC[:, kvv:kvv + 1], -1.0,
                                   op0=ALU.mult, op1=ALU.mult))
        if DEV_TAP == "bt" and qt == 0:
            dump_fm(k, BT.unsqueeze(1), k.out_h, 1)
            return
        k.s_v.wait(te)

        for c in range(4):
            nkt = 4 * c + 4
            k.s_v.wait(te)
            for kt in range(nkt):
                masked = kt >= 4 * c
                off = 128 * (kt - 4 * c) if masked else 0
                w = 512 - off
                te.wait_ge(k.s_s.h, k.s_s.n)
                for jj in range(4):
                    te.matmul(k.psA[:, jj, off:512], ones1b[32 * jj:32 * jj + 1, 0:128],
                              BT[32 * jj:32 * jj + 1, 512 * c + off:512 * c + 512],
                              start=True, stop=False, tile_position=(32 * jj, 0))
                    mm = te.matmul(k.psA[:, jj, off:512],
                                   KARR[32 * jj:32 * jj + 32, kvv, 128 * kt:128 * kt + 128],
                                   Q[32 * jj:32 * jj + 32, qt, 512 * c + off:512 * c + 512],
                                   start=False, stop=True, tile_position=(32 * jj, 0))
                    k.s_mm.inc(mm)
                k.s_mm.wait(se)
                k.s_v.wait(se)
                m01b2 = M01[:].unsqueeze(1).broadcast_to((128, 2, 128))
                for half in range(2):
                    jjs = (0, 1) if half == 0 else (2, 3)
                    se.activation(UT[:, jjs[0], off:512], k.psA[:, jjs[0], off:512], AF.Exp)
                    k.s_s.inc(se.activation(UT[:, jjs[1], off:512],
                                            k.psA[:, jjs[1], off:512], AF.Exp))
                    if masked:
                        k.s_s.wait(ve)
                        uview = UT[:, jjs[0]:jjs[0] + 2, off:off + 128]
                        k.s_v.inc(ve.tensor_tensor(uview, uview, m01b2, op=ALU.mult))
                    k.s_s.wait(te)
                    if masked:
                        k.s_v.wait(te)
                    for jj in jjs:
                        k.s_mm.inc(te.matmul(k.psB[0:65, jj, off:512], VT[:, kt, kvv, 0:65],
                                             UT[:, jj, off:512],
                                             start=(kt == 0), stop=(kt == nkt - 1)))
                if DEV_TAP == "ut" and qt == 0 and c == 3 and kt == 0:
                    full_barrier(k)
                    dump_fm(k, UT.rearrange("p a b -> p (a b)").unsqueeze(1), k.out_h, 1)
                    return
            k.s_mm.wait(ve)
            for jj in range(4):
                ve.tensor_copy(OE[0:65, jj, :], k.psB[0:65, jj, :])
            k.s_v.inc(ve.drain())
            k.s_v.wait(te)
            i = 0
            for cc in range(4):
                for jj in range(4):
                    if i >= 4:
                        te.wait_ge(k.s_v.h, k.s_v.n - 3)
                    pv = k.psA[:, i % 4, :].bitcast(BF16)[:, 0:65]
                    k.s_mm.inc(te.transpose(pv, OE[0:65, jj, 128 * cc:128 * cc + 128],
                                            k.idb[0:65, 0:65]))
                    k.s_mm.wait(ve)
                    k.s_v.inc(ve.tensor_copy(OTT[:, cc, jj, 0:65], pv))
                    i += 1
            if DEV_TAP == "ott" and qt == 0 and c == 0:
                full_barrier(k)
                dump_fm(k, OTT.rearrange("p a b c -> p (a b c)").unsqueeze(1), k.out_h, 1)
                return
            ve.drain()
            ve.tensor_copy(RC[:, 0:16].rearrange("p (b a) -> p b a", b=4),
                           OTT[:, :, :, 64:65].rearrange("p a b 1 -> p a b").transpose([0, 2, 1]))
            ve.drain()
            ve.reciprocal(RC[:, 0:16], RC[:, 0:16])
            ve.drain()
            for hh2 in range(2):
                hglob = 2 * qt + hh2
                j1, j2 = 2 * hh2, 2 * hh2 + 1
                r1 = RC[:, 4 * j1:4 * j1 + 4].unsqueeze(2).broadcast_to((128, 4, 64))
                ve.tensor_tensor(YC[:, :, hh2, :], OTT[:, :, j1, 0:64], r1, op=ALU.mult)
                ve.tensor_scalar(RC[:, 16 + 4 * hh2:20 + 4 * hh2], RC[:, 4 * j2:4 * j2 + 4],
                                 NLAM[:, hglob:hglob + 1], None, op0=ALU.mult)
            ve.drain()
            for hh2 in range(2):
                j2 = 2 * hh2 + 1
                c2b = RC[:, 16 + 4 * hh2:20 + 4 * hh2].unsqueeze(2).broadcast_to((128, 4, 64))
                ve.tensor_tensor(OTT[:, :, j2, 0:64], OTT[:, :, j2, 0:64], c2b, op=ALU.mult)
            ve.drain()
            ve.tensor_tensor(YC[:], YC[:], OTT[:, :, 1::2, 0:64], op=ALU.add)
            k.s_v.inc(ve.drain())
            if DEV_TAP == "yc" and qt == 0 and c == 0:
                full_barrier(k)
                dump_fm(k, YC.rearrange("p a b c -> p (a b c)").unsqueeze(1), k.out_h, 1)
                return
            if DEV_TAP == "rc" and qt == 0 and c == 0:
                full_barrier(k)
                dump_fm(k, RC.bitcast(BF16)[:, 0:48].unsqueeze(1), k.out_h, 1)
                return
            k.s_v.wait(se)
            se.activation(BTS[:, 0:512], YC[:].rearrange("p a b d -> p (a b d)"), AF.Square)
            k.s_s.inc(se.drain())
            k.s_s.wait(ve)
            ve.tensor_reduce(RC[:, 0:8], BTS[:, 0:512].rearrange("p (a d) -> p a d", d=64),
                             axis=AX.X, op=ALU.add)
            ve.drain()
            ve.tensor_scalar(RC[:, 0:8], RC[:, 0:8], 1.0 / 64.0, 1e-6, op0=ALU.mult, op1=ALU.add)
            ve.drain()
            ve.reciprocal(RC[:, 0:8], RC[:, 0:8])
            k.s_v.inc(ve.drain())
            k.s_v.wait(se)
            se.activation(RC[:, 0:8], RC[:, 0:8], AF.Sqrt)
            k.s_s.inc(se.drain())
            k.s_s.wait(ve)
            rib = RC[:, 0:8].rearrange("p (a b) -> p a b", b=2).unsqueeze(3) \
                .broadcast_to((128, 4, 2, 64))
            ve.tensor_tensor(YC[:], YC[:], rib, op=ALU.mult)
            ve.drain()
            wrb = WROW.unsqueeze(1).unsqueeze(1).broadcast_to((128, 4, 2, 64))
            ve.tensor_tensor(YC[:], YC[:], wrb, op=ALU.mult)
            k.s_v.inc(ve.drain())
            if DEV_TAP == "yn" and qt == 0 and c == 0:
                full_barrier(k)
                dump_fm(k, YC.rearrange("p a b c -> p (a b c)").unsqueeze(1), k.out_h, 1)
                return
            k.s_v.wait(te)
            for cc in range(4):
                pv = k.psA[:, cc, :].bitcast(BF16)[:, 0:128]
                k.s_mm.inc(te.transpose(pv, YC[:, cc, :, :].rearrange("p a b -> p (a b)"),
                                        k.idb[:]))
            k.s_mm.wait(ve)
            for cc in range(4):
                ve.tensor_copy(Q[:, qt, 512 * c + 128 * cc:512 * c + 128 * cc + 128],
                               k.psA[:, cc, :].bitcast(BF16)[:, 0:128])
            k.s_v.inc(ve.drain())
            if DEV_TAP in ("y5", "y6", "y7", "y8", "y9", "yz") and qt == {"y5":2,"y6":3,"y7":4,"y8":5,"y9":6,"yz":7}[DEV_TAP] and c == 3:
                dump_fm(k, Q, k.out_h, 8)
                return
            if DEV_TAP == "y3" and qt == 0 and c == 3:
                dump_fm(k, Q, k.out_h, 8)
                return
            if DEV_TAP == "y4" and qt == 1 and c == 3:
                dump_fm(k, Q, k.out_h, 8)
                return
            if DEV_TAP == "y2" and qt == 0 and c == 0:
                dump_fm(k, Q, k.out_h, 8)
                return
            if DEV_TAP == "y1" and qt == 0 and c == 0:
                full_barrier(k)
                dump_fm(k, Q[:, 0:1, 0:512].rearrange("p a b -> p (a b)").unsqueeze(1), k.out_h, 1)
                return

    full_barrier(k)
    ar.release()
    ar.release()
    ar.mark()
    X1T = ar.f32(2, D)
    WOB = ar.bf(8, D)
    k.s_lg.inc(ge.dma_start(out=WOB, in_=h["woT"].rearrange("(a p) o -> p a o", p=128)), 16)
    k.s_ld.wait(te)
    k.s_lg.wait(te)
    x1r = h["x1_s"].rearrange("(j p) f -> p j f", p=128)
    outr2 = out_h.rearrange("(j p) f -> p j f", p=128)
    for j in range(NT):
        k.s_st.wait(ve)
        for oc in range(2):
            pb = k.psB[:, 2 * (j % 2) + oc, :]
            te.wait_ge(k.s_v.h, max(0, k.s_v.n - 3))
            for ft in range(8):
                mm = te.matmul(pb, Q[:, ft, 128 * j:128 * j + 128],
                               WOB[:, ft, 512 * oc:512 * oc + 512],
                               start=(ft == 0), stop=(ft == 7))
            k.s_mm.inc(mm)
            k.s_mm.wait(ve)
            k.s_v.inc(ve.tensor_copy(X1T[:, j % 2, 512 * oc:512 * oc + 512], pb))
        k.s_v.wait(sy)
        k.s_st.inc(sy.dma_start(out=x1r[:, j, :], in_=X1T[:, j % 2, :]), 16)
        if DEV_STAGE == "l0attn":
            k.s_st.inc(sy.dma_start(out=outr2[:, j, :], in_=X1T[:, j % 2, :]), 16)
    if DEV_STAGE == "l0attn":
        k.s_st.wait(sy)
        return
    full_barrier(k)
    ar.release()
    ar.release()
    emit_l0_ffn(k, h, out_h)


def emit_l0_wquant(k, h, WLD, WQB, STG):
    te, ve, se, ge, sy = k.te, k.ve, k.se, k.ge, k.sy
    specs = [(h["wg0"], h["wqgT_s"], 22, 1024, 2),
             (h["wu0"], h["wquT_s"], 22, 1024, 2),
             (h["wd0"], h["wdqT_s"], 8, 2816, 1)]
    for src, dst, nrt, W, PR in specs:
        srcr = src.rearrange("(r p) w -> p r w", p=128)
        dstr = dst.rearrange("(bb p) o -> p bb o", p=128)
        nb = W // 128
        for r0 in range(0, nrt, PR):
            WT = PR * W
            k.s_v.wait(sy)
            k.s_g.wait(sy)
            k.s_la.inc(sy.dma_start(out=WLD[:, 0:WT].rearrange("p (r w) -> p r w", r=PR),
                                    in_=srcr[:, r0:r0 + PR, :]), 16)
            k.s_la.wait(ve)
            k.s_la.wait(ge)
            emit_quant(k, WLD[:, 0:WT], WQB[:, 0:WT])
            k.s_g.wait(te)
            k.s_v.wait(te)
            k.s_st.wait(ve)
            pairs = [(WQB[:, 128 * b:128 * b + 128], STG[:, 128 * b:128 * b + 128])
                     for b in range(PR * nb)]
            emit_transposes(k, pairs, k.idb[:], deep=True)
            k.s_mm.wait(ve)
            k.s_mm.wait(ge)
            k.s_v.wait(sy)
            for rr in range(PR):
                k.s_st.inc(sy.dma_start(
                    out=dstr[:, 0:nb, 128 * (r0 + rr):128 * (r0 + rr) + 128],
                    in_=STG[:, rr * W:(rr + 1) * W].rearrange("p (bb c) -> p bb c", c=128)), 16)


def emit_l0_ffn(k, h, out_h):
    te, ve, se, ge, sy = k.te, k.ve, k.se, k.ge, k.sy
    ar = k.ar
    XQF = k.arXQF
    ar.mark()
    WGS = ar.bf(8, 512)
    WUS = ar.bf(8, 512)
    WDS = ar.bf(22, 1024)
    HQ = [ar.bf(22, 128) for _ in range(4)]
    X1T = ar.f32(2, D)
    wds_src = h["wdqT_s"].rearrange("(c p) o -> p c o", p=128)
    k.s_ld.inc(sy.dma_start(out=WDS, in_=wds_src), 16)
    wg_src = h["wqgT_s"].rearrange("(c p) o -> p c o", p=128)
    wu_src = h["wquT_s"].rearrange("(c p) o -> p c o", p=128)
    xr = h["x_in"].rearrange("(j p) f -> p j f", p=128)
    x1r = h["x1_s"].rearrange("(j p) f -> p j f", p=128)
    outr = out_h.rearrange("(j p) f -> p j f", p=128)
    HGF = ar.f32(2048)
    HQB4 = ar.bf(2048)

    for jg in range(4):
        k.s_mm.wait(ve)
        k.s_mm.wait(se)
        for oc in range(6):
            ocw = 512 if oc < 5 else 256
            k.s_mm.wait(sy)
            k.s_la.inc(sy.dma_start(out=WGS[:, :, 0:ocw], in_=wg_src[:, :, 512 * oc:512 * oc + ocw]), 16)
            k.s_lb.inc(sy.dma_start(out=WUS[:, :, 0:ocw], in_=wu_src[:, :, 512 * oc:512 * oc + ocw]), 16)
            k.s_la.wait(te)
            k.s_lb.wait(te)
            for j4 in range(4):
                j = 4 * jg + j4
                te.wait_ge(k.s_s.h, k.s_s.n)
                te.wait_ge(k.s_v.h, k.s_v.n)
                for ic in range(8):
                    te.matmul(k.psA[:, 0, 0:ocw], XQF[:, ic, 128 * j:128 * j + 128],
                              WGS[:, ic, 0:ocw], start=(ic == 0), stop=(ic == 7))
                for ic in range(8):
                    mm = te.matmul(k.psA[:, 1, 0:ocw], XQF[:, ic, 128 * j:128 * j + 128],
                                   WUS[:, ic, 0:ocw], start=(ic == 0), stop=(ic == 7))
                k.s_mm.inc(mm)
                k.s_mm.wait(se)
                k.s_s.inc(se.activation(HGF[:, ocw * j4:ocw * j4 + ocw],
                                        k.psA[:, 0, 0:ocw], AF.Silu))
                k.s_s.wait(ve)
                k.s_v.inc(ve.tensor_tensor(HGF[:, ocw * j4:ocw * j4 + ocw],
                                           HGF[:, ocw * j4:ocw * j4 + ocw],
                                           k.psA[:, 1, 0:ocw], op=ALU.mult))
            ve.drain()
            emit_quant(k, HGF[:, 0:4 * ocw], HQB4[:, 0:4 * ocw])
            k.s_g.wait(te)
            k.s_v.wait(te)
            pairs = [(HQB4[:, ocw * j4 + 128 * b:ocw * j4 + 128 * b + 128],
                      HQ[j4][:, 4 * oc + b, :])
                     for j4 in range(4) for b in range(ocw // 128)]
            emit_transposes(k, pairs, k.idb[:], deep=True)
            k.s_mm.wait(ve)
            k.s_mm.wait(ge)
            k.s_mm.wait(se)
        k.s_ld.wait(te)
        for j4 in range(4):
            j = 4 * jg + j4
            k.s_st.wait(sy)
            k.s_lc.inc(sy.dma_start(out=X1T[:, j4 % 2, :], in_=x1r[:, j, :]), 16)
            for dc in range(2):
                te.wait_ge(k.s_v.h, k.s_v.n)
                for c in range(22):
                    mm = te.matmul(k.psB[:, dc, :], HQ[j4][:, c, :],
                                   WDS[:, c, 512 * dc:512 * dc + 512],
                                   start=(c == 0), stop=(c == 21))
                k.s_mm.inc(mm)
            k.s_mm.wait(ve)
            k.s_lc.wait(ve)
            for dc in range(2):
                ve.tensor_tensor(X1T[:, j4 % 2, 512 * dc:512 * dc + 512],
                                 X1T[:, j4 % 2, 512 * dc:512 * dc + 512],
                                 k.psB[:, dc, :], op=ALU.add)
            k.s_v.inc(ve.drain())
            k.s_v.wait(ge)
            k.s_lg.inc(ge.dma_start(out=X1T[:, j4 % 2, :], in_=xr[:, j, :], accum_op=ALU.add), 16)
            k.s_lg.wait(sy)
            k.s_st.inc(sy.dma_start(out=x1r[:, j, :], in_=X1T[:, j4 % 2, :]), 16)
            if DEV_STAGE == "l0":
                k.s_st.inc(sy.dma_start(out=outr[:, j, :], in_=X1T[:, j4 % 2, :]), 16)
    full_barrier(k)
    ar.release()
    if DEV_STAGE == "l0":
        k.s_st.wait(sy)
        return
    ar.release()
    emit_l1(k, h, out_h)


def emit_l1(k, h, out_h):
    te, ve, se, ge, sy = k.te, k.ve, k.se, k.ge, k.sy
    ar = k.ar

    XINT = ar.bf(NT, D)
    ar.mark()
    XNF = ar.bf(8, T)
    XTa = ar.f32(2, D)

    emit_ln_stats(k, h["x1_s"], XTa, k.qA[:, 0:2 * D])
    xr = h["x1_s"].rearrange("(j p) f -> p j f", p=128)
    ssum = k.stats[:, 0:NT]
    inv = k.stats[:, 2 * NT:3 * NT]
    sems = (k.s_la, k.s_lb)
    for j in range(min(2, NT)):
        sems[j % 2].inc(sy.dma_start(out=XTa[:, j % 2, :], in_=xr[:, j, :]), 16)
    for j in range(NT):
        xt = XTa[:, j % 2, :]
        sems[j % 2].wait(ve)
        k.s_s.wait(ve)
        ve.tensor_scalar(XINT[:, j, :], xt, ssum[:, j:j + 1], inv[:, j:j + 1],
                         op0=ALU.subtract, op1=ALU.mult)
        k.s_v.inc(ve.drain())
        if j + 2 < NT:
            k.s_v.wait(sy)
            sems[j % 2].inc(sy.dma_start(out=XTa[:, j % 2, :], in_=xr[:, j + 2, :]), 16)
        k.s_v.wait(te)
        pairs = [(XINT[:, j, 128 * fb:128 * fb + 128], XNF[:, fb, 128 * j:128 * j + 128])
                 for fb in range(8)]
        emit_transposes(k, pairs, k.idb[:], deep=True)
        k.s_mm.wait(ve)

    if DEV_STAGE == "l1ln":
        dump_xint(k, out_h, XINT)
        return

    TH = ar.bf(2, T)
    PHI = ar.bf(NT, 256)
    GT = ar.bf(NT, D)
    MB = ar.bf(2, D)
    WTH = ar.bf(8, 256)
    WPH = ar.bf(8, 256)
    WG = ar.bf(8, D)
    k.s_lg.inc(ge.dma_start(out=WTH, in_=h["wthT"].rearrange("(a p) o -> p a o", p=128)), 16)
    k.s_lg.inc(ge.dma_start(out=WPH, in_=h["wphT"].rearrange("(a p) o -> p a o", p=128)), 16)
    k.s_lg.inc(ge.dma_start(out=WG, in_=h["wgT"].rearrange("(a p) o -> p a o", p=128)), 16)
    full_barrier(k)

    for oc in range(2):
        for tc in range(4):
            pb = k.psA[:, tc % 4, :]
            te.wait_ge(k.s_s.h, max(0, k.s_s.n - 3))
            for ic in range(8):
                mm = te.matmul(pb, WTH[:, ic, 128 * oc:128 * oc + 128],
                               XNF[:, ic, 512 * tc:512 * tc + 512],
                               start=(ic == 0), stop=(ic == 7))
            k.s_mm.inc(mm)
            k.s_mm.wait(se)
            k.s_s.inc(se.activation(TH[:, oc, 512 * tc:512 * tc + 512], pb, AF.Copy, bias=0.0))
    def tproj(W, dst, ncol):
        for j in range(NT):
            noc = (ncol + 511) // 512
            for oc in range(noc):
                ow = min(512, ncol - 512 * oc)
                pb = k.psB[:, (j * noc + oc) % 4, :]
                te.wait_ge(k.s_s.h, max(0, k.s_s.n - 3))
                for ic in range(8):
                    mm = te.matmul(pb[:, 0:ow], XNF[:, ic, 128 * j:128 * j + 128],
                                   W[:, ic, 512 * oc:512 * oc + ow],
                                   start=(ic == 0), stop=(ic == 7))
                k.s_mm.inc(mm)
                k.s_mm.wait(se)
                k.s_s.inc(se.activation(dst[:, j, 512 * oc:512 * oc + ow], pb[:, 0:ow],
                                        AF.Copy, bias=0.0))
    tproj(WPH, PHI, 256)
    tproj(WG, GT, 1024)
    full_barrier(k)

    for lc in range(2):
        for ocg in range(2):
            pb = k.psA[:, 2 * lc + ocg, :]
            for j in range(NT):
                mm = te.matmul(pb, PHI[:, j, 128 * lc:128 * lc + 128],
                               GT[:, j, 512 * ocg:512 * ocg + 512],
                               start=(j == 0), stop=(j == NT - 1))
            k.s_mm.inc(mm)
            k.s_mm.wait(ve)
            k.s_v.inc(ve.tensor_scalar(MB[:, lc, 512 * ocg:512 * ocg + 512], pb,
                                       0.03125, None, op0=ALU.mult))
    k.s_v.wait(te)
    X1T = XTa
    x1r2 = h["x1_s"].rearrange("(j p) f -> p j f", p=128)
    for j in range(NT):
        k.s_st.wait(sy)
        k.s_lc.inc(sy.dma_start(out=X1T[:, j % 2, :], in_=x1r2[:, j, :]), 16)
        te.wait_ge(k.s_v.h, max(0, k.s_v.n - 1))
        for oc in range(2):
            pb = k.psB[:, 2 * (j % 2) + oc, :]
            for lc in range(2):
                mm = te.matmul(pb, TH[:, lc, 128 * j:128 * j + 128],
                               MB[:, lc, 512 * oc:512 * oc + 512],
                               start=(lc == 0), stop=(lc == 1))
            k.s_mm.inc(mm)
        k.s_mm.wait(ve)
        k.s_lc.wait(ve)
        for oc in range(2):
            ve.tensor_tensor(X1T[:, j % 2, 512 * oc:512 * oc + 512],
                             X1T[:, j % 2, 512 * oc:512 * oc + 512],
                             k.psB[:, 2 * (j % 2) + oc, :], op=ALU.add)
        k.s_v.inc(ve.drain())
        k.s_v.wait(sy)
        k.s_st.inc(sy.dma_start(out=x1r2[:, j, :], in_=X1T[:, j % 2, :]), 16)
    full_barrier(k)
    ar.release()

    if DEV_STAGE == "l1attn":
        ar.mark()
        XTb = ar.f32(2, D)
        outr = out_h.rearrange("(j p) f -> p j f", p=128)
        for j in range(NT):
            k.s_la.inc(sy.dma_start(out=XTb[:, j % 2, :], in_=x1r2[:, j, :]), 16)
            k.s_la.wait(sy)
            k.s_st.inc(sy.dma_start(out=outr[:, j, :], in_=XTb[:, j % 2, :]), 16)
            k.s_st.wait(sy)
        ar.release()
        return

    emit_l1_ffn(k, h, out_h, XINT)


def emit_l1_ffn(k, h, out_h, XINT):
    nc = k.nc
    te, ve, se, ge, sy = k.te, k.ve, k.se, k.ge, k.sy
    ar = k.ar

    CQ = ar.bf(T)
    SQ = ar.bf(T)
    CK = ar.bf(T)
    SK = ar.bf(T)

    P64R = k.sb("P64R", [128, 128], BF16)
    M01L = k.sb("M01L", [128, 128], BF16)
    PHB = k.sb("PHB", [128, 2], BF16)
    E2B = k.sb("E2B", [2, 128], BF16)
    E2Q = k.sb("E2Q", [2, 128], BF16)
    E2K = k.sb("E2K", [2, 128], BF16)
    QSR = k.sb("QSR", [2, 256], BF16)
    LNGC = k.sb("LNGC", [128, 8], F32)
    LNBC = k.sb("LNBC", [128, 8], F32)
    B1AC = k.sb("B1AC", [128, 22], F32)
    B1GC = k.sb("B1GC", [128, 22], F32)
    st4 = nc.dram_tensor("st4", [4, 2049], F32, kind="Internal").ap()

    full_barrier(k)

    PM32 = k.smallc[:, 8:9]
    LNBc = k.smallc[:, 9:10]
    INVF = k.smallc[:, 10:11]
    ZCOL = k.smallc[:, 15:16]
    ve.memset(ZCOL, 0.0)
    ve.tensor_scalar(PM32, k.pcol[:], -15.5, 0.03125, op0=ALU.add, op1=ALU.mult)
    ve.drain()
    ve.tensor_scalar(PM32, PM32, k.mcol[:, 0:1], k.mcol[:, 0:1], op0=ALU.add, op1=ALU.subtract)
    ve.drain()
    ve.scalar_tensor_tensor(PM32, PM32, -32.0, k.pcol[:], op0=ALU.mult, op1=ALU.add)
    ve.drain()
    ve.tensor_scalar(INVF, PM32, -0.03125, None, op0=ALU.mult)
    ve.drain()
    ve.tensor_scalar(INVF, INVF, k.ln4, None, op0=ALU.mult)
    ve.tensor_scalar(LNBc, PM32, 2.0, 25.6, op0=ALU.mult, op1=ALU.add)
    ve.drain()
    ve.tensor_scalar(LNBc, LNBc, 1.0 / 89.6, None, op0=ALU.mult)
    k.s_v.inc(ve.drain())
    k.s_v.wait(se)
    se.activation(INVF, INVF, AF.Exp)
    se.activation(LNBc, LNBc, AF.Ln)
    k.s_s.inc(se.drain())

    DIF = k.qA[:, 0:128]
    CM64 = k.qA[:, 128:256]
    S1 = k.qA[:, 256:384]
    S2 = k.qA[:, 384:512]
    k.s_s.wait(ge)
    ge.iota(DIF, pattern=[[1, 128]], base=0, channel_multiplier=-1,
            allow_small_or_imprecise_dtypes=True)
    ge.iota(CM64, pattern=[[0, 2], [1, 64]], base=0, channel_multiplier=0,
            allow_small_or_imprecise_dtypes=True)
    k.s_g.inc(ge.drain())
    k.s_g.wait(ve)
    k.s_s.wait(ve)
    ve.tensor_scalar(P64R[:], DIF, 32.0, None, op0=ALU.is_equal)
    ve.tensor_scalar(S1, DIF, -32.0, None, op0=ALU.is_equal)
    ve.tensor_scalar(S2, CM64, 32.0, None, op0=ALU.is_ge)
    ve.tensor_scalar(M01L[:], DIF, 0.0, None, op0=ALU.is_le)
    ve.drain()
    ve.tensor_tensor(P64R[:], P64R[:], S2, op=ALU.mult)
    ve.drain()
    ve.tensor_scalar(S2, S2, -1.0, 1.0, op0=ALU.mult, op1=ALU.add)
    ve.drain()
    ve.tensor_tensor(S1, S1, S2, op=ALU.mult)
    ve.drain()
    ve.tensor_tensor(P64R[:], P64R[:], S1, op=ALU.subtract)
    ve.memset(PHB[:], 0.0)
    ve.drain()
    ve.tensor_scalar(PHB[:, 0:1], k.pcol[:], 64.0, None, op0=ALU.is_lt)
    ve.tensor_scalar(PHB[:, 1:2], k.pcol[:], 64.0, None, op0=ALU.is_ge)
    k.s_v.inc(ve.drain())
    k.s_v.wait(te)
    pvE = k.psA[:, 0, :].bitcast(BF16)[0:2, 0:128]
    k.s_mm.inc(te.transpose(pvE, PHB[:], k.idb[:]))
    k.s_mm.wait(ve)
    k.s_v.inc(ve.tensor_copy(E2B[:], pvE))

    TG = k.qA[:, 0:T]
    ARG = k.qB[:, 0:T]
    RED = k.qC[:, 0:T]
    k.s_v.wait(ge)
    k.s_g.inc(ge.iota(TG, pattern=[[1, T]], base=0, channel_multiplier=0,
                      allow_small_or_imprecise_dtypes=True))
    k.s_g.wait(ve)

    def trig2(dst, use_hpi, pw_mult, finalmult):
        k.s_s.wait(ve)
        if use_hpi:
            ve.tensor_scalar(ARG, TG, INVF, k.hpi, op0=ALU.mult, op1=ALU.add)
        else:
            ve.tensor_scalar(ARG, TG, INVF, None, op0=ALU.mult)
        ve.drain()
        ve.tensor_scalar(RED, ARG, k.i2pi, None, op0=ALU.mult)
        ve.drain()
        ve.tensor_scalar(RED, RED, k.mcol[:, 0:1], k.mcol[:, 0:1],
                         op0=ALU.add, op1=ALU.subtract)
        ve.drain()
        ve.scalar_tensor_tensor(RED, RED, k.twopi, ARG, op0=ALU.mult, op1=ALU.subtract)
        ve.drain()
        ve.tensor_scalar(RED, RED, k.pin, k.pip, op0=ALU.max, op1=ALU.min)
        k.s_v.inc(ve.drain())
        k.s_v.wait(se)
        k.s_s.inc(se.activation(RED, RED, AF.Sin))
        k.s_s.wait(ve)
        ve.tensor_scalar(ARG, TG, -1024.0, pw_mult, op0=ALU.add, op1=ALU.mult)
        ve.drain()
        ve.tensor_scalar(ARG, ARG, LNBc, None, op0=ALU.mult)
        k.s_v.inc(ve.drain())
        k.s_v.wait(se)
        k.s_s.inc(se.activation(ARG, ARG, AF.Exp))
        k.s_s.wait(ve)
        ve.tensor_tensor(RED, RED, ARG, op=ALU.mult)
        ve.drain()
        ve.tensor_scalar(dst, RED, finalmult, None, op0=ALU.mult)
        k.s_v.inc(ve.drain())

    trig2(CQ, True, 0.00390625, 8.0)
    trig2(SQ, False, 0.00390625, 8.0)
    trig2(CK, True, -0.00390625, -1.0)
    trig2(SK, False, -0.00390625, -1.0)

    for l in range(2):
        emit_l1f_attn(k, h, out_h, XINT, l, CQ, SQ, CK, SK,
                      P64R, M01L, PHB, E2B, E2Q, E2K, QSR)
        if DEV_STAGE == f"l1f{l}a":
            dump_xint(k, out_h, XINT)
            return
        emit_l1f_mlp(k, h, out_h, XINT, l, LNGC, LNBC, B1AC, B1GC, st4)
        if DEV_STAGE == f"l1f{l}":
            dump_xint(k, out_h, XINT)
            return

    full_barrier(k)
    ar.mark()
    XO = ar.f32(2, D)
    x1r = h["x1_s"].rearrange("(j p) f -> p j f", p=128)
    outr = out_h.rearrange("(j p) f -> p j f", p=128)
    for j in range(NT):
        k.s_st.wait(sy)
        k.s_lc.inc(sy.dma_start(out=XO[:, j % 2, :], in_=x1r[:, j, :]), 16)
        k.s_lc.wait(ve)
        ve.tensor_tensor(XO[:, j % 2, :], XO[:, j % 2, :], XINT[:, j, :], op=ALU.add)
        k.s_v.inc(ve.drain())
        k.s_v.wait(sy)
        k.s_st.inc(sy.dma_start(out=outr[:, j, :], in_=XO[:, j % 2, :]), 16)
    k.s_st.wait(sy)
    k.s_st.wait(te)
    ar.release()


def dump_xint(k, out_h, XINT):
    full_barrier(k)
    outr = out_h.rearrange("(j p) f -> p j f", p=128)
    for j in range(NT):
        k.s_dmp.inc(k.ge.dma_start(out=outr[:, j, :], in_=XINT[:, j, :]), 16)
    k.s_dmp.wait(k.ge)
    k.s_dmp.wait(k.sy)


def emit_l1f_attn(k, h, out_h, XINT, l, CQ, SQ, CK, SK,
                  P64R, M01L, PHB, E2B, E2Q, E2K, QSR):
    te, ve, se, ge, sy = k.te, k.ve, k.se, k.ge, k.sy
    ar = k.ar
    M01 = k.M01

    full_barrier(k)
    ar.mark()
    QKVB = ar.bf(12, T)

    k.s_lg.inc(ge.dma_start(out=QSR[:, 0:128],
                            in_=_dc.replace(h["qs"][l], ap=[[0, 2], [1, 128]])), 16)
    k.s_lh.inc(ge.dma_start(out=QSR[:, 128:256],
                            in_=_dc.replace(h["ks"][l], ap=[[0, 2], [1, 128]])), 16)
    k.s_lg.wait(ve)
    k.s_lh.wait(ve)
    ve.tensor_tensor(E2Q[:], E2B[:], QSR[:, 0:128], op=ALU.mult)
    ve.tensor_tensor(E2K[:], E2B[:], QSR[:, 128:256], op=ALU.mult)
    k.s_v.inc(ve.drain())
    ge.wait_ge(k.s_v.h, k.s_v.n)

    ar.mark()
    XFM = ar.bf(8, 2056)
    WSL = ar.bf(2, 8, 128)
    ve.memset(XFM[:, :, 0:1], 0.0)
    k.s_v.inc(ve.drain())
    k.s_v.wait(te)
    for j in range(NT):
        pairs = [(XINT[:, j, 128 * fb:128 * fb + 128], XFM[:, fb, 1 + 128 * j:1 + 128 * j + 128])
                 for fb in range(8)]
        emit_transposes(k, pairs, k.idb[:], deep=True)
    k.s_mm.wait(ve)

    k.s_v.wait(te)
    wsrc = h["wqkv"][l].rearrange("(a p) o -> p a o", p=128)
    lastuse = [0, 0]
    sems = (k.s_lg, k.s_lh)
    for oc in range(12):
        b = oc % 2
        ge.wait_ge(k.s_mm.h, lastuse[b])
        sems[b].inc(ge.dma_start(out=WSL[:, b, :, :], in_=wsrc[:, :, 128 * oc:128 * oc + 128]), 16)
        sems[b].wait(te)
        for tc in range(4):
            pb = k.psA[:, tc % 4, :]
            te.wait_ge(k.s_s.h, max(0, k.s_s.n - 3))
            for ic in range(8):
                soff = 1 if ic < 4 else 0
                mm = te.matmul(pb, WSL[:, b, ic, :],
                               XFM[:, ic, soff + 512 * tc: soff + 512 * tc + 512],
                               start=(ic == 0), stop=(ic == 7))
            k.s_mm.inc(mm)
            k.s_mm.wait(se)
            k.s_s.inc(se.activation(QKVB[:, oc, 512 * tc:512 * tc + 512], pb, AF.Copy, bias=0.0))
        lastuse[b] = k.s_mm.n
    full_barrier(k)
    ar.release()

    if DEV_TAP == f"f{l}qkv":
        dump_fm(k, QKVB, out_h, 12)
        return

    VT1 = ar.bf(4, NT, 2, 68)
    UTT = ar.bf(2, 8, 512)
    UT = ar.bf(2, 640)
    OE = ar.bf(2, 512)
    OTT = ar.bf(4, 2, 72)
    YC = ar.bf(4, 2, 64)
    RC = ar.f32(16)
    WO1b = ar.bf(4, D)

    ve.memset(VT1[:], 1.0)
    k.s_v.inc(ve.drain())
    k.s_v.wait(te)
    base_v = k.s_v.n
    i = 0
    for t4 in range(4):
        for h2 in range(2):
            fr = 64 * h2
            for j in range(NT):
                if i >= 4:
                    te.wait_ge(k.s_v.h, base_v + i - 3)
                pv = k.psA[:, i % 4, :].bitcast(BF16)[:, 0:64]
                idsl = k.idb[fr:fr + 64, fr:fr + 64]
                k.s_mm.inc(te.transpose(pv, QKVB[fr:fr + 64, 8 + t4, 128 * j:128 * j + 128], idsl))
                k.s_mm.wait(ve)
                k.s_v.inc(ve.tensor_copy(VT1[:, t4, j, h2, 0:64], pv))
                i += 1

    SQB = k.qAb[:, 0:T]
    RT = k.qAb[:, T:2 * T]
    NRM = k.qB[0:2, 0:T]
    INVB = k.qCb[0:2, 0:T]

    def normrope(oc, COS, SIN, E2X):
        k.s_v.wait(se)
        k.s_g.wait(se)
        k.s_s.inc(se.activation(SQB, QKVB[:, oc, :], AF.Square))
        k.s_s.wait(te)
        k.s_v.wait(te)
        for cc in range(4):
            k.s_mm.inc(te.matmul(k.psB[0:2, cc, :], PHB[:], SQB[:, 512 * cc:512 * cc + 512],
                                 start=True, stop=True))
        k.s_mm.wait(se)
        for cc in range(4):
            se.activation(NRM[:, 512 * cc:512 * cc + 512], k.psB[0:2, cc, :], AF.Sqrt)
        k.s_s.inc(se.drain())
        k.s_s.wait(ve)
        ve.tensor_scalar(NRM[:], NRM[:], 1e-12, None, op0=ALU.max)
        ve.drain()
        ve.reciprocal(NRM[:], NRM[:])
        ve.drain()
        ve.tensor_copy(INVB[:], NRM[:])
        k.s_v.inc(ve.drain())
        k.s_v.wait(te)
        for cc in range(4):
            k.s_mm.inc(te.matmul(k.psA[:, cc, :], E2X[:], INVB[:, 512 * cc:512 * cc + 512],
                                 start=True, stop=True))
        k.s_mm.wait(ve)
        for cc in range(4):
            ve.tensor_tensor(QKVB[:, oc, 512 * cc:512 * cc + 512],
                             QKVB[:, oc, 512 * cc:512 * cc + 512], k.psA[:, cc, :], op=ALU.mult)
        k.s_v.inc(ve.drain())
        k.s_v.wait(te)
        for cc in range(4):
            k.s_mm.inc(te.matmul(k.psB[:, cc, :], P64R[:], QKVB[:, oc, 512 * cc:512 * cc + 512],
                                 start=True, stop=True))
        ve.tensor_tensor(RT, QKVB[:, oc, :], COS, op=ALU.mult)
        k.s_mm.wait(ve)
        for cc in range(4):
            ve.tensor_tensor(k.psB[:, cc, :], k.psB[:, cc, :],
                             SIN[:, 512 * cc:512 * cc + 512], op=ALU.mult)
        ve.drain()
        for cc in range(4):
            ve.tensor_tensor(QKVB[:, oc, 512 * cc:512 * cc + 512],
                             RT[:, 512 * cc:512 * cc + 512], k.psB[:, cc, :], op=ALU.add)
        k.s_v.inc(ve.drain())

    for t4 in range(4):
        normrope(t4, CQ, SQ, E2Q)
    for t4 in range(4):
        normrope(4 + t4, CK, SK, E2K)
    full_barrier(k, "tvs")
    if DEV_TAP == f"f{l}q":
        dump_fm(k, QKVB[:, 0:4, :], out_h, 4)
        return
    if DEV_TAP == f"f{l}k":
        dump_fm(k, QKVB[:, 4:8, :], out_h, 4)
        return

    NM = k.stats[:, 44:46]
    for t4 in range(4):
        for w in range(4):
            for lq in range(4):
                qc = 4 * w + lq
                kts = list(range(max(0, qc - 4), qc + 1))
                nblk = len(kts)
                wcols = 128 * nblk
                te.wait_ge(k.s_s.h, k.s_s.n)
                te.wait_ge(k.s_v.h, k.s_v.n)
                for jj in range(2):
                    lhs = QKVB[64 * jj:64 * jj + 64, t4, 128 * qc:128 * qc + 128]
                    kbase = 128 * kts[0]
                    w0 = min(512, wcols)
                    mm = te.matmul(k.psA[:, 2 * jj, 0:w0], lhs,
                                   QKVB[64 * jj:64 * jj + 64, 4 + t4, kbase:kbase + w0],
                                   start=True, stop=True, tile_position=(64 * jj, 0))
                    if wcols > 512:
                        mm = te.matmul(k.psA[:, 2 * jj + 1, 0:wcols - 512], lhs,
                                       QKVB[64 * jj:64 * jj + 64, 4 + t4,
                                            kbase + 512:kbase + wcols],
                                       start=True, stop=True, tile_position=(64 * jj, 0))
                    k.s_mm.inc(mm)
                k.s_mm.wait(ve)
                ve.wait_ge(k.s_s.h, k.s_s.n)
                ecnt = []
                for jj in range(2):
                    flat = k.psA[:, 2 * jj:2 * jj + 2, :].rearrange("p a b -> p (a b)")
                    k.s_v.inc(ve.tensor_reduce(NM[:, jj:jj + 1], flat[:, 0:wcols],
                                               axis=AX.X, op=ALU.min))
                    se.wait_ge(k.s_v.h, k.s_v.n)
                    flat2 = k.psA[:, 2 * jj:2 * jj + 2, :].rearrange("p a b -> p (a b)")
                    k.s_s.inc(se.activation(UT[:, jj, 0:wcols], flat2[:, 0:wcols], AF.Exp,
                                            bias=NM[:, jj:jj + 1], scale=-1.0))
                    ecnt.append(k.s_s.n)
                vb2 = k.s_v.n
                i2 = 0
                for jj in range(2):
                    te.wait_ge(k.s_s.h, ecnt[jj])
                    for b, kt in enumerate(kts):
                        ktr = kt - 4 * w + 4
                        if i2 >= 2:
                            te.wait_ge(k.s_v.h, vb2 + i2 - 1)
                        pvb = k.psB[:, 2 + i2 % 2, :].bitcast(BF16)[:, 0:128]
                        k.s_mm.inc(te.transpose(pvb, UT[:, jj, 128 * b:128 * b + 128], k.idb[:]))
                        k.s_mm.wait(ve)
                        dst = UTT[:, jj, ktr, 128 * lq:128 * lq + 128]
                        if b == 0 and qc >= 4:
                            k.s_v.inc(ve.tensor_tensor(dst, pvb, M01L[:], op=ALU.mult))
                        elif b == nblk - 1:
                            k.s_v.inc(ve.tensor_tensor(dst, pvb, M01[:], op=ALU.mult))
                        else:
                            k.s_v.inc(ve.tensor_copy(dst, pvb))
                        i2 += 1
            te.wait_ge(k.s_v.h, k.s_v.n)
            ktrs = [3, 4, 0, 1, 2, 5, 6, 7] if w > 0 else [4, 5, 6, 7]
            for jj in range(2):
                for idx, ktr in enumerate(ktrs):
                    kt = ktr + 4 * w - 4
                    c0 = 128 * max(0, ktr - 4)
                    c1 = 128 * (min(3, ktr) + 1)
                    mm = te.matmul(k.psB[0:65, jj, c0:c1], VT1[:, t4, kt, jj, 0:65],
                                   UTT[:, jj, ktr, c0:c1],
                                   start=(idx == 0), stop=(idx == len(ktrs) - 1))
                k.s_mm.inc(mm)
            k.s_mm.wait(ve)
            for jj in range(2):
                ve.tensor_copy(OE[0:65, jj, :], k.psB[0:65, jj, :])
            k.s_v.inc(ve.drain())
            k.s_v.wait(te)
            vb3 = k.s_v.n
            i3 = 0
            for cc in range(4):
                for jj in range(2):
                    if i3 >= 4:
                        te.wait_ge(k.s_v.h, vb3 + i3 - 3)
                    pv = k.psA[:, i3 % 4, :].bitcast(BF16)[:, 0:65]
                    k.s_mm.inc(te.transpose(pv, OE[0:65, jj, 128 * cc:128 * cc + 128],
                                            k.idb[0:65, 0:65]))
                    k.s_mm.wait(ve)
                    k.s_v.inc(ve.tensor_copy(OTT[:, cc, jj, 0:65], pv))
                    i3 += 1
            ve.drain()
            ve.tensor_copy(RC[:, 0:8].rearrange("p (b a) -> p b a", b=2),
                           OTT[:, :, :, 64:65].rearrange("p a b 1 -> p a b").transpose([0, 2, 1]))
            ve.drain()
            ve.reciprocal(RC[:, 0:8], RC[:, 0:8])
            ve.drain()
            for jj in range(2):
                rb = RC[:, 4 * jj:4 * jj + 4].unsqueeze(2).broadcast_to((128, 4, 64))
                ve.tensor_tensor(YC[:, :, jj, :], OTT[:, :, jj, 0:64], rb, op=ALU.mult)
            k.s_v.inc(ve.drain())
            k.s_v.wait(te)
            for cc in range(4):
                pv = k.psA[:, cc, :].bitcast(BF16)[:, 0:128]
                k.s_mm.inc(te.transpose(pv, YC[:, cc, :, :].rearrange("p a b -> p (a b)"),
                                        k.idb[:]))
            k.s_mm.wait(ve)
            for cc in range(4):
                ve.tensor_copy(QKVB[:, t4, 512 * w + 128 * cc:512 * w + 128 * cc + 128],
                               k.psA[:, cc, :].bitcast(BF16)[:, 0:128])
            k.s_v.inc(ve.drain())

    full_barrier(k, "tvs")
    k.s_lg.inc(ge.dma_start(out=WO1b, in_=h["wo1"][l].rearrange("(a p) o -> p a o", p=128)), 16)
    k.s_lg.wait(te)
    for j in range(NT):
        te.wait_ge(k.s_v.h, max(0, k.s_v.n - 1))
        for dc in range(2):
            pb = k.psB[:, 2 * (j % 2) + dc, :]
            for ft in range(4):
                mm = te.matmul(pb, QKVB[:, ft, 128 * j:128 * j + 128],
                               WO1b[:, ft, 512 * dc:512 * dc + 512],
                               start=(ft == 0), stop=(ft == 3))
            k.s_mm.inc(mm)
        k.s_mm.wait(ve)
        for dc in range(2):
            ve.tensor_tensor(XINT[:, j, 512 * dc:512 * dc + 512],
                             XINT[:, j, 512 * dc:512 * dc + 512],
                             k.psB[:, 2 * (j % 2) + dc, :], op=ALU.add)
        k.s_v.inc(ve.drain())
    full_barrier(k)
    ar.release()


def emit_l1f_mlp(k, h, out_h, XINT, l, LNGC, LNBC, B1AC, B1GC, st4):
    te, ve, se, ge, sy = k.te, k.ve, k.se, k.ge, k.sy
    ar = k.ar
    ar.mark()
    W2R = ar.bf(22, D)
    BCOL = ar.bf(8)
    ROWB = ar.bf(4, 2048)

    k.s_ld.inc(sy.dma_start(out=LNGC[:], in_=h["lngT"][l]), 16)
    k.s_ld.inc(sy.dma_start(out=LNBC[:], in_=h["lnbT"][l]), 16)
    k.s_ld.inc(sy.dma_start(out=B1AC[:], in_=h["b1ag"][l, 0]), 16)
    k.s_ld.inc(sy.dma_start(out=B1GC[:], in_=h["b1ag"][l, 1]), 16)
    w2src = h["w2"][l]
    k.s_lg.inc(ge.dma_start(out=W2R[:, 0:21, :],
                            in_=w2src[0:2688].rearrange("(a p) o -> p a o", p=128)), 16)
    k.s_lg.inc(ge.dma_start(out=W2R[0:42, 21, :], in_=w2src[2688:2730]), 16)

    sA = k.qA[:, 0:16]
    sB = k.qA[:, 16:32]
    qA_ = k.qA[:, 32:48]
    qB_ = k.qA[:, 48:64]
    SCAs = (k.qC[:, 1024:1536], k.qC[:, 1536:2048])
    SCBs = (k.qC[:, 2048:2560], k.qB[:, 2048:2560])
    sbase = k.s_s.n
    for j in range(NT):
        ve.tensor_reduce(sA[:, j:j + 1], XINT[:, j, 0:512], axis=AX.X, op=ALU.add)
        ve.tensor_reduce(sB[:, j:j + 1], XINT[:, j, 512:1024], axis=AX.X, op=ALU.add)
        if j:
            se.wait_ge(k.s_s.h, sbase + j)
        se.activation(SCAs[j % 2], XINT[:, j, 0:512], AF.Square, accum_out=qA_[:, j:j + 1])
        se.activation(SCBs[j % 2], XINT[:, j, 512:1024], AF.Square, accum_out=qB_[:, j:j + 1])
        k.s_s.inc(se.drain())
    k.s_v.inc(ve.drain())
    k.s_s.inc(se.drain())
    k.s_v.wait(te)
    k.s_s.wait(te)
    STG16s = (k.qC[0:16, 2048:2176], k.qC[0:16, 2176:2304])
    for st in range(4):
        k.s_st.inc(sy.dma_start(out=st4[st, 0:1], in_=k.smallc[0:1, 15:16]), 16)
        pv = k.psA[0:16, st % 4, 0:128]
        k.s_mm.inc(te.transpose(pv, k.qA[:, 16 * st:16 * st + 16], k.idf[:]))
        k.s_mm.wait(ve)
        k.s_st.wait(ve)
        k.s_v.inc(ve.tensor_copy(STG16s[st % 2], pv))
        k.s_v.wait(sy)
        k.s_st.inc(sy.dma_start(out=st4[st, 1:2049].rearrange("(j c) -> j c", c=128),
                                in_=STG16s[st % 2]), 16)
    k.s_st.wait(sy)
    RA = k.qA[0:1, 0:2048]
    RQA = k.qB[0:1, 0:2048]
    RC_ = k.qC[0:1, 0:2048]
    IVB = ROWB[0:1, 0, :]
    CHB = ROWB[0:1, 1, :]
    CLB = ROWB[0:1, 2, :]
    B2H = ROWB[0:1, 3, 0:1024]
    B2L = ROWB[0:1, 3, 1024:2048]
    k.s_ld.inc(sy.dma_start(out=RA, in_=st4[0, 1:2049].unsqueeze(0)), 16)
    k.s_ld.inc(sy.dma_start(out=RC_, in_=st4[1, 0:2048].unsqueeze(0)), 16)
    k.s_ld.wait(ve)
    ve.tensor_tensor(RA, RA, RC_, op=ALU.add)
    k.s_v.inc(ve.drain())
    k.s_v.wait(sy)
    k.s_ld.inc(sy.dma_start(out=RQA, in_=st4[2, 1:2049].unsqueeze(0)), 16)
    k.s_ld.inc(sy.dma_start(out=RC_, in_=st4[3, 0:2048].unsqueeze(0)), 16)
    k.s_ld.wait(ve)
    ve.tensor_tensor(RQA, RQA, RC_, op=ALU.add)
    ve.drain()
    ve.tensor_scalar(RA, RA, 1.0 / D, None, op0=ALU.mult)
    ve.tensor_scalar(RQA, RQA, 1.0 / D, None, op0=ALU.mult)
    ve.drain()
    ve.tensor_tensor(RC_, RA, RA, op=ALU.mult)
    ve.drain()
    ve.tensor_tensor(RQA, RQA, RC_, op=ALU.subtract)
    ve.drain()
    ve.tensor_scalar(RQA, RQA, 1e-5, None, op0=ALU.add)
    k.s_v.inc(ve.drain())
    k.s_v.wait(se)
    k.s_s.inc(se.activation(RQA, RQA, AF.Sqrt))
    k.s_s.wait(ve)
    ve.reciprocal(RQA, RQA)
    ve.drain()
    ve.tensor_tensor(RC_, RA, RQA, op=ALU.mult)
    ve.drain()
    ve.tensor_scalar(RC_, RC_, -1.0, None, op0=ALU.mult)
    ve.drain()
    ve.tensor_copy(IVB, RQA)
    ve.tensor_copy(CHB, RC_)
    ve.drain()
    ve.tensor_tensor(CLB, RC_, CHB, op=ALU.subtract)
    k.s_v.inc(ve.drain())
    b2f = h["b2"][l].rearrange("a b -> (a b)")
    B2F = k.qA[0:1, 0:1024]
    k.s_v.wait(sy)
    k.s_ld.inc(sy.dma_start(out=B2F, in_=b2f.unsqueeze(0)), 16)
    k.s_ld.wait(ve)
    ve.tensor_copy(B2H, B2F)
    ve.drain()
    ve.tensor_tensor(B2L, B2F, B2H, op=ALU.subtract)
    k.s_v.inc(ve.drain())

    w1src = h["w1"][l].rearrange("(a p) o -> p a o", p=128)
    GH = k.qCb[:, 4096:4608]
    AT = k.qCb[:, 4608:5120]

    for hf in range(2):
        ar.mark()
        XHh = ar.bf(8, 1024)
        ar.mark()
        XFMh = ar.bf(8, 1032)
        full_barrier(k, "tvs")
        k.s_ld.wait(se)
        k.s_ld.wait(ve)
        k.s_lg.wait(te)
        if hf == 0:
            ve.memset(XFMh[:, :, 0:1], 0.0)
        else:
            ve.tensor_copy(XFMh[:, :, 0:1].rearrange("p a 1 -> p a"), BCOL)
        k.s_v.inc(ve.drain())
        k.s_v.wait(te)
        for jj2 in range(8):
            j = 8 * hf + jj2
            pairs = [(XINT[:, j, 128 * fb:128 * fb + 128],
                      XFMh[:, fb, 1 + 128 * jj2:1 + 128 * jj2 + 128]) for fb in range(8)]
            emit_transposes(k, pairs, k.idb[:], deep=True)
        k.s_mm.wait(ve)
        if hf == 0:
            ve.drain()
            k.s_v.inc(ve.tensor_copy(BCOL, XFMh[:, :, 1024:1025].rearrange("p a 1 -> p a")))
        for tc in range(2):
            gcol = 1024 * hf + 512 * tc
            te.wait_ge(k.s_v.h, k.s_v.n)
            te.wait_ge(k.s_s.h, k.s_s.n)
            k.s_mm.inc(te.matmul(k.psB[:, 2, :], k.ones1b[0:1, 0:128],
                                 IVB[0:1, gcol:gcol + 512], start=True, stop=True))
            te.matmul(k.psB[:, 3, :], k.ones1b[0:1, 0:128],
                      CHB[0:1, gcol:gcol + 512], start=True, stop=False)
            k.s_mm.inc(te.matmul(k.psB[:, 3, :], k.ones1b[0:1, 0:128],
                                 CLB[0:1, gcol:gcol + 512], start=False, stop=True))
            k.s_mm.wait(ve)
            for ft in range(8):
                soff = 1 if ft < 4 else 0
                ve.tensor_tensor(XHh[:, ft, 512 * tc:512 * tc + 512],
                                 XFMh[:, ft, soff + 512 * tc: soff + 512 * tc + 512],
                                 k.psB[:, 2, :], op=ALU.mult)
            ve.drain()
            for ft in range(8):
                ve.tensor_tensor(XHh[:, ft, 512 * tc:512 * tc + 512],
                                 XHh[:, ft, 512 * tc:512 * tc + 512],
                                 k.psB[:, 3, :], op=ALU.add)
            k.s_v.inc(ve.drain())
            k.s_v.wait(se)
            for ft in range(8):
                se.activation(XHh[:, ft, 512 * tc:512 * tc + 512],
                              XHh[:, ft, 512 * tc:512 * tc + 512], AF.Identity,
                              scale=LNGC[:, ft:ft + 1], bias=LNBC[:, ft:ft + 1])
            k.s_s.inc(se.drain())
            k.s_s.wait(ve)
            k.s_s.wait(te)
        full_barrier(k, "tvs")
        ar.release()
        AHq = ar.bf(22, 512)
        W1S = ar.bf(2, 2, 8, 128)

        for qt in range(2):
            toff = 512 * qt
            glast = [0, 0]
            wsems = (k.s_lg, k.s_lh)
            for hc in range(22):
                ocw = 128 if hc < 21 else 42
                b = hc % 2
                ge.wait_ge(k.s_mm.h, glast[b])
                wsems[b].inc(ge.dma_start(out=W1S[:, 0, b, :, 0:ocw],
                                          in_=w1src[:, :, 128 * hc:128 * hc + ocw]), 16)
                wsems[b].inc(ge.dma_start(out=W1S[:, 1, b, :, 0:ocw],
                                          in_=w1src[:, :, 2730 + 128 * hc:2730 + 128 * hc + ocw]), 16)
                wsems[b].wait(te)
                te.wait_ge(k.s_g.h, k.s_g.n)
                te.wait_ge(k.s_s.h, k.s_s.n)
                for ag in range(2):
                    for ic in range(8):
                        mm = te.matmul(k.psA[0:ocw, ag, :], W1S[:, ag, b, ic, 0:ocw],
                                       XHh[:, ic, toff:toff + 512],
                                       start=(ic == 0), stop=(ic == 7))
                    k.s_mm.inc(mm)
                glast[b] = k.s_mm.n
                k.s_mm.wait(se)
                k.s_g.wait(se)
                se.activation(GH[0:ocw, :], k.psA[0:ocw, 1, :], AF.Gelu,
                              bias=B1GC[0:ocw, hc:hc + 1])
                se.activation(AT[0:ocw, :], k.psA[0:ocw, 0, :], AF.Identity,
                              bias=B1AC[0:ocw, hc:hc + 1])
                k.s_s.inc(se.drain())
                k.s_s.wait(ge)
                k.s_g.inc(ge.tensor_tensor(AHq[0:ocw, hc, :], AT[0:ocw, :], GH[0:ocw, :],
                                           op=ALU.mult))
            k.s_g.wait(te)
            k.s_lg.wait(te)
            for tch in range(4):
                j = 4 * (2 * hf + qt) + tch
                te.wait_ge(k.s_v.h, max(0, k.s_v.n - 1))
                for dc in range(2):
                    pb = k.psB[:, 2 * (tch % 2) + dc, :]
                    te.matmul(pb, k.ones1b[0:1, 0:128], B2H[0:1, 512 * dc:512 * dc + 512],
                              start=True, stop=False)
                    te.matmul(pb, k.ones1b[0:1, 0:128], B2L[0:1, 512 * dc:512 * dc + 512],
                              start=False, stop=False)
                    for hc in range(22):
                        kcw = 128 if hc < 21 else 42
                        mm = te.matmul(pb, AHq[0:kcw, hc, 128 * tch:128 * tch + 128],
                                       W2R[0:kcw, hc, 512 * dc:512 * dc + 512],
                                       start=False, stop=(hc == 21))
                    k.s_mm.inc(mm)
                k.s_mm.wait(ve)
                for dc in range(2):
                    ve.tensor_tensor(XINT[:, j, 512 * dc:512 * dc + 512],
                                     XINT[:, j, 512 * dc:512 * dc + 512],
                                     k.psB[:, 2 * (tch % 2) + dc, :], op=ALU.add)
                k.s_v.inc(ve.drain())
        full_barrier(k)
        ar.release()
    ar.release()



def prep_host_tensors(inputs):
    t = {}
    perm = []
    for hh in range(16):
        for half in range(2):
            base = 64 * hh + 32 * half
            perm += [base + i for i in range(0, 32, 2)]
            perm += [base + i for i in range(1, 32, 2)]
    permq = np.array(perm, dtype=np.int64)
    kperm = permq[:256]
    Wq = np.asarray(inputs["L0_attn_Wq"], np.float32)
    Wkv = np.asarray(inputs["L0_attn_Wkv"], np.float32)
    t["wqT"] = np.ascontiguousarray(Wq.T[:, permq])
    wk = Wkv[0:256][kperm]
    wv = Wkv[256:512]
    t["wkvT"] = np.ascontiguousarray(np.concatenate([wk, wv], 0).T)
    t["woT"] = np.ascontiguousarray(np.asarray(inputs["L0_attn_Wout"], np.float32).T)
    t["lam"] = np.ascontiguousarray(np.asarray(inputs["L0_attn_lambda_init"], np.float32).reshape(16, 1))
    t["rmsw"] = np.ascontiguousarray(
        np.tile(np.asarray(inputs["L0_attn_rms_weight"], np.float32)[None, :], (128, 1)))
    t["wg0"] = np.ascontiguousarray(np.asarray(inputs["L0_ffn_w_gate"], np.float32))
    t["wu0"] = np.ascontiguousarray(np.asarray(inputs["L0_ffn_w_up"], np.float32))
    t["wd0"] = np.ascontiguousarray(np.asarray(inputs["L0_ffn_w_down"], np.float32))
    t["wthT"] = np.ascontiguousarray(np.asarray(inputs["L1_attn_W_theta"], np.float32).T)
    t["wphT"] = np.ascontiguousarray(np.asarray(inputs["L1_attn_W_phi"], np.float32).T)
    t["wgT"] = np.ascontiguousarray(np.asarray(inputs["L1_attn_W_g"], np.float32).T)
    t["wqkv"] = np.ascontiguousarray(np.asarray(inputs["L1_ffn_wqkv"], np.float32))
    t["qs"] = np.ascontiguousarray(np.tile(np.asarray(inputs["L1_ffn_q_scale"], np.float32), (1, 2)))
    t["ks"] = np.ascontiguousarray(np.tile(np.asarray(inputs["L1_ffn_k_scale"], np.float32), (1, 2)))
    t["wo1"] = np.ascontiguousarray(np.asarray(inputs["L1_ffn_wo"], np.float32))
    t["lngT"] = np.ascontiguousarray(np.asarray(inputs["L1_ffn_ln_g"], np.float32).reshape(2, 8, 128).transpose(0, 2, 1))
    t["lnbT"] = np.ascontiguousarray(np.asarray(inputs["L1_ffn_ln_b"], np.float32).reshape(2, 8, 128).transpose(0, 2, 1))
    t["w1"] = np.ascontiguousarray(np.asarray(inputs["L1_ffn_w1"], np.float32))
    b1 = np.asarray(inputs["L1_ffn_b1"], np.float32)
    b1p = np.zeros((2, 2, 2816), np.float32)
    b1p[:, 0, 0:2730] = b1[:, 0:2730]
    b1p[:, 1, 0:2730] = b1[:, 2730:5460]
    t["b1ag"] = np.ascontiguousarray(b1p.reshape(2, 2, 22, 128).transpose(0, 1, 3, 2))
    t["w2"] = np.ascontiguousarray(np.asarray(inputs["L1_ffn_w2"], np.float32))
    t["b2"] = np.ascontiguousarray(np.asarray(inputs["L1_ffn_b2"], np.float32).reshape(2, 8, 128))
    return t


def build_program():
    nc = bass.Bass("TRN2", target_bir_lowering=False)
    ctx = ExitStack()
    h = {}
    h["x_in"] = nc.dram_tensor("x_in", [T, D], F32, kind="ExternalInput").ap()
    out_h = nc.dram_tensor("out", [T, D], F32, kind="ExternalOutput").ap()
    for name, shape in [
        ("wqT", [D, D]), ("wkvT", [D, 512]), ("woT", [D, D]),
        ("lam", [16, 1]), ("rmsw", [128, 64]),
        ("wg0", [2816, D]), ("wu0", [2816, D]), ("wd0", [D, 2816]),
        ("wthT", [D, 256]), ("wphT", [D, 256]), ("wgT", [D, D]),
        ("wqkv", [2, D, 1536]), ("qs", [2, 128]), ("ks", [2, 128]),
        ("wo1", [2, 512, D]),
        ("w1", [2, D, 5460]), ("b1ag", [2, 2, 128, 22]), ("w2", [2, 2730, D]),
        ("b2", [2, 8, 128]), ("lngT", [2, 128, 8]), ("lnbT", [2, 128, 8]),
    ]:
        h[name] = nc.dram_tensor(name, shape, F32, kind="ExternalInput").ap()
    h["wqgT_s"] = nc.dram_tensor("wqgT_s", [D, 2816], BF16, kind="Internal").ap()
    h["wquT_s"] = nc.dram_tensor("wquT_s", [D, 2816], BF16, kind="Internal").ap()
    h["wdqT_s"] = nc.dram_tensor("wdqT_s", [2816, D], BF16, kind="Internal").ap()
    h["x1_s"] = nc.dram_tensor("x1_s", [T, D], F32, kind="Internal").ap()

    k = KB(nc, ctx)
    k.out_h = out_h
    blk = ctx.enter_context(nc.Block())
    k.psA = k.ps("psA", [128, 4, 512])
    k.psB = k.ps("psB", [128, 4, 512])
    k.qA = k.sb("qA", [128, 2816], F32)
    k.qB = k.sb("qB", [128, 2816], F32)
    k.qC = k.sb("qC", [128, 2816], F32)
    k.qAb = k.qA[:].bitcast(BF16)
    k.qBb = k.qB[:].bitcast(BF16)
    k.qCb = k.qC[:].bitcast(BF16)
    build_constants(k)
    k.ar = Arena(k, ARENA_N)
    k.ar.mark()
    k.arXQF = k.ar.bf(8, T)
    k.ar.mark()
    k.arQ = k.ar.bf(8, T)
    k.arK2 = k.ar.bf(2, T)
    k.arVT = k.ar.bf(NT, 4, 68)

    emit_l0(k, h, out_h)

    ctx.close()
    return nc


def kernel(**inputs):
    nb = 8
    x_full = np.ascontiguousarray(np.asarray(inputs["L0_attn_x"], np.float32))
    nc = build_program()
    host_tensors = prep_host_tensors(inputs)
    in_maps = []
    for c in range(nb):
        m = {"x_in": np.ascontiguousarray(x_full[c])}
        m.update(host_tensors)
        in_maps.append(m)
    res = run_bass_kernel_spmd(nc, in_maps, core_ids=list(range(nb)))
    return np.stack([r["out"] for r in res.results], axis=0)
